# version 06
# baseline (speedup 1.0000x reference)
import contextlib

import numpy as np
import ml_dtypes

import concourse.bass as bass
import concourse.mybir as mybir
from concourse.bass_utils import run_bass_kernel_spmd

F32 = mybir.dt.float32
BF16 = mybir.dt.bfloat16
AF = mybir.ActivationFunctionType
ALU = mybir.AluOpType
AX = mybir.AxisListType

NCORES = 8
B = 4
S = 8192
SL = S // NCORES
D = 1024
N = 256
TOK = B * SL
E = 8
I = 2048
H4 = 4096
EPS = 1e-5
BIG = 1e30

ENGINES = ("sync", "tensor", "scalar", "vector", "gpsimd")


class Step:
    __slots__ = ("eng", "chan", "emit", "deps", "inc", "after", "needed", "waits")

    def __init__(self, eng, emit, deps, inc, chan=None):
        self.eng = eng
        self.chan = chan or (eng + ("_d" if inc == 16 else "_c"))
        self.emit = emit
        self.deps = deps
        self.inc = inc
        self.after = 0
        self.needed = False
        self.waits = []


class Slot:

    __slots__ = ("writer", "readers")

    def __init__(self):
        self.writer = None
        self.readers = []


class Plan:

    def __init__(self):
        self.per_engine = {e: [] for e in ENGINES}
        self._barrier = None
        self._barrier_pending = set()

    def set_barrier(self):
        b = [self.per_engine[e][-1] for e in ENGINES if self.per_engine[e]]
        self._barrier = b
        self._barrier_pending = set(ENGINES)

    def add(self, eng, emit, deps=(), dma=False, cc=False):
        deps = [d for d in deps if d is not None]
        if self._barrier is not None and eng in self._barrier_pending:
            deps = deps + self._barrier
            self._barrier_pending.discard(eng)
        s = Step(eng, emit, deps, 16 if (dma or cc) else 1,
                 chan=(eng + "_cc") if cc else None)
        self.per_engine[eng].append(s)
        return s

    def step(self, eng, emit, reads=(), writes=(), extra=(), dma=False, cc=False):
        deps = list(extra)
        for sl in reads:
            if sl.writer is not None:
                deps.append(sl.writer)
        for sl in writes:
            if sl.writer is not None:
                deps.append(sl.writer)
            deps.extend(sl.readers)
        st = self.add(eng, emit, deps, dma, cc)
        for sl in reads:
            sl.readers.append(st)
        for sl in writes:
            sl.writer = st
            sl.readers = []
        return st

    def finalize(self):
        for e in ENGINES:
            for s in self.per_engine[e]:
                if s.inc == 16:
                    s.needed = True
                for d in s.deps:
                    if not (d.eng == "tensor" and s.eng == "tensor"):
                        d.needed = True
        chans = {}
        for e in ENGINES:
            for s in self.per_engine[e]:
                if s.needed:
                    chans[s.chan] = chans.get(s.chan, 0) + s.inc
                s.after = chans.get(s.chan, 0)
        for e in ENGINES:
            prev_dma_after = 0
            for s in self.per_engine[e]:
                w = {}
                for d in s.deps:
                    if d.eng == "tensor" and s.eng == "tensor":
                        continue
                    assert d.needed and d.after > 0
                    w[d.chan] = max(w.get(d.chan, 0), d.after)
                if s.inc == 16 and s.chan.endswith("_d"):
                    if prev_dma_after:
                        w[s.chan] = max(w.get(s.chan, 0), prev_dma_after)
                    prev_dma_after = s.after
                s.waits = sorted(w.items())
        self.chans = chans

    def build(self, nc):
        self.finalize()
        with contextlib.ExitStack() as st:
            sems = {ch: st.enter_context(nc.semaphore(f"ch_{ch}"))
                    for ch in self.chans}
            block = st.enter_context(nc.Block())

            def section(eng_name):
                steps = self.per_engine[eng_name]

                def body(eng):
                    waited = {}
                    for s in steps:
                        for (src, val) in s.waits:
                            if val > waited.get(src, 0):
                                eng.wait_ge(sems[src], val)
                                waited[src] = val
                        inst = s.emit(eng)
                        if s.needed:
                            inst.then_inc(sems[s.chan], s.inc)
                    if eng_name == "gpsimd":
                        dch = "gpsimd_d"
                        tot = self.chans.get(dch, 0)
                        if tot:
                            eng.wait_ge(sems[dch], tot)
                return body

            for e in ENGINES:
                getattr(block, e)(section(e))


def _mk_nc():
    return bass.Bass("TRN2", target_bir_lowering=False)


class Ctx:

    def __init__(self):
        self.nc = _mk_nc()
        self.plan = Plan()
        self.stack = contextlib.ExitStack()
        self.scopes = [self.stack]
        self.pfx = ""
        self._cbuf = None
        self._cmap = {}

    def push_scope(self, pfx):
        self.scopes.append(contextlib.ExitStack())
        self.pfx = pfx

    def pop_scope(self):
        self.scopes.pop().close()
        self.pfx = ""
        self.plan.set_barrier()

    def cval(self, value):
        if value not in self._cmap:
            if self._cbuf is None:
                self._cbuf = self.sbuf("cconst", [128, 8], F32)
            k = len(self._cmap)
            ap = self._cbuf[:, k:k + 1]
            slot = Slot()
            self.plan.step("vector", lambda e, ap=ap, value=value: e.memset(
                ap, value), writes=[slot])
            self._cmap[value] = (ap, slot)
        return self._cmap[value]

    def sbuf(self, name, shape, dtype):
        return self.scopes[-1].enter_context(
            self.nc.sbuf_tensor(self.pfx + name, shape, dtype))

    def psum(self, name, shape, dtype=F32):
        return self.scopes[-1].enter_context(
            self.nc.psum_tensor(self.pfx + name, shape, dtype))

    def inp(self, name, shape, dtype=F32):
        return self.nc.dram_tensor(name, shape, dtype, kind="ExternalInput").ap()

    def out(self, name, shape, dtype=F32):
        return self.nc.dram_tensor(name, shape, dtype, kind="ExternalOutput").ap()

    def done(self):
        self.plan.build(self.nc)
        self.stack.close()
        return self.nc

    def softplus(self, out_ap, z_ap, m_ap, a_ap, wslots, rslots):
        p = self.plan
        oslot, mslot, aslot = wslots
        s1 = p.step("vector", lambda e: e.tensor_single_scalar(
            m_ap, z_ap, 0.0, ALU.max), reads=rslots, writes=[mslot])
        s2 = p.step("scalar", lambda e: e.activation(
            a_ap, z_ap, AF.Abs), reads=rslots, writes=[aslot])
        s3 = p.step("scalar", lambda e: e.activation(
            a_ap, a_ap, AF.Exp, scale=-1.0), reads=[], writes=[aslot])
        onea, onesl = self.cval(1.0)
        s4 = p.step("scalar", lambda e: e.activation(
            a_ap, a_ap, AF.Ln, bias=onea), reads=[onesl], writes=[aslot])
        s5 = p.step("vector", lambda e: e.tensor_add(
            out_ap, m_ap, a_ap), reads=[mslot, aslot], writes=[oslot])
        return s5


def _proj_setup(c, wd_dram, wi_dram, bd_dram, bi_dram, al_dram, tag):
    p = c.plan
    wdf = c.sbuf(f"wdf{tag}", [128, 8 * N], F32)
    wdb = c.sbuf(f"wdb{tag}", [128, 8 * N], BF16)
    wif = c.sbuf(f"wif{tag}", [128, 8 * N], F32)
    wib = c.sbuf(f"wib{tag}", [128, 8 * N], BF16)
    bd = c.sbuf(f"bd{tag}", [128, 2], F32)
    bi = c.sbuf(f"bi{tag}", [128, 2], F32)
    al = c.sbuf(f"al{tag}", [128, 2], F32)
    spa = c.sbuf(f"spa{tag}", [128, 2], F32)
    spm = c.sbuf(f"spm{tag}", [128, 2], F32)
    sl = {k: Slot() for k in ("wdf", "wdb", "wif", "wib", "bd", "bi", "al",
                              "spa", "spm", "spaa")}
    ld1 = p.step("sync", lambda e: e.dma_start(
        out=wdf[:, :], in_=wd_dram.rearrange("(dc p) n -> p dc n", p=128)),
        writes=[sl["wdf"]], dma=True)
    p.step("gpsimd", lambda e: e.tensor_copy(out=wdb[:, :], in_=wdf[:, :]),
           reads=[sl["wdf"]], writes=[sl["wdb"]])
    ld2 = p.step("sync", lambda e: e.dma_start(
        out=wif[:, :], in_=wi_dram.rearrange("(dc p) n -> p dc n", p=128)),
        writes=[sl["wif"]], dma=True)
    p.step("gpsimd", lambda e: e.tensor_copy(out=wib[:, :], in_=wif[:, :]),
           reads=[sl["wif"]], writes=[sl["wib"]])
    p.step("sync", lambda e: e.dma_start(out=bd[:, :], in_=bd_dram),
           writes=[sl["bd"]], dma=True)
    p.step("sync", lambda e: e.dma_start(out=bi[:, :], in_=bi_dram),
           writes=[sl["bi"]], dma=True)
    p.step("sync", lambda e: e.dma_start(out=al[:, :], in_=al_dram),
           writes=[sl["al"]], dma=True)
    c.softplus(spa[:, :], al[:, :], spm[:, :], spa[:, :],
               (sl["spaa"], sl["spm"], sl["spa"]), [sl["al"]])
    return dict(wdb=wdb, wib=wib, bd=bd, bi=bi, spa=spa, sl=sl)


def _proj_emit(c, pr, rhs_sb, rhs_slots, rhs_col0, dec_dram, drv_dram,
               b, s0, scratch, ps_d, ps_u, psl):
    p = c.plan
    wdb, wib, bd, bi, spa = pr["wdb"], pr["wib"], pr["bd"], pr["bi"], pr["spa"]
    prsl = pr["sl"]
    for ncase, (wsb, wslot, psb, pslot) in enumerate(
            [(wdb, prsl["wdb"], ps_d, psl[0]), (wib, prsl["wib"], ps_u, psl[1])]):
        for nch in range(2):
            for dc in range(8):
                lhsT = wsb[:, dc * N + nch * 128: dc * N + (nch + 1) * 128]
                rhs = rhs_sb[:, rhs_col0 + dc * 512: rhs_col0 + (dc + 1) * 512]
                p.step("tensor", lambda e, lhsT=lhsT, rhs=rhs, psb=psb, nch=nch,
                       dc=dc: e.matmul(psb[:, nch * 512:(nch + 1) * 512],
                                       lhsT, rhs, start=(dc == 0), stop=(dc == 7)),
                       reads=[wslot] + rhs_slots[dc], writes=[pslot[nch]])
    for nch in range(2):
        zd, zd_s = scratch["zd"]
        zu, zu_s = scratch["zu"]
        sm, sm_s = scratch["sm"]
        sa, sa_s = scratch["sa"]
        dr, dr_s = scratch["dr"]
        o = nch * 512
        p.step("scalar", lambda e, nch=nch, o=o: e.activation(
            zd[:, o:o + 512], ps_d[:, o:o + 512], AF.Identity, bias=bd[:, nch:nch + 1]),
            reads=[psl[0][nch], prsl["bd"]], writes=[zd_s[nch]])
        p.step("scalar", lambda e, nch=nch, o=o: e.activation(
            zu[:, o:o + 512], ps_u[:, o:o + 512], AF.Identity, bias=bi[:, nch:nch + 1]),
            reads=[psl[1][nch], prsl["bi"]], writes=[zu_s[nch]])
        p.step("vector", lambda e, o=o: e.tensor_single_scalar(
            sm[:, o:o + 512], zd[:, o:o + 512], 0.0, ALU.max),
            reads=[zd_s[nch]], writes=[sm_s[nch]])
        p.step("scalar", lambda e, o=o: e.activation(
            sa[:, o:o + 512], zd[:, o:o + 512], AF.Abs),
            reads=[zd_s[nch]], writes=[sa_s[nch]])
        p.step("scalar", lambda e, o=o: e.activation(
            sa[:, o:o + 512], sa[:, o:o + 512], AF.Exp, scale=-1.0),
            writes=[sa_s[nch]])
        onea, onesl = c.cval(1.0)
        p.step("scalar", lambda e, o=o, onea=onea: e.activation(
            sa[:, o:o + 512], sa[:, o:o + 512], AF.Ln, bias=onea),
            reads=[onesl], writes=[sa_s[nch]])
        p.step("vector", lambda e, o=o: e.tensor_add(
            zd[:, o:o + 512], sm[:, o:o + 512], sa[:, o:o + 512]),
            reads=[sm_s[nch], sa_s[nch]], writes=[zd_s[nch]])
        p.step("vector", lambda e, nch=nch, o=o: e.tensor_single_scalar(
            sm[:, o:o + 512], zd[:, o:o + 512], spa[:, nch:nch + 1], ALU.mult),
            reads=[zd_s[nch], prsl["spaa"]], writes=[sm_s[nch]])
        p.step("scalar", lambda e, o=o: e.activation(
            sa[:, o:o + 512], sm[:, o:o + 512], AF.Exp, scale=-1.0),
            reads=[sm_s[nch]], writes=[sa_s[nch]])
        p.step("vector", lambda e, o=o: e.tensor_mul(
            dr[:, o:o + 512], zd[:, o:o + 512], zu[:, o:o + 512]),
            reads=[zd_s[nch], zu_s[nch]], writes=[dr_s[nch]])
        p.step("gpsimd", lambda e, nch=nch, o=o, b=b, s0=s0: e.dma_start(
            out=dec_dram[b, nch * 128:(nch + 1) * 128, s0:s0 + 512],
            in_=sa[:, o:o + 512]), reads=[sa_s[nch]], dma=True)
        p.step("gpsimd", lambda e, nch=nch, o=o, b=b, s0=s0: e.dma_start(
            out=drv_dram[b, nch * 128:(nch + 1) * 128, s0:s0 + 512],
            in_=dr[:, o:o + 512]), reads=[dr_s[nch]], dma=True)


def build_A(ngroups=8, parts=("tr", "store", "proj")):
    c = Ctx()
    io = {}
    io["x"] = c.inp("x", [TOK, D])
    io["wd"] = c.inp("wd", [D, N])
    io["wi"] = c.inp("wi", [D, N])
    io["bd"] = c.inp("bd", [128, 2])
    io["bi"] = c.inp("bi", [128, 2])
    io["al"] = c.inp("al", [128, 2])
    io["idn"] = c.inp("idn", [128, 128], BF16)
    io["xnT"] = c.out("xnT", [D, TOK], BF16)
    io["dec"] = c.out("dec", [B, N, SL])
    io["drv"] = c.out("drv", [B, N, SL])
    emit_A(c, io, ngroups, parts)
    return c.done()


def emit_A(c, io, ngroups=8, parts=("tr", "store", "proj"), seqlen=SL):
    p = c.plan
    x = io["x"]
    wd = io["wd"]
    wi = io["wi"]
    bdd = io["bd"]
    bid = io["bi"]
    ald = io["al"]
    idn = io["idn"]
    xnT = io["xnT"]
    dec = io["dec"]
    drv = io["drv"]

    idn_sb = c.sbuf("idn_sb", [128, 128], BF16)
    idn_sl = Slot()
    p.step("sync", lambda e: e.dma_start(out=idn_sb[:, :], in_=idn),
           writes=[idn_sl], dma=True)
    pr = _proj_setup(c, wd, wi, bdd, bid, ald, "0") if "proj" in parts else None

    xs = c.sbuf("xs", [128, 8 * 1024], F32)
    xc = c.sbuf("xc", [128, 2 * 1024], F32)
    sq = c.sbuf("sq", [128, 2 * 1024], F32)
    cols = c.sbuf("cols", [128, 2 * 16], F32)
    xnb = c.sbuf("xnb", [128, 2 * 4 * 1024], BF16)
    xnt = c.sbuf("xnt", [128, 2 * 8 * 512], BF16)
    ps_tr = c.psum("ps_tr", [128, 2048], BF16)
    ps_d = c.psum("ps_d", [128, 1024], F32)
    ps_u = c.psum("ps_u", [128, 1024], F32)

    xs_sl = [[Slot() for _ in range(4)] for _ in range(2)]
    xc_sl = [Slot() for _ in range(2)]
    sq_sl = [Slot() for _ in range(2)]
    col_sl = [[Slot() for _ in range(4)] for _ in range(2)]
    xnb_sl = [[Slot() for _ in range(4)] for _ in range(2)]
    xnt_sl = [[[Slot() for _ in range(4)] for _ in range(8)] for _ in range(2)]
    tr_sl = [Slot() for _ in range(4)]
    psl = [[Slot(), Slot()], [Slot(), Slot()]]

    scr = {}
    for nm in ("zd", "zu", "sm", "sa", "dr"):
        t = c.sbuf(f"{nm}0s", [128, 2 * 1024], F32)
        scr[nm] = (t, [[Slot(), Slot()] for _ in range(2)])

    for g in range(ngroups):
        r = g % 2
        b = (g * 512) // seqlen
        s0 = (g * 512) % seqlen
        for st in range(4):
            tok0 = g * 512 + st * 128
            xo = (r * 4 + st) * 1024
            p.step("sync", lambda e, xo=xo, tok0=tok0: e.dma_start(
                out=xs[:, xo:xo + 1024], in_=x[tok0:tok0 + 128, :]),
                writes=[xs_sl[r][st]], dma=True)
            mu = cols[:, r * 16 + st * 4 + 0: r * 16 + st * 4 + 1]
            var = cols[:, r * 16 + st * 4 + 1: r * 16 + st * 4 + 2]
            std = cols[:, r * 16 + st * 4 + 2: r * 16 + st * 4 + 3]
            rstd = cols[:, r * 16 + st * 4 + 3: r * 16 + st * 4 + 4]
            cs = col_sl[r][st]
            p.step("vector", lambda e, mu=mu, xo=xo: e.reduce_sum(
                mu, xs[:, xo:xo + 1024], axis=AX.X),
                reads=[xs_sl[r][st]], writes=[cs])
            p.step("scalar", lambda e, mu=mu: e.mul(mu, mu, 1.0 / D), writes=[cs])
            p.step("vector", lambda e, mu=mu, xo=xo, r=r: e.tensor_single_scalar(
                xc[:, r * 1024:(r + 1) * 1024], xs[:, xo:xo + 1024], mu, ALU.subtract),
                reads=[xs_sl[r][st], cs], writes=[xc_sl[r]])
            p.step("scalar", lambda e, var=var, r=r: e.activation(
                sq[:, r * 1024:(r + 1) * 1024], xc[:, r * 1024:(r + 1) * 1024],
                AF.Square, accum_out=var),
                reads=[xc_sl[r]], writes=[sq_sl[r], cs])
            epsa, epssl = c.cval(EPS)
            p.step("scalar", lambda e, std=std, var=var, epsa=epsa: e.activation(
                std, var, AF.Sqrt, scale=1.0 / D, bias=epsa),
                reads=[epssl], writes=[cs])
            p.step("vector", lambda e, rstd=rstd, std=std: e.reciprocal(
                rstd, std), writes=[cs])
            xno = (r * 4 + st) * 1024
            p.step("vector", lambda e, rstd=rstd, xno=xno, r=r: e.tensor_single_scalar(
                xnb[:, xno:xno + 1024], xc[:, r * 1024:(r + 1) * 1024], rstd, ALU.mult),
                reads=[xc_sl[r], cs], writes=[xnb_sl[r][st]])
            for dc in range(8 if "tr" in parts else 0):
                pt = dc % 2
                trs = tr_sl[pt]
                p.step("tensor", lambda e, xno=xno, dc=dc, pt=pt: e.transpose(
                    ps_tr[:, pt * 1024: pt * 1024 + 128],
                    xnb[:, xno + dc * 128: xno + (dc + 1) * 128], idn_sb[:, :]),
                    reads=[xnb_sl[r][st], idn_sl], writes=[trs])
                dst = (r * 8 + dc) * 512 + st * 128
                p.step("scalar", lambda e, dst=dst, pt=pt: e.copy(
                    xnt[:, dst:dst + 128], ps_tr[:, pt * 1024: pt * 1024 + 128]),
                    reads=[trs], writes=[xnt_sl[r][dc][st]])
        for dc in range(8 if "store" in parts else 0):
            src = (r * 8 + dc) * 512
            p.step("gpsimd", lambda e, src=src, dc=dc, g=g: e.dma_start(
                out=xnT[dc * 128:(dc + 1) * 128, g * 512:(g + 1) * 512],
                in_=xnt[:, src:src + 512]),
                reads=xnt_sl[r][dc], dma=True)
        if "proj" in parts:
            scratch = {nm: (scr[nm][0][:, r * 1024:(r + 1) * 1024], scr[nm][1][r])
                       for nm in scr}
            rhs_slots = [xnt_sl[r][dc] for dc in range(8)]
            _proj_emit(c, pr, xnt, rhs_slots, r * 8 * 512, dec, drv, b, s0,
                       scratch, ps_d, ps_u, psl)
    return


def build_scan(chunk=2048):
    c = Ctx()
    p = c.plan
    dec = c.inp("dec", [128, S])
    drv = c.inp("drv", [128, S])
    hs = c.out("hs", [128, S])
    nch = S // chunk
    sd = c.sbuf("sd", [128, 2 * chunk], F32)
    sv = c.sbuf("sv", [128, 2 * chunk], F32)
    sh = c.sbuf("sh", [128, 2 * chunk], F32)
    sd_sl = [Slot(), Slot()]
    sv_sl = [Slot(), Slot()]
    sh_sl = [Slot(), Slot()]
    for ci in range(nch):
        r = ci % 2
        o = r * chunk
        p.step("sync", lambda e, o=o, ci=ci: e.dma_start(
            out=sd[:, o:o + chunk], in_=dec[:, ci * chunk:(ci + 1) * chunk]),
            writes=[sd_sl[r]], dma=True)
        p.step("sync", lambda e, o=o, ci=ci: e.dma_start(
            out=sv[:, o:o + chunk], in_=drv[:, ci * chunk:(ci + 1) * chunk]),
            writes=[sv_sl[r]], dma=True)
        init = 0.0 if ci == 0 else sh[:, (1 - r) * chunk + chunk - 1:(1 - r) * chunk + chunk]
        reads = [sd_sl[r], sv_sl[r]] + ([sh_sl[1 - r]] if ci else [])
        p.step("vector", lambda e, o=o, init=init: e.tensor_tensor_scan(
            out=sh[:, o:o + chunk], data0=sd[:, o:o + chunk],
            data1=sv[:, o:o + chunk], initial=init,
            op0=ALU.mult, op1=ALU.add),
            reads=reads, writes=[sh_sl[r]])
        p.step("gpsimd", lambda e, o=o, ci=ci: e.dma_start(
            out=hs[:, ci * chunk:(ci + 1) * chunk], in_=sh[:, o:o + chunk]),
            reads=[sh_sl[r]], dma=True)
    return c.done()


def build_C():
    c = Ctx()
    io = {}
    io["hs0"] = c.inp("hs0", [B, N, SL])
    io["f0t"] = c.inp("f0t", [N, B])
    io["wo"] = c.inp("wo", [N, D])
    io["ws"] = c.inp("ws", [N, D])
    io["bob"] = c.inp("bob", [128, 8])
    io["bsb"] = c.inp("bsb", [128, 8])
    io["wd"] = c.inp("wd", [D, N])
    io["wi"] = c.inp("wi", [D, N])
    io["bd"] = c.inp("bd", [128, 2])
    io["bi"] = c.inp("bi", [128, 2])
    io["al"] = c.inp("al", [128, 2])
    io["dec"] = c.out("dec", [B, N, SL])
    io["drv"] = c.out("drv", [B, N, SL])
    emit_C(c, io)
    return c.done()


def emit_C(c, io, ngroups=8, seqlen=SL):
    p = c.plan
    hs0 = io["hs0"]
    f0t = io["f0t"]
    wo = io["wo"]
    ws = io["ws"]
    bob = io["bob"]
    bsb = io["bsb"]
    wd = io["wd"]
    wi = io["wi"]
    bdd = io["bd"]
    bid = io["bi"]
    ald = io["al"]
    dec = io["dec"]
    drv = io["drv"]

    pr = _proj_setup(c, wd, wi, bdd, bid, ald, "1")

    wof = c.sbuf("wof", [128, 2 * D], F32)
    wob = c.sbuf("wob", [128, 2 * D], BF16)
    wsf = c.sbuf("wsf", [128, 2 * D], F32)
    wsb = c.sbuf("wsb", [128, 2 * D], BF16)
    f0f = c.sbuf("f0f", [128, 2 * B], F32)
    f0b = c.sbuf("f0b", [128, 2 * B], BF16)
    bo_sb = c.sbuf("bo_sb", [128, 8], F32)
    bs_sb = c.sbuf("bs_sb", [128, 8], F32)
    bsum = c.sbuf("bsum", [128, 8], F32)
    cls = c.sbuf("cls", [128, 32], F32)
    sl = {k: Slot() for k in ("wof", "wob", "wsf", "wsb", "f0f", "f0b",
                              "bo", "bs", "bsum", "cls")}
    for nm, dst, src in (("wof", wof, wo), ("wsf", wsf, ws)):
        p.step("sync", lambda e, dst=dst, src=src: e.dma_start(
            out=dst[:, :], in_=src.rearrange("(nc p) d -> p nc d", p=128)),
            writes=[sl[nm]], dma=True)
    p.step("gpsimd", lambda e: e.tensor_copy(out=wob[:, :], in_=wof[:, :]),
           reads=[sl["wof"]], writes=[sl["wob"]])
    p.step("gpsimd", lambda e: e.tensor_copy(out=wsb[:, :], in_=wsf[:, :]),
           reads=[sl["wsf"]], writes=[sl["wsb"]])
    p.step("sync", lambda e: e.dma_start(
        out=f0f[:, :], in_=f0t.rearrange("(nc p) b -> p nc b", p=128)),
        writes=[sl["f0f"]], dma=True)
    p.step("gpsimd", lambda e: e.tensor_copy(out=f0b[:, :], in_=f0f[:, :]),
           reads=[sl["f0f"]], writes=[sl["f0b"]])
    p.step("sync", lambda e: e.dma_start(out=bo_sb[:, :], in_=bob),
           writes=[sl["bo"]], dma=True)
    p.step("sync", lambda e: e.dma_start(out=bs_sb[:, :], in_=bsb),
           writes=[sl["bs"]], dma=True)
    p.step("vector", lambda e: e.tensor_add(bsum[:, :], bo_sb[:, :], bs_sb[:, :]),
           reads=[sl["bo"], sl["bs"]], writes=[sl["bsum"]])

    ps_c = c.psum("ps_c", [128, 512], F32)
    psc_sl = Slot()
    for dc in range(8):
        for nch in range(2):
            p.step("tensor", lambda e, dc=dc, nch=nch: e.matmul(
                ps_c[:, 0:4],
                wsb[:, nch * D + dc * 128: nch * D + (dc + 1) * 128],
                f0b[:, nch * B:(nch + 1) * B],
                start=(nch == 0), stop=(nch == 1)),
                reads=[sl["wsb"], sl["f0b"]], writes=[psc_sl])
        p.step("scalar", lambda e, dc=dc: e.activation(
            cls[:, dc * 4:(dc + 1) * 4], ps_c[:, 0:4],
            AF.Identity, bias=bsum[:, dc:dc + 1]),
            reads=[psc_sl, sl["bsum"]], writes=[sl["cls"]])

    hsf = c.sbuf("hsf", [128, 2 * 2 * 512], F32)
    hsb = c.sbuf("hsb", [128, 2 * 2 * 512], BF16)
    h1f = c.sbuf("h1f", [128, 2 * 512], F32)
    h1b = c.sbuf("h1b", [128, 2 * 8 * 512], BF16)
    ps_y = c.psum("ps_y", [128, 1024], F32)
    ps_d = c.psum("ps_d", [128, 1024], F32)
    ps_u = c.psum("ps_u", [128, 1024], F32)
    hsf_sl = [[Slot(), Slot()] for _ in range(2)]
    hsb_sl = [[Slot(), Slot()] for _ in range(2)]
    h1f_sl = [Slot(), Slot()]
    h1b_sl = [[Slot() for _ in range(8)] for _ in range(2)]
    psy_sl = [Slot(), Slot()]
    psl = [[Slot(), Slot()], [Slot(), Slot()]]
    scr = {}
    for nm in ("zd", "zu", "sm", "sa", "dr"):
        t = c.sbuf(f"{nm}1s", [128, 2 * 1024], F32)
        scr[nm] = (t, [[Slot(), Slot()] for _ in range(2)])

    for g in range(ngroups):
        r = g % 2
        b = (g * 512) // seqlen
        s0 = (g * 512) % seqlen
        for nch in range(2):
            o = (r * 2 + nch) * 512
            p.step("sync", lambda e, o=o, b=b, nch=nch, s0=s0: e.dma_start(
                out=hsf[:, o:o + 512],
                in_=hs0[b, nch * 128:(nch + 1) * 128, s0:s0 + 512]),
                writes=[hsf_sl[r][nch]], dma=True)
            p.step("gpsimd", lambda e, o=o: e.tensor_copy(
                out=hsb[:, o:o + 512], in_=hsf[:, o:o + 512]),
                reads=[hsf_sl[r][nch]], writes=[hsb_sl[r][nch]])
        for dc in range(8):
            pr_i = dc % 2
            for nch in range(2):
                p.step("tensor", lambda e, dc=dc, nch=nch, r=r, pr_i=pr_i: e.matmul(
                    ps_y[:, pr_i * 512:(pr_i + 1) * 512],
                    wob[:, nch * D + dc * 128: nch * D + (dc + 1) * 128],
                    hsb[:, (r * 2 + nch) * 512:(r * 2 + nch + 1) * 512],
                    start=(nch == 0), stop=(nch == 1)),
                    reads=[sl["wob"], hsb_sl[r][nch]], writes=[psy_sl[pr_i]])
            p.step("scalar", lambda e, dc=dc, b=b, r=r, pr_i=pr_i: e.activation(
                h1f[:, r * 512:(r + 1) * 512], ps_y[:, pr_i * 512:(pr_i + 1) * 512],
                AF.Identity, bias=cls[:, dc * 4 + b: dc * 4 + b + 1]),
                reads=[psy_sl[pr_i], sl["cls"]], writes=[h1f_sl[r]])
            p.step("vector", lambda e, dc=dc, r=r: e.tensor_copy(
                out=h1b[:, (r * 8 + dc) * 512:(r * 8 + dc + 1) * 512],
                in_=h1f[:, r * 512:(r + 1) * 512]),
                reads=[h1f_sl[r]], writes=[h1b_sl[r][dc]])
        scratch = {nm: (scr[nm][0][:, r * 1024:(r + 1) * 1024], scr[nm][1][r])
                   for nm in scr}
        rhs_slots = [[h1b_sl[r][dc]] for dc in range(8)]
        _proj_emit(c, pr, h1b, rhs_slots, r * 8 * 512, dec, drv, b, s0,
                   scratch, ps_d, ps_u, psl)
    return


def build_E1():
    c = Ctx()
    io = {}
    io["hs1"] = c.inp("hs1", [B, N, SL])
    io["f1t"] = c.inp("f1t", [N, B])
    io["wo"] = c.inp("wo", [N, D])
    io["ws"] = c.inp("ws", [N, D])
    io["bor"] = c.inp("bor", [1, D])
    io["bsr"] = c.inp("bsr", [1, D])
    io["s0"] = c.out("s0", [TOK, D])
    emit_E1(c, io)
    return c.done()


def emit_E1(c, io):
    p = c.plan
    hs1 = io["hs1"]
    f1t = io["f1t"]
    wo = io["wo"]
    ws = io["ws"]
    bor = io["bor"]
    bsr = io["bsr"]
    s0o = io["s0"]

    wof = c.sbuf("wof", [128, 2 * D], F32)
    wob = c.sbuf("wob", [128, 2 * D], BF16)
    wsf = c.sbuf("wsf", [128, 2 * D], F32)
    wsb = c.sbuf("wsb", [128, 2 * D], BF16)
    f1f = c.sbuf("f1f", [128, 2 * B], F32)
    f1b = c.sbuf("f1b", [128, 2 * B], BF16)
    brow = c.sbuf("brow", [1, D], F32)
    bsrow = c.sbuf("bsrow", [1, D], F32)
    ones4 = c.sbuf("ones4", [1, B], F32)
    ones1 = c.sbuf("ones1", [1, 128], F32)
    rall = c.sbuf("rall", [B, D], F32)
    rflat = c.sbuf("rflat", [1, B * D], F32)
    rhi = c.sbuf("rhi", [1, B * D], BF16)
    rhif = c.sbuf("rhif", [1, B * D], F32)
    rlo = c.sbuf("rlo", [1, B * D], BF16)
    onesb = c.sbuf("onesb", [1, 128], BF16)
    sl = {k: Slot() for k in ("wof", "wob", "wsf", "wsb", "f1f", "f1b", "bo",
                              "bs", "ones4", "ones1", "rall", "rflat", "rhi",
                              "rhif", "rlo", "onesb")}
    for nm, dst, src in (("wof", wof, wo), ("wsf", wsf, ws)):
        p.step("sync", lambda e, dst=dst, src=src: e.dma_start(
            out=dst[:, :], in_=src.rearrange("(nc p) d -> p nc d", p=128)),
            writes=[sl[nm]], dma=True)
    p.step("gpsimd", lambda e: e.tensor_copy(out=wob[:, :], in_=wof[:, :]),
           reads=[sl["wof"]], writes=[sl["wob"]])
    p.step("gpsimd", lambda e: e.tensor_copy(out=wsb[:, :], in_=wsf[:, :]),
           reads=[sl["wsf"]], writes=[sl["wsb"]])
    p.step("sync", lambda e: e.dma_start(
        out=f1f[:, :], in_=f1t.rearrange("(nc p) b -> p nc b", p=128)),
        writes=[sl["f1f"]], dma=True)
    p.step("gpsimd", lambda e: e.tensor_copy(out=f1b[:, :], in_=f1f[:, :]),
           reads=[sl["f1f"]], writes=[sl["f1b"]])
    p.step("sync", lambda e: e.dma_start(out=brow[:, :], in_=bor),
           writes=[sl["bo"]], dma=True)
    p.step("sync", lambda e: e.dma_start(out=bsrow[:, :], in_=bsr),
           writes=[sl["bs"]], dma=True)
    p.step("vector", lambda e: e.memset(ones4[:, :], 1.0), writes=[sl["ones4"]])
    p.step("vector", lambda e: e.memset(ones1[:, :], 1.0), writes=[sl["ones1"]])
    p.step("vector", lambda e: e.memset(onesb[:, :], 1.0), writes=[sl["onesb"]])

    ps_c = c.psum("ps_c", [128, 1024], F32)
    psc_sl = Slot()
    for dg in range(2):
        for nch in range(2):
            p.step("tensor", lambda e, dg=dg, nch=nch: e.matmul(
                ps_c[0:4, dg * 512:(dg + 1) * 512],
                f1b[:, nch * B:(nch + 1) * B],
                wsb[:, nch * D + dg * 512: nch * D + (dg + 1) * 512],
                start=(nch == 0), stop=False),
                reads=[sl["f1b"], sl["wsb"]], writes=[psc_sl])
        for bi_, (src, slot) in enumerate(((brow, sl["bo"]), (bsrow, sl["bs"]))):
            p.step("tensor", lambda e, dg=dg, src=src, bi_=bi_: e.matmul(
                ps_c[0:4, dg * 512:(dg + 1) * 512], ones4[:, :],
                src[:, dg * 512:(dg + 1) * 512], start=False, stop=(bi_ == 1)),
                reads=[sl["ones4"], slot], writes=[psc_sl])
    p.step("scalar", lambda e: e.copy(rall[:, :], ps_c[0:4, :]),
           reads=[psc_sl], writes=[sl["rall"]])
    for b in range(B):
        p.step("sync", lambda e, b=b: e.dma_start(
            out=rflat[:, b * D:(b + 1) * D], in_=rall[b:b + 1, :]),
            reads=[sl["rall"]], writes=[sl["rflat"]], dma=True)
    p.step("scalar", lambda e: e.copy(rhi[:, :], rflat[:, :]),
           reads=[sl["rflat"]], writes=[sl["rhi"]])
    p.step("scalar", lambda e: e.copy(rhif[:, :], rhi[:, :]),
           reads=[sl["rhi"]], writes=[sl["rhif"]])
    p.step("vector", lambda e: e.tensor_sub(rhif[:, :], rflat[:, :], rhif[:, :]),
           reads=[sl["rflat"]], writes=[sl["rhif"]])
    p.step("scalar", lambda e: e.copy(rlo[:, :], rhif[:, :]),
           reads=[sl["rhif"]], writes=[sl["rlo"]])

    hsf = c.sbuf("hsf", [128, 2 * 2 * 512], F32)
    hsb = c.sbuf("hsb", [128, 2 * 2 * 512], BF16)
    s0b = c.sbuf("s0b", [128, 2 * 4 * 1024], F32)
    ps_y = c.psum("ps_y", [128, 2 * 1024], F32)
    hsf_sl = [[Slot(), Slot()] for _ in range(2)]
    hsb_sl = [[Slot(), Slot()] for _ in range(2)]
    s0_sl = [[Slot() for _ in range(4)] for _ in range(2)]
    psy_sl = [[Slot(), Slot()], [Slot(), Slot()]]
    for g in range(8):
        r = g % 2
        b = g // 2
        s0 = (g % 2) * 512
        for nch in range(2):
            o = (r * 2 + nch) * 512
            p.step("sync", lambda e, o=o, b=b, nch=nch, s0=s0: e.dma_start(
                out=hsf[:, o:o + 512],
                in_=hs1[b, nch * 128:(nch + 1) * 128, s0:s0 + 512]),
                writes=[hsf_sl[r][nch]], dma=True)
            p.step("gpsimd", lambda e, o=o: e.tensor_copy(
                out=hsb[:, o:o + 512], in_=hsf[:, o:o + 512]),
                reads=[hsf_sl[r][nch]], writes=[hsb_sl[r][nch]])
        for st in range(4):
            rr = st % 2
            for dg in range(2):
                po = (rr * 2 + dg) * 512
                for nch in range(2):
                    p.step("tensor", lambda e, po=po, nch=nch, st=st, dg=dg, r=r: e.matmul(
                        ps_y[:, po:po + 512],
                        hsb[:, (r * 2 + nch) * 512 + st * 128:(r * 2 + nch) * 512 + (st + 1) * 128],
                        wob[:, nch * D + dg * 512: nch * D + (dg + 1) * 512],
                        start=(nch == 0), stop=False),
                        reads=[hsb_sl[r][nch], sl["wob"]], writes=[psy_sl[rr][dg]])
                for bi_, (src, slot) in enumerate(((rhi, sl["rhi"]), (rlo, sl["rlo"]))):
                    p.step("tensor", lambda e, po=po, src=src, b=b, dg=dg, bi_=bi_: e.matmul(
                        ps_y[:, po:po + 512], onesb[:, :],
                        src[:, b * D + dg * 512: b * D + (dg + 1) * 512],
                        start=False, stop=(bi_ == 1)),
                        reads=[sl["onesb"], slot], writes=[psy_sl[rr][dg]])
                p.step("scalar", lambda e, po=po, r=r, st=st, dg=dg: e.copy(
                    s0b[:, (r * 4 + st) * 1024 + dg * 512:(r * 4 + st) * 1024 + (dg + 1) * 512],
                    ps_y[:, po:po + 512]),
                    reads=[psy_sl[rr][dg]], writes=[s0_sl[r][st]])
            tok0 = g * 512 + st * 128
            p.step("gpsimd", lambda e, r=r, st=st, tok0=tok0: e.dma_start(
                out=s0o[tok0:tok0 + 128, :],
                in_=s0b[:, (r * 4 + st) * 1024:(r * 4 + st + 1) * 1024]),
                reads=[s0_sl[r][st]], dma=True)
    return


def build_E2():
    c = Ctx()
    io = {}
    io["xnT"] = c.inp("xnT", [D, TOK], BF16)
    io["gwt"] = c.inp("gwt", [D, E])
    io["w1t"] = c.inp("w1t", [E, D, I])
    io["w3t"] = c.inp("w3t", [E, D, I])
    io["w2t"] = c.inp("w2t", [E, I, D])
    io["g"] = c.out("g", [TOK, D])
    emit_E2(c, io)
    return c.done()


def emit_E2(c, io):
    p = c.plan
    xnT = io["xnT"]
    gwt = io["gwt"]
    w1t = io["w1t"]
    w3t = io["w3t"]
    w2t = io["w2t"]
    go = io["g"]

    gwf = c.sbuf("gwf", [128, 8 * E], F32)
    gwb = c.sbuf("gwb", [128, 8 * E], BF16)
    sl_gw = Slot()
    sl_gwb = Slot()
    p.step("sync", lambda e: e.dma_start(
        out=gwf[:, :], in_=gwt.rearrange("(dc p) ee -> p dc ee", p=128)),
        writes=[sl_gw], dma=True)
    p.step("gpsimd", lambda e: e.tensor_copy(out=gwb[:, :], in_=gwf[:, :]),
           reads=[sl_gw], writes=[sl_gwb])

    TG = 1024
    xnb = c.sbuf("xnb", [128, 8 * TG], BF16)
    outg = c.sbuf("outg", [128, 8 * 1024], F32)
    stg = c.sbuf("stg", [128, 4096], F32)
    w1h = c.sbuf("w1h", [128, 2 * 8192], BF16)
    w3h = c.sbuf("w3h", [128, 2 * 8192], BF16)
    w2b = c.sbuf("w2b", [128, 16 * 1024], BF16)
    actb = c.sbuf("actb", [128, 2 * 256], BF16)
    silu_t = c.sbuf("silu_t", [128, 2 * 256], F32)
    lg = c.sbuf("lg", [128, 2 * 8], F32)
    gsc = c.sbuf("gsc", [128, 6 * 8], F32)
    gs2 = c.sbuf("gs2", [128, 3 * 8], F32)
    gcol = c.sbuf("gcol", [128, 8], F32)
    dwb = c.sbuf("dwb", [128, 8 * 8], F32)
    gsb = c.sbuf("gsb", [128, 2 * 1024], F32)

    ps_l = c.psum("ps_l", [128, 512], F32)
    ps_h1 = c.psum("ps_h1", [128, 512], F32)
    ps_h3 = c.psum("ps_h3", [128, 512], F32)
    ps_y = c.psum("ps_y", [128, 4 * 512], F32)

    xnb_sl = [Slot() for _ in range(8)]
    outg_sl = [Slot() for _ in range(8)]
    stg_sl = Slot()
    w1h_sl = [[Slot(), Slot()] for _ in range(2)]
    w3h_sl = [[Slot(), Slot()] for _ in range(2)]
    w2b_sl = [Slot(), Slot()]
    actb_sl = [Slot(), Slot()]
    silu_sl = [Slot(), Slot()]
    lg_sl = [Slot(), Slot()]
    gsc_sl = [Slot() for _ in range(6)]
    gs2_sl = [Slot() for _ in range(3)]
    gcol_sl = [Slot() for _ in range(8)]
    dwb_sl = [Slot() for _ in range(8)]
    gsb_sl = [Slot(), Slot()]
    psl_l = Slot()
    psl_h1 = Slot()
    psl_h3 = Slot()
    psl_y = [Slot() for _ in range(4)]

    for tg in range(4):
        t0 = tg * TG
        for dc in range(8):
            p.step("sync", lambda e, dc=dc, t0=t0: e.dma_start(
                out=xnb[:, dc * TG:(dc + 1) * TG],
                in_=xnT[dc * 128:(dc + 1) * 128, t0:t0 + TG]),
                writes=[xnb_sl[dc]], dma=True)
        for tt in range(8):
            r = tt % 2
            lgr = lg[:, r * 8:(r + 1) * 8]
            lgs = lg_sl[r]
            for dc in range(8):
                p.step("tensor", lambda e, dc=dc, tt=tt: e.matmul(
                    ps_l[:, 0:E], xnb[:, dc * TG + tt * 128: dc * TG + (tt + 1) * 128],
                    gwb[:, dc * E:(dc + 1) * E], start=(dc == 0), stop=(dc == 7)),
                    reads=[xnb_sl[dc], sl_gwb], writes=[psl_l])
            p.step("scalar", lambda e, lgr=lgr: e.copy(lgr, ps_l[:, 0:E]),
                   reads=[psl_l], writes=[lgs])
            oh1, ab, f1, tml, m1, i1 = (gsc[:, k * 8:(k + 1) * 8] for k in range(6))
            s2v, oh2, e2v = (gs2[:, k * 8:(k + 1) * 8] for k in range(3))
            c1, c1n, c2, c3, c4, c4n, c5, c6 = (gcol[:, k:k + 1] for k in range(8))
            st_ = p.step
            st_("vector", lambda e, lgr=lgr, c1=c1: e.reduce_max(c1, lgr, axis=AX.X),
                reads=[lgs], writes=[gcol_sl[0]])
            st_("vector", lambda e, lgr=lgr, oh1=oh1, c1=c1: e.tensor_single_scalar(
                oh1, lgr, c1, ALU.is_equal), reads=[lgs, gcol_sl[0]], writes=[gsc_sl[0]])
            st_("scalar", lambda e, lgr=lgr, ab=ab: e.activation(ab, lgr, AF.Abs),
                reads=[lgs], writes=[gsc_sl[1]])
            st_("vector", lambda e, ab=ab, f1=f1, c1=c1: e.tensor_single_scalar(
                f1, ab, c1, ALU.max), reads=[gsc_sl[1], gcol_sl[0]], writes=[gsc_sl[2]])
            st_("vector", lambda e, lgr=lgr, tml=tml, c1=c1: e.tensor_scalar(
                tml, lgr, c1, -1.0, ALU.subtract, ALU.mult),
                reads=[lgs, gcol_sl[0]], writes=[gsc_sl[3]])
            st_("vector", lambda e, f1=f1, tml=tml, m1=m1: e.scalar_tensor_tensor(
                m1, f1, 0.02, tml, ALU.mult, ALU.is_lt),
                reads=[gsc_sl[2], gsc_sl[3]], writes=[gsc_sl[4]])
            st_("vector", lambda e, m1=m1, i1=i1: e.tensor_scalar(
                i1, m1, -1.0, 1.0, ALU.mult, ALU.add),
                reads=[gsc_sl[4]], writes=[gsc_sl[5]])
            st_("vector", lambda e, c1=c1, c1n=c1n: e.tensor_single_scalar(
                c1n, c1, -1.0, ALU.mult), reads=[gcol_sl[0]], writes=[gcol_sl[1]])
            st_("scalar", lambda e, lgr=lgr, e2v=e2v, c1n=c1n: e.activation(
                e2v, lgr, AF.Exp, bias=c1n), reads=[lgs, gcol_sl[1]], writes=[gs2_sl[2]])
            st_("vector", lambda e, e2v=e2v, i1=i1: e.tensor_mul(e2v, e2v, i1),
                reads=[gsc_sl[5]], writes=[gs2_sl[2]])
            st_("vector", lambda e, e2v=e2v, c2=c2: e.reduce_sum(c2, e2v, axis=AX.X),
                reads=[gs2_sl[2]], writes=[gcol_sl[2]])
            st_("vector", lambda e, c2=c2, c3=c3: e.reciprocal(c3, c2),
                reads=[gcol_sl[2]], writes=[gcol_sl[3]])
            st_("vector", lambda e, oh1=oh1, lgr=lgr, s2v=s2v: e.scalar_tensor_tensor(
                s2v, oh1, -BIG, lgr, ALU.mult, ALU.add),
                reads=[gsc_sl[0], lgs], writes=[gs2_sl[0]])
            st_("vector", lambda e, s2v=s2v, c4=c4: e.reduce_max(c4, s2v, axis=AX.X),
                reads=[gs2_sl[0]], writes=[gcol_sl[4]])
            st_("vector", lambda e, s2v=s2v, oh2=oh2, c4=c4: e.tensor_single_scalar(
                oh2, s2v, c4, ALU.is_equal), reads=[gs2_sl[0], gcol_sl[4]], writes=[gs2_sl[1]])
            st_("vector", lambda e, ab=ab, f1=f1, c4=c4: e.tensor_single_scalar(
                f1, ab, c4, ALU.max), reads=[gsc_sl[1], gcol_sl[4]], writes=[gsc_sl[2]])
            st_("vector", lambda e, lgr=lgr, tml=tml, c4=c4: e.tensor_scalar(
                tml, lgr, c4, -1.0, ALU.subtract, ALU.mult),
                reads=[lgs, gcol_sl[4]], writes=[gsc_sl[3]])
            st_("vector", lambda e, f1=f1, tml=tml, m1=m1: e.scalar_tensor_tensor(
                m1, f1, 0.02, tml, ALU.mult, ALU.is_lt),
                reads=[gsc_sl[2], gsc_sl[3]], writes=[gsc_sl[4]])
            st_("vector", lambda e, m1=m1, i1=i1: e.tensor_scalar(
                i1, m1, -1.0, 1.0, ALU.mult, ALU.add),
                reads=[gsc_sl[4]], writes=[gsc_sl[5]])
            st_("vector", lambda e, c4=c4, c4n=c4n: e.tensor_single_scalar(
                c4n, c4, -1.0, ALU.mult), reads=[gcol_sl[4]], writes=[gcol_sl[5]])
            st_("scalar", lambda e, s2v=s2v, e2v=e2v, c4n=c4n: e.activation(
                e2v, s2v, AF.Exp, bias=c4n), reads=[gs2_sl[0], gcol_sl[5]], writes=[gs2_sl[2]])
            st_("vector", lambda e, e2v=e2v, i1=i1: e.tensor_mul(e2v, e2v, i1),
                reads=[gsc_sl[5]], writes=[gs2_sl[2]])
            st_("vector", lambda e, e2v=e2v, c5=c5: e.reduce_sum(c5, e2v, axis=AX.X),
                reads=[gs2_sl[2]], writes=[gcol_sl[6]])
            st_("vector", lambda e, c5=c5, c6=c6: e.reciprocal(c6, c5),
                reads=[gcol_sl[6]], writes=[gcol_sl[7]])
            st_("vector", lambda e, oh1=oh1, c3=c3: e.tensor_single_scalar(
                oh1, oh1, c3, ALU.mult), reads=[gcol_sl[3]], writes=[gsc_sl[0]])
            st_("vector", lambda e, oh1=oh1, oh2=oh2, c6=c6, tt=tt: e.scalar_tensor_tensor(
                dwb[:, tt * 8:(tt + 1) * 8], oh2, c6, oh1, ALU.mult, ALU.add),
                reads=[gs2_sl[1], gcol_sl[7], gsc_sl[0]], writes=[dwb_sl[tt]])
        for tt in range(8):
            p.step("vector", lambda e, tt=tt: e.memset(
                outg[:, tt * 1024:(tt + 1) * 1024], 0.0), writes=[outg_sl[tt]])
        for ex in range(E):
            for half in range(2):
                for q in range(2):
                    p.step("sync", lambda e, ex=ex, half=half, q=q: e.dma_start(
                        out=stg[:, :],
                        in_=w1t[ex, :, half * 1024 + q * 512:
                                half * 1024 + (q + 1) * 512].rearrange(
                            "(dc p) i -> p dc i", p=128)),
                        writes=[stg_sl], dma=True)
                    p.step("gpsimd", lambda e, half=half, q=q: e.tensor_copy(
                        out=w1h[:, half * 8192 + q * 4096:
                                half * 8192 + (q + 1) * 4096].rearrange(
                            "p (dc i) -> p dc i", dc=8),
                        in_=stg[:, :].rearrange("p (dc i) -> p dc i", dc=8)),
                        reads=[stg_sl], writes=[w1h_sl[half][q]])
                    p.step("sync", lambda e, ex=ex, half=half, q=q: e.dma_start(
                        out=stg[:, :],
                        in_=w3t[ex, :, half * 1024 + q * 512:
                                half * 1024 + (q + 1) * 512].rearrange(
                            "(dc p) i -> p dc i", p=128)),
                        writes=[stg_sl], dma=True)
                    p.step("gpsimd", lambda e, half=half, q=q: e.tensor_copy(
                        out=w3h[:, half * 8192 + q * 4096:
                                half * 8192 + (q + 1) * 4096].rearrange(
                            "p (dc i) -> p dc i", dc=8),
                        in_=stg[:, :].rearrange("p (dc i) -> p dc i", dc=8)),
                        reads=[stg_sl], writes=[w3h_sl[half][q]])
                    p.step("sync", lambda e, ex=ex, half=half, q=q: e.dma_start(
                        out=stg[:, :],
                        in_=w2t[ex, half * 1024 + q * 512:
                                half * 1024 + (q + 1) * 512, :].rearrange(
                            "(ic p) dd -> p ic dd", p=128)),
                        writes=[stg_sl], dma=True)
                    p.step("gpsimd", lambda e, half=half, q=q: e.tensor_copy(
                        out=w2b[:, half * 8192 + q * 4096:
                                half * 8192 + (q + 1) * 4096],
                        in_=stg[:, :]),
                        reads=[stg_sl], writes=[w2b_sl[half]])
            for ts in range(4):
                to = ts * 256
                for ic in range(16):
                    half, icl = divmod(ic, 8)
                    q, icq = divmod(icl, 4)
                    w1c = half * 8192 + q * 4096
                    ar = ic % 2
                    for dc in range(8):
                        col = w1c + dc * 512 + icq * 128
                        p.step("tensor", lambda e, col=col, dc=dc, to=to: e.matmul(
                            ps_h1[:, 0:256],
                            w1h[:, col:col + 128],
                            xnb[:, dc * TG + to: dc * TG + to + 256],
                            start=(dc == 0), stop=(dc == 7)),
                            reads=[w1h_sl[half][q], xnb_sl[dc]], writes=[psl_h1])
                        p.step("tensor", lambda e, col=col, dc=dc, to=to: e.matmul(
                            ps_h3[:, 0:256],
                            w3h[:, col:col + 128],
                            xnb[:, dc * TG + to: dc * TG + to + 256],
                            start=(dc == 0), stop=(dc == 7)),
                            reads=[w3h_sl[half][q], xnb_sl[dc]], writes=[psl_h3])
                    p.step("scalar", lambda e, ar=ar: e.activation(
                        silu_t[:, ar * 256:(ar + 1) * 256], ps_h1[:, 0:256], AF.Silu),
                        reads=[psl_h1], writes=[silu_sl[ar]])
                    p.step("vector", lambda e, ar=ar: e.tensor_mul(
                        actb[:, ar * 256:(ar + 1) * 256],
                        silu_t[:, ar * 256:(ar + 1) * 256], ps_h3[:, 0:256]),
                        reads=[silu_sl[ar], psl_h3], writes=[actb_sl[ar]])
                    for st4 in range(2):
                        for dg in range(2):
                            yk = st4 * 2 + dg
                            p.step("tensor", lambda e, ar=ar, st4=st4, dg=dg,
                                   yk=yk, ic=ic, half=half: e.matmul(
                                ps_y[:, yk * 512:(yk + 1) * 512],
                                actb[:, ar * 256 + st4 * 128: ar * 256 + (st4 + 1) * 128],
                                w2b[:, ic * 1024 + dg * 512: ic * 1024 + (dg + 1) * 512],
                                start=(ic == 0), stop=(ic == 15)),
                                reads=[actb_sl[ar], w2b_sl[half]], writes=[psl_y[yk]])
                for st4 in range(2):
                    tt = (to + st4 * 128) // 128
                    for dg in range(2):
                        yk = st4 * 2 + dg
                        p.step("vector", lambda e, tt=tt, dg=dg, ex=ex, yk=yk: e.scalar_tensor_tensor(
                            outg[:, tt * 1024 + dg * 512: tt * 1024 + (dg + 1) * 512],
                            ps_y[:, yk * 512:(yk + 1) * 512],
                            dwb[:, tt * 8 + ex: tt * 8 + ex + 1],
                            outg[:, tt * 1024 + dg * 512: tt * 1024 + (dg + 1) * 512],
                            ALU.mult, ALU.add),
                            reads=[psl_y[yk], dwb_sl[tt]], writes=[outg_sl[tt]])
        for tt in range(8):
            r = tt % 2
            p.step("scalar", lambda e, tt=tt, r=r: e.activation(
                gsb[:, r * 1024:(r + 1) * 1024], outg[:, tt * 1024:(tt + 1) * 1024],
                AF.Silu), reads=[outg_sl[tt]], writes=[gsb_sl[r]])
            p.step("gpsimd", lambda e, tt=tt, r=r, t0=t0: e.dma_start(
                out=go[t0 + tt * 128: t0 + (tt + 1) * 128, :],
                in_=gsb[:, r * 1024:(r + 1) * 1024]),
                reads=[gsb_sl[r]], dma=True)
    return


def build_E3():
    c = Ctx()
    io = {}
    io["x"] = c.inp("x", [TOK, D])
    io["s0"] = c.inp("s0", [TOK, D])
    io["g"] = c.inp("g", [TOK, D])
    io["idn"] = c.inp("idn", [128, 128], BF16)
    io["x1"] = c.out("x1", [TOK, D])
    io["xn1"] = c.out("xn1", [TOK, D])
    io["xn1T"] = c.out("xn1T", [D, TOK], BF16)
    emit_E3(c, io)
    return c.done()


def emit_E3(c, io):
    p = c.plan
    x = io["x"]
    s0i = io["s0"]
    gi = io["g"]
    idn = io["idn"]
    x1o = io["x1"]
    xn1o = io["xn1"]
    xn1T = io["xn1T"]

    idn_sb = c.sbuf("idn_sb", [128, 128], BF16)
    idn_sl = Slot()
    p.step("sync", lambda e: e.dma_start(out=idn_sb[:, :], in_=idn),
           writes=[idn_sl], dma=True)
    xs = c.sbuf("xs", [128, 2 * 1024], F32)
    ss = c.sbuf("ss", [128, 2 * 1024], F32)
    gs = c.sbuf("gs", [128, 2 * 1024], F32)
    x1t = c.sbuf("x1t", [128, 2 * 1024], F32)
    xc = c.sbuf("xc", [128, 2 * 1024], F32)
    sq = c.sbuf("sq", [128, 2 * 1024], F32)
    xnf = c.sbuf("xnf", [128, 2 * 1024], F32)
    xnb = c.sbuf("xnb", [128, 2 * 1024], BF16)
    xnt = c.sbuf("xnt", [128, 2 * 8 * 512], BF16)
    cols = c.sbuf("cols", [128, 2 * 4], F32)
    ps_tr = c.psum("ps_tr", [128, 2048], BF16)
    sls = {nm: [Slot(), Slot()] for nm in
           ("xs", "ss", "gs", "x1t", "xc", "sq", "xnf", "xnb", "cols")}
    xnt_sl = [[[Slot() for _ in range(4)] for _ in range(8)] for _ in range(2)]
    tr_sl = [Slot() for _ in range(4)]

    for g in range(8):
        rg = g % 2
        for st in range(4):
            tok0 = g * 512 + st * 128
            r = st % 2
            o = r * 1024
            for nm, buf, src in (("xs", xs, x), ("ss", ss, s0i), ("gs", gs, gi)):
                p.step("sync", lambda e, buf=buf, src=src, o=o, tok0=tok0: e.dma_start(
                    out=buf[:, o:o + 1024], in_=src[tok0:tok0 + 128, :]),
                    writes=[sls[nm][r]], dma=True)
            p.step("vector", lambda e, o=o: e.tensor_mul(
                gs[:, o:o + 1024], ss[:, o:o + 1024], gs[:, o:o + 1024]),
                reads=[sls["ss"][r]], writes=[sls["gs"][r]])
            p.step("vector", lambda e, o=o: e.tensor_add(
                x1t[:, o:o + 1024], xs[:, o:o + 1024], gs[:, o:o + 1024]),
                reads=[sls["xs"][r], sls["gs"][r]], writes=[sls["x1t"][r]])
            p.step("gpsimd", lambda e, o=o, tok0=tok0: e.dma_start(
                out=x1o[tok0:tok0 + 128, :], in_=x1t[:, o:o + 1024]),
                reads=[sls["x1t"][r]], dma=True)
            mu = cols[:, r * 4 + 0: r * 4 + 1]
            var = cols[:, r * 4 + 1: r * 4 + 2]
            std = cols[:, r * 4 + 2: r * 4 + 3]
            rstd = cols[:, r * 4 + 3: r * 4 + 4]
            cs = sls["cols"][r]
            p.step("vector", lambda e, mu=mu, o=o: e.reduce_sum(
                mu, x1t[:, o:o + 1024], axis=AX.X),
                reads=[sls["x1t"][r]], writes=[cs])
            p.step("scalar", lambda e, mu=mu: e.mul(mu, mu, 1.0 / D), writes=[cs])
            p.step("vector", lambda e, mu=mu, o=o: e.tensor_single_scalar(
                xc[:, o:o + 1024], x1t[:, o:o + 1024], mu, ALU.subtract),
                reads=[sls["x1t"][r], cs], writes=[sls["xc"][r]])
            p.step("scalar", lambda e, var=var, o=o: e.activation(
                sq[:, o:o + 1024], xc[:, o:o + 1024], AF.Square, accum_out=var),
                reads=[sls["xc"][r]], writes=[sls["sq"][r], cs])
            epsa, epssl = c.cval(EPS)
            p.step("scalar", lambda e, std=std, var=var, epsa=epsa: e.activation(
                std, var, AF.Sqrt, scale=1.0 / D, bias=epsa),
                reads=[epssl], writes=[cs])
            p.step("vector", lambda e, rstd=rstd, std=std: e.reciprocal(
                rstd, std), writes=[cs])
            p.step("vector", lambda e, rstd=rstd, o=o: e.tensor_single_scalar(
                xnf[:, o:o + 1024], xc[:, o:o + 1024], rstd, ALU.mult),
                reads=[sls["xc"][r], cs], writes=[sls["xnf"][r]])
            p.step("gpsimd", lambda e, o=o, tok0=tok0: e.dma_start(
                out=xn1o[tok0:tok0 + 128, :], in_=xnf[:, o:o + 1024]),
                reads=[sls["xnf"][r]], dma=True)
            p.step("vector", lambda e, o=o: e.tensor_copy(
                out=xnb[:, o:o + 1024], in_=xnf[:, o:o + 1024]),
                reads=[sls["xnf"][r]], writes=[sls["xnb"][r]])
            for dc in range(8):
                pt = dc % 2
                p.step("tensor", lambda e, o=o, dc=dc, pt=pt: e.transpose(
                    ps_tr[:, pt * 1024: pt * 1024 + 128],
                    xnb[:, o + dc * 128: o + (dc + 1) * 128], idn_sb[:, :]),
                    reads=[sls["xnb"][r], idn_sl], writes=[tr_sl[pt]])
                dst = (rg * 8 + dc) * 512 + st * 128
                p.step("scalar", lambda e, dst=dst, pt=pt: e.copy(
                    xnt[:, dst:dst + 128], ps_tr[:, pt * 1024: pt * 1024 + 128]),
                    reads=[tr_sl[pt]], writes=[xnt_sl[rg][dc][st]])
        for dc in range(8):
            src = (rg * 8 + dc) * 512
            p.step("gpsimd", lambda e, src=src, dc=dc, g=g: e.dma_start(
                out=xn1T[dc * 128:(dc + 1) * 128, g * 512:(g + 1) * 512],
                in_=xnt[:, src:src + 512]),
                reads=xnt_sl[rg][dc], dma=True)
    return


def build_E4():
    c = Ctx()
    io = {}
    io["xn1"] = c.inp("xn1", [TOK, D])
    io["xn1T"] = c.inp("xn1T", [D, TOK], BF16)
    io["wqt"] = c.inp("wqt", [D, D])
    io["wkt"] = c.inp("wkt", [D, D])
    io["s1"] = c.out("s1", [TOK, D])
    emit_E4(c, io)
    return c.done()


def emit_E4(c, io):
    p = c.plan
    xn1 = io["xn1"]
    xn1T = io["xn1T"]
    wqt = io["wqt"]
    wkt = io["wkt"]
    s1o = io["s1"]

    stg = c.sbuf("stg", [128, 4096], F32)
    wqb = c.sbuf("wqb", [128, 8 * 1024], BF16)
    wkb = c.sbuf("wkb", [128, 8 * 1024], BF16)
    xts = c.sbuf("xts", [128, 8 * 4096], BF16)
    stg_sl = Slot()
    wq_sl = Slot()
    wk_sl = Slot()
    xts_sl = Slot()
    for wsrc, wdst, wsl in ((wqt, wqb, wq_sl), (wkt, wkb, wk_sl)):
        for dg in range(2):
            p.step("sync", lambda e, wsrc=wsrc, dg=dg: e.dma_start(
                out=stg[:, :],
                in_=wsrc[:, dg * 512:(dg + 1) * 512].rearrange(
                    "(dc p) ee -> p dc ee", p=128)),
                writes=[stg_sl], dma=True)
            p.step("gpsimd", lambda e, wdst=wdst, dg=dg: e.tensor_copy(
                out=wdst[:, :].rearrange("p (dc ee) -> p dc ee",
                                         ee=1024)[:, :, dg * 512:(dg + 1) * 512],
                in_=stg[:, :].rearrange("p (dc ee) -> p dc ee", dc=8)),
                reads=[stg_sl], writes=[wsl])
    p.step("sync", lambda e: e.dma_start(
        out=xts[:, :], in_=xn1T.rearrange("(dc p) t -> p dc t", p=128)),
        writes=[xts_sl], dma=True)

    qsb = c.sbuf("qsb", [128, 4 * 1024], F32)
    ksb = c.sbuf("ksb", [128, 4 * 1024], F32)
    xtt = c.sbuf("xtt", [128, 4 * 1024], F32)
    qkw = c.sbuf("qkw", [128, 1024], F32)
    sc = c.sbuf("sc", [128, 16], F32)
    th = c.sbuf("th", [128, 16], F32)
    exb = c.sbuf("exb", [128, 16], F32)
    alb = c.sbuf("alb", [128, 16], F32)
    mcol = c.sbuf("mcol", [128, 4], F32)
    ob = c.sbuf("ob", [128, 2 * 1024], F32)
    ps_q = c.psum("ps_q", [128, 512], F32)
    ps_k = c.psum("ps_k", [128, 512], F32)
    q_sl = [Slot() for _ in range(4)]
    k_sl = [Slot() for _ in range(4)]
    xt_sl = [Slot() for _ in range(4)]
    qkw_sl = Slot()
    sc_sl = [Slot() for _ in range(16)]
    th_sl = Slot()
    ex_sl = [Slot() for _ in range(4)]
    al_sl = [Slot() for _ in range(4)]
    mc_sl = [Slot() for _ in range(4)]
    ob_sl = [Slot(), Slot()]
    psq_sl = Slot()
    psk_sl = Slot()

    for j in range(8):
        for s in range(4):
            t = s * 8 + j
            for dg in range(2):
                for dc in range(8):
                    p.step("tensor", lambda e, dc=dc, t=t, dg=dg: e.matmul(
                        ps_q[:, :], xts[:, dc * 4096 + t * 128: dc * 4096 + (t + 1) * 128],
                        wqb[:, dc * 1024 + dg * 512: dc * 1024 + (dg + 1) * 512],
                        start=(dc == 0), stop=(dc == 7)),
                        reads=[xts_sl, wq_sl], writes=[psq_sl])
                    p.step("tensor", lambda e, dc=dc, t=t, dg=dg: e.matmul(
                        ps_k[:, :], xts[:, dc * 4096 + t * 128: dc * 4096 + (t + 1) * 128],
                        wkb[:, dc * 1024 + dg * 512: dc * 1024 + (dg + 1) * 512],
                        start=(dc == 0), stop=(dc == 7)),
                        reads=[xts_sl, wk_sl], writes=[psk_sl])
                p.step("scalar", lambda e, s=s, dg=dg: e.copy(
                    qsb[:, s * 1024 + dg * 512: s * 1024 + (dg + 1) * 512], ps_q[:, :]),
                    reads=[psq_sl], writes=[q_sl[s]])
                p.step("scalar", lambda e, s=s, dg=dg: e.copy(
                    ksb[:, s * 1024 + dg * 512: s * 1024 + (dg + 1) * 512], ps_k[:, :]),
                    reads=[psk_sl], writes=[k_sl[s]])
            tok0 = t * 128
            p.step("sync", lambda e, s=s, tok0=tok0: e.dma_start(
                out=xtt[:, s * 1024:(s + 1) * 1024], in_=xn1[tok0:tok0 + 128, :]),
                writes=[xt_sl[s]], dma=True)
        for s in range(4):
            for t4 in range(4):
                k = s * 4 + t4
                p.step("vector", lambda e, s=s, t4=t4: e.tensor_mul(
                    qkw[:, :], qsb[:, s * 1024:(s + 1) * 1024],
                    ksb[:, t4 * 1024:(t4 + 1) * 1024]),
                    reads=[q_sl[s], k_sl[t4]], writes=[qkw_sl])
                p.step("vector", lambda e, k=k: e.reduce_sum(
                    sc[:, k:k + 1], qkw[:, :], axis=AX.X),
                    reads=[qkw_sl], writes=[sc_sl[k]])
        p.step("scalar", lambda e: e.activation(th[:, :], sc[:, :], AF.Tanh,
                                                scale=1.0 / 32.0),
               reads=sc_sl, writes=[th_sl])
        for s in range(4):
            p.step("vector", lambda e, s=s: e.reduce_max(
                mcol[:, s:s + 1], th[:, s * 4:(s + 1) * 4], axis=AX.X),
                reads=[th_sl], writes=[mc_sl[s]])
            p.step("vector", lambda e, s=s: e.tensor_single_scalar(
                mcol[:, s:s + 1], mcol[:, s:s + 1], -1.0, ALU.mult),
                writes=[mc_sl[s]])
            p.step("scalar", lambda e, s=s: e.activation(
                exb[:, s * 4:(s + 1) * 4], th[:, s * 4:(s + 1) * 4], AF.Exp,
                bias=mcol[:, s:s + 1]),
                reads=[th_sl, mc_sl[s]], writes=[ex_sl[s]])
            p.step("vector", lambda e, s=s: e.reduce_sum(
                mcol[:, s:s + 1], exb[:, s * 4:(s + 1) * 4], axis=AX.X),
                reads=[ex_sl[s]], writes=[mc_sl[s]])
            p.step("vector", lambda e, s=s: e.reciprocal(
                mcol[:, s:s + 1], mcol[:, s:s + 1]), writes=[mc_sl[s]])
            p.step("vector", lambda e, s=s: e.tensor_single_scalar(
                alb[:, s * 4:(s + 1) * 4], exb[:, s * 4:(s + 1) * 4],
                mcol[:, s:s + 1], ALU.mult),
                reads=[ex_sl[s], mc_sl[s]], writes=[al_sl[s]])
        for s in range(4):
            r = s % 2
            o = r * 1024
            p.step("vector", lambda e, s=s, o=o: e.tensor_single_scalar(
                ob[:, o:o + 1024], xtt[:, 0:1024], alb[:, s * 4:s * 4 + 1], ALU.mult),
                reads=[xt_sl[0], al_sl[s]], writes=[ob_sl[r]])
            for t4 in range(1, 4):
                p.step("vector", lambda e, s=s, t4=t4, o=o: e.scalar_tensor_tensor(
                    ob[:, o:o + 1024], xtt[:, t4 * 1024:(t4 + 1) * 1024],
                    alb[:, s * 4 + t4: s * 4 + t4 + 1], ob[:, o:o + 1024],
                    ALU.mult, ALU.add),
                    reads=[xt_sl[t4], al_sl[s]], writes=[ob_sl[r]])
            tok0 = (s * 8 + j) * 128
            p.step("gpsimd", lambda e, o=o, tok0=tok0: e.dma_start(
                out=s1o[tok0:tok0 + 128, :], in_=ob[:, o:o + 1024]),
                reads=[ob_sl[r]], dma=True)
    return


def build_E5a():
    c = Ctx()
    io = {}
    io["xn1"] = c.inp("xn1", [TOK, D])
    io["w1t"] = c.inp("w1t", [D, H4])
    io["b1c"] = c.inp("b1c", [128, 32])
    io["g1r"] = c.inp("g1r", [1, D])
    io["b1r"] = c.inp("b1r", [1, D])
    io["idn"] = c.inp("idn", [128, 128], BF16)
    io["hG"] = c.out("hG", [H4, TOK])
    emit_E5a(c, io)
    return c.done()


def emit_E5a(c, io):
    p = c.plan
    xn1 = io["xn1"]
    w1t = io["w1t"]
    b1c = io["b1c"]
    g1r = io["g1r"]
    b1r = io["b1r"]
    idn = io["idn"]
    hG = io["hG"]

    idn_sb = c.sbuf("idn_sb", [128, 128], BF16)
    idn_sl = Slot()
    p.step("sync", lambda e: e.dma_start(out=idn_sb[:, :], in_=idn),
           writes=[idn_sl], dma=True)
    ones1 = c.sbuf("ones1", [1, 128], F32)
    onesc = c.sbuf("onesc", [128, 1], F32)
    o1_sl = Slot()
    oc_sl = Slot()
    p.step("vector", lambda e: e.memset(ones1[:, :], 1.0), writes=[o1_sl])
    p.step("vector", lambda e: e.memset(onesc[:, :], 1.0), writes=[oc_sl])
    g1row = c.sbuf("g1row", [1, D], F32)
    b1row = c.sbuf("b1row", [1, D], F32)
    g1B = c.sbuf("g1B", [128, D], F32)
    b1B = c.sbuf("b1B", [128, D], F32)
    b1cs = c.sbuf("b1cs", [128, 32], F32)
    grow_sl = Slot()
    brow_sl = Slot()
    g1B_sl = Slot()
    b1B_sl = Slot()
    b1c_sl = Slot()
    p.step("sync", lambda e: e.dma_start(out=g1row[:, :], in_=g1r),
           writes=[grow_sl], dma=True)
    p.step("sync", lambda e: e.dma_start(out=b1row[:, :], in_=b1r),
           writes=[brow_sl], dma=True)
    p.step("sync", lambda e: e.dma_start(out=b1cs[:, :], in_=b1c),
           writes=[b1c_sl], dma=True)
    ps_b = c.psum("ps_b", [128, 1024], F32)
    psb_sl = Slot()
    for dg in range(2):
        p.step("tensor", lambda e, dg=dg: e.matmul(
            ps_b[:, dg * 512:(dg + 1) * 512], ones1[:, :],
            g1row[:, dg * 512:(dg + 1) * 512], start=True, stop=True),
            reads=[o1_sl, grow_sl], writes=[psb_sl])
    p.step("scalar", lambda e: e.copy(g1B[:, :], ps_b[:, :]),
           reads=[psb_sl], writes=[g1B_sl])
    for dg in range(2):
        p.step("tensor", lambda e, dg=dg: e.matmul(
            ps_b[:, dg * 512:(dg + 1) * 512], ones1[:, :],
            b1row[:, dg * 512:(dg + 1) * 512], start=True, stop=True),
            reads=[o1_sl, brow_sl], writes=[psb_sl])
    p.step("scalar", lambda e: e.copy(b1B[:, :], ps_b[:, :]),
           reads=[psb_sl], writes=[b1B_sl])

    stg = c.sbuf("stg", [128, 4096], F32)
    stg_sl = [Slot()]
    mc8 = c.sbuf("mc8", [128, 8], F32)
    mc8_sl = [Slot() for _ in range(8)]
    mtot = c.sbuf("mtot", [128, 1], F32)
    mval = c.sbuf("mval", [1, 1], F32)
    m1P = c.sbuf("m1P", [128, 1], F32)
    thr = c.sbuf("thr", [128, 1], F32)
    nthr = c.sbuf("nthr", [128, 1], F32)
    mtot_sl = Slot()
    mval_sl = Slot()
    m1P_sl = Slot()
    thr_sl = Slot()
    nthr_sl = Slot()
    ps_m = c.psum("ps_m", [128, 512], F32)
    psm_sl = Slot()
    for dc in range(8):
        p.step("sync", lambda e, dc=dc: e.dma_start(
            out=stg[:, :], in_=w1t[dc * 128:(dc + 1) * 128, :]),
            writes=[stg_sl[0]], dma=True)
        p.step("vector", lambda e, dc=dc: e.tensor_reduce(
            mc8[:, dc:dc + 1], stg[:, :], AX.X, ALU.add,
            apply_absolute_value=True),
            reads=[stg_sl[0]], writes=[mc8_sl[dc]])
    p.step("vector", lambda e: e.reduce_sum(mtot[:, :], mc8[:, :], axis=AX.X),
           reads=mc8_sl, writes=[mtot_sl])
    p.step("tensor", lambda e: e.matmul(ps_m[:1, :1], mtot[:, :], onesc[:, :],
                                        start=True, stop=True),
           reads=[mtot_sl, oc_sl], writes=[psm_sl])
    p.step("scalar", lambda e: e.activation(mval[:, :], ps_m[:1, :1], AF.Identity,
                                            scale=1.0 / (D * H4)),
           reads=[psm_sl], writes=[mval_sl])
    p.step("vector", lambda e: e.tensor_single_scalar(
        mval[:, :], mval[:, :], 1e-5, ALU.max), writes=[mval_sl])
    p.step("tensor", lambda e: e.matmul(ps_m[:, 0:1], ones1[:, :], mval[:, :],
                                        start=True, stop=True),
           reads=[o1_sl, mval_sl], writes=[psm_sl])
    p.step("scalar", lambda e: e.copy(m1P[:, :], ps_m[:, 0:1]),
           reads=[psm_sl], writes=[m1P_sl])
    p.step("vector", lambda e: e.tensor_single_scalar(
        thr[:, :], m1P[:, :], 0.5, ALU.mult), reads=[m1P_sl], writes=[thr_sl])
    p.step("vector", lambda e: e.tensor_single_scalar(
        nthr[:, :], m1P[:, :], -0.5, ALU.mult), reads=[m1P_sl], writes=[nthr_sl])

    xgT = c.sbuf("xgT", [128, 8 * 4096], BF16)
    xgT_sl = [[Slot() for _ in range(32)] for _ in range(8)]
    xs = c.sbuf("xs", [128, 2 * 1024], F32)
    xc = c.sbuf("xc", [128, 2 * 1024], F32)
    xg = c.sbuf("xg", [128, 2 * 1024], F32)
    sq = xg
    xgb = c.sbuf("xgb", [128, 2 * 1024], BF16)
    cols = c.sbuf("cols", [128, 2 * 4], F32)
    ps_tr = c.psum("ps_tr", [128, 2048], BF16)
    sls = {nm: [Slot(), Slot()] for nm in ("xs", "xc", "xg", "xgb", "cols")}
    sls["sq"] = sls["xg"]
    tr_sl = [Slot() for _ in range(4)]
    for ti in range(32):
        r = ti % 2
        o = r * 1024
        tok0 = ti * 128
        p.step("sync", lambda e, o=o, tok0=tok0: e.dma_start(
            out=xs[:, o:o + 1024], in_=xn1[tok0:tok0 + 128, :]),
            writes=[sls["xs"][r]], dma=True)
        mu = cols[:, r * 4 + 0: r * 4 + 1]
        var = cols[:, r * 4 + 1: r * 4 + 2]
        std = cols[:, r * 4 + 2: r * 4 + 3]
        rstd = cols[:, r * 4 + 3: r * 4 + 4]
        cs = sls["cols"][r]
        p.step("vector", lambda e, mu=mu, o=o: e.reduce_sum(
            mu, xs[:, o:o + 1024], axis=AX.X), reads=[sls["xs"][r]], writes=[cs])
        p.step("scalar", lambda e, mu=mu: e.mul(mu, mu, 1.0 / D), writes=[cs])
        p.step("vector", lambda e, mu=mu, o=o: e.tensor_single_scalar(
            xc[:, o:o + 1024], xs[:, o:o + 1024], mu, ALU.subtract),
            reads=[sls["xs"][r], cs], writes=[sls["xc"][r]])
        p.step("scalar", lambda e, var=var, o=o: e.activation(
            sq[:, o:o + 1024], xc[:, o:o + 1024], AF.Square, accum_out=var),
            reads=[sls["xc"][r]], writes=[sls["sq"][r], cs])
        epsa, epssl = c.cval(EPS)
        p.step("scalar", lambda e, std=std, var=var, epsa=epsa: e.activation(
            std, var, AF.Sqrt, scale=1.0 / D, bias=epsa),
            reads=[epssl], writes=[cs])
        p.step("vector", lambda e, rstd=rstd, std=std: e.reciprocal(rstd, std),
               writes=[cs])
        p.step("vector", lambda e, rstd=rstd, o=o: e.tensor_single_scalar(
            xg[:, o:o + 1024], xc[:, o:o + 1024], rstd, ALU.mult),
            reads=[sls["xc"][r], cs], writes=[sls["xg"][r]])
        p.step("vector", lambda e, o=o: e.tensor_mul(
            xg[:, o:o + 1024], xg[:, o:o + 1024], g1B[:, :]),
            reads=[g1B_sl], writes=[sls["xg"][r]])
        p.step("vector", lambda e, o=o: e.tensor_add(
            xgb[:, o:o + 1024], xg[:, o:o + 1024], b1B[:, :]),
            reads=[sls["xg"][r], b1B_sl], writes=[sls["xgb"][r]])
        for dc in range(8):
            pt = dc % 2
            p.step("tensor", lambda e, o=o, dc=dc, pt=pt: e.transpose(
                ps_tr[:, pt * 1024: pt * 1024 + 128],
                xgb[:, o + dc * 128: o + (dc + 1) * 128], idn_sb[:, :]),
                reads=[sls["xgb"][r], idn_sl], writes=[tr_sl[pt]])
            p.step("scalar", lambda e, dc=dc, ti=ti, pt=pt: e.copy(
                xgT[:, dc * 4096 + ti * 128: dc * 4096 + (ti + 1) * 128],
                ps_tr[:, pt * 1024: pt * 1024 + 128]),
                reads=[tr_sl[pt]], writes=[xgT_sl[dc][ti]])

    stw = c.sbuf("stw", [128, 2 * 1024], F32)
    qa = c.sbuf("qa", [128, 2 * 1024], F32)
    qb = c.sbuf("qb", [128, 2 * 1024], F32)
    w1q = c.sbuf("w1q", [128, 2 * 1024], BF16)
    hsp = c.sbuf("hsp", [128, 2 * 512], F32)
    stw_sl = [Slot(), Slot()]
    qa_sl = [Slot(), Slot()]
    qb_sl = [Slot(), Slot()]
    w1q_sl = [Slot(), Slot()]
    hsp_sl = [Slot(), Slot()]
    ps_h = c.psum("ps_h", [128, 2 * 512], F32)
    psh_sl = [Slot(), Slot()]
    for hc in range(32):
        r = hc % 2
        o = r * 1024
        p.step("sync", lambda e, hc=hc, o=o: e.dma_start(
            out=stw[:, o:o + 1024],
            in_=w1t[:, hc * 128:(hc + 1) * 128].rearrange("(dc p) h -> p dc h", p=128)),
            writes=[stw_sl[r]], dma=True)
        p.step("vector", lambda e, o=o: e.tensor_single_scalar(
            qa[:, o:o + 1024], stw[:, o:o + 1024], thr[:, :], ALU.is_gt),
            reads=[stw_sl[r], thr_sl], writes=[qa_sl[r]])
        p.step("vector", lambda e, o=o: e.tensor_single_scalar(
            qb[:, o:o + 1024], stw[:, o:o + 1024], nthr[:, :], ALU.is_lt),
            reads=[stw_sl[r], nthr_sl], writes=[qb_sl[r]])
        p.step("vector", lambda e, o=o: e.tensor_sub(
            w1q[:, o:o + 1024], qa[:, o:o + 1024], qb[:, o:o + 1024]),
            reads=[qa_sl[r], qb_sl[r]], writes=[w1q_sl[r]])
        for tt in range(8):
            rr = tt % 2
            for dc in range(8):
                p.step("tensor", lambda e, o=o, dc=dc, tt=tt, rr=rr: e.matmul(
                    ps_h[:, rr * 512:(rr + 1) * 512],
                    w1q[:, o + dc * 128: o + (dc + 1) * 128],
                    xgT[:, dc * 4096 + tt * 512: dc * 4096 + (tt + 1) * 512],
                    start=(dc == 0), stop=(dc == 7)),
                    reads=[w1q_sl[r]] + [xgT_sl[dc][tt * 4 + q] for q in range(4)],
                    writes=[psh_sl[rr]])
            p.step("scalar", lambda e, hc=hc, rr=rr: e.activation(
                hsp[:, rr * 512:(rr + 1) * 512], ps_h[:, rr * 512:(rr + 1) * 512],
                AF.Gelu, scale=m1P[:, :], bias=b1cs[:, hc:hc + 1]),
                reads=[psh_sl[rr], m1P_sl, b1c_sl], writes=[hsp_sl[rr]])
            p.step("gpsimd", lambda e, hc=hc, tt=tt, rr=rr: e.dma_start(
                out=hG[hc * 128:(hc + 1) * 128, tt * 512:(tt + 1) * 512],
                in_=hsp[:, rr * 512:(rr + 1) * 512]),
                reads=[hsp_sl[rr]], dma=True)
    return


def build_E5b():
    c = Ctx()
    io = {}
    io["hG"] = c.inp("hG", [H4, TOK])
    io["w2t"] = c.inp("w2t", [H4, D])
    io["b2r"] = c.inp("b2r", [1, D])
    io["g2c"] = c.inp("g2c", [128, 32])
    io["b2c"] = c.inp("b2c", [128, 32])
    io["x1"] = c.inp("x1", [TOK, D])
    io["s1"] = c.inp("s1", [TOK, D])
    io["out"] = c.out("out", [TOK, D])
    emit_E5b(c, io)
    return c.done()


def emit_E5b(c, io):
    p = c.plan
    hGi = io["hG"]
    w2t = io["w2t"]
    b2r = io["b2r"]
    g2c = io["g2c"]
    b2c = io["b2c"]
    x1i = io["x1"]
    s1i = io["s1"]
    oo = io["out"]

    ones1 = c.sbuf("ones1", [1, 128], F32)
    onesc = c.sbuf("onesc", [128, 1], F32)
    onescb = c.sbuf("onescb", [128, 1], BF16)
    o1_sl = Slot()
    oc_sl = Slot()
    ocb_sl = Slot()
    p.step("vector", lambda e: e.memset(ones1[:, :], 1.0), writes=[o1_sl])
    p.step("vector", lambda e: e.memset(onesc[:, :], 1.0), writes=[oc_sl])
    p.step("vector", lambda e: e.memset(onescb[:, :], 1.0), writes=[ocb_sl])
    g2cs = c.sbuf("g2cs", [128, 32], F32)
    b2cs = c.sbuf("b2cs", [128, 32], F32)
    g2_sl = Slot()
    b2_sl = Slot()
    p.step("sync", lambda e: e.dma_start(out=g2cs[:, :], in_=g2c),
           writes=[g2_sl], dma=True)
    p.step("sync", lambda e: e.dma_start(out=b2cs[:, :], in_=b2c),
           writes=[b2_sl], dma=True)
    b2row = c.sbuf("b2row", [1, D], F32)
    b2B = c.sbuf("b2B", [128, D], F32)
    b2r_sl = Slot()
    b2B_sl = Slot()
    p.step("sync", lambda e: e.dma_start(out=b2row[:, :], in_=b2r),
           writes=[b2r_sl], dma=True)
    ps_b = c.psum("ps_b", [128, 1024], F32)
    psb_sl = Slot()
    for dg in range(2):
        p.step("tensor", lambda e, dg=dg: e.matmul(
            ps_b[:, dg * 512:(dg + 1) * 512], ones1[:, :],
            b2row[:, dg * 512:(dg + 1) * 512], start=True, stop=True),
            reads=[o1_sl, b2r_sl], writes=[psb_sl])
    p.step("scalar", lambda e: e.copy(b2B[:, :], ps_b[:, :]),
           reads=[psb_sl], writes=[b2B_sl])

    stg = c.sbuf("stg", [128, 2 * 1024], F32)
    stg_sl = [Slot(), Slot()]
    mc32 = c.sbuf("mc32", [128, 32], F32)
    mc_sl = [Slot() for _ in range(32)]
    mtot = c.sbuf("mtot", [128, 1], F32)
    mval = c.sbuf("mval", [1, 1], F32)
    m2P = c.sbuf("m2P", [128, 1], F32)
    thr = c.sbuf("thr", [128, 1], F32)
    nthr = c.sbuf("nthr", [128, 1], F32)
    mtot_sl = Slot()
    mval_sl = Slot()
    m2P_sl = Slot()
    thr_sl = Slot()
    nthr_sl = Slot()
    ps_m = c.psum("ps_m", [128, 512], F32)
    psm_sl = Slot()
    for hc in range(32):
        r = hc % 2
        p.step("sync", lambda e, hc=hc, r=r: e.dma_start(
            out=stg[:, r * 1024:(r + 1) * 1024],
            in_=w2t[hc * 128:(hc + 1) * 128, :]),
            writes=[stg_sl[r]], dma=True)
        p.step("vector", lambda e, hc=hc, r=r: e.tensor_reduce(
            mc32[:, hc:hc + 1], stg[:, r * 1024:(r + 1) * 1024], AX.X, ALU.add,
            apply_absolute_value=True),
            reads=[stg_sl[r]], writes=[mc_sl[hc]])
    p.step("vector", lambda e: e.reduce_sum(mtot[:, :], mc32[:, :], axis=AX.X),
           reads=mc_sl, writes=[mtot_sl])
    p.step("tensor", lambda e: e.matmul(ps_m[:1, :1], mtot[:, :], onesc[:, :],
                                        start=True, stop=True),
           reads=[mtot_sl, oc_sl], writes=[psm_sl])
    p.step("scalar", lambda e: e.activation(mval[:, :], ps_m[:1, :1], AF.Identity,
                                            scale=1.0 / (D * H4)),
           reads=[psm_sl], writes=[mval_sl])
    p.step("vector", lambda e: e.tensor_single_scalar(
        mval[:, :], mval[:, :], 1e-5, ALU.max), writes=[mval_sl])
    p.step("tensor", lambda e: e.matmul(ps_m[:, 0:1], ones1[:, :], mval[:, :],
                                        start=True, stop=True),
           reads=[o1_sl, mval_sl], writes=[psm_sl])
    p.step("scalar", lambda e: e.copy(m2P[:, :], ps_m[:, 0:1]),
           reads=[psm_sl], writes=[m2P_sl])
    p.step("vector", lambda e: e.tensor_single_scalar(
        thr[:, :], m2P[:, :], 0.5, ALU.mult), reads=[m2P_sl], writes=[thr_sl])
    p.step("vector", lambda e: e.tensor_single_scalar(
        nthr[:, :], m2P[:, :], -0.5, ALU.mult), reads=[m2P_sl], writes=[nthr_sl])
    w2q = c.sbuf("w2q", [128, 32 * 1024], BF16)
    w2q_sl = [Slot() for _ in range(32)]
    qa = c.sbuf("qa", [128, 2 * 1024], F32)
    qb = c.sbuf("qb", [128, 2 * 1024], F32)
    qa_sl = [Slot(), Slot()]
    qb_sl = [Slot(), Slot()]
    for hc in range(32):
        r = hc % 2
        o = r * 1024
        p.step("sync", lambda e, hc=hc, o=o: e.dma_start(
            out=stg[:, o:o + 1024], in_=w2t[hc * 128:(hc + 1) * 128, :]),
            writes=[stg_sl[r]], dma=True)
        p.step("vector", lambda e, o=o: e.tensor_single_scalar(
            qa[:, o:o + 1024], stg[:, o:o + 1024], thr[:, :], ALU.is_gt),
            reads=[stg_sl[r], thr_sl], writes=[qa_sl[r]])
        p.step("vector", lambda e, o=o: e.tensor_single_scalar(
            qb[:, o:o + 1024], stg[:, o:o + 1024], nthr[:, :], ALU.is_lt),
            reads=[stg_sl[r], nthr_sl], writes=[qb_sl[r]])
        p.step("vector", lambda e, o=o, hc=hc: e.tensor_sub(
            w2q[:, hc * 1024:(hc + 1) * 1024], qa[:, o:o + 1024], qb[:, o:o + 1024]),
            reads=[qa_sl[r], qb_sl[r]], writes=[w2q_sl[hc]])

    hgb = c.sbuf("hgb", [128, 2 * 512], F32)
    sqb = c.sbuf("sqb", [128, 2 * 512], BF16)
    xn2b = c.sbuf("xn2b", [128, 32 * 512], BF16)
    t1b = c.sbuf("t1b", [128, 2 * 512], F32)
    muB = c.sbuf("muB", [128, 512], F32)
    rsB = c.sbuf("rsB", [128, 512], F32)
    mrow = c.sbuf("mrow", [1, 512], F32)
    vrow = c.sbuf("vrow", [1, 512], F32)
    ot1 = c.sbuf("ot1", [128, 2 * 512], F32)
    gg = c.sbuf("gg", [128, 2 * 512], F32)
    x1t = c.sbuf("x1t", [128, 2 * 512], F32)
    s1t = c.sbuf("s1t", [128, 2 * 512], F32)
    fo = c.sbuf("fo", [128, 2 * 512], F32)
    hgb_sl = [Slot(), Slot()]
    sqb_sl = [Slot(), Slot()]
    xn2_sl = [Slot() for _ in range(32)]
    t1_sl = [Slot(), Slot()]
    muB_sl = Slot()
    rsB_sl = Slot()
    mrow_sl = Slot()
    vrow_sl = Slot()
    ot1_sl = [Slot(), Slot()]
    gg_sl = [Slot(), Slot()]
    x1_sl = [Slot(), Slot()]
    s1_sl = [Slot(), Slot()]
    fo_sl = [Slot(), Slot()]
    ps_s = c.psum("ps_s", [128, 512], F32)
    ps_v = c.psum("ps_v", [128, 512], F32)
    ps_bc = c.psum("ps_bc", [128, 512], F32)
    ps_o = c.psum("ps_o", [128, 2 * 512], F32)
    pss_sl = Slot()
    psv_sl = Slot()
    psbc_sl = Slot()
    pso_sl = [Slot(), Slot()]

    for tg in range(8):
        t0 = tg * 512
        for hc in range(32):
            r = hc % 2
            o = r * 512
            p.step("sync", lambda e, hc=hc, o=o, t0=t0: e.dma_start(
                out=hgb[:, o:o + 512], in_=hGi[hc * 128:(hc + 1) * 128, t0:t0 + 512]),
                writes=[hgb_sl[r]], dma=True)
            p.step("tensor", lambda e, hc=hc, o=o: e.matmul(
                ps_s[0:1, :], onesc[:, :], hgb[:, o:o + 512],
                start=(hc == 0), stop=(hc == 31)),
                reads=[oc_sl, hgb_sl[r]], writes=[pss_sl])
            p.step("scalar", lambda e, o=o: e.activation(
                sqb[:, o:o + 512], hgb[:, o:o + 512], AF.Square),
                reads=[hgb_sl[r]], writes=[sqb_sl[r]])
            p.step("tensor", lambda e, hc=hc, o=o: e.matmul(
                ps_v[0:1, :], onescb[:, :], sqb[:, o:o + 512],
                start=(hc == 0), stop=(hc == 31)),
                reads=[ocb_sl, sqb_sl[r]], writes=[psv_sl])
        p.step("scalar", lambda e: e.activation(mrow[:, :], ps_s[0:1, :], AF.Identity,
                                                scale=1.0 / H4),
               reads=[pss_sl], writes=[mrow_sl])
        p.step("scalar", lambda e: e.activation(vrow[:, :], ps_v[0:1, :], AF.Identity,
                                                scale=1.0 / H4),
               reads=[psv_sl], writes=[vrow_sl])
        p.step("vector", lambda e: e.tensor_tensor(
            t1b[:1, :512], mrow[:, :], mrow[:, :], ALU.mult),
            reads=[mrow_sl], writes=[t1_sl[0]])
        p.step("vector", lambda e: e.tensor_sub(
            vrow[:, :], vrow[:, :], t1b[:1, :512]),
            reads=[t1_sl[0]], writes=[vrow_sl])
        epsa, epssl = c.cval(EPS)
        p.step("scalar", lambda e, epsa=epsa: e.activation(
            vrow[:, :], vrow[:, :], AF.Sqrt, bias=epsa[0:1, :]),
            reads=[epssl], writes=[vrow_sl])
        p.step("vector", lambda e: e.reciprocal(vrow[:, :], vrow[:, :]),
               writes=[vrow_sl])
        p.step("tensor", lambda e: e.matmul(ps_bc[:, :], ones1[:, :], mrow[:, :],
                                            start=True, stop=True),
               reads=[o1_sl, mrow_sl], writes=[psbc_sl])
        p.step("scalar", lambda e: e.copy(muB[:, :], ps_bc[:, :]),
               reads=[psbc_sl], writes=[muB_sl])
        p.step("tensor", lambda e: e.matmul(ps_bc[:, :], ones1[:, :], vrow[:, :],
                                            start=True, stop=True),
               reads=[o1_sl, vrow_sl], writes=[psbc_sl])
        p.step("scalar", lambda e: e.copy(rsB[:, :], ps_bc[:, :]),
               reads=[psbc_sl], writes=[rsB_sl])
        for hc in range(32):
            r = hc % 2
            o = r * 512
            p.step("sync", lambda e, hc=hc, o=o, t0=t0: e.dma_start(
                out=hgb[:, o:o + 512], in_=hGi[hc * 128:(hc + 1) * 128, t0:t0 + 512]),
                writes=[hgb_sl[r]], dma=True)
            p.step("vector", lambda e, o=o: e.tensor_sub(
                t1b[:, o:o + 512], hgb[:, o:o + 512], muB[:, :]),
                reads=[hgb_sl[r], muB_sl], writes=[t1_sl[r]])
            p.step("vector", lambda e, o=o: e.tensor_mul(
                t1b[:, o:o + 512], t1b[:, o:o + 512], rsB[:, :]),
                reads=[rsB_sl], writes=[t1_sl[r]])
            p.step("scalar", lambda e, hc=hc, o=o: e.activation(
                xn2b[:, hc * 512:(hc + 1) * 512], t1b[:, o:o + 512],
                AF.Identity, scale=g2cs[:, hc:hc + 1], bias=b2cs[:, hc:hc + 1]),
                reads=[t1_sl[r], g2_sl, b2_sl], writes=[xn2_sl[hc]])
        for st in range(4):
            r = st % 2
            tok0 = t0 + st * 128
            for dg in range(2):
                for hc in range(32):
                    p.step("tensor", lambda e, hc=hc, st=st, dg=dg: e.matmul(
                        ps_o[:, dg * 512:(dg + 1) * 512],
                        xn2b[:, hc * 512 + st * 128: hc * 512 + (st + 1) * 128],
                        w2q[:, hc * 1024 + dg * 512: hc * 1024 + (dg + 1) * 512],
                        start=(hc == 0), stop=(hc == 31)),
                        reads=[xn2_sl[hc], w2q_sl[hc]], writes=[pso_sl[dg]])
                p.step("scalar", lambda e, dg=dg, r=r: e.activation(
                    ot1[:, r * 512:(r + 1) * 512], ps_o[:, dg * 512:(dg + 1) * 512],
                    AF.Copy, scale=m2P[:, :]),
                    reads=[pso_sl[dg], m2P_sl], writes=[ot1_sl[r]])
                p.step("vector", lambda e, dg=dg, r=r: e.tensor_add(
                    ot1[:, r * 512:(r + 1) * 512], ot1[:, r * 512:(r + 1) * 512],
                    b2B[:, dg * 512:(dg + 1) * 512]),
                    reads=[b2B_sl], writes=[ot1_sl[r]])
                p.step("scalar", lambda e, r=r: e.activation(
                    gg[:, r * 512:(r + 1) * 512], ot1[:, r * 512:(r + 1) * 512],
                    AF.Silu), reads=[ot1_sl[r]], writes=[gg_sl[r]])
                p.step("sync", lambda e, r=r, tok0=tok0, dg=dg: e.dma_start(
                    out=x1t[:, r * 512:(r + 1) * 512],
                    in_=x1i[tok0:tok0 + 128, dg * 512:(dg + 1) * 512]),
                    writes=[x1_sl[r]], dma=True)
                p.step("sync", lambda e, r=r, tok0=tok0, dg=dg: e.dma_start(
                    out=s1t[:, r * 512:(r + 1) * 512],
                    in_=s1i[tok0:tok0 + 128, dg * 512:(dg + 1) * 512]),
                    writes=[s1_sl[r]], dma=True)
                p.step("vector", lambda e, r=r: e.tensor_mul(
                    gg[:, r * 512:(r + 1) * 512], gg[:, r * 512:(r + 1) * 512],
                    s1t[:, r * 512:(r + 1) * 512]),
                    reads=[s1_sl[r]], writes=[gg_sl[r]])
                p.step("vector", lambda e, r=r: e.tensor_add(
                    fo[:, r * 512:(r + 1) * 512], x1t[:, r * 512:(r + 1) * 512],
                    gg[:, r * 512:(r + 1) * 512]),
                    reads=[x1_sl[r], gg_sl[r]], writes=[fo_sl[r]])
                p.step("gpsimd", lambda e, r=r, tok0=tok0, dg=dg: e.dma_start(
                    out=oo[tok0:tok0 + 128, dg * 512:(dg + 1) * 512],
                    in_=fo[:, r * 512:(r + 1) * 512]),
                    reads=[fo_sl[r]], dma=True)
    return


def emit_scanfull(c, io, chunk=2048):
    p = c.plan
    dec = io["dec"]
    drv = io["drv"]
    hs = io["hs"]
    ft = io["ft"]
    nch_ = S // chunk
    sd = c.sbuf("sd", [128, 2 * chunk], F32)
    sv = c.sbuf("sv", [128, 2 * chunk], F32)
    sh = c.sbuf("sh", [128, 2 * chunk], F32)
    sd_sl = [Slot(), Slot()]
    sv_sl = [Slot(), Slot()]
    sh_sl = [Slot(), Slot()]
    for t in range(8):
        b, nc2 = divmod(t, 2)
        rows = slice(nc2 * 128, (nc2 + 1) * 128)
        prev_scan = None
        for ci in range(nch_):
            r = ci % 2
            o = r * chunk
            p.step("sync", lambda e, o=o, ci=ci, b=b, rows=rows: e.dma_start(
                out=sd[:, o:o + chunk], in_=dec[b, rows, ci * chunk:(ci + 1) * chunk]),
                writes=[sd_sl[r]], dma=True)
            p.step("sync", lambda e, o=o, ci=ci, b=b, rows=rows: e.dma_start(
                out=sv[:, o:o + chunk], in_=drv[b, rows, ci * chunk:(ci + 1) * chunk]),
                writes=[sv_sl[r]], dma=True)
            init = 0.0 if ci == 0 else sh[:, (1 - r) * chunk + chunk - 1:(1 - r) * chunk + chunk]
            reads = [sd_sl[r], sv_sl[r]] + ([sh_sl[1 - r]] if ci else [])
            p.step("vector", lambda e, o=o, init=init: e.tensor_tensor_scan(
                out=sh[:, o:o + chunk], data0=sd[:, o:o + chunk],
                data1=sv[:, o:o + chunk], initial=init,
                op0=ALU.mult, op1=ALU.add),
                reads=reads, writes=[sh_sl[r]])
            p.step("gpsimd", lambda e, o=o, ci=ci, b=b, rows=rows: e.dma_start(
                out=hs[b, rows, ci * chunk:(ci + 1) * chunk], in_=sh[:, o:o + chunk]),
                reads=[sh_sl[r]], dma=True)
        p.step("gpsimd", lambda e, b=b, rows=rows: e.dma_start(
            out=ft[rows, b:b + 1],
            in_=sh[:, ((nch_ - 1) % 2) * chunk + chunk - 1:((nch_ - 1) % 2) * chunk + chunk],
            allow_slow_non_contiguous=True),
            reads=[sh_sl[(nch_ - 1) % 2]], dma=True)


def emit_dyncopy(c, io):
    p = c.plan
    nc = c.nc
    xnT_f = io["xnT_f"]
    xnT_l = io["xnT_l"]
    hs1_f = io["hs1_f"]
    hs1_l = io["hs1_l"]
    pid_sb = c.sbuf("pid", [1, 1], mybir.dt.uint32)
    pid_sl = Slot()
    p.step("sync", lambda e: e.dma_start(
        out=pid_sb[:, :], in_=nc.partition_id_tensor.ap()),
        writes=[pid_sl], dma=True)
    stg = c.sbuf("dstg", [128, 2 * B * SL], BF16)
    stg2 = c.sbuf("dstg2", [128, B * SL], F32)
    stg_sl = [Slot(), Slot()]
    stg2_sl = Slot()
    rg = nc.gpsimd.alloc_register(c.pfx + "rgp")
    rgo = nc.gpsimd.alloc_register(c.pfx + "rgo")
    rs = nc.scalar.alloc_register(c.pfx + "rsp")
    rso = nc.scalar.alloc_register(c.pfx + "rso")
    first = [True, True]
    for dc in range(8):
        r = dc % 2
        off_c = (dc * 128) * (B * S)

        def _ld(e, off_c=off_c, r=r):
            if first[0]:
                first[0] = False
                e.reg_load(rg, pid_sb[:1, :1])
                e.reg_mul(rg, rg, SL)
            e.reg_add(rgo, rg, off_c)
            return e.dma_start(
                out=stg[:, r * B * SL:(r + 1) * B * SL],
                in_=bass.AP(xnT_f, rgo, [[B * S, 128], [S, B], [1, SL]]))
        p.step("gpsimd", _ld, extra=[pid_sl.writer], writes=[stg_sl[r]], dma=True)
        p.step("sync", lambda e, dc=dc, r=r: e.dma_start(
            out=xnT_l[dc * 128:(dc + 1) * 128, :],
            in_=stg[:, r * B * SL:(r + 1) * B * SL]),
            reads=[stg_sl[r]], dma=True)
    for nc2 in range(2):
        off_c = nc2 * 128 * S

        def _ld2(e, off_c=off_c, nc2=nc2):
            if first[1]:
                first[1] = False
                e.reg_load(rs, pid_sb[:1, :1])
                e.reg_mul(rs, rs, SL)
            e.reg_add(rso, rs, off_c)
            return e.dma_start(
                out=stg2[:, :],
                in_=bass.AP(hs1_f, rso, [[S, 128], [N * S, B], [1, SL]]))
        p.step("scalar", _ld2, extra=[pid_sl.writer], writes=[stg2_sl],
               dma=True)
        p.step("sync", lambda e, nc2=nc2: e.dma_start(
            out=bass.AP(hs1_l.tensor, nc2 * 128 * SL,
                        [[SL, 128], [N * SL, B], [1, SL]]),
            in_=stg2[:, :]),
            reads=[stg2_sl], dma=True)


def build_fused():
    c = Ctx()
    nc = c.nc
    i_ = {}
    i_["xf"] = c.inp("xf", [B * S, D])
    i_["x"] = c.inp("x", [TOK, D])
    i_["idn"] = c.inp("idn", [128, 128], BF16)
    for l in range(2):
        i_[f"wd{l}"] = c.inp(f"wd{l}", [D, N])
        i_[f"wi{l}"] = c.inp(f"wi{l}", [D, N])
        i_[f"bd{l}"] = c.inp(f"bd{l}", [128, 2])
        i_[f"bi{l}"] = c.inp(f"bi{l}", [128, 2])
        i_[f"al{l}"] = c.inp(f"al{l}", [128, 2])
        i_[f"wo{l}"] = c.inp(f"wo{l}", [N, D])
        i_[f"ws{l}"] = c.inp(f"ws{l}", [N, D])
    i_["bob0"] = c.inp("bob0", [128, 8])
    i_["bsb0"] = c.inp("bsb0", [128, 8])
    i_["bor1"] = c.inp("bor1", [1, D])
    i_["bsr1"] = c.inp("bsr1", [1, D])
    i_["gwt"] = c.inp("gwt", [D, E])
    i_["w1t"] = c.inp("w1t", [E, D, I])
    i_["w3t"] = c.inp("w3t", [E, D, I])
    i_["w2t"] = c.inp("w2t", [E, I, D])
    i_["wqt"] = c.inp("wqt", [D, D])
    i_["wkt"] = c.inp("wkt", [D, D])
    i_["w1tL"] = c.inp("w1tL", [D, H4])
    i_["b1cL"] = c.inp("b1cL", [128, 32])
    i_["g1rL"] = c.inp("g1rL", [1, D])
    i_["b1rL"] = c.inp("b1rL", [1, D])
    i_["w2tL"] = c.inp("w2tL", [H4, D])
    i_["b2rL"] = c.inp("b2rL", [1, D])
    i_["g2cL"] = c.inp("g2cL", [128, 32])
    i_["b2cL"] = c.inp("b2cL", [128, 32])
    out = c.out("out", [TOK, D])
    d_ = {}
    xnT_f_h = nc.dram_tensor("i_xnTf", [D, B * S], BF16)
    d_["xnTf"] = xnT_f_h.ap()
    for l in range(2):
        d_[f"dec{l}"] = nc.dram_tensor(f"i_dec{l}", [B, N, S], F32).ap()
        d_[f"drv{l}"] = nc.dram_tensor(f"i_drv{l}", [B, N, S], F32).ap()
        d_[f"ft{l}"] = nc.dram_tensor(f"i_ft{l}", [N, B], F32).ap()
    hs0_h = nc.dram_tensor("i_hs0", [B, N, S], F32)
    hs1_h = nc.dram_tensor("i_hs1", [B, N, S], F32)
    d_["hs0"] = hs0_h.ap()
    d_["hs1"] = hs1_h.ap()
    d_["xnT_l"] = nc.dram_tensor("i_xnTl", [D, TOK], BF16).ap()
    d_["hs1_l"] = nc.dram_tensor("i_hs1l", [B, N, SL], F32).ap()
    for nm, shp in (("s0", [TOK, D]), ("g", [TOK, D]), ("x1", [TOK, D]),
                    ("xn1", [TOK, D]), ("s1", [TOK, D])):
        d_[nm] = nc.dram_tensor("i_" + nm, shp, F32).ap()
    d_["xn1T"] = nc.dram_tensor("i_xn1T", [D, TOK], BF16).ap()
    d_["hG"] = nc.dram_tensor("i_hG", [H4, TOK], F32).ap()
    c.cval(1.0)
    c.cval(EPS)

    c.push_scope("pA_")
    emit_A(c, dict(x=i_["xf"], wd=i_["wd0"], wi=i_["wi0"], bd=i_["bd0"],
                   bi=i_["bi0"], al=i_["al0"], idn=i_["idn"],
                   xnT=d_["xnTf"], dec=d_["dec0"], drv=d_["drv0"]),
           ngroups=64, seqlen=S)
    c.pop_scope()
    c.push_scope("pS0_")
    emit_scanfull(c, dict(dec=d_["dec0"], drv=d_["drv0"], hs=d_["hs0"],
                          ft=d_["ft0"]))
    c.pop_scope()
    c.push_scope("pC_")
    emit_C(c, dict(hs0=d_["hs0"], f0t=d_["ft0"], wo=i_["wo0"], ws=i_["ws0"],
                   bob=i_["bob0"], bsb=i_["bsb0"], wd=i_["wd1"], wi=i_["wi1"],
                   bd=i_["bd1"], bi=i_["bi1"], al=i_["al1"],
                   dec=d_["dec1"], drv=d_["drv1"]), ngroups=64, seqlen=S)
    c.pop_scope()
    c.push_scope("pS1_")
    emit_scanfull(c, dict(dec=d_["dec1"], drv=d_["drv1"], hs=d_["hs1"],
                          ft=d_["ft1"]))
    c.pop_scope()
    c.push_scope("pD_")
    emit_dyncopy(c, dict(xnT_f=xnT_f_h, xnT_l=d_["xnT_l"], hs1_f=hs1_h,
                         hs1_l=d_["hs1_l"]))
    c.pop_scope()
    c.push_scope("pE1_")
    emit_E1(c, dict(hs1=d_["hs1_l"], f1t=d_["ft1"], wo=i_["wo1"], ws=i_["ws1"],
                    bor=i_["bor1"], bsr=i_["bsr1"], s0=d_["s0"]))
    c.pop_scope()
    c.push_scope("pE2_")
    emit_E2(c, dict(xnT=d_["xnT_l"], gwt=i_["gwt"], w1t=i_["w1t"],
                    w3t=i_["w3t"], w2t=i_["w2t"], g=d_["g"]))
    c.pop_scope()
    c.push_scope("pE3_")
    emit_E3(c, dict(x=i_["x"], s0=d_["s0"], g=d_["g"], idn=i_["idn"],
                    x1=d_["x1"], xn1=d_["xn1"], xn1T=d_["xn1T"]))
    c.pop_scope()
    c.push_scope("pE4_")
    emit_E4(c, dict(xn1=d_["xn1"], xn1T=d_["xn1T"], wqt=i_["wqt"],
                    wkt=i_["wkt"], s1=d_["s1"]))
    c.pop_scope()
    c.push_scope("pE5a_")
    emit_E5a(c, dict(xn1=d_["xn1"], w1t=i_["w1tL"], b1c=i_["b1cL"],
                     g1r=i_["g1rL"], b1r=i_["b1rL"], idn=i_["idn"],
                     hG=d_["hG"]))
    c.pop_scope()
    c.push_scope("pE5b_")
    emit_E5b(c, dict(hG=d_["hG"], w2t=i_["w2tL"], b2r=i_["b2rL"],
                     g2c=i_["g2cL"], b2c=i_["b2cL"], x1=d_["x1"],
                     s1=d_["s1"], out=out))
    c.pop_scope()
    return c.done()


def _run(nc, in_maps):
    return run_bass_kernel_spmd(nc, in_maps, core_ids=list(range(NCORES)))


def kernel_multi(**inp):
    f32 = np.float32
    x_full = np.ascontiguousarray(inp["L0_attn_shot_embeddings"], dtype=f32)
    Wd = np.ascontiguousarray(inp["L0_attn_Wd"], dtype=f32)
    bd = np.ascontiguousarray(inp["L0_attn_bd"], dtype=f32)
    Wi = np.ascontiguousarray(inp["L0_attn_Wi"], dtype=f32)
    bi = np.ascontiguousarray(inp["L0_attn_bi"], dtype=f32)
    A_log = np.ascontiguousarray(inp["L0_attn_A_log"], dtype=f32)
    Wo = np.ascontiguousarray(inp["L0_attn_Wo"], dtype=f32)
    bo = np.ascontiguousarray(inp["L0_attn_bo"], dtype=f32)
    Ws = np.ascontiguousarray(inp["L0_attn_Ws"], dtype=f32)
    bs = np.ascontiguousarray(inp["L0_attn_bs"], dtype=f32)
    gate_w = np.ascontiguousarray(inp["L0_ffn_gate_w"], dtype=f32)
    w1 = inp["L0_ffn_w1"]
    w2 = inp["L0_ffn_w2"]
    w3 = inp["L0_ffn_w3"]
    Wq = np.ascontiguousarray(inp["L1_attn_Wq"], dtype=f32)
    Wk = np.ascontiguousarray(inp["L1_attn_Wk"], dtype=f32)
    ln1_g = np.ascontiguousarray(inp["L1_ffn_ln1_g"], dtype=f32)
    ln1_b = np.ascontiguousarray(inp["L1_ffn_ln1_b"], dtype=f32)
    w1L = np.ascontiguousarray(inp["L1_ffn_w1"], dtype=f32)
    b1L = np.ascontiguousarray(inp["L1_ffn_b1"], dtype=f32)
    ln2_g = np.ascontiguousarray(inp["L1_ffn_ln2_g"], dtype=f32)
    ln2_b = np.ascontiguousarray(inp["L1_ffn_ln2_b"], dtype=f32)
    w2L = np.ascontiguousarray(inp["L1_ffn_w2"], dtype=f32)
    b2L = np.ascontiguousarray(inp["L1_ffn_b2"], dtype=f32)

    idn = np.eye(128, dtype=ml_dtypes.bfloat16)
    C = np.ascontiguousarray

    def col2(v):
        return C(v.reshape(2, 128).T)

    xk = [C(x_full[:, k * SL:(k + 1) * SL, :].reshape(TOK, D)) for k in range(NCORES)]

    ncA = build_A()
    maps = [dict(x=xk[k], wd=Wd[0], wi=Wi[0], bd=col2(bd[0]), bi=col2(bi[0]),
                 al=col2(A_log[0]), idn=idn) for k in range(NCORES)]
    resA = _run(ncA, maps).results
    xnT_k = [r["xnT"] for r in resA]
    dec_full = np.concatenate([r["dec"] for r in resA], axis=2).reshape(B * N, S)
    drv_full = np.concatenate([r["drv"] for r in resA], axis=2).reshape(B * N, S)

    ncS = build_scan()
    maps = [dict(dec=C(dec_full[k * 128:(k + 1) * 128]),
                 drv=C(drv_full[k * 128:(k + 1) * 128])) for k in range(NCORES)]
    resS = _run(ncS, maps).results
    hs0 = np.concatenate([r["hs"] for r in resS], axis=0)
    f0 = C(hs0[:, -1].reshape(B, N).T)
    hs0_k = [C(hs0.reshape(B, N, S)[:, :, k * SL:(k + 1) * SL]) for k in range(NCORES)]

    ncC = build_C()
    maps = [dict(hs0=hs0_k[k], f0t=f0, wo=Wo[0], ws=Ws[0],
                 bob=C(bo[0].reshape(8, 128).T), bsb=C(bs[0].reshape(8, 128).T),
                 wd=Wd[1], wi=Wi[1], bd=col2(bd[1]), bi=col2(bi[1]),
                 al=col2(A_log[1])) for k in range(NCORES)]
    resC = _run(ncC, maps).results
    dec_full = np.concatenate([r["dec"] for r in resC], axis=2).reshape(B * N, S)
    drv_full = np.concatenate([r["drv"] for r in resC], axis=2).reshape(B * N, S)

    maps = [dict(dec=C(dec_full[k * 128:(k + 1) * 128]),
                 drv=C(drv_full[k * 128:(k + 1) * 128])) for k in range(NCORES)]
    resS = _run(ncS, maps).results
    hs1 = np.concatenate([r["hs"] for r in resS], axis=0)
    f1 = C(hs1[:, -1].reshape(B, N).T)
    hs1_k = [C(hs1.reshape(B, N, S)[:, :, k * SL:(k + 1) * SL]) for k in range(NCORES)]

    ncE1 = build_E1()
    maps = [dict(hs1=hs1_k[k], f1t=f1, wo=Wo[1], ws=Ws[1],
                 bor=C(bo[1].reshape(1, D)), bsr=C(bs[1].reshape(1, D)))
            for k in range(NCORES)]
    resE1 = _run(ncE1, maps).results
    s0_k = [r["s0"] for r in resE1]

    ncE2 = build_E2()
    gwt = C(gate_w.T)
    w1t = C(np.transpose(w1, (0, 2, 1)))
    w3t = C(np.transpose(w3, (0, 2, 1)))
    w2t = C(np.transpose(w2, (0, 2, 1)))
    maps = [dict(xnT=xnT_k[k], gwt=gwt, w1t=w1t, w3t=w3t, w2t=w2t)
            for k in range(NCORES)]
    resE2 = _run(ncE2, maps).results
    g_k = [r["g"] for r in resE2]

    ncE3 = build_E3()
    maps = [dict(x=xk[k], s0=s0_k[k], g=g_k[k], idn=idn) for k in range(NCORES)]
    resE3 = _run(ncE3, maps).results
    x1_k = [r["x1"] for r in resE3]
    xn1_k = [r["xn1"] for r in resE3]
    xn1T_k = [r["xn1T"] for r in resE3]

    ncE4 = build_E4()
    wqt = C(Wq.T)
    wkt = C(Wk.T)
    maps = [dict(xn1=xn1_k[k], xn1T=xn1T_k[k], wqt=wqt, wkt=wkt)
            for k in range(NCORES)]
    resE4 = _run(ncE4, maps).results
    s1_k = [r["s1"] for r in resE4]

    ncE5a = build_E5a()
    w1Lt = C(w1L.T)
    maps = [dict(xn1=xn1_k[k], w1t=w1Lt, b1c=C(b1L.reshape(32, 128).T),
                 g1r=C(ln1_g.reshape(1, D)), b1r=C(ln1_b.reshape(1, D)), idn=idn)
            for k in range(NCORES)]
    resE5a = _run(ncE5a, maps).results
    hG_k = [r["hG"] for r in resE5a]

    ncE5b = build_E5b()
    w2Lt = C(w2L.T)
    maps = [dict(hG=hG_k[k], w2t=w2Lt, b2r=C(b2L.reshape(1, D)),
                 g2c=C(ln2_g.reshape(32, 128).T), b2c=C(ln2_b.reshape(32, 128).T),
                 x1=x1_k[k], s1=s1_k[k]) for k in range(NCORES)]
    resE5b = _run(ncE5b, maps).results
    out_k = [r["out"] for r in resE5b]

    out = np.concatenate([o.reshape(B, SL, D) for o in out_k], axis=1)
    return np.ascontiguousarray(out, dtype=np.float32)


def kernel(**inp):
    f32 = np.float32
    C = np.ascontiguousarray
    x_full = C(inp["L0_attn_shot_embeddings"], dtype=f32)
    Wd = C(inp["L0_attn_Wd"], dtype=f32)
    bd = C(inp["L0_attn_bd"], dtype=f32)
    Wi = C(inp["L0_attn_Wi"], dtype=f32)
    bi = C(inp["L0_attn_bi"], dtype=f32)
    A_log = C(inp["L0_attn_A_log"], dtype=f32)
    Wo = C(inp["L0_attn_Wo"], dtype=f32)
    bo = C(inp["L0_attn_bo"], dtype=f32)
    Ws = C(inp["L0_attn_Ws"], dtype=f32)
    bs = C(inp["L0_attn_bs"], dtype=f32)
    gate_w = C(inp["L0_ffn_gate_w"], dtype=f32)
    w1 = inp["L0_ffn_w1"]
    w2 = inp["L0_ffn_w2"]
    w3 = inp["L0_ffn_w3"]
    Wq = C(inp["L1_attn_Wq"], dtype=f32)
    Wk = C(inp["L1_attn_Wk"], dtype=f32)
    ln1_g = C(inp["L1_ffn_ln1_g"], dtype=f32)
    ln1_b = C(inp["L1_ffn_ln1_b"], dtype=f32)
    w1L = C(inp["L1_ffn_w1"], dtype=f32)
    b1L = C(inp["L1_ffn_b1"], dtype=f32)
    ln2_g = C(inp["L1_ffn_ln2_g"], dtype=f32)
    ln2_b = C(inp["L1_ffn_ln2_b"], dtype=f32)
    w2L = C(inp["L1_ffn_w2"], dtype=f32)
    b2L = C(inp["L1_ffn_b2"], dtype=f32)

    idn = np.eye(128, dtype=ml_dtypes.bfloat16)

    def col2(v):
        return C(v.reshape(2, 128).T)

    xk = [C(x_full[:, k * SL:(k + 1) * SL, :].reshape(TOK, D))
          for k in range(NCORES)]
    shared = dict(
        idn=idn,
        bob0=C(bo[0].reshape(8, 128).T), bsb0=C(bs[0].reshape(8, 128).T),
        bor1=C(bo[1].reshape(1, D)), bsr1=C(bs[1].reshape(1, D)),
        gwt=C(gate_w.T),
        w1t=C(np.transpose(w1, (0, 2, 1))), w3t=C(np.transpose(w3, (0, 2, 1))),
        w2t=C(np.transpose(w2, (0, 2, 1))),
        wqt=C(Wq.T), wkt=C(Wk.T),
        w1tL=C(w1L.T), b1cL=C(b1L.reshape(32, 128).T),
        g1rL=C(ln1_g.reshape(1, D)), b1rL=C(ln1_b.reshape(1, D)),
        w2tL=C(w2L.T), b2rL=C(b2L.reshape(1, D)),
        g2cL=C(ln2_g.reshape(32, 128).T), b2cL=C(ln2_b.reshape(32, 128).T),
    )
    for l in range(2):
        shared[f"wd{l}"] = Wd[l]
        shared[f"wi{l}"] = Wi[l]
        shared[f"bd{l}"] = col2(bd[l])
        shared[f"bi{l}"] = col2(bi[l])
        shared[f"al{l}"] = col2(A_log[l])
        shared[f"wo{l}"] = Wo[l]
        shared[f"ws{l}"] = Ws[l]
    shared["xf"] = C(x_full.reshape(B * S, D))
    nc = build_fused()
    maps = [dict(shared, x=xk[k]) for k in range(NCORES)]
    res = _run(nc, maps).results
    out = np.concatenate([r["out"].reshape(B, SL, D) for r in res], axis=1)
    return np.ascontiguousarray(out, dtype=np.float32)


# version 07
# speedup vs baseline: 1.7815x; 1.7815x over previous
import contextlib

import numpy as np
import ml_dtypes

import concourse.bass as bass
import concourse.mybir as mybir
from concourse.bass_utils import run_bass_kernel_spmd

F32 = mybir.dt.float32
BF16 = mybir.dt.bfloat16
AF = mybir.ActivationFunctionType
ALU = mybir.AluOpType
AX = mybir.AxisListType

NCORES = 8
B = 4
S = 8192
SL = S // NCORES
D = 1024
N = 256
TOK = B * SL
E = 8
I = 2048
H4 = 4096
EPS = 1e-5
BIG = 1e30

ENGINES = ("sync", "tensor", "scalar", "vector", "gpsimd")


class Step:
    __slots__ = ("eng", "chan", "emit", "deps", "inc", "after", "needed", "waits")

    def __init__(self, eng, emit, deps, inc):
        self.eng = eng
        self.chan = eng + ("_d" if inc == 16 else "_c")
        self.emit = emit
        self.deps = deps
        self.inc = inc
        self.after = 0
        self.needed = False
        self.waits = []


class Slot:

    __slots__ = ("writer", "readers")

    def __init__(self):
        self.writer = None
        self.readers = []


class Plan:

    def __init__(self):
        self.per_engine = {e: [] for e in ENGINES}

    def add(self, eng, emit, deps=(), dma=False):
        deps = [d for d in deps if d is not None]
        s = Step(eng, emit, deps, 16 if dma else 1)
        self.per_engine[eng].append(s)
        return s

    def step(self, eng, emit, reads=(), writes=(), extra=(), dma=False):
        deps = list(extra)
        for sl in reads:
            if sl.writer is not None:
                deps.append(sl.writer)
        for sl in writes:
            if sl.writer is not None:
                deps.append(sl.writer)
            deps.extend(sl.readers)
        st = self.add(eng, emit, deps, dma)
        for sl in reads:
            sl.readers.append(st)
        for sl in writes:
            sl.writer = st
            sl.readers = []
        return st

    def finalize(self):
        for e in ENGINES:
            for s in self.per_engine[e]:
                if s.inc == 16:
                    s.needed = True
                for d in s.deps:
                    if not (d.eng == "tensor" and s.eng == "tensor"):
                        d.needed = True
        chans = {}
        for e in ENGINES:
            for s in self.per_engine[e]:
                if s.needed:
                    chans[s.chan] = chans.get(s.chan, 0) + s.inc
                s.after = chans.get(s.chan, 0)
        for e in ENGINES:
            prev_dma_after = 0
            for s in self.per_engine[e]:
                w = {}
                for d in s.deps:
                    if d.eng == "tensor" and s.eng == "tensor":
                        continue
                    assert d.needed and d.after > 0
                    w[d.chan] = max(w.get(d.chan, 0), d.after)
                if s.inc == 16:
                    if prev_dma_after:
                        w[s.chan] = max(w.get(s.chan, 0), prev_dma_after)
                    prev_dma_after = s.after
                s.waits = sorted(w.items())
        self.chans = chans

    def build(self, nc):
        self.finalize()
        with contextlib.ExitStack() as st:
            sems = {ch: st.enter_context(nc.semaphore(f"ch_{ch}"))
                    for ch in self.chans}
            block = st.enter_context(nc.Block())

            def section(eng_name):
                steps = self.per_engine[eng_name]

                def body(eng):
                    waited = {}
                    for s in steps:
                        for (src, val) in s.waits:
                            if val > waited.get(src, 0):
                                eng.wait_ge(sems[src], val)
                                waited[src] = val
                        inst = s.emit(eng)
                        if s.needed:
                            inst.then_inc(sems[s.chan], s.inc)
                    if eng_name == "gpsimd":
                        dch = "gpsimd_d"
                        tot = self.chans.get(dch, 0)
                        if tot:
                            eng.wait_ge(sems[dch], tot)
                return body

            for e in ENGINES:
                getattr(block, e)(section(e))


def _mk_nc():
    return bass.Bass("TRN2", target_bir_lowering=False)


class Ctx:

    def __init__(self):
        self.nc = _mk_nc()
        self.plan = Plan()
        self.stack = contextlib.ExitStack()
        self._cbuf = None
        self._cmap = {}

    def cval(self, value):
        if value not in self._cmap:
            if self._cbuf is None:
                self._cbuf = self.sbuf("cconst", [128, 8], F32)
            k = len(self._cmap)
            ap = self._cbuf[:, k:k + 1]
            slot = Slot()
            self.plan.step("vector", lambda e, ap=ap, value=value: e.memset(
                ap, value), writes=[slot])
            self._cmap[value] = (ap, slot)
        return self._cmap[value]

    def sbuf(self, name, shape, dtype):
        return self.stack.enter_context(self.nc.sbuf_tensor(name, shape, dtype))

    def psum(self, name, shape, dtype=F32):
        return self.stack.enter_context(self.nc.psum_tensor(name, shape, dtype))

    def inp(self, name, shape, dtype=F32):
        return self.nc.dram_tensor(name, shape, dtype, kind="ExternalInput").ap()

    def out(self, name, shape, dtype=F32):
        return self.nc.dram_tensor(name, shape, dtype, kind="ExternalOutput").ap()

    def done(self):
        self.plan.build(self.nc)
        self.stack.close()
        return self.nc

    def softplus(self, out_ap, z_ap, m_ap, a_ap, wslots, rslots):
        p = self.plan
        oslot, mslot, aslot = wslots
        s1 = p.step("vector", lambda e: e.tensor_single_scalar(
            m_ap, z_ap, 0.0, ALU.max), reads=rslots, writes=[mslot])
        s2 = p.step("scalar", lambda e: e.activation(
            a_ap, z_ap, AF.Abs), reads=rslots, writes=[aslot])
        s3 = p.step("scalar", lambda e: e.activation(
            a_ap, a_ap, AF.Exp, scale=-1.0), reads=[], writes=[aslot])
        onea, onesl = self.cval(1.0)
        s4 = p.step("scalar", lambda e: e.activation(
            a_ap, a_ap, AF.Ln, bias=onea), reads=[onesl], writes=[aslot])
        s5 = p.step("vector", lambda e: e.tensor_add(
            out_ap, m_ap, a_ap), reads=[mslot, aslot], writes=[oslot])
        return s5


def _proj_setup(c, wd_dram, wi_dram, bd_dram, bi_dram, al_dram, tag):
    p = c.plan
    wdf = c.sbuf(f"wdf{tag}", [128, 8 * N], F32)
    wdb = c.sbuf(f"wdb{tag}", [128, 8 * N], BF16)
    wif = c.sbuf(f"wif{tag}", [128, 8 * N], F32)
    wib = c.sbuf(f"wib{tag}", [128, 8 * N], BF16)
    bd = c.sbuf(f"bd{tag}", [128, 2], F32)
    bi = c.sbuf(f"bi{tag}", [128, 2], F32)
    al = c.sbuf(f"al{tag}", [128, 2], F32)
    spa = c.sbuf(f"spa{tag}", [128, 2], F32)
    spm = c.sbuf(f"spm{tag}", [128, 2], F32)
    sl = {k: Slot() for k in ("wdf", "wdb", "wif", "wib", "bd", "bi", "al",
                              "spa", "spm", "spaa")}
    ld1 = p.step("sync", lambda e: e.dma_start(
        out=wdf[:, :], in_=wd_dram.rearrange("(dc p) n -> p dc n", p=128)),
        writes=[sl["wdf"]], dma=True)
    p.step("gpsimd", lambda e: e.tensor_copy(out=wdb[:, :], in_=wdf[:, :]),
           reads=[sl["wdf"]], writes=[sl["wdb"]])
    ld2 = p.step("sync", lambda e: e.dma_start(
        out=wif[:, :], in_=wi_dram.rearrange("(dc p) n -> p dc n", p=128)),
        writes=[sl["wif"]], dma=True)
    p.step("gpsimd", lambda e: e.tensor_copy(out=wib[:, :], in_=wif[:, :]),
           reads=[sl["wif"]], writes=[sl["wib"]])
    p.step("sync", lambda e: e.dma_start(out=bd[:, :], in_=bd_dram),
           writes=[sl["bd"]], dma=True)
    p.step("sync", lambda e: e.dma_start(out=bi[:, :], in_=bi_dram),
           writes=[sl["bi"]], dma=True)
    p.step("sync", lambda e: e.dma_start(out=al[:, :], in_=al_dram),
           writes=[sl["al"]], dma=True)
    c.softplus(spa[:, :], al[:, :], spm[:, :], spa[:, :],
               (sl["spaa"], sl["spm"], sl["spa"]), [sl["al"]])
    return dict(wdb=wdb, wib=wib, bd=bd, bi=bi, spa=spa, sl=sl)


def _proj_emit(c, pr, rhs_sb, rhs_slots, rhs_col0, dec_dram, drv_dram,
               b, s0, scratch, ps_d, ps_u, psl):
    p = c.plan
    wdb, wib, bd, bi, spa = pr["wdb"], pr["wib"], pr["bd"], pr["bi"], pr["spa"]
    prsl = pr["sl"]
    for ncase, (wsb, wslot, psb, pslot) in enumerate(
            [(wdb, prsl["wdb"], ps_d, psl[0]), (wib, prsl["wib"], ps_u, psl[1])]):
        for nch in range(2):
            for dc in range(8):
                lhsT = wsb[:, dc * N + nch * 128: dc * N + (nch + 1) * 128]
                rhs = rhs_sb[:, rhs_col0 + dc * 512: rhs_col0 + (dc + 1) * 512]
                p.step("tensor", lambda e, lhsT=lhsT, rhs=rhs, psb=psb, nch=nch,
                       dc=dc: e.matmul(psb[:, nch * 512:(nch + 1) * 512],
                                       lhsT, rhs, start=(dc == 0), stop=(dc == 7)),
                       reads=[wslot] + rhs_slots[dc], writes=[pslot[nch]])
    for nch in range(2):
        zd, zd_s = scratch["zd"]
        zu, zu_s = scratch["zu"]
        sm, sm_s = scratch["sm"]
        sa, sa_s = scratch["sa"]
        dr, dr_s = scratch["dr"]
        o = nch * 512
        p.step("scalar", lambda e, nch=nch, o=o: e.activation(
            zd[:, o:o + 512], ps_d[:, o:o + 512], AF.Identity, bias=bd[:, nch:nch + 1]),
            reads=[psl[0][nch], prsl["bd"]], writes=[zd_s[nch]])
        p.step("scalar", lambda e, nch=nch, o=o: e.activation(
            zu[:, o:o + 512], ps_u[:, o:o + 512], AF.Identity, bias=bi[:, nch:nch + 1]),
            reads=[psl[1][nch], prsl["bi"]], writes=[zu_s[nch]])
        p.step("vector", lambda e, o=o: e.tensor_single_scalar(
            sm[:, o:o + 512], zd[:, o:o + 512], 0.0, ALU.max),
            reads=[zd_s[nch]], writes=[sm_s[nch]])
        p.step("scalar", lambda e, o=o: e.activation(
            sa[:, o:o + 512], zd[:, o:o + 512], AF.Abs),
            reads=[zd_s[nch]], writes=[sa_s[nch]])
        p.step("scalar", lambda e, o=o: e.activation(
            sa[:, o:o + 512], sa[:, o:o + 512], AF.Exp, scale=-1.0),
            writes=[sa_s[nch]])
        onea, onesl = c.cval(1.0)
        p.step("scalar", lambda e, o=o, onea=onea: e.activation(
            sa[:, o:o + 512], sa[:, o:o + 512], AF.Ln, bias=onea),
            reads=[onesl], writes=[sa_s[nch]])
        p.step("vector", lambda e, o=o: e.tensor_add(
            zd[:, o:o + 512], sm[:, o:o + 512], sa[:, o:o + 512]),
            reads=[sm_s[nch], sa_s[nch]], writes=[zd_s[nch]])
        p.step("vector", lambda e, nch=nch, o=o: e.tensor_single_scalar(
            sm[:, o:o + 512], zd[:, o:o + 512], spa[:, nch:nch + 1], ALU.mult),
            reads=[zd_s[nch], prsl["spaa"]], writes=[sm_s[nch]])
        p.step("scalar", lambda e, o=o: e.activation(
            sa[:, o:o + 512], sm[:, o:o + 512], AF.Exp, scale=-1.0),
            reads=[sm_s[nch]], writes=[sa_s[nch]])
        p.step("vector", lambda e, o=o: e.tensor_mul(
            dr[:, o:o + 512], zd[:, o:o + 512], zu[:, o:o + 512]),
            reads=[zd_s[nch], zu_s[nch]], writes=[dr_s[nch]])
        p.step("gpsimd", lambda e, nch=nch, o=o, b=b, s0=s0: e.dma_start(
            out=dec_dram[b, nch * 128:(nch + 1) * 128, s0:s0 + 512],
            in_=sa[:, o:o + 512]), reads=[sa_s[nch]], dma=True)
        p.step("gpsimd", lambda e, nch=nch, o=o, b=b, s0=s0: e.dma_start(
            out=drv_dram[b, nch * 128:(nch + 1) * 128, s0:s0 + 512],
            in_=dr[:, o:o + 512]), reads=[dr_s[nch]], dma=True)


def build_A(ngroups=8, parts=("tr", "store", "proj")):
    c = Ctx()
    p = c.plan
    x = c.inp("x", [TOK, D])
    wd = c.inp("wd", [D, N])
    wi = c.inp("wi", [D, N])
    bdd = c.inp("bd", [128, 2])
    bid = c.inp("bi", [128, 2])
    ald = c.inp("al", [128, 2])
    idn = c.inp("idn", [128, 128], BF16)
    xnT = c.out("xnT", [D, TOK], BF16)
    dec = c.out("dec", [B, N, SL])
    drv = c.out("drv", [B, N, SL])

    idn_sb = c.sbuf("idn_sb", [128, 128], BF16)
    idn_sl = Slot()
    p.step("sync", lambda e: e.dma_start(out=idn_sb[:, :], in_=idn),
           writes=[idn_sl], dma=True)
    pr = _proj_setup(c, wd, wi, bdd, bid, ald, "0") if "proj" in parts else None

    xs = c.sbuf("xs", [128, 8 * 1024], F32)
    xc = c.sbuf("xc", [128, 2 * 1024], F32)
    sq = c.sbuf("sq", [128, 2 * 1024], F32)
    cols = c.sbuf("cols", [128, 2 * 16], F32)
    xnb = c.sbuf("xnb", [128, 2 * 4 * 1024], BF16)
    xnt = c.sbuf("xnt", [128, 2 * 8 * 512], BF16)
    ps_tr = c.psum("ps_tr", [128, 2048], BF16)
    ps_d = c.psum("ps_d", [128, 1024], F32)
    ps_u = c.psum("ps_u", [128, 1024], F32)

    xs_sl = [[Slot() for _ in range(4)] for _ in range(2)]
    xc_sl = [Slot() for _ in range(2)]
    sq_sl = [Slot() for _ in range(2)]
    col_sl = [[Slot() for _ in range(4)] for _ in range(2)]
    xnb_sl = [[Slot() for _ in range(4)] for _ in range(2)]
    xnt_sl = [[[Slot() for _ in range(4)] for _ in range(8)] for _ in range(2)]
    tr_sl = [Slot() for _ in range(4)]
    psl = [[Slot(), Slot()], [Slot(), Slot()]]

    scr = {}
    for nm in ("zd", "zu", "sm", "sa", "dr"):
        t = c.sbuf(f"{nm}0s", [128, 2 * 1024], F32)
        scr[nm] = (t, [[Slot(), Slot()] for _ in range(2)])

    for g in range(ngroups):
        r = g % 2
        b = g // 2
        s0 = (g % 2) * 512
        for st in range(4):
            tok0 = g * 512 + st * 128
            xo = (r * 4 + st) * 1024
            p.step("sync", lambda e, xo=xo, tok0=tok0: e.dma_start(
                out=xs[:, xo:xo + 1024], in_=x[tok0:tok0 + 128, :]),
                writes=[xs_sl[r][st]], dma=True)
            mu = cols[:, r * 16 + st * 4 + 0: r * 16 + st * 4 + 1]
            var = cols[:, r * 16 + st * 4 + 1: r * 16 + st * 4 + 2]
            std = cols[:, r * 16 + st * 4 + 2: r * 16 + st * 4 + 3]
            rstd = cols[:, r * 16 + st * 4 + 3: r * 16 + st * 4 + 4]
            cs = col_sl[r][st]
            p.step("vector", lambda e, mu=mu, xo=xo: e.reduce_sum(
                mu, xs[:, xo:xo + 1024], axis=AX.X),
                reads=[xs_sl[r][st]], writes=[cs])
            p.step("scalar", lambda e, mu=mu: e.mul(mu, mu, 1.0 / D), writes=[cs])
            p.step("vector", lambda e, mu=mu, xo=xo, r=r: e.tensor_single_scalar(
                xc[:, r * 1024:(r + 1) * 1024], xs[:, xo:xo + 1024], mu, ALU.subtract),
                reads=[xs_sl[r][st], cs], writes=[xc_sl[r]])
            p.step("scalar", lambda e, var=var, r=r: e.activation(
                sq[:, r * 1024:(r + 1) * 1024], xc[:, r * 1024:(r + 1) * 1024],
                AF.Square, accum_out=var),
                reads=[xc_sl[r]], writes=[sq_sl[r], cs])
            epsa, epssl = c.cval(EPS)
            p.step("scalar", lambda e, std=std, var=var, epsa=epsa: e.activation(
                std, var, AF.Sqrt, scale=1.0 / D, bias=epsa),
                reads=[epssl], writes=[cs])
            p.step("vector", lambda e, rstd=rstd, std=std: e.reciprocal(
                rstd, std), writes=[cs])
            xno = (r * 4 + st) * 1024
            p.step("vector", lambda e, rstd=rstd, xno=xno, r=r: e.tensor_single_scalar(
                xnb[:, xno:xno + 1024], xc[:, r * 1024:(r + 1) * 1024], rstd, ALU.mult),
                reads=[xc_sl[r], cs], writes=[xnb_sl[r][st]])
            for dc in range(8 if "tr" in parts else 0):
                pt = dc % 2
                trs = tr_sl[pt]
                p.step("tensor", lambda e, xno=xno, dc=dc, pt=pt: e.transpose(
                    ps_tr[:, pt * 1024: pt * 1024 + 128],
                    xnb[:, xno + dc * 128: xno + (dc + 1) * 128], idn_sb[:, :]),
                    reads=[xnb_sl[r][st], idn_sl], writes=[trs])
                dst = (r * 8 + dc) * 512 + st * 128
                p.step("scalar", lambda e, dst=dst, pt=pt: e.copy(
                    xnt[:, dst:dst + 128], ps_tr[:, pt * 1024: pt * 1024 + 128]),
                    reads=[trs], writes=[xnt_sl[r][dc][st]])
        for dc in range(8 if "store" in parts else 0):
            src = (r * 8 + dc) * 512
            p.step("gpsimd", lambda e, src=src, dc=dc, g=g: e.dma_start(
                out=xnT[dc * 128:(dc + 1) * 128, g * 512:(g + 1) * 512],
                in_=xnt[:, src:src + 512]),
                reads=xnt_sl[r][dc], dma=True)
        if "proj" in parts:
            scratch = {nm: (scr[nm][0][:, r * 1024:(r + 1) * 1024], scr[nm][1][r])
                       for nm in scr}
            rhs_slots = [xnt_sl[r][dc] for dc in range(8)]
            _proj_emit(c, pr, xnt, rhs_slots, r * 8 * 512, dec, drv, b, s0,
                       scratch, ps_d, ps_u, psl)
    return c.done()


def build_scan(chunk=2048):
    c = Ctx()
    p = c.plan
    dec = c.inp("dec", [128, S])
    drv = c.inp("drv", [128, S])
    hs = c.out("hs", [128, S])
    nch = S // chunk
    sd = c.sbuf("sd", [128, 2 * chunk], F32)
    sv = c.sbuf("sv", [128, 2 * chunk], F32)
    sh = c.sbuf("sh", [128, 2 * chunk], F32)
    sd_sl = [Slot(), Slot()]
    sv_sl = [Slot(), Slot()]
    sh_sl = [Slot(), Slot()]
    for ci in range(nch):
        r = ci % 2
        o = r * chunk
        p.step("sync", lambda e, o=o, ci=ci: e.dma_start(
            out=sd[:, o:o + chunk], in_=dec[:, ci * chunk:(ci + 1) * chunk]),
            writes=[sd_sl[r]], dma=True)
        p.step("sync", lambda e, o=o, ci=ci: e.dma_start(
            out=sv[:, o:o + chunk], in_=drv[:, ci * chunk:(ci + 1) * chunk]),
            writes=[sv_sl[r]], dma=True)
        init = 0.0 if ci == 0 else sh[:, (1 - r) * chunk + chunk - 1:(1 - r) * chunk + chunk]
        reads = [sd_sl[r], sv_sl[r]] + ([sh_sl[1 - r]] if ci else [])
        p.step("vector", lambda e, o=o, init=init: e.tensor_tensor_scan(
            out=sh[:, o:o + chunk], data0=sd[:, o:o + chunk],
            data1=sv[:, o:o + chunk], initial=init,
            op0=ALU.mult, op1=ALU.add),
            reads=reads, writes=[sh_sl[r]])
        p.step("gpsimd", lambda e, o=o, ci=ci: e.dma_start(
            out=hs[:, ci * chunk:(ci + 1) * chunk], in_=sh[:, o:o + chunk]),
            reads=[sh_sl[r]], dma=True)
    return c.done()


def build_C():
    c = Ctx()
    p = c.plan
    hs0 = c.inp("hs0", [B, N, SL])
    f0t = c.inp("f0t", [N, B])
    wo = c.inp("wo", [N, D])
    ws = c.inp("ws", [N, D])
    bob = c.inp("bob", [128, 8])
    bsb = c.inp("bsb", [128, 8])
    wd = c.inp("wd", [D, N])
    wi = c.inp("wi", [D, N])
    bdd = c.inp("bd", [128, 2])
    bid = c.inp("bi", [128, 2])
    ald = c.inp("al", [128, 2])
    dec = c.out("dec", [B, N, SL])
    drv = c.out("drv", [B, N, SL])

    pr = _proj_setup(c, wd, wi, bdd, bid, ald, "1")

    wof = c.sbuf("wof", [128, 2 * D], F32)
    wob = c.sbuf("wob", [128, 2 * D], BF16)
    wsf = c.sbuf("wsf", [128, 2 * D], F32)
    wsb = c.sbuf("wsb", [128, 2 * D], BF16)
    f0f = c.sbuf("f0f", [128, 2 * B], F32)
    f0b = c.sbuf("f0b", [128, 2 * B], BF16)
    bo_sb = c.sbuf("bo_sb", [128, 8], F32)
    bs_sb = c.sbuf("bs_sb", [128, 8], F32)
    bsum = c.sbuf("bsum", [128, 8], F32)
    cls = c.sbuf("cls", [128, 32], F32)
    sl = {k: Slot() for k in ("wof", "wob", "wsf", "wsb", "f0f", "f0b",
                              "bo", "bs", "bsum", "cls")}
    for nm, dst, src in (("wof", wof, wo), ("wsf", wsf, ws)):
        p.step("sync", lambda e, dst=dst, src=src: e.dma_start(
            out=dst[:, :], in_=src.rearrange("(nc p) d -> p nc d", p=128)),
            writes=[sl[nm]], dma=True)
    p.step("gpsimd", lambda e: e.tensor_copy(out=wob[:, :], in_=wof[:, :]),
           reads=[sl["wof"]], writes=[sl["wob"]])
    p.step("gpsimd", lambda e: e.tensor_copy(out=wsb[:, :], in_=wsf[:, :]),
           reads=[sl["wsf"]], writes=[sl["wsb"]])
    p.step("sync", lambda e: e.dma_start(
        out=f0f[:, :], in_=f0t.rearrange("(nc p) b -> p nc b", p=128)),
        writes=[sl["f0f"]], dma=True)
    p.step("gpsimd", lambda e: e.tensor_copy(out=f0b[:, :], in_=f0f[:, :]),
           reads=[sl["f0f"]], writes=[sl["f0b"]])
    p.step("sync", lambda e: e.dma_start(out=bo_sb[:, :], in_=bob),
           writes=[sl["bo"]], dma=True)
    p.step("sync", lambda e: e.dma_start(out=bs_sb[:, :], in_=bsb),
           writes=[sl["bs"]], dma=True)
    p.step("vector", lambda e: e.tensor_add(bsum[:, :], bo_sb[:, :], bs_sb[:, :]),
           reads=[sl["bo"], sl["bs"]], writes=[sl["bsum"]])

    ps_c = c.psum("ps_c", [128, 512], F32)
    psc_sl = Slot()
    for dc in range(8):
        for nch in range(2):
            p.step("tensor", lambda e, dc=dc, nch=nch: e.matmul(
                ps_c[:, 0:4],
                wsb[:, nch * D + dc * 128: nch * D + (dc + 1) * 128],
                f0b[:, nch * B:(nch + 1) * B],
                start=(nch == 0), stop=(nch == 1)),
                reads=[sl["wsb"], sl["f0b"]], writes=[psc_sl])
        p.step("scalar", lambda e, dc=dc: e.activation(
            cls[:, dc * 4:(dc + 1) * 4], ps_c[:, 0:4],
            AF.Identity, bias=bsum[:, dc:dc + 1]),
            reads=[psc_sl, sl["bsum"]], writes=[sl["cls"]])

    hsf = c.sbuf("hsf", [128, 2 * 2 * 512], F32)
    hsb = c.sbuf("hsb", [128, 2 * 2 * 512], BF16)
    h1f = c.sbuf("h1f", [128, 2 * 512], F32)
    h1b = c.sbuf("h1b", [128, 2 * 8 * 512], BF16)
    ps_y = c.psum("ps_y", [128, 1024], F32)
    ps_d = c.psum("ps_d", [128, 1024], F32)
    ps_u = c.psum("ps_u", [128, 1024], F32)
    hsf_sl = [[Slot(), Slot()] for _ in range(2)]
    hsb_sl = [[Slot(), Slot()] for _ in range(2)]
    h1f_sl = [Slot(), Slot()]
    h1b_sl = [[Slot() for _ in range(8)] for _ in range(2)]
    psy_sl = [Slot(), Slot()]
    psl = [[Slot(), Slot()], [Slot(), Slot()]]
    scr = {}
    for nm in ("zd", "zu", "sm", "sa", "dr"):
        t = c.sbuf(f"{nm}1s", [128, 2 * 1024], F32)
        scr[nm] = (t, [[Slot(), Slot()] for _ in range(2)])

    for g in range(8):
        r = g % 2
        b = g // 2
        s0 = (g % 2) * 512
        for nch in range(2):
            o = (r * 2 + nch) * 512
            p.step("sync", lambda e, o=o, b=b, nch=nch, s0=s0: e.dma_start(
                out=hsf[:, o:o + 512],
                in_=hs0[b, nch * 128:(nch + 1) * 128, s0:s0 + 512]),
                writes=[hsf_sl[r][nch]], dma=True)
            p.step("gpsimd", lambda e, o=o: e.tensor_copy(
                out=hsb[:, o:o + 512], in_=hsf[:, o:o + 512]),
                reads=[hsf_sl[r][nch]], writes=[hsb_sl[r][nch]])
        for dc in range(8):
            pr_i = dc % 2
            for nch in range(2):
                p.step("tensor", lambda e, dc=dc, nch=nch, r=r, pr_i=pr_i: e.matmul(
                    ps_y[:, pr_i * 512:(pr_i + 1) * 512],
                    wob[:, nch * D + dc * 128: nch * D + (dc + 1) * 128],
                    hsb[:, (r * 2 + nch) * 512:(r * 2 + nch + 1) * 512],
                    start=(nch == 0), stop=(nch == 1)),
                    reads=[sl["wob"], hsb_sl[r][nch]], writes=[psy_sl[pr_i]])
            p.step("scalar", lambda e, dc=dc, b=b, r=r, pr_i=pr_i: e.activation(
                h1f[:, r * 512:(r + 1) * 512], ps_y[:, pr_i * 512:(pr_i + 1) * 512],
                AF.Identity, bias=cls[:, dc * 4 + b: dc * 4 + b + 1]),
                reads=[psy_sl[pr_i], sl["cls"]], writes=[h1f_sl[r]])
            p.step("vector", lambda e, dc=dc, r=r: e.tensor_copy(
                out=h1b[:, (r * 8 + dc) * 512:(r * 8 + dc + 1) * 512],
                in_=h1f[:, r * 512:(r + 1) * 512]),
                reads=[h1f_sl[r]], writes=[h1b_sl[r][dc]])
        scratch = {nm: (scr[nm][0][:, r * 1024:(r + 1) * 1024], scr[nm][1][r])
                   for nm in scr}
        rhs_slots = [[h1b_sl[r][dc]] for dc in range(8)]
        _proj_emit(c, pr, h1b, rhs_slots, r * 8 * 512, dec, drv, b, s0,
                   scratch, ps_d, ps_u, psl)
    return c.done()


def build_E1():
    c = Ctx()
    p = c.plan
    hs1 = c.inp("hs1", [B, N, SL])
    f1t = c.inp("f1t", [N, B])
    wo = c.inp("wo", [N, D])
    ws = c.inp("ws", [N, D])
    bor = c.inp("bor", [1, D])
    bsr = c.inp("bsr", [1, D])
    s0o = c.out("s0", [TOK, D])

    wof = c.sbuf("wof", [128, 2 * D], F32)
    wob = c.sbuf("wob", [128, 2 * D], BF16)
    wsf = c.sbuf("wsf", [128, 2 * D], F32)
    wsb = c.sbuf("wsb", [128, 2 * D], BF16)
    f1f = c.sbuf("f1f", [128, 2 * B], F32)
    f1b = c.sbuf("f1b", [128, 2 * B], BF16)
    brow = c.sbuf("brow", [1, D], F32)
    bsrow = c.sbuf("bsrow", [1, D], F32)
    ones4 = c.sbuf("ones4", [1, B], F32)
    ones1 = c.sbuf("ones1", [1, 128], F32)
    rall = c.sbuf("rall", [B, D], F32)
    rflat = c.sbuf("rflat", [1, B * D], F32)
    rhi = c.sbuf("rhi", [1, B * D], BF16)
    rhif = c.sbuf("rhif", [1, B * D], F32)
    rlo = c.sbuf("rlo", [1, B * D], BF16)
    onesb = c.sbuf("onesb", [1, 128], BF16)
    sl = {k: Slot() for k in ("wof", "wob", "wsf", "wsb", "f1f", "f1b", "bo",
                              "bs", "ones4", "ones1", "rall", "rflat", "rhi",
                              "rhif", "rlo", "onesb")}
    for nm, dst, src in (("wof", wof, wo), ("wsf", wsf, ws)):
        p.step("sync", lambda e, dst=dst, src=src: e.dma_start(
            out=dst[:, :], in_=src.rearrange("(nc p) d -> p nc d", p=128)),
            writes=[sl[nm]], dma=True)
    p.step("gpsimd", lambda e: e.tensor_copy(out=wob[:, :], in_=wof[:, :]),
           reads=[sl["wof"]], writes=[sl["wob"]])
    p.step("gpsimd", lambda e: e.tensor_copy(out=wsb[:, :], in_=wsf[:, :]),
           reads=[sl["wsf"]], writes=[sl["wsb"]])
    p.step("sync", lambda e: e.dma_start(
        out=f1f[:, :], in_=f1t.rearrange("(nc p) b -> p nc b", p=128)),
        writes=[sl["f1f"]], dma=True)
    p.step("gpsimd", lambda e: e.tensor_copy(out=f1b[:, :], in_=f1f[:, :]),
           reads=[sl["f1f"]], writes=[sl["f1b"]])
    p.step("sync", lambda e: e.dma_start(out=brow[:, :], in_=bor),
           writes=[sl["bo"]], dma=True)
    p.step("sync", lambda e: e.dma_start(out=bsrow[:, :], in_=bsr),
           writes=[sl["bs"]], dma=True)
    p.step("vector", lambda e: e.memset(ones4[:, :], 1.0), writes=[sl["ones4"]])
    p.step("vector", lambda e: e.memset(ones1[:, :], 1.0), writes=[sl["ones1"]])
    p.step("vector", lambda e: e.memset(onesb[:, :], 1.0), writes=[sl["onesb"]])

    ps_c = c.psum("ps_c", [128, 1024], F32)
    psc_sl = Slot()
    for dg in range(2):
        for nch in range(2):
            p.step("tensor", lambda e, dg=dg, nch=nch: e.matmul(
                ps_c[0:4, dg * 512:(dg + 1) * 512],
                f1b[:, nch * B:(nch + 1) * B],
                wsb[:, nch * D + dg * 512: nch * D + (dg + 1) * 512],
                start=(nch == 0), stop=False),
                reads=[sl["f1b"], sl["wsb"]], writes=[psc_sl])
        for bi_, (src, slot) in enumerate(((brow, sl["bo"]), (bsrow, sl["bs"]))):
            p.step("tensor", lambda e, dg=dg, src=src, bi_=bi_: e.matmul(
                ps_c[0:4, dg * 512:(dg + 1) * 512], ones4[:, :],
                src[:, dg * 512:(dg + 1) * 512], start=False, stop=(bi_ == 1)),
                reads=[sl["ones4"], slot], writes=[psc_sl])
    p.step("scalar", lambda e: e.copy(rall[:, :], ps_c[0:4, :]),
           reads=[psc_sl], writes=[sl["rall"]])
    for b in range(B):
        p.step("sync", lambda e, b=b: e.dma_start(
            out=rflat[:, b * D:(b + 1) * D], in_=rall[b:b + 1, :]),
            reads=[sl["rall"]], writes=[sl["rflat"]], dma=True)
    p.step("scalar", lambda e: e.copy(rhi[:, :], rflat[:, :]),
           reads=[sl["rflat"]], writes=[sl["rhi"]])
    p.step("scalar", lambda e: e.copy(rhif[:, :], rhi[:, :]),
           reads=[sl["rhi"]], writes=[sl["rhif"]])
    p.step("vector", lambda e: e.tensor_sub(rhif[:, :], rflat[:, :], rhif[:, :]),
           reads=[sl["rflat"]], writes=[sl["rhif"]])
    p.step("scalar", lambda e: e.copy(rlo[:, :], rhif[:, :]),
           reads=[sl["rhif"]], writes=[sl["rlo"]])

    hsf = c.sbuf("hsf", [128, 2 * 2 * 512], F32)
    hsb = c.sbuf("hsb", [128, 2 * 2 * 512], BF16)
    s0b = c.sbuf("s0b", [128, 2 * 4 * 1024], F32)
    ps_y = c.psum("ps_y", [128, 2 * 1024], F32)
    hsf_sl = [[Slot(), Slot()] for _ in range(2)]
    hsb_sl = [[Slot(), Slot()] for _ in range(2)]
    s0_sl = [[Slot() for _ in range(4)] for _ in range(2)]
    psy_sl = [[Slot(), Slot()], [Slot(), Slot()]]
    for g in range(8):
        r = g % 2
        b = g // 2
        s0 = (g % 2) * 512
        for nch in range(2):
            o = (r * 2 + nch) * 512
            p.step("sync", lambda e, o=o, b=b, nch=nch, s0=s0: e.dma_start(
                out=hsf[:, o:o + 512],
                in_=hs1[b, nch * 128:(nch + 1) * 128, s0:s0 + 512]),
                writes=[hsf_sl[r][nch]], dma=True)
            p.step("gpsimd", lambda e, o=o: e.tensor_copy(
                out=hsb[:, o:o + 512], in_=hsf[:, o:o + 512]),
                reads=[hsf_sl[r][nch]], writes=[hsb_sl[r][nch]])
        for st in range(4):
            rr = st % 2
            for dg in range(2):
                po = (rr * 2 + dg) * 512
                for nch in range(2):
                    p.step("tensor", lambda e, po=po, nch=nch, st=st, dg=dg, r=r: e.matmul(
                        ps_y[:, po:po + 512],
                        hsb[:, (r * 2 + nch) * 512 + st * 128:(r * 2 + nch) * 512 + (st + 1) * 128],
                        wob[:, nch * D + dg * 512: nch * D + (dg + 1) * 512],
                        start=(nch == 0), stop=False),
                        reads=[hsb_sl[r][nch], sl["wob"]], writes=[psy_sl[rr][dg]])
                for bi_, (src, slot) in enumerate(((rhi, sl["rhi"]), (rlo, sl["rlo"]))):
                    p.step("tensor", lambda e, po=po, src=src, b=b, dg=dg, bi_=bi_: e.matmul(
                        ps_y[:, po:po + 512], onesb[:, :],
                        src[:, b * D + dg * 512: b * D + (dg + 1) * 512],
                        start=False, stop=(bi_ == 1)),
                        reads=[sl["onesb"], slot], writes=[psy_sl[rr][dg]])
                p.step("scalar", lambda e, po=po, r=r, st=st, dg=dg: e.copy(
                    s0b[:, (r * 4 + st) * 1024 + dg * 512:(r * 4 + st) * 1024 + (dg + 1) * 512],
                    ps_y[:, po:po + 512]),
                    reads=[psy_sl[rr][dg]], writes=[s0_sl[r][st]])
            tok0 = g * 512 + st * 128
            p.step("gpsimd", lambda e, r=r, st=st, tok0=tok0: e.dma_start(
                out=s0o[tok0:tok0 + 128, :],
                in_=s0b[:, (r * 4 + st) * 1024:(r * 4 + st + 1) * 1024]),
                reads=[s0_sl[r][st]], dma=True)
    return c.done()


def build_E2():
    c = Ctx()
    p = c.plan
    xnT = c.inp("xnT", [D, TOK], BF16)
    gwt = c.inp("gwt", [D, E])
    w1t = c.inp("w1t", [E, D, I])
    w3t = c.inp("w3t", [E, D, I])
    w2t = c.inp("w2t", [E, I, D])
    go = c.out("g", [TOK, D])

    w1s = c.nc.dram_tensor("w1s", [E, 128, 16384], BF16).ap()
    w3s = c.nc.dram_tensor("w3s", [E, 128, 16384], BF16).ap()
    w2s = c.nc.dram_tensor("w2s", [E, 128, 16384], BF16).ap()

    gwf = c.sbuf("gwf", [128, 8 * E], F32)
    gwb = c.sbuf("gwb", [128, 8 * E], BF16)
    sl_gw = Slot()
    sl_gwb = Slot()
    p.step("sync", lambda e: e.dma_start(
        out=gwf[:, :], in_=gwt.rearrange("(dc p) ee -> p dc ee", p=128)),
        writes=[sl_gw], dma=True)
    p.step("gpsimd", lambda e: e.tensor_copy(out=gwb[:, :], in_=gwf[:, :]),
           reads=[sl_gw], writes=[sl_gwb])

    TG = 1024
    xnb = c.sbuf("xnb", [128, 8 * TG], BF16)
    outg = c.sbuf("outg", [128, 8 * 1024], F32)
    stg = c.sbuf("stg", [128, 4096], F32)
    stg2 = c.sbuf("stg2", [128, 4096], F32)
    w1h = c.sbuf("w1h", [128, 2 * 8192], BF16)
    w3h = c.sbuf("w3h", [128, 2 * 8192], BF16)
    w2b = c.sbuf("w2b", [128, 16 * 1024], BF16)
    actb = c.sbuf("actb", [128, 2 * 256], BF16)
    silu_t = c.sbuf("silu_t", [128, 2 * 256], F32)
    lg = c.sbuf("lg", [128, 2 * 8], F32)
    gsc = c.sbuf("gsc", [128, 6 * 8], F32)
    gs2 = c.sbuf("gs2", [128, 3 * 8], F32)
    gcol = c.sbuf("gcol", [128, 8], F32)
    dwb = c.sbuf("dwb", [128, 8 * 8], F32)
    gsb = c.sbuf("gsb", [128, 2 * 1024], F32)

    ps_l = c.psum("ps_l", [128, 512], F32)
    ps_h1 = c.psum("ps_h1", [128, 512], F32)
    ps_h3 = c.psum("ps_h3", [128, 512], F32)
    ps_y = c.psum("ps_y", [128, 4 * 512], F32)

    xnb_sl = [Slot() for _ in range(8)]
    outg_sl = [Slot() for _ in range(8)]
    stg_sl = Slot()
    stg2_sl = Slot()
    w1s_sl = [Slot() for _ in range(E)]
    w3s_sl = [Slot() for _ in range(E)]
    w2s_sl = [Slot() for _ in range(E)]
    w1h_sl = [[Slot(), Slot()] for _ in range(2)]
    w3h_sl = [[Slot(), Slot()] for _ in range(2)]
    w2b_sl = [Slot(), Slot()]
    actb_sl = [Slot(), Slot()]
    silu_sl = [Slot(), Slot()]
    lg_sl = [Slot(), Slot()]
    gsc_sl = [Slot() for _ in range(6)]
    gs2_sl = [Slot() for _ in range(3)]
    gcol_sl = [Slot() for _ in range(8)]
    dwb_sl = [Slot() for _ in range(8)]
    gsb_sl = [Slot(), Slot()]
    psl_l = Slot()
    psl_h1 = Slot()
    psl_h3 = Slot()
    psl_y = [Slot() for _ in range(4)]

    for tg in range(4):
        t0 = tg * TG
        for dc in range(8):
            p.step("sync", lambda e, dc=dc, t0=t0: e.dma_start(
                out=xnb[:, dc * TG:(dc + 1) * TG],
                in_=xnT[dc * 128:(dc + 1) * 128, t0:t0 + TG]),
                writes=[xnb_sl[dc]], dma=True)
        for tt in range(8):
            r = tt % 2
            lgr = lg[:, r * 8:(r + 1) * 8]
            lgs = lg_sl[r]
            for dc in range(8):
                p.step("tensor", lambda e, dc=dc, tt=tt: e.matmul(
                    ps_l[:, 0:E], xnb[:, dc * TG + tt * 128: dc * TG + (tt + 1) * 128],
                    gwb[:, dc * E:(dc + 1) * E], start=(dc == 0), stop=(dc == 7)),
                    reads=[xnb_sl[dc], sl_gwb], writes=[psl_l])
            p.step("scalar", lambda e, lgr=lgr: e.copy(lgr, ps_l[:, 0:E]),
                   reads=[psl_l], writes=[lgs])
            oh1, ab, f1, tml, m1, i1 = (gsc[:, k * 8:(k + 1) * 8] for k in range(6))
            s2v, oh2, e2v = (gs2[:, k * 8:(k + 1) * 8] for k in range(3))
            c1, c1n, c2, c3, c4, c4n, c5, c6 = (gcol[:, k:k + 1] for k in range(8))
            st_ = p.step
            st_("vector", lambda e, lgr=lgr, c1=c1: e.reduce_max(c1, lgr, axis=AX.X),
                reads=[lgs], writes=[gcol_sl[0]])
            st_("vector", lambda e, lgr=lgr, oh1=oh1, c1=c1: e.tensor_single_scalar(
                oh1, lgr, c1, ALU.is_equal), reads=[lgs, gcol_sl[0]], writes=[gsc_sl[0]])
            st_("scalar", lambda e, lgr=lgr, ab=ab: e.activation(ab, lgr, AF.Abs),
                reads=[lgs], writes=[gsc_sl[1]])
            st_("vector", lambda e, ab=ab, f1=f1, c1=c1: e.tensor_single_scalar(
                f1, ab, c1, ALU.max), reads=[gsc_sl[1], gcol_sl[0]], writes=[gsc_sl[2]])
            st_("vector", lambda e, lgr=lgr, tml=tml, c1=c1: e.tensor_scalar(
                tml, lgr, c1, -1.0, ALU.subtract, ALU.mult),
                reads=[lgs, gcol_sl[0]], writes=[gsc_sl[3]])
            st_("vector", lambda e, f1=f1, tml=tml, m1=m1: e.scalar_tensor_tensor(
                m1, f1, 0.02, tml, ALU.mult, ALU.is_lt),
                reads=[gsc_sl[2], gsc_sl[3]], writes=[gsc_sl[4]])
            st_("vector", lambda e, m1=m1, i1=i1: e.tensor_scalar(
                i1, m1, -1.0, 1.0, ALU.mult, ALU.add),
                reads=[gsc_sl[4]], writes=[gsc_sl[5]])
            st_("vector", lambda e, c1=c1, c1n=c1n: e.tensor_single_scalar(
                c1n, c1, -1.0, ALU.mult), reads=[gcol_sl[0]], writes=[gcol_sl[1]])
            st_("scalar", lambda e, lgr=lgr, e2v=e2v, c1n=c1n: e.activation(
                e2v, lgr, AF.Exp, bias=c1n), reads=[lgs, gcol_sl[1]], writes=[gs2_sl[2]])
            st_("vector", lambda e, e2v=e2v, i1=i1: e.tensor_mul(e2v, e2v, i1),
                reads=[gsc_sl[5]], writes=[gs2_sl[2]])
            st_("vector", lambda e, e2v=e2v, c2=c2: e.reduce_sum(c2, e2v, axis=AX.X),
                reads=[gs2_sl[2]], writes=[gcol_sl[2]])
            st_("vector", lambda e, c2=c2, c3=c3: e.reciprocal(c3, c2),
                reads=[gcol_sl[2]], writes=[gcol_sl[3]])
            st_("vector", lambda e, oh1=oh1, lgr=lgr, s2v=s2v: e.scalar_tensor_tensor(
                s2v, oh1, -BIG, lgr, ALU.mult, ALU.add),
                reads=[gsc_sl[0], lgs], writes=[gs2_sl[0]])
            st_("vector", lambda e, s2v=s2v, c4=c4: e.reduce_max(c4, s2v, axis=AX.X),
                reads=[gs2_sl[0]], writes=[gcol_sl[4]])
            st_("vector", lambda e, s2v=s2v, oh2=oh2, c4=c4: e.tensor_single_scalar(
                oh2, s2v, c4, ALU.is_equal), reads=[gs2_sl[0], gcol_sl[4]], writes=[gs2_sl[1]])
            st_("vector", lambda e, ab=ab, f1=f1, c4=c4: e.tensor_single_scalar(
                f1, ab, c4, ALU.max), reads=[gsc_sl[1], gcol_sl[4]], writes=[gsc_sl[2]])
            st_("vector", lambda e, lgr=lgr, tml=tml, c4=c4: e.tensor_scalar(
                tml, lgr, c4, -1.0, ALU.subtract, ALU.mult),
                reads=[lgs, gcol_sl[4]], writes=[gsc_sl[3]])
            st_("vector", lambda e, f1=f1, tml=tml, m1=m1: e.scalar_tensor_tensor(
                m1, f1, 0.02, tml, ALU.mult, ALU.is_lt),
                reads=[gsc_sl[2], gsc_sl[3]], writes=[gsc_sl[4]])
            st_("vector", lambda e, m1=m1, i1=i1: e.tensor_scalar(
                i1, m1, -1.0, 1.0, ALU.mult, ALU.add),
                reads=[gsc_sl[4]], writes=[gsc_sl[5]])
            st_("vector", lambda e, c4=c4, c4n=c4n: e.tensor_single_scalar(
                c4n, c4, -1.0, ALU.mult), reads=[gcol_sl[4]], writes=[gcol_sl[5]])
            st_("scalar", lambda e, s2v=s2v, e2v=e2v, c4n=c4n: e.activation(
                e2v, s2v, AF.Exp, bias=c4n), reads=[gs2_sl[0], gcol_sl[5]], writes=[gs2_sl[2]])
            st_("vector", lambda e, e2v=e2v, i1=i1: e.tensor_mul(e2v, e2v, i1),
                reads=[gsc_sl[5]], writes=[gs2_sl[2]])
            st_("vector", lambda e, e2v=e2v, c5=c5: e.reduce_sum(c5, e2v, axis=AX.X),
                reads=[gs2_sl[2]], writes=[gcol_sl[6]])
            st_("vector", lambda e, c5=c5, c6=c6: e.reciprocal(c6, c5),
                reads=[gcol_sl[6]], writes=[gcol_sl[7]])
            st_("vector", lambda e, oh1=oh1, c3=c3: e.tensor_single_scalar(
                oh1, oh1, c3, ALU.mult), reads=[gcol_sl[3]], writes=[gsc_sl[0]])
            st_("vector", lambda e, oh1=oh1, oh2=oh2, c6=c6, tt=tt: e.scalar_tensor_tensor(
                dwb[:, tt * 8:(tt + 1) * 8], oh2, c6, oh1, ALU.mult, ALU.add),
                reads=[gs2_sl[1], gcol_sl[7], gsc_sl[0]], writes=[dwb_sl[tt]])
        for tt in range(8):
            p.step("vector", lambda e, tt=tt: e.memset(
                outg[:, tt * 1024:(tt + 1) * 1024], 0.0), writes=[outg_sl[tt]])
        for ex in range(E):
            if tg == 0:
                li = 0
                for half in range(2):
                    for q in range(2):
                        sb, ssl = (stg, stg_sl) if li % 2 == 0 else (stg2, stg2_sl)
                        li += 1
                        p.step("sync", lambda e, ex=ex, half=half, q=q, sb=sb: e.dma_start(
                            out=sb[:, :],
                            in_=w1t[ex, :, half * 1024 + q * 512:
                                    half * 1024 + (q + 1) * 512].rearrange(
                                "(dc p) i -> p dc i", p=128)),
                            writes=[ssl], dma=True)
                        p.step("vector", lambda e, half=half, q=q, sb=sb: e.tensor_copy(
                            out=w1h[:, half * 8192 + q * 4096:
                                    half * 8192 + (q + 1) * 4096].rearrange(
                                "p (dc i) -> p dc i", dc=8),
                            in_=sb[:, :].rearrange("p (dc i) -> p dc i", dc=8)),
                            reads=[ssl], writes=[w1h_sl[half][q]])
                        sb, ssl = (stg, stg_sl) if li % 2 == 0 else (stg2, stg2_sl)
                        li += 1
                        p.step("sync", lambda e, ex=ex, half=half, q=q, sb=sb: e.dma_start(
                            out=sb[:, :],
                            in_=w3t[ex, :, half * 1024 + q * 512:
                                    half * 1024 + (q + 1) * 512].rearrange(
                                "(dc p) i -> p dc i", p=128)),
                            writes=[ssl], dma=True)
                        p.step("vector", lambda e, half=half, q=q, sb=sb: e.tensor_copy(
                            out=w3h[:, half * 8192 + q * 4096:
                                    half * 8192 + (q + 1) * 4096].rearrange(
                                "p (dc i) -> p dc i", dc=8),
                            in_=sb[:, :].rearrange("p (dc i) -> p dc i", dc=8)),
                            reads=[ssl], writes=[w3h_sl[half][q]])
                        sb, ssl = (stg, stg_sl) if li % 2 == 0 else (stg2, stg2_sl)
                        li += 1
                        p.step("sync", lambda e, ex=ex, half=half, q=q, sb=sb: e.dma_start(
                            out=sb[:, :],
                            in_=w2t[ex, half * 1024 + q * 512:
                                    half * 1024 + (q + 1) * 512, :].rearrange(
                                "(ic p) dd -> p ic dd", p=128)),
                            writes=[ssl], dma=True)
                        p.step("scalar", lambda e, half=half, q=q, sb=sb: e.copy(
                            out=w2b[:, half * 8192 + q * 4096:
                                    half * 8192 + (q + 1) * 4096],
                            in_=sb[:, :]),
                            reads=[ssl], writes=[w2b_sl[half]])
                p.step("gpsimd", lambda e, ex=ex: e.dma_start(
                    out=w1s[ex, :, :], in_=w1h[:, :]),
                    reads=[w1h_sl[0][0], w1h_sl[0][1], w1h_sl[1][0], w1h_sl[1][1]],
                    writes=[w1s_sl[ex]], dma=True)
                p.step("gpsimd", lambda e, ex=ex: e.dma_start(
                    out=w3s[ex, :, :], in_=w3h[:, :]),
                    reads=[w3h_sl[0][0], w3h_sl[0][1], w3h_sl[1][0], w3h_sl[1][1]],
                    writes=[w3s_sl[ex]], dma=True)
                p.step("gpsimd", lambda e, ex=ex: e.dma_start(
                    out=w2s[ex, :, :], in_=w2b[:, :]),
                    reads=[w2b_sl[0], w2b_sl[1]],
                    writes=[w2s_sl[ex]], dma=True)
            else:
                p.step("sync", lambda e, ex=ex: e.dma_start(
                    out=w1h[:, :], in_=w1s[ex, :, :]),
                    reads=[w1s_sl[ex]],
                    writes=[w1h_sl[0][0], w1h_sl[0][1], w1h_sl[1][0], w1h_sl[1][1]],
                    dma=True)
                p.step("gpsimd", lambda e, ex=ex: e.dma_start(
                    out=w2b[:, :], in_=w2s[ex, :, :]),
                    reads=[w2s_sl[ex]],
                    writes=[w2b_sl[0], w2b_sl[1]], dma=True)
                p.step("sync", lambda e, ex=ex: e.dma_start(
                    out=w3h[:, :], in_=w3s[ex, :, :]),
                    reads=[w3s_sl[ex]],
                    writes=[w3h_sl[0][0], w3h_sl[0][1], w3h_sl[1][0], w3h_sl[1][1]],
                    dma=True)
            for ts in range(4):
                to = ts * 256
                for ic in range(16):
                    half, icl = divmod(ic, 8)
                    q, icq = divmod(icl, 4)
                    w1c = half * 8192 + q * 4096
                    ar = ic % 2
                    for dc in range(8):
                        col = w1c + dc * 512 + icq * 128
                        p.step("tensor", lambda e, col=col, dc=dc, to=to: e.matmul(
                            ps_h1[:, 0:256],
                            w1h[:, col:col + 128],
                            xnb[:, dc * TG + to: dc * TG + to + 256],
                            start=(dc == 0), stop=(dc == 7)),
                            reads=[w1h_sl[half][q], xnb_sl[dc]], writes=[psl_h1])
                        p.step("tensor", lambda e, col=col, dc=dc, to=to: e.matmul(
                            ps_h3[:, 0:256],
                            w3h[:, col:col + 128],
                            xnb[:, dc * TG + to: dc * TG + to + 256],
                            start=(dc == 0), stop=(dc == 7)),
                            reads=[w3h_sl[half][q], xnb_sl[dc]], writes=[psl_h3])
                    p.step("scalar", lambda e, ar=ar: e.activation(
                        silu_t[:, ar * 256:(ar + 1) * 256], ps_h1[:, 0:256], AF.Silu),
                        reads=[psl_h1], writes=[silu_sl[ar]])
                    p.step("vector", lambda e, ar=ar: e.tensor_mul(
                        actb[:, ar * 256:(ar + 1) * 256],
                        silu_t[:, ar * 256:(ar + 1) * 256], ps_h3[:, 0:256]),
                        reads=[silu_sl[ar], psl_h3], writes=[actb_sl[ar]])
                    for st4 in range(2):
                        for dg in range(2):
                            yk = st4 * 2 + dg
                            p.step("tensor", lambda e, ar=ar, st4=st4, dg=dg,
                                   yk=yk, ic=ic, half=half: e.matmul(
                                ps_y[:, yk * 512:(yk + 1) * 512],
                                actb[:, ar * 256 + st4 * 128: ar * 256 + (st4 + 1) * 128],
                                w2b[:, ic * 1024 + dg * 512: ic * 1024 + (dg + 1) * 512],
                                start=(ic == 0), stop=(ic == 15)),
                                reads=[actb_sl[ar], w2b_sl[half]], writes=[psl_y[yk]])
                for st4 in range(2):
                    tt = (to + st4 * 128) // 128
                    for dg in range(2):
                        yk = st4 * 2 + dg
                        p.step("vector", lambda e, tt=tt, dg=dg, ex=ex, yk=yk: e.scalar_tensor_tensor(
                            outg[:, tt * 1024 + dg * 512: tt * 1024 + (dg + 1) * 512],
                            ps_y[:, yk * 512:(yk + 1) * 512],
                            dwb[:, tt * 8 + ex: tt * 8 + ex + 1],
                            outg[:, tt * 1024 + dg * 512: tt * 1024 + (dg + 1) * 512],
                            ALU.mult, ALU.add),
                            reads=[psl_y[yk], dwb_sl[tt]], writes=[outg_sl[tt]])
        for tt in range(8):
            r = tt % 2
            p.step("scalar", lambda e, tt=tt, r=r: e.activation(
                gsb[:, r * 1024:(r + 1) * 1024], outg[:, tt * 1024:(tt + 1) * 1024],
                AF.Silu), reads=[outg_sl[tt]], writes=[gsb_sl[r]])
            p.step("gpsimd", lambda e, tt=tt, r=r, t0=t0: e.dma_start(
                out=go[t0 + tt * 128: t0 + (tt + 1) * 128, :],
                in_=gsb[:, r * 1024:(r + 1) * 1024]),
                reads=[gsb_sl[r]], dma=True)
    return c.done()


def build_E3():
    c = Ctx()
    p = c.plan
    x = c.inp("x", [TOK, D])
    s0i = c.inp("s0", [TOK, D])
    gi = c.inp("g", [TOK, D])
    idn = c.inp("idn", [128, 128], BF16)
    x1o = c.out("x1", [TOK, D])
    xn1o = c.out("xn1", [TOK, D])
    xn1T = c.out("xn1T", [D, TOK], BF16)

    idn_sb = c.sbuf("idn_sb", [128, 128], BF16)
    idn_sl = Slot()
    p.step("sync", lambda e: e.dma_start(out=idn_sb[:, :], in_=idn),
           writes=[idn_sl], dma=True)
    xs = c.sbuf("xs", [128, 2 * 1024], F32)
    ss = c.sbuf("ss", [128, 2 * 1024], F32)
    gs = c.sbuf("gs", [128, 2 * 1024], F32)
    x1t = c.sbuf("x1t", [128, 2 * 1024], F32)
    xc = c.sbuf("xc", [128, 2 * 1024], F32)
    sq = c.sbuf("sq", [128, 2 * 1024], F32)
    xnf = c.sbuf("xnf", [128, 2 * 1024], F32)
    xnb = c.sbuf("xnb", [128, 2 * 1024], BF16)
    xnt = c.sbuf("xnt", [128, 2 * 8 * 512], BF16)
    cols = c.sbuf("cols", [128, 2 * 4], F32)
    ps_tr = c.psum("ps_tr", [128, 2048], BF16)
    sls = {nm: [Slot(), Slot()] for nm in
           ("xs", "ss", "gs", "x1t", "xc", "sq", "xnf", "xnb", "cols")}
    xnt_sl = [[[Slot() for _ in range(4)] for _ in range(8)] for _ in range(2)]
    tr_sl = [Slot() for _ in range(4)]

    for g in range(8):
        rg = g % 2
        for st in range(4):
            tok0 = g * 512 + st * 128
            r = st % 2
            o = r * 1024
            for nm, buf, src in (("xs", xs, x), ("ss", ss, s0i), ("gs", gs, gi)):
                p.step("sync", lambda e, buf=buf, src=src, o=o, tok0=tok0: e.dma_start(
                    out=buf[:, o:o + 1024], in_=src[tok0:tok0 + 128, :]),
                    writes=[sls[nm][r]], dma=True)
            p.step("vector", lambda e, o=o: e.tensor_mul(
                gs[:, o:o + 1024], ss[:, o:o + 1024], gs[:, o:o + 1024]),
                reads=[sls["ss"][r]], writes=[sls["gs"][r]])
            p.step("vector", lambda e, o=o: e.tensor_add(
                x1t[:, o:o + 1024], xs[:, o:o + 1024], gs[:, o:o + 1024]),
                reads=[sls["xs"][r], sls["gs"][r]], writes=[sls["x1t"][r]])
            p.step("gpsimd", lambda e, o=o, tok0=tok0: e.dma_start(
                out=x1o[tok0:tok0 + 128, :], in_=x1t[:, o:o + 1024]),
                reads=[sls["x1t"][r]], dma=True)
            mu = cols[:, r * 4 + 0: r * 4 + 1]
            var = cols[:, r * 4 + 1: r * 4 + 2]
            std = cols[:, r * 4 + 2: r * 4 + 3]
            rstd = cols[:, r * 4 + 3: r * 4 + 4]
            cs = sls["cols"][r]
            p.step("vector", lambda e, mu=mu, o=o: e.reduce_sum(
                mu, x1t[:, o:o + 1024], axis=AX.X),
                reads=[sls["x1t"][r]], writes=[cs])
            p.step("scalar", lambda e, mu=mu: e.mul(mu, mu, 1.0 / D), writes=[cs])
            p.step("vector", lambda e, mu=mu, o=o: e.tensor_single_scalar(
                xc[:, o:o + 1024], x1t[:, o:o + 1024], mu, ALU.subtract),
                reads=[sls["x1t"][r], cs], writes=[sls["xc"][r]])
            p.step("scalar", lambda e, var=var, o=o: e.activation(
                sq[:, o:o + 1024], xc[:, o:o + 1024], AF.Square, accum_out=var),
                reads=[sls["xc"][r]], writes=[sls["sq"][r], cs])
            epsa, epssl = c.cval(EPS)
            p.step("scalar", lambda e, std=std, var=var, epsa=epsa: e.activation(
                std, var, AF.Sqrt, scale=1.0 / D, bias=epsa),
                reads=[epssl], writes=[cs])
            p.step("vector", lambda e, rstd=rstd, std=std: e.reciprocal(
                rstd, std), writes=[cs])
            p.step("vector", lambda e, rstd=rstd, o=o: e.tensor_single_scalar(
                xnf[:, o:o + 1024], xc[:, o:o + 1024], rstd, ALU.mult),
                reads=[sls["xc"][r], cs], writes=[sls["xnf"][r]])
            p.step("gpsimd", lambda e, o=o, tok0=tok0: e.dma_start(
                out=xn1o[tok0:tok0 + 128, :], in_=xnf[:, o:o + 1024]),
                reads=[sls["xnf"][r]], dma=True)
            p.step("vector", lambda e, o=o: e.tensor_copy(
                out=xnb[:, o:o + 1024], in_=xnf[:, o:o + 1024]),
                reads=[sls["xnf"][r]], writes=[sls["xnb"][r]])
            for dc in range(8):
                pt = dc % 2
                p.step("tensor", lambda e, o=o, dc=dc, pt=pt: e.transpose(
                    ps_tr[:, pt * 1024: pt * 1024 + 128],
                    xnb[:, o + dc * 128: o + (dc + 1) * 128], idn_sb[:, :]),
                    reads=[sls["xnb"][r], idn_sl], writes=[tr_sl[pt]])
                dst = (rg * 8 + dc) * 512 + st * 128
                p.step("scalar", lambda e, dst=dst, pt=pt: e.copy(
                    xnt[:, dst:dst + 128], ps_tr[:, pt * 1024: pt * 1024 + 128]),
                    reads=[tr_sl[pt]], writes=[xnt_sl[rg][dc][st]])
        for dc in range(8):
            src = (rg * 8 + dc) * 512
            p.step("gpsimd", lambda e, src=src, dc=dc, g=g: e.dma_start(
                out=xn1T[dc * 128:(dc + 1) * 128, g * 512:(g + 1) * 512],
                in_=xnt[:, src:src + 512]),
                reads=xnt_sl[rg][dc], dma=True)
    return c.done()


def build_E4():
    c = Ctx()
    p = c.plan
    xn1 = c.inp("xn1", [TOK, D])
    xn1T = c.inp("xn1T", [D, TOK], BF16)
    wqt = c.inp("wqt", [D, D])
    wkt = c.inp("wkt", [D, D])
    s1o = c.out("s1", [TOK, D])

    stg = c.sbuf("stg", [128, 4096], F32)
    wqb = c.sbuf("wqb", [128, 8 * 1024], BF16)
    wkb = c.sbuf("wkb", [128, 8 * 1024], BF16)
    xts = c.sbuf("xts", [128, 8 * 4096], BF16)
    stg_sl = Slot()
    wq_sl = Slot()
    wk_sl = Slot()
    xts_sl = Slot()
    for wsrc, wdst, wsl in ((wqt, wqb, wq_sl), (wkt, wkb, wk_sl)):
        for dg in range(2):
            p.step("sync", lambda e, wsrc=wsrc, dg=dg: e.dma_start(
                out=stg[:, :],
                in_=wsrc[:, dg * 512:(dg + 1) * 512].rearrange(
                    "(dc p) ee -> p dc ee", p=128)),
                writes=[stg_sl], dma=True)
            p.step("gpsimd", lambda e, wdst=wdst, dg=dg: e.tensor_copy(
                out=wdst[:, :].rearrange("p (dc ee) -> p dc ee",
                                         ee=1024)[:, :, dg * 512:(dg + 1) * 512],
                in_=stg[:, :].rearrange("p (dc ee) -> p dc ee", dc=8)),
                reads=[stg_sl], writes=[wsl])
    p.step("sync", lambda e: e.dma_start(
        out=xts[:, :], in_=xn1T.rearrange("(dc p) t -> p dc t", p=128)),
        writes=[xts_sl], dma=True)

    qsb = c.sbuf("qsb", [128, 4 * 1024], F32)
    ksb = c.sbuf("ksb", [128, 4 * 1024], F32)
    xtt = c.sbuf("xtt", [128, 4 * 1024], F32)
    qkw = c.sbuf("qkw", [128, 1024], F32)
    sc = c.sbuf("sc", [128, 16], F32)
    th = c.sbuf("th", [128, 16], F32)
    exb = c.sbuf("exb", [128, 16], F32)
    alb = c.sbuf("alb", [128, 16], F32)
    mcol = c.sbuf("mcol", [128, 4], F32)
    ob = c.sbuf("ob", [128, 2 * 1024], F32)
    ps_q = c.psum("ps_q", [128, 512], F32)
    ps_k = c.psum("ps_k", [128, 512], F32)
    q_sl = [Slot() for _ in range(4)]
    k_sl = [Slot() for _ in range(4)]
    xt_sl = [Slot() for _ in range(4)]
    qkw_sl = Slot()
    sc_sl = [Slot() for _ in range(16)]
    th_sl = Slot()
    ex_sl = [Slot() for _ in range(4)]
    al_sl = [Slot() for _ in range(4)]
    mc_sl = [Slot() for _ in range(4)]
    ob_sl = [Slot(), Slot()]
    psq_sl = Slot()
    psk_sl = Slot()

    for j in range(8):
        for s in range(4):
            t = s * 8 + j
            for dg in range(2):
                for dc in range(8):
                    p.step("tensor", lambda e, dc=dc, t=t, dg=dg: e.matmul(
                        ps_q[:, :], xts[:, dc * 4096 + t * 128: dc * 4096 + (t + 1) * 128],
                        wqb[:, dc * 1024 + dg * 512: dc * 1024 + (dg + 1) * 512],
                        start=(dc == 0), stop=(dc == 7)),
                        reads=[xts_sl, wq_sl], writes=[psq_sl])
                    p.step("tensor", lambda e, dc=dc, t=t, dg=dg: e.matmul(
                        ps_k[:, :], xts[:, dc * 4096 + t * 128: dc * 4096 + (t + 1) * 128],
                        wkb[:, dc * 1024 + dg * 512: dc * 1024 + (dg + 1) * 512],
                        start=(dc == 0), stop=(dc == 7)),
                        reads=[xts_sl, wk_sl], writes=[psk_sl])
                p.step("scalar", lambda e, s=s, dg=dg: e.copy(
                    qsb[:, s * 1024 + dg * 512: s * 1024 + (dg + 1) * 512], ps_q[:, :]),
                    reads=[psq_sl], writes=[q_sl[s]])
                p.step("scalar", lambda e, s=s, dg=dg: e.copy(
                    ksb[:, s * 1024 + dg * 512: s * 1024 + (dg + 1) * 512], ps_k[:, :]),
                    reads=[psk_sl], writes=[k_sl[s]])
            tok0 = t * 128
            p.step("sync", lambda e, s=s, tok0=tok0: e.dma_start(
                out=xtt[:, s * 1024:(s + 1) * 1024], in_=xn1[tok0:tok0 + 128, :]),
                writes=[xt_sl[s]], dma=True)
        for s in range(4):
            for t4 in range(4):
                k = s * 4 + t4
                p.step("vector", lambda e, s=s, t4=t4: e.tensor_mul(
                    qkw[:, :], qsb[:, s * 1024:(s + 1) * 1024],
                    ksb[:, t4 * 1024:(t4 + 1) * 1024]),
                    reads=[q_sl[s], k_sl[t4]], writes=[qkw_sl])
                p.step("vector", lambda e, k=k: e.reduce_sum(
                    sc[:, k:k + 1], qkw[:, :], axis=AX.X),
                    reads=[qkw_sl], writes=[sc_sl[k]])
        p.step("scalar", lambda e: e.activation(th[:, :], sc[:, :], AF.Tanh,
                                                scale=1.0 / 32.0),
               reads=sc_sl, writes=[th_sl])
        for s in range(4):
            p.step("vector", lambda e, s=s: e.reduce_max(
                mcol[:, s:s + 1], th[:, s * 4:(s + 1) * 4], axis=AX.X),
                reads=[th_sl], writes=[mc_sl[s]])
            p.step("vector", lambda e, s=s: e.tensor_single_scalar(
                mcol[:, s:s + 1], mcol[:, s:s + 1], -1.0, ALU.mult),
                writes=[mc_sl[s]])
            p.step("scalar", lambda e, s=s: e.activation(
                exb[:, s * 4:(s + 1) * 4], th[:, s * 4:(s + 1) * 4], AF.Exp,
                bias=mcol[:, s:s + 1]),
                reads=[th_sl, mc_sl[s]], writes=[ex_sl[s]])
            p.step("vector", lambda e, s=s: e.reduce_sum(
                mcol[:, s:s + 1], exb[:, s * 4:(s + 1) * 4], axis=AX.X),
                reads=[ex_sl[s]], writes=[mc_sl[s]])
            p.step("vector", lambda e, s=s: e.reciprocal(
                mcol[:, s:s + 1], mcol[:, s:s + 1]), writes=[mc_sl[s]])
            p.step("vector", lambda e, s=s: e.tensor_single_scalar(
                alb[:, s * 4:(s + 1) * 4], exb[:, s * 4:(s + 1) * 4],
                mcol[:, s:s + 1], ALU.mult),
                reads=[ex_sl[s], mc_sl[s]], writes=[al_sl[s]])
        for s in range(4):
            r = s % 2
            o = r * 1024
            p.step("vector", lambda e, s=s, o=o: e.tensor_single_scalar(
                ob[:, o:o + 1024], xtt[:, 0:1024], alb[:, s * 4:s * 4 + 1], ALU.mult),
                reads=[xt_sl[0], al_sl[s]], writes=[ob_sl[r]])
            for t4 in range(1, 4):
                p.step("vector", lambda e, s=s, t4=t4, o=o: e.scalar_tensor_tensor(
                    ob[:, o:o + 1024], xtt[:, t4 * 1024:(t4 + 1) * 1024],
                    alb[:, s * 4 + t4: s * 4 + t4 + 1], ob[:, o:o + 1024],
                    ALU.mult, ALU.add),
                    reads=[xt_sl[t4], al_sl[s]], writes=[ob_sl[r]])
            tok0 = (s * 8 + j) * 128
            p.step("gpsimd", lambda e, o=o, tok0=tok0: e.dma_start(
                out=s1o[tok0:tok0 + 128, :], in_=ob[:, o:o + 1024]),
                reads=[ob_sl[r]], dma=True)
    return c.done()


def build_E5a():
    c = Ctx()
    p = c.plan
    xn1 = c.inp("xn1", [TOK, D])
    w1t = c.inp("w1t", [D, H4])
    b1c = c.inp("b1c", [128, 32])
    g1r = c.inp("g1r", [1, D])
    b1r = c.inp("b1r", [1, D])
    idn = c.inp("idn", [128, 128], BF16)
    hG = c.out("hG", [H4, TOK])

    idn_sb = c.sbuf("idn_sb", [128, 128], BF16)
    idn_sl = Slot()
    p.step("sync", lambda e: e.dma_start(out=idn_sb[:, :], in_=idn),
           writes=[idn_sl], dma=True)
    ones1 = c.sbuf("ones1", [1, 128], F32)
    onesc = c.sbuf("onesc", [128, 1], F32)
    o1_sl = Slot()
    oc_sl = Slot()
    p.step("vector", lambda e: e.memset(ones1[:, :], 1.0), writes=[o1_sl])
    p.step("vector", lambda e: e.memset(onesc[:, :], 1.0), writes=[oc_sl])
    g1row = c.sbuf("g1row", [1, D], F32)
    b1row = c.sbuf("b1row", [1, D], F32)
    g1B = c.sbuf("g1B", [128, D], F32)
    b1B = c.sbuf("b1B", [128, D], F32)
    b1cs = c.sbuf("b1cs", [128, 32], F32)
    grow_sl = Slot()
    brow_sl = Slot()
    g1B_sl = Slot()
    b1B_sl = Slot()
    b1c_sl = Slot()
    p.step("sync", lambda e: e.dma_start(out=g1row[:, :], in_=g1r),
           writes=[grow_sl], dma=True)
    p.step("sync", lambda e: e.dma_start(out=b1row[:, :], in_=b1r),
           writes=[brow_sl], dma=True)
    p.step("sync", lambda e: e.dma_start(out=b1cs[:, :], in_=b1c),
           writes=[b1c_sl], dma=True)
    ps_b = c.psum("ps_b", [128, 1024], F32)
    psb_sl = Slot()
    for dg in range(2):
        p.step("tensor", lambda e, dg=dg: e.matmul(
            ps_b[:, dg * 512:(dg + 1) * 512], ones1[:, :],
            g1row[:, dg * 512:(dg + 1) * 512], start=True, stop=True),
            reads=[o1_sl, grow_sl], writes=[psb_sl])
    p.step("scalar", lambda e: e.copy(g1B[:, :], ps_b[:, :]),
           reads=[psb_sl], writes=[g1B_sl])
    for dg in range(2):
        p.step("tensor", lambda e, dg=dg: e.matmul(
            ps_b[:, dg * 512:(dg + 1) * 512], ones1[:, :],
            b1row[:, dg * 512:(dg + 1) * 512], start=True, stop=True),
            reads=[o1_sl, brow_sl], writes=[psb_sl])
    p.step("scalar", lambda e: e.copy(b1B[:, :], ps_b[:, :]),
           reads=[psb_sl], writes=[b1B_sl])

    stg = c.sbuf("stg", [128, 4096], F32)
    stg_sl = [Slot()]
    mc8 = c.sbuf("mc8", [128, 8], F32)
    mc8_sl = [Slot() for _ in range(8)]
    mtot = c.sbuf("mtot", [128, 1], F32)
    mval = c.sbuf("mval", [1, 1], F32)
    m1P = c.sbuf("m1P", [128, 1], F32)
    thr = c.sbuf("thr", [128, 1], F32)
    nthr = c.sbuf("nthr", [128, 1], F32)
    mtot_sl = Slot()
    mval_sl = Slot()
    m1P_sl = Slot()
    thr_sl = Slot()
    nthr_sl = Slot()
    ps_m = c.psum("ps_m", [128, 512], F32)
    psm_sl = Slot()
    for dc in range(8):
        p.step("sync", lambda e, dc=dc: e.dma_start(
            out=stg[:, :], in_=w1t[dc * 128:(dc + 1) * 128, :]),
            writes=[stg_sl[0]], dma=True)
        p.step("vector", lambda e, dc=dc: e.tensor_reduce(
            mc8[:, dc:dc + 1], stg[:, :], AX.X, ALU.add,
            apply_absolute_value=True),
            reads=[stg_sl[0]], writes=[mc8_sl[dc]])
    p.step("vector", lambda e: e.reduce_sum(mtot[:, :], mc8[:, :], axis=AX.X),
           reads=mc8_sl, writes=[mtot_sl])
    p.step("tensor", lambda e: e.matmul(ps_m[:1, :1], mtot[:, :], onesc[:, :],
                                        start=True, stop=True),
           reads=[mtot_sl, oc_sl], writes=[psm_sl])
    p.step("scalar", lambda e: e.activation(mval[:, :], ps_m[:1, :1], AF.Identity,
                                            scale=1.0 / (D * H4)),
           reads=[psm_sl], writes=[mval_sl])
    p.step("vector", lambda e: e.tensor_single_scalar(
        mval[:, :], mval[:, :], 1e-5, ALU.max), writes=[mval_sl])
    p.step("tensor", lambda e: e.matmul(ps_m[:, 0:1], ones1[:, :], mval[:, :],
                                        start=True, stop=True),
           reads=[o1_sl, mval_sl], writes=[psm_sl])
    p.step("scalar", lambda e: e.copy(m1P[:, :], ps_m[:, 0:1]),
           reads=[psm_sl], writes=[m1P_sl])
    p.step("vector", lambda e: e.tensor_single_scalar(
        thr[:, :], m1P[:, :], 0.5, ALU.mult), reads=[m1P_sl], writes=[thr_sl])
    p.step("vector", lambda e: e.tensor_single_scalar(
        nthr[:, :], m1P[:, :], -0.5, ALU.mult), reads=[m1P_sl], writes=[nthr_sl])

    xgT = c.sbuf("xgT", [128, 8 * 4096], BF16)
    xgT_sl = [[Slot() for _ in range(32)] for _ in range(8)]
    xs = c.sbuf("xs", [128, 2 * 1024], F32)
    xc = c.sbuf("xc", [128, 2 * 1024], F32)
    xg = c.sbuf("xg", [128, 2 * 1024], F32)
    sq = xg
    xgb = c.sbuf("xgb", [128, 2 * 1024], BF16)
    cols = c.sbuf("cols", [128, 2 * 4], F32)
    ps_tr = c.psum("ps_tr", [128, 2048], BF16)
    sls = {nm: [Slot(), Slot()] for nm in ("xs", "xc", "xg", "xgb", "cols")}
    sls["sq"] = sls["xg"]
    tr_sl = [Slot() for _ in range(4)]
    for ti in range(32):
        r = ti % 2
        o = r * 1024
        tok0 = ti * 128
        p.step("sync", lambda e, o=o, tok0=tok0: e.dma_start(
            out=xs[:, o:o + 1024], in_=xn1[tok0:tok0 + 128, :]),
            writes=[sls["xs"][r]], dma=True)
        mu = cols[:, r * 4 + 0: r * 4 + 1]
        var = cols[:, r * 4 + 1: r * 4 + 2]
        std = cols[:, r * 4 + 2: r * 4 + 3]
        rstd = cols[:, r * 4 + 3: r * 4 + 4]
        cs = sls["cols"][r]
        p.step("vector", lambda e, mu=mu, o=o: e.reduce_sum(
            mu, xs[:, o:o + 1024], axis=AX.X), reads=[sls["xs"][r]], writes=[cs])
        p.step("scalar", lambda e, mu=mu: e.mul(mu, mu, 1.0 / D), writes=[cs])
        p.step("vector", lambda e, mu=mu, o=o: e.tensor_single_scalar(
            xc[:, o:o + 1024], xs[:, o:o + 1024], mu, ALU.subtract),
            reads=[sls["xs"][r], cs], writes=[sls["xc"][r]])
        p.step("scalar", lambda e, var=var, o=o: e.activation(
            sq[:, o:o + 1024], xc[:, o:o + 1024], AF.Square, accum_out=var),
            reads=[sls["xc"][r]], writes=[sls["sq"][r], cs])
        epsa, epssl = c.cval(EPS)
        p.step("scalar", lambda e, std=std, var=var, epsa=epsa: e.activation(
            std, var, AF.Sqrt, scale=1.0 / D, bias=epsa),
            reads=[epssl], writes=[cs])
        p.step("vector", lambda e, rstd=rstd, std=std: e.reciprocal(rstd, std),
               writes=[cs])
        p.step("vector", lambda e, rstd=rstd, o=o: e.tensor_single_scalar(
            xg[:, o:o + 1024], xc[:, o:o + 1024], rstd, ALU.mult),
            reads=[sls["xc"][r], cs], writes=[sls["xg"][r]])
        p.step("vector", lambda e, o=o: e.tensor_mul(
            xg[:, o:o + 1024], xg[:, o:o + 1024], g1B[:, :]),
            reads=[g1B_sl], writes=[sls["xg"][r]])
        p.step("vector", lambda e, o=o: e.tensor_add(
            xgb[:, o:o + 1024], xg[:, o:o + 1024], b1B[:, :]),
            reads=[sls["xg"][r], b1B_sl], writes=[sls["xgb"][r]])
        for dc in range(8):
            pt = dc % 2
            p.step("tensor", lambda e, o=o, dc=dc, pt=pt: e.transpose(
                ps_tr[:, pt * 1024: pt * 1024 + 128],
                xgb[:, o + dc * 128: o + (dc + 1) * 128], idn_sb[:, :]),
                reads=[sls["xgb"][r], idn_sl], writes=[tr_sl[pt]])
            p.step("scalar", lambda e, dc=dc, ti=ti, pt=pt: e.copy(
                xgT[:, dc * 4096 + ti * 128: dc * 4096 + (ti + 1) * 128],
                ps_tr[:, pt * 1024: pt * 1024 + 128]),
                reads=[tr_sl[pt]], writes=[xgT_sl[dc][ti]])

    stw = c.sbuf("stw", [128, 2 * 1024], F32)
    qa = c.sbuf("qa", [128, 2 * 1024], F32)
    qb = c.sbuf("qb", [128, 2 * 1024], F32)
    w1q = c.sbuf("w1q", [128, 2 * 1024], BF16)
    hsp = c.sbuf("hsp", [128, 2 * 512], F32)
    stw_sl = [Slot(), Slot()]
    qa_sl = [Slot(), Slot()]
    qb_sl = [Slot(), Slot()]
    w1q_sl = [Slot(), Slot()]
    hsp_sl = [Slot(), Slot()]
    ps_h = c.psum("ps_h", [128, 2 * 512], F32)
    psh_sl = [Slot(), Slot()]
    for hc in range(32):
        r = hc % 2
        o = r * 1024
        p.step("sync", lambda e, hc=hc, o=o: e.dma_start(
            out=stw[:, o:o + 1024],
            in_=w1t[:, hc * 128:(hc + 1) * 128].rearrange("(dc p) h -> p dc h", p=128)),
            writes=[stw_sl[r]], dma=True)
        p.step("vector", lambda e, o=o: e.tensor_single_scalar(
            qa[:, o:o + 1024], stw[:, o:o + 1024], thr[:, :], ALU.is_gt),
            reads=[stw_sl[r], thr_sl], writes=[qa_sl[r]])
        p.step("vector", lambda e, o=o: e.tensor_single_scalar(
            qb[:, o:o + 1024], stw[:, o:o + 1024], nthr[:, :], ALU.is_lt),
            reads=[stw_sl[r], nthr_sl], writes=[qb_sl[r]])
        p.step("vector", lambda e, o=o: e.tensor_sub(
            w1q[:, o:o + 1024], qa[:, o:o + 1024], qb[:, o:o + 1024]),
            reads=[qa_sl[r], qb_sl[r]], writes=[w1q_sl[r]])
        for tt in range(8):
            rr = tt % 2
            for dc in range(8):
                p.step("tensor", lambda e, o=o, dc=dc, tt=tt, rr=rr: e.matmul(
                    ps_h[:, rr * 512:(rr + 1) * 512],
                    w1q[:, o + dc * 128: o + (dc + 1) * 128],
                    xgT[:, dc * 4096 + tt * 512: dc * 4096 + (tt + 1) * 512],
                    start=(dc == 0), stop=(dc == 7)),
                    reads=[w1q_sl[r]] + [xgT_sl[dc][tt * 4 + q] for q in range(4)],
                    writes=[psh_sl[rr]])
            p.step("scalar", lambda e, hc=hc, rr=rr: e.activation(
                hsp[:, rr * 512:(rr + 1) * 512], ps_h[:, rr * 512:(rr + 1) * 512],
                AF.Gelu, scale=m1P[:, :], bias=b1cs[:, hc:hc + 1]),
                reads=[psh_sl[rr], m1P_sl, b1c_sl], writes=[hsp_sl[rr]])
            p.step("gpsimd", lambda e, hc=hc, tt=tt, rr=rr: e.dma_start(
                out=hG[hc * 128:(hc + 1) * 128, tt * 512:(tt + 1) * 512],
                in_=hsp[:, rr * 512:(rr + 1) * 512]),
                reads=[hsp_sl[rr]], dma=True)
    return c.done()


def build_E5b():
    c = Ctx()
    p = c.plan
    hGi = c.inp("hG", [H4, TOK])
    w2t = c.inp("w2t", [H4, D])
    b2r = c.inp("b2r", [1, D])
    g2c = c.inp("g2c", [128, 32])
    b2c = c.inp("b2c", [128, 32])
    x1i = c.inp("x1", [TOK, D])
    s1i = c.inp("s1", [TOK, D])
    oo = c.out("out", [TOK, D])

    ones1 = c.sbuf("ones1", [1, 128], F32)
    onesc = c.sbuf("onesc", [128, 1], F32)
    onescb = c.sbuf("onescb", [128, 1], BF16)
    o1_sl = Slot()
    oc_sl = Slot()
    ocb_sl = Slot()
    p.step("vector", lambda e: e.memset(ones1[:, :], 1.0), writes=[o1_sl])
    p.step("vector", lambda e: e.memset(onesc[:, :], 1.0), writes=[oc_sl])
    p.step("vector", lambda e: e.memset(onescb[:, :], 1.0), writes=[ocb_sl])
    g2cs = c.sbuf("g2cs", [128, 32], F32)
    b2cs = c.sbuf("b2cs", [128, 32], F32)
    g2_sl = Slot()
    b2_sl = Slot()
    p.step("sync", lambda e: e.dma_start(out=g2cs[:, :], in_=g2c),
           writes=[g2_sl], dma=True)
    p.step("sync", lambda e: e.dma_start(out=b2cs[:, :], in_=b2c),
           writes=[b2_sl], dma=True)
    b2row = c.sbuf("b2row", [1, D], F32)
    b2B = c.sbuf("b2B", [128, D], F32)
    b2r_sl = Slot()
    b2B_sl = Slot()
    p.step("sync", lambda e: e.dma_start(out=b2row[:, :], in_=b2r),
           writes=[b2r_sl], dma=True)
    ps_b = c.psum("ps_b", [128, 1024], F32)
    psb_sl = Slot()
    for dg in range(2):
        p.step("tensor", lambda e, dg=dg: e.matmul(
            ps_b[:, dg * 512:(dg + 1) * 512], ones1[:, :],
            b2row[:, dg * 512:(dg + 1) * 512], start=True, stop=True),
            reads=[o1_sl, b2r_sl], writes=[psb_sl])
    p.step("scalar", lambda e: e.copy(b2B[:, :], ps_b[:, :]),
           reads=[psb_sl], writes=[b2B_sl])

    stg = c.sbuf("stg", [128, 2 * 1024], F32)
    stg_sl = [Slot(), Slot()]
    mc32 = c.sbuf("mc32", [128, 32], F32)
    mc_sl = [Slot() for _ in range(32)]
    mtot = c.sbuf("mtot", [128, 1], F32)
    mval = c.sbuf("mval", [1, 1], F32)
    m2P = c.sbuf("m2P", [128, 1], F32)
    thr = c.sbuf("thr", [128, 1], F32)
    nthr = c.sbuf("nthr", [128, 1], F32)
    mtot_sl = Slot()
    mval_sl = Slot()
    m2P_sl = Slot()
    thr_sl = Slot()
    nthr_sl = Slot()
    ps_m = c.psum("ps_m", [128, 512], F32)
    psm_sl = Slot()
    for hc in range(32):
        r = hc % 2
        p.step("sync", lambda e, hc=hc, r=r: e.dma_start(
            out=stg[:, r * 1024:(r + 1) * 1024],
            in_=w2t[hc * 128:(hc + 1) * 128, :]),
            writes=[stg_sl[r]], dma=True)
        p.step("vector", lambda e, hc=hc, r=r: e.tensor_reduce(
            mc32[:, hc:hc + 1], stg[:, r * 1024:(r + 1) * 1024], AX.X, ALU.add,
            apply_absolute_value=True),
            reads=[stg_sl[r]], writes=[mc_sl[hc]])
    p.step("vector", lambda e: e.reduce_sum(mtot[:, :], mc32[:, :], axis=AX.X),
           reads=mc_sl, writes=[mtot_sl])
    p.step("tensor", lambda e: e.matmul(ps_m[:1, :1], mtot[:, :], onesc[:, :],
                                        start=True, stop=True),
           reads=[mtot_sl, oc_sl], writes=[psm_sl])
    p.step("scalar", lambda e: e.activation(mval[:, :], ps_m[:1, :1], AF.Identity,
                                            scale=1.0 / (D * H4)),
           reads=[psm_sl], writes=[mval_sl])
    p.step("vector", lambda e: e.tensor_single_scalar(
        mval[:, :], mval[:, :], 1e-5, ALU.max), writes=[mval_sl])
    p.step("tensor", lambda e: e.matmul(ps_m[:, 0:1], ones1[:, :], mval[:, :],
                                        start=True, stop=True),
           reads=[o1_sl, mval_sl], writes=[psm_sl])
    p.step("scalar", lambda e: e.copy(m2P[:, :], ps_m[:, 0:1]),
           reads=[psm_sl], writes=[m2P_sl])
    p.step("vector", lambda e: e.tensor_single_scalar(
        thr[:, :], m2P[:, :], 0.5, ALU.mult), reads=[m2P_sl], writes=[thr_sl])
    p.step("vector", lambda e: e.tensor_single_scalar(
        nthr[:, :], m2P[:, :], -0.5, ALU.mult), reads=[m2P_sl], writes=[nthr_sl])
    w2q = c.sbuf("w2q", [128, 32 * 1024], BF16)
    w2q_sl = [Slot() for _ in range(32)]
    qa = c.sbuf("qa", [128, 2 * 1024], F32)
    qb = c.sbuf("qb", [128, 2 * 1024], F32)
    qa_sl = [Slot(), Slot()]
    qb_sl = [Slot(), Slot()]
    for hc in range(32):
        r = hc % 2
        o = r * 1024
        p.step("sync", lambda e, hc=hc, o=o: e.dma_start(
            out=stg[:, o:o + 1024], in_=w2t[hc * 128:(hc + 1) * 128, :]),
            writes=[stg_sl[r]], dma=True)
        p.step("vector", lambda e, o=o: e.tensor_single_scalar(
            qa[:, o:o + 1024], stg[:, o:o + 1024], thr[:, :], ALU.is_gt),
            reads=[stg_sl[r], thr_sl], writes=[qa_sl[r]])
        p.step("vector", lambda e, o=o: e.tensor_single_scalar(
            qb[:, o:o + 1024], stg[:, o:o + 1024], nthr[:, :], ALU.is_lt),
            reads=[stg_sl[r], nthr_sl], writes=[qb_sl[r]])
        p.step("vector", lambda e, o=o, hc=hc: e.tensor_sub(
            w2q[:, hc * 1024:(hc + 1) * 1024], qa[:, o:o + 1024], qb[:, o:o + 1024]),
            reads=[qa_sl[r], qb_sl[r]], writes=[w2q_sl[hc]])

    hgb = c.sbuf("hgb", [128, 2 * 512], F32)
    sqb = c.sbuf("sqb", [128, 2 * 512], BF16)
    xn2b = c.sbuf("xn2b", [128, 32 * 512], BF16)
    t1b = c.sbuf("t1b", [128, 2 * 512], F32)
    muB = c.sbuf("muB", [128, 512], F32)
    rsB = c.sbuf("rsB", [128, 512], F32)
    mrow = c.sbuf("mrow", [1, 512], F32)
    vrow = c.sbuf("vrow", [1, 512], F32)
    ot1 = c.sbuf("ot1", [128, 2 * 512], F32)
    gg = c.sbuf("gg", [128, 2 * 512], F32)
    x1t = c.sbuf("x1t", [128, 2 * 512], F32)
    s1t = c.sbuf("s1t", [128, 2 * 512], F32)
    fo = c.sbuf("fo", [128, 2 * 512], F32)
    hgb_sl = [Slot(), Slot()]
    sqb_sl = [Slot(), Slot()]
    xn2_sl = [Slot() for _ in range(32)]
    t1_sl = [Slot(), Slot()]
    muB_sl = Slot()
    rsB_sl = Slot()
    mrow_sl = Slot()
    vrow_sl = Slot()
    ot1_sl = [Slot(), Slot()]
    gg_sl = [Slot(), Slot()]
    x1_sl = [Slot(), Slot()]
    s1_sl = [Slot(), Slot()]
    fo_sl = [Slot(), Slot()]
    ps_s = c.psum("ps_s", [128, 512], F32)
    ps_v = c.psum("ps_v", [128, 512], F32)
    ps_bc = c.psum("ps_bc", [128, 512], F32)
    ps_o = c.psum("ps_o", [128, 2 * 512], F32)
    pss_sl = Slot()
    psv_sl = Slot()
    psbc_sl = Slot()
    pso_sl = [Slot(), Slot()]

    for tg in range(8):
        t0 = tg * 512
        for hc in range(32):
            r = hc % 2
            o = r * 512
            p.step("sync", lambda e, hc=hc, o=o, t0=t0: e.dma_start(
                out=hgb[:, o:o + 512], in_=hGi[hc * 128:(hc + 1) * 128, t0:t0 + 512]),
                writes=[hgb_sl[r]], dma=True)
            p.step("tensor", lambda e, hc=hc, o=o: e.matmul(
                ps_s[0:1, :], onesc[:, :], hgb[:, o:o + 512],
                start=(hc == 0), stop=(hc == 31)),
                reads=[oc_sl, hgb_sl[r]], writes=[pss_sl])
            p.step("scalar", lambda e, o=o: e.activation(
                sqb[:, o:o + 512], hgb[:, o:o + 512], AF.Square),
                reads=[hgb_sl[r]], writes=[sqb_sl[r]])
            p.step("tensor", lambda e, hc=hc, o=o: e.matmul(
                ps_v[0:1, :], onescb[:, :], sqb[:, o:o + 512],
                start=(hc == 0), stop=(hc == 31)),
                reads=[ocb_sl, sqb_sl[r]], writes=[psv_sl])
        p.step("scalar", lambda e: e.activation(mrow[:, :], ps_s[0:1, :], AF.Identity,
                                                scale=1.0 / H4),
               reads=[pss_sl], writes=[mrow_sl])
        p.step("scalar", lambda e: e.activation(vrow[:, :], ps_v[0:1, :], AF.Identity,
                                                scale=1.0 / H4),
               reads=[psv_sl], writes=[vrow_sl])
        p.step("vector", lambda e: e.tensor_tensor(
            t1b[:1, :512], mrow[:, :], mrow[:, :], ALU.mult),
            reads=[mrow_sl], writes=[t1_sl[0]])
        p.step("vector", lambda e: e.tensor_sub(
            vrow[:, :], vrow[:, :], t1b[:1, :512]),
            reads=[t1_sl[0]], writes=[vrow_sl])
        epsa, epssl = c.cval(EPS)
        p.step("scalar", lambda e, epsa=epsa: e.activation(
            vrow[:, :], vrow[:, :], AF.Sqrt, bias=epsa[0:1, :]),
            reads=[epssl], writes=[vrow_sl])
        p.step("vector", lambda e: e.reciprocal(vrow[:, :], vrow[:, :]),
               writes=[vrow_sl])
        p.step("tensor", lambda e: e.matmul(ps_bc[:, :], ones1[:, :], mrow[:, :],
                                            start=True, stop=True),
               reads=[o1_sl, mrow_sl], writes=[psbc_sl])
        p.step("scalar", lambda e: e.copy(muB[:, :], ps_bc[:, :]),
               reads=[psbc_sl], writes=[muB_sl])
        p.step("tensor", lambda e: e.matmul(ps_bc[:, :], ones1[:, :], vrow[:, :],
                                            start=True, stop=True),
               reads=[o1_sl, vrow_sl], writes=[psbc_sl])
        p.step("scalar", lambda e: e.copy(rsB[:, :], ps_bc[:, :]),
               reads=[psbc_sl], writes=[rsB_sl])
        for hc in range(32):
            r = hc % 2
            o = r * 512
            p.step("sync", lambda e, hc=hc, o=o, t0=t0: e.dma_start(
                out=hgb[:, o:o + 512], in_=hGi[hc * 128:(hc + 1) * 128, t0:t0 + 512]),
                writes=[hgb_sl[r]], dma=True)
            p.step("vector", lambda e, o=o: e.tensor_sub(
                t1b[:, o:o + 512], hgb[:, o:o + 512], muB[:, :]),
                reads=[hgb_sl[r], muB_sl], writes=[t1_sl[r]])
            p.step("vector", lambda e, o=o: e.tensor_mul(
                t1b[:, o:o + 512], t1b[:, o:o + 512], rsB[:, :]),
                reads=[rsB_sl], writes=[t1_sl[r]])
            p.step("scalar", lambda e, hc=hc, o=o: e.activation(
                xn2b[:, hc * 512:(hc + 1) * 512], t1b[:, o:o + 512],
                AF.Identity, scale=g2cs[:, hc:hc + 1], bias=b2cs[:, hc:hc + 1]),
                reads=[t1_sl[r], g2_sl, b2_sl], writes=[xn2_sl[hc]])
        for st in range(4):
            r = st % 2
            tok0 = t0 + st * 128
            for dg in range(2):
                for hc in range(32):
                    p.step("tensor", lambda e, hc=hc, st=st, dg=dg: e.matmul(
                        ps_o[:, dg * 512:(dg + 1) * 512],
                        xn2b[:, hc * 512 + st * 128: hc * 512 + (st + 1) * 128],
                        w2q[:, hc * 1024 + dg * 512: hc * 1024 + (dg + 1) * 512],
                        start=(hc == 0), stop=(hc == 31)),
                        reads=[xn2_sl[hc], w2q_sl[hc]], writes=[pso_sl[dg]])
                p.step("scalar", lambda e, dg=dg, r=r: e.activation(
                    ot1[:, r * 512:(r + 1) * 512], ps_o[:, dg * 512:(dg + 1) * 512],
                    AF.Copy, scale=m2P[:, :]),
                    reads=[pso_sl[dg], m2P_sl], writes=[ot1_sl[r]])
                p.step("vector", lambda e, dg=dg, r=r: e.tensor_add(
                    ot1[:, r * 512:(r + 1) * 512], ot1[:, r * 512:(r + 1) * 512],
                    b2B[:, dg * 512:(dg + 1) * 512]),
                    reads=[b2B_sl], writes=[ot1_sl[r]])
                p.step("scalar", lambda e, r=r: e.activation(
                    gg[:, r * 512:(r + 1) * 512], ot1[:, r * 512:(r + 1) * 512],
                    AF.Silu), reads=[ot1_sl[r]], writes=[gg_sl[r]])
                p.step("sync", lambda e, r=r, tok0=tok0, dg=dg: e.dma_start(
                    out=x1t[:, r * 512:(r + 1) * 512],
                    in_=x1i[tok0:tok0 + 128, dg * 512:(dg + 1) * 512]),
                    writes=[x1_sl[r]], dma=True)
                p.step("sync", lambda e, r=r, tok0=tok0, dg=dg: e.dma_start(
                    out=s1t[:, r * 512:(r + 1) * 512],
                    in_=s1i[tok0:tok0 + 128, dg * 512:(dg + 1) * 512]),
                    writes=[s1_sl[r]], dma=True)
                p.step("vector", lambda e, r=r: e.tensor_mul(
                    gg[:, r * 512:(r + 1) * 512], gg[:, r * 512:(r + 1) * 512],
                    s1t[:, r * 512:(r + 1) * 512]),
                    reads=[s1_sl[r]], writes=[gg_sl[r]])
                p.step("vector", lambda e, r=r: e.tensor_add(
                    fo[:, r * 512:(r + 1) * 512], x1t[:, r * 512:(r + 1) * 512],
                    gg[:, r * 512:(r + 1) * 512]),
                    reads=[x1_sl[r], gg_sl[r]], writes=[fo_sl[r]])
                p.step("gpsimd", lambda e, r=r, tok0=tok0, dg=dg: e.dma_start(
                    out=oo[tok0:tok0 + 128, dg * 512:(dg + 1) * 512],
                    in_=fo[:, r * 512:(r + 1) * 512]),
                    reads=[fo_sl[r]], dma=True)
    return c.done()


def _run(nc, in_maps):
    return run_bass_kernel_spmd(nc, in_maps, core_ids=list(range(NCORES)))


def kernel(**inp):
    f32 = np.float32
    x_full = np.ascontiguousarray(inp["L0_attn_shot_embeddings"], dtype=f32)
    Wd = np.ascontiguousarray(inp["L0_attn_Wd"], dtype=f32)
    bd = np.ascontiguousarray(inp["L0_attn_bd"], dtype=f32)
    Wi = np.ascontiguousarray(inp["L0_attn_Wi"], dtype=f32)
    bi = np.ascontiguousarray(inp["L0_attn_bi"], dtype=f32)
    A_log = np.ascontiguousarray(inp["L0_attn_A_log"], dtype=f32)
    Wo = np.ascontiguousarray(inp["L0_attn_Wo"], dtype=f32)
    bo = np.ascontiguousarray(inp["L0_attn_bo"], dtype=f32)
    Ws = np.ascontiguousarray(inp["L0_attn_Ws"], dtype=f32)
    bs = np.ascontiguousarray(inp["L0_attn_bs"], dtype=f32)
    gate_w = np.ascontiguousarray(inp["L0_ffn_gate_w"], dtype=f32)
    w1 = inp["L0_ffn_w1"]
    w2 = inp["L0_ffn_w2"]
    w3 = inp["L0_ffn_w3"]
    Wq = np.ascontiguousarray(inp["L1_attn_Wq"], dtype=f32)
    Wk = np.ascontiguousarray(inp["L1_attn_Wk"], dtype=f32)
    ln1_g = np.ascontiguousarray(inp["L1_ffn_ln1_g"], dtype=f32)
    ln1_b = np.ascontiguousarray(inp["L1_ffn_ln1_b"], dtype=f32)
    w1L = np.ascontiguousarray(inp["L1_ffn_w1"], dtype=f32)
    b1L = np.ascontiguousarray(inp["L1_ffn_b1"], dtype=f32)
    ln2_g = np.ascontiguousarray(inp["L1_ffn_ln2_g"], dtype=f32)
    ln2_b = np.ascontiguousarray(inp["L1_ffn_ln2_b"], dtype=f32)
    w2L = np.ascontiguousarray(inp["L1_ffn_w2"], dtype=f32)
    b2L = np.ascontiguousarray(inp["L1_ffn_b2"], dtype=f32)

    idn = np.eye(128, dtype=ml_dtypes.bfloat16)
    C = np.ascontiguousarray

    def col2(v):
        return C(v.reshape(2, 128).T)

    xk = [C(x_full[:, k * SL:(k + 1) * SL, :].reshape(TOK, D)) for k in range(NCORES)]

    ncA = build_A()
    maps = [dict(x=xk[k], wd=Wd[0], wi=Wi[0], bd=col2(bd[0]), bi=col2(bi[0]),
                 al=col2(A_log[0]), idn=idn) for k in range(NCORES)]
    resA = _run(ncA, maps).results
    xnT_k = [r["xnT"] for r in resA]
    dec_full = np.concatenate([r["dec"] for r in resA], axis=2).reshape(B * N, S)
    drv_full = np.concatenate([r["drv"] for r in resA], axis=2).reshape(B * N, S)

    ncS = build_scan()
    maps = [dict(dec=C(dec_full[k * 128:(k + 1) * 128]),
                 drv=C(drv_full[k * 128:(k + 1) * 128])) for k in range(NCORES)]
    resS = _run(ncS, maps).results
    hs0 = np.concatenate([r["hs"] for r in resS], axis=0)
    f0 = C(hs0[:, -1].reshape(B, N).T)
    hs0_k = [C(hs0.reshape(B, N, S)[:, :, k * SL:(k + 1) * SL]) for k in range(NCORES)]

    ncC = build_C()
    maps = [dict(hs0=hs0_k[k], f0t=f0, wo=Wo[0], ws=Ws[0],
                 bob=C(bo[0].reshape(8, 128).T), bsb=C(bs[0].reshape(8, 128).T),
                 wd=Wd[1], wi=Wi[1], bd=col2(bd[1]), bi=col2(bi[1]),
                 al=col2(A_log[1])) for k in range(NCORES)]
    resC = _run(ncC, maps).results
    dec_full = np.concatenate([r["dec"] for r in resC], axis=2).reshape(B * N, S)
    drv_full = np.concatenate([r["drv"] for r in resC], axis=2).reshape(B * N, S)

    maps = [dict(dec=C(dec_full[k * 128:(k + 1) * 128]),
                 drv=C(drv_full[k * 128:(k + 1) * 128])) for k in range(NCORES)]
    resS = _run(ncS, maps).results
    hs1 = np.concatenate([r["hs"] for r in resS], axis=0)
    f1 = C(hs1[:, -1].reshape(B, N).T)
    hs1_k = [C(hs1.reshape(B, N, S)[:, :, k * SL:(k + 1) * SL]) for k in range(NCORES)]

    ncE1 = build_E1()
    maps = [dict(hs1=hs1_k[k], f1t=f1, wo=Wo[1], ws=Ws[1],
                 bor=C(bo[1].reshape(1, D)), bsr=C(bs[1].reshape(1, D)))
            for k in range(NCORES)]
    resE1 = _run(ncE1, maps).results
    s0_k = [r["s0"] for r in resE1]

    ncE2 = build_E2()
    gwt = C(gate_w.T)
    w1t = C(np.transpose(w1, (0, 2, 1)))
    w3t = C(np.transpose(w3, (0, 2, 1)))
    w2t = C(np.transpose(w2, (0, 2, 1)))
    maps = [dict(xnT=xnT_k[k], gwt=gwt, w1t=w1t, w3t=w3t, w2t=w2t)
            for k in range(NCORES)]
    resE2 = _run(ncE2, maps).results
    g_k = [r["g"] for r in resE2]

    ncE3 = build_E3()
    maps = [dict(x=xk[k], s0=s0_k[k], g=g_k[k], idn=idn) for k in range(NCORES)]
    resE3 = _run(ncE3, maps).results
    x1_k = [r["x1"] for r in resE3]
    xn1_k = [r["xn1"] for r in resE3]
    xn1T_k = [r["xn1T"] for r in resE3]

    ncE4 = build_E4()
    wqt = C(Wq.T)
    wkt = C(Wk.T)
    maps = [dict(xn1=xn1_k[k], xn1T=xn1T_k[k], wqt=wqt, wkt=wkt)
            for k in range(NCORES)]
    resE4 = _run(ncE4, maps).results
    s1_k = [r["s1"] for r in resE4]

    ncE5a = build_E5a()
    w1Lt = C(w1L.T)
    maps = [dict(xn1=xn1_k[k], w1t=w1Lt, b1c=C(b1L.reshape(32, 128).T),
                 g1r=C(ln1_g.reshape(1, D)), b1r=C(ln1_b.reshape(1, D)), idn=idn)
            for k in range(NCORES)]
    resE5a = _run(ncE5a, maps).results
    hG_k = [r["hG"] for r in resE5a]

    ncE5b = build_E5b()
    w2Lt = C(w2L.T)
    maps = [dict(hG=hG_k[k], w2t=w2Lt, b2r=C(b2L.reshape(1, D)),
                 g2c=C(ln2_g.reshape(32, 128).T), b2c=C(ln2_b.reshape(32, 128).T),
                 x1=x1_k[k], s1=s1_k[k]) for k in range(NCORES)]
    resE5b = _run(ncE5b, maps).results
    out_k = [r["out"] for r in resE5b]

    out = np.concatenate([o.reshape(B, SL, D) for o in out_k], axis=1)
    return np.ascontiguousarray(out, dtype=np.float32)


# version 08
# speedup vs baseline: 1.8654x; 1.0471x over previous
import contextlib

import numpy as np
import ml_dtypes

import concourse.bass as bass
import concourse.mybir as mybir
from concourse.bass_utils import run_bass_kernel_spmd

F32 = mybir.dt.float32
BF16 = mybir.dt.bfloat16
AF = mybir.ActivationFunctionType
ALU = mybir.AluOpType
AX = mybir.AxisListType

NCORES = 8
B = 4
S = 8192
SL = S // NCORES
D = 1024
N = 256
TOK = B * SL
E = 8
I = 2048
H4 = 4096
EPS = 1e-5
BIG = 1e30

ENGINES = ("sync", "tensor", "scalar", "vector", "gpsimd")


class Step:
    __slots__ = ("eng", "chan", "emit", "deps", "inc", "after", "needed", "waits")

    def __init__(self, eng, emit, deps, inc):
        self.eng = eng
        self.chan = eng + ("_d" if inc == 16 else "_c")
        self.emit = emit
        self.deps = deps
        self.inc = inc
        self.after = 0
        self.needed = False
        self.waits = []


class Slot:

    __slots__ = ("writer", "readers")

    def __init__(self):
        self.writer = None
        self.readers = []


class Plan:

    def __init__(self):
        self.per_engine = {e: [] for e in ENGINES}

    def add(self, eng, emit, deps=(), dma=False):
        deps = [d for d in deps if d is not None]
        s = Step(eng, emit, deps, 16 if dma else 1)
        self.per_engine[eng].append(s)
        return s

    def step(self, eng, emit, reads=(), writes=(), extra=(), dma=False):
        deps = list(extra)
        for sl in reads:
            if sl.writer is not None:
                deps.append(sl.writer)
        for sl in writes:
            if sl.writer is not None:
                deps.append(sl.writer)
            deps.extend(sl.readers)
        st = self.add(eng, emit, deps, dma)
        for sl in reads:
            sl.readers.append(st)
        for sl in writes:
            sl.writer = st
            sl.readers = []
        return st

    def finalize(self):
        for e in ENGINES:
            for s in self.per_engine[e]:
                if s.inc == 16:
                    s.needed = True
                for d in s.deps:
                    if not (d.eng == "tensor" and s.eng == "tensor"):
                        d.needed = True
        chans = {}
        for e in ENGINES:
            for s in self.per_engine[e]:
                if s.needed:
                    chans[s.chan] = chans.get(s.chan, 0) + s.inc
                s.after = chans.get(s.chan, 0)
        for e in ENGINES:
            prev_dma_after = 0
            for s in self.per_engine[e]:
                w = {}
                for d in s.deps:
                    if d.eng == "tensor" and s.eng == "tensor":
                        continue
                    assert d.needed and d.after > 0
                    w[d.chan] = max(w.get(d.chan, 0), d.after)
                if s.inc == 16:
                    if prev_dma_after:
                        w[s.chan] = max(w.get(s.chan, 0), prev_dma_after)
                    prev_dma_after = s.after
                s.waits = sorted(w.items())
        self.chans = chans

    def build(self, nc):
        self.finalize()
        with contextlib.ExitStack() as st:
            sems = {ch: st.enter_context(nc.semaphore(f"ch_{ch}"))
                    for ch in self.chans}
            block = st.enter_context(nc.Block())

            def section(eng_name):
                steps = self.per_engine[eng_name]

                def body(eng):
                    waited = {}
                    for s in steps:
                        for (src, val) in s.waits:
                            if val > waited.get(src, 0):
                                eng.wait_ge(sems[src], val)
                                waited[src] = val
                        inst = s.emit(eng)
                        if s.needed:
                            inst.then_inc(sems[s.chan], s.inc)
                    if eng_name == "gpsimd":
                        dch = "gpsimd_d"
                        tot = self.chans.get(dch, 0)
                        if tot:
                            eng.wait_ge(sems[dch], tot)
                return body

            for e in ENGINES:
                getattr(block, e)(section(e))


def _mk_nc():
    return bass.Bass("TRN2", target_bir_lowering=False)


class Ctx:

    def __init__(self):
        self.nc = _mk_nc()
        self.plan = Plan()
        self.stack = contextlib.ExitStack()
        self._cbuf = None
        self._cmap = {}

    def cval(self, value):
        if value not in self._cmap:
            if self._cbuf is None:
                self._cbuf = self.sbuf("cconst", [128, 8], F32)
            k = len(self._cmap)
            ap = self._cbuf[:, k:k + 1]
            slot = Slot()
            self.plan.step("vector", lambda e, ap=ap, value=value: e.memset(
                ap, value), writes=[slot])
            self._cmap[value] = (ap, slot)
        return self._cmap[value]

    def sbuf(self, name, shape, dtype):
        return self.stack.enter_context(self.nc.sbuf_tensor(name, shape, dtype))

    def psum(self, name, shape, dtype=F32):
        return self.stack.enter_context(self.nc.psum_tensor(name, shape, dtype))

    def inp(self, name, shape, dtype=F32):
        return self.nc.dram_tensor(name, shape, dtype, kind="ExternalInput").ap()

    def out(self, name, shape, dtype=F32):
        return self.nc.dram_tensor(name, shape, dtype, kind="ExternalOutput").ap()

    def done(self):
        self.plan.build(self.nc)
        self.stack.close()
        return self.nc

    def softplus(self, out_ap, z_ap, m_ap, a_ap, wslots, rslots):
        p = self.plan
        oslot, mslot, aslot = wslots
        s1 = p.step("vector", lambda e: e.tensor_single_scalar(
            m_ap, z_ap, 0.0, ALU.max), reads=rslots, writes=[mslot])
        s2 = p.step("scalar", lambda e: e.activation(
            a_ap, z_ap, AF.Abs), reads=rslots, writes=[aslot])
        s3 = p.step("scalar", lambda e: e.activation(
            a_ap, a_ap, AF.Exp, scale=-1.0), reads=[], writes=[aslot])
        onea, onesl = self.cval(1.0)
        s4 = p.step("scalar", lambda e: e.activation(
            a_ap, a_ap, AF.Ln, bias=onea), reads=[onesl], writes=[aslot])
        s5 = p.step("vector", lambda e: e.tensor_add(
            out_ap, m_ap, a_ap), reads=[mslot, aslot], writes=[oslot])
        return s5


def _proj_setup(c, wd_dram, wi_dram, bd_dram, bi_dram, al_dram, tag):
    p = c.plan
    wdf = c.sbuf(f"wdf{tag}", [128, 8 * N], F32)
    wdb = c.sbuf(f"wdb{tag}", [128, 8 * N], BF16)
    wif = c.sbuf(f"wif{tag}", [128, 8 * N], F32)
    wib = c.sbuf(f"wib{tag}", [128, 8 * N], BF16)
    bd = c.sbuf(f"bd{tag}", [128, 2], F32)
    bi = c.sbuf(f"bi{tag}", [128, 2], F32)
    al = c.sbuf(f"al{tag}", [128, 2], F32)
    spa = c.sbuf(f"spa{tag}", [128, 2], F32)
    spm = c.sbuf(f"spm{tag}", [128, 2], F32)
    sl = {k: Slot() for k in ("wdf", "wdb", "wif", "wib", "bd", "bi", "al",
                              "spa", "spm", "spaa")}
    ld1 = p.step("sync", lambda e: e.dma_start(
        out=wdf[:, :], in_=wd_dram.rearrange("(dc p) n -> p dc n", p=128)),
        writes=[sl["wdf"]], dma=True)
    p.step("gpsimd", lambda e: e.tensor_copy(out=wdb[:, :], in_=wdf[:, :]),
           reads=[sl["wdf"]], writes=[sl["wdb"]])
    ld2 = p.step("sync", lambda e: e.dma_start(
        out=wif[:, :], in_=wi_dram.rearrange("(dc p) n -> p dc n", p=128)),
        writes=[sl["wif"]], dma=True)
    p.step("gpsimd", lambda e: e.tensor_copy(out=wib[:, :], in_=wif[:, :]),
           reads=[sl["wif"]], writes=[sl["wib"]])
    p.step("sync", lambda e: e.dma_start(out=bd[:, :], in_=bd_dram),
           writes=[sl["bd"]], dma=True)
    p.step("sync", lambda e: e.dma_start(out=bi[:, :], in_=bi_dram),
           writes=[sl["bi"]], dma=True)
    p.step("sync", lambda e: e.dma_start(out=al[:, :], in_=al_dram),
           writes=[sl["al"]], dma=True)
    c.softplus(spa[:, :], al[:, :], spm[:, :], spa[:, :],
               (sl["spaa"], sl["spm"], sl["spa"]), [sl["al"]])
    return dict(wdb=wdb, wib=wib, bd=bd, bi=bi, spa=spa, sl=sl)


def _proj_emit(c, pr, rhs_sb, rhs_slots, rhs_col0, dec_dram, drv_dram,
               b, s0, scratch, ps_d, ps_u, psl):
    p = c.plan
    wdb, wib, bd, bi, spa = pr["wdb"], pr["wib"], pr["bd"], pr["bi"], pr["spa"]
    prsl = pr["sl"]
    for ncase, (wsb, wslot, psb, pslot) in enumerate(
            [(wdb, prsl["wdb"], ps_d, psl[0]), (wib, prsl["wib"], ps_u, psl[1])]):
        for nch in range(2):
            for dc in range(8):
                lhsT = wsb[:, dc * N + nch * 128: dc * N + (nch + 1) * 128]
                rhs = rhs_sb[:, rhs_col0 + dc * 512: rhs_col0 + (dc + 1) * 512]
                p.step("tensor", lambda e, lhsT=lhsT, rhs=rhs, psb=psb, nch=nch,
                       dc=dc: e.matmul(psb[:, nch * 512:(nch + 1) * 512],
                                       lhsT, rhs, start=(dc == 0), stop=(dc == 7)),
                       reads=[wslot] + rhs_slots[dc], writes=[pslot[nch]])
    for nch in range(2):
        zd, zd_s = scratch["zd"]
        zu, zu_s = scratch["zu"]
        sm, sm_s = scratch["sm"]
        sa, sa_s = scratch["sa"]
        dr, dr_s = scratch["dr"]
        o = nch * 512
        p.step("scalar", lambda e, nch=nch, o=o: e.activation(
            zd[:, o:o + 512], ps_d[:, o:o + 512], AF.Identity, bias=bd[:, nch:nch + 1]),
            reads=[psl[0][nch], prsl["bd"]], writes=[zd_s[nch]])
        p.step("scalar", lambda e, nch=nch, o=o: e.activation(
            zu[:, o:o + 512], ps_u[:, o:o + 512], AF.Identity, bias=bi[:, nch:nch + 1]),
            reads=[psl[1][nch], prsl["bi"]], writes=[zu_s[nch]])
        p.step("vector", lambda e, o=o: e.tensor_single_scalar(
            sm[:, o:o + 512], zd[:, o:o + 512], 0.0, ALU.max),
            reads=[zd_s[nch]], writes=[sm_s[nch]])
        p.step("scalar", lambda e, o=o: e.activation(
            sa[:, o:o + 512], zd[:, o:o + 512], AF.Abs),
            reads=[zd_s[nch]], writes=[sa_s[nch]])
        p.step("scalar", lambda e, o=o: e.activation(
            sa[:, o:o + 512], sa[:, o:o + 512], AF.Exp, scale=-1.0),
            writes=[sa_s[nch]])
        onea, onesl = c.cval(1.0)
        p.step("scalar", lambda e, o=o, onea=onea: e.activation(
            sa[:, o:o + 512], sa[:, o:o + 512], AF.Ln, bias=onea),
            reads=[onesl], writes=[sa_s[nch]])
        p.step("vector", lambda e, o=o: e.tensor_add(
            zd[:, o:o + 512], sm[:, o:o + 512], sa[:, o:o + 512]),
            reads=[sm_s[nch], sa_s[nch]], writes=[zd_s[nch]])
        p.step("vector", lambda e, nch=nch, o=o: e.tensor_single_scalar(
            sm[:, o:o + 512], zd[:, o:o + 512], spa[:, nch:nch + 1], ALU.mult),
            reads=[zd_s[nch], prsl["spaa"]], writes=[sm_s[nch]])
        p.step("scalar", lambda e, o=o: e.activation(
            sa[:, o:o + 512], sm[:, o:o + 512], AF.Exp, scale=-1.0),
            reads=[sm_s[nch]], writes=[sa_s[nch]])
        p.step("vector", lambda e, o=o: e.tensor_mul(
            dr[:, o:o + 512], zd[:, o:o + 512], zu[:, o:o + 512]),
            reads=[zd_s[nch], zu_s[nch]], writes=[dr_s[nch]])
        p.step("gpsimd", lambda e, nch=nch, o=o, b=b, s0=s0: e.dma_start(
            out=dec_dram[b, nch * 128:(nch + 1) * 128, s0:s0 + 512],
            in_=sa[:, o:o + 512]), reads=[sa_s[nch]], dma=True)
        p.step("gpsimd", lambda e, nch=nch, o=o, b=b, s0=s0: e.dma_start(
            out=drv_dram[b, nch * 128:(nch + 1) * 128, s0:s0 + 512],
            in_=dr[:, o:o + 512]), reads=[dr_s[nch]], dma=True)


def build_A(ngroups=8, parts=("tr", "store", "proj")):
    c = Ctx()
    p = c.plan
    x = c.inp("x", [TOK, D])
    wd = c.inp("wd", [D, N])
    wi = c.inp("wi", [D, N])
    bdd = c.inp("bd", [128, 2])
    bid = c.inp("bi", [128, 2])
    ald = c.inp("al", [128, 2])
    idn = c.inp("idn", [128, 128], BF16)
    xnT = c.out("xnT", [D, TOK], BF16)
    dec = c.out("dec", [B, N, SL])
    drv = c.out("drv", [B, N, SL])

    idn_sb = c.sbuf("idn_sb", [128, 128], BF16)
    idn_sl = Slot()
    p.step("sync", lambda e: e.dma_start(out=idn_sb[:, :], in_=idn),
           writes=[idn_sl], dma=True)
    pr = _proj_setup(c, wd, wi, bdd, bid, ald, "0") if "proj" in parts else None

    xs = c.sbuf("xs", [128, 8 * 1024], F32)
    xc = c.sbuf("xc", [128, 2 * 1024], F32)
    sq = c.sbuf("sq", [128, 2 * 1024], F32)
    cols = c.sbuf("cols", [128, 2 * 16], F32)
    xnb = c.sbuf("xnb", [128, 2 * 4 * 1024], BF16)
    xnt = c.sbuf("xnt", [128, 2 * 8 * 512], BF16)
    ps_tr = c.psum("ps_tr", [128, 2048], BF16)
    ps_d = c.psum("ps_d", [128, 1024], F32)
    ps_u = c.psum("ps_u", [128, 1024], F32)

    xs_sl = [[Slot() for _ in range(4)] for _ in range(2)]
    xc_sl = [Slot() for _ in range(2)]
    sq_sl = [Slot() for _ in range(2)]
    col_sl = [[Slot() for _ in range(4)] for _ in range(2)]
    xnb_sl = [[Slot() for _ in range(4)] for _ in range(2)]
    xnt_sl = [[[Slot() for _ in range(4)] for _ in range(8)] for _ in range(2)]
    tr_sl = [Slot() for _ in range(4)]
    psl = [[Slot(), Slot()], [Slot(), Slot()]]

    scr = {}
    for nm in ("zd", "zu", "sm", "sa", "dr"):
        t = c.sbuf(f"{nm}0s", [128, 2 * 1024], F32)
        scr[nm] = (t, [[Slot(), Slot()] for _ in range(2)])

    for g in range(ngroups):
        r = g % 2
        b = g // 2
        s0 = (g % 2) * 512
        for st in range(4):
            tok0 = g * 512 + st * 128
            xo = (r * 4 + st) * 1024
            p.step("sync", lambda e, xo=xo, tok0=tok0: e.dma_start(
                out=xs[:, xo:xo + 1024], in_=x[tok0:tok0 + 128, :]),
                writes=[xs_sl[r][st]], dma=True)
            mu = cols[:, r * 16 + st * 4 + 0: r * 16 + st * 4 + 1]
            var = cols[:, r * 16 + st * 4 + 1: r * 16 + st * 4 + 2]
            std = cols[:, r * 16 + st * 4 + 2: r * 16 + st * 4 + 3]
            rstd = cols[:, r * 16 + st * 4 + 3: r * 16 + st * 4 + 4]
            cs = col_sl[r][st]
            p.step("vector", lambda e, mu=mu, xo=xo: e.reduce_sum(
                mu, xs[:, xo:xo + 1024], axis=AX.X),
                reads=[xs_sl[r][st]], writes=[cs])
            p.step("scalar", lambda e, mu=mu: e.mul(mu, mu, 1.0 / D), writes=[cs])
            p.step("vector", lambda e, mu=mu, xo=xo, r=r: e.tensor_single_scalar(
                xc[:, r * 1024:(r + 1) * 1024], xs[:, xo:xo + 1024], mu, ALU.subtract),
                reads=[xs_sl[r][st], cs], writes=[xc_sl[r]])
            p.step("scalar", lambda e, var=var, r=r: e.activation(
                sq[:, r * 1024:(r + 1) * 1024], xc[:, r * 1024:(r + 1) * 1024],
                AF.Square, accum_out=var),
                reads=[xc_sl[r]], writes=[sq_sl[r], cs])
            epsa, epssl = c.cval(EPS)
            p.step("scalar", lambda e, std=std, var=var, epsa=epsa: e.activation(
                std, var, AF.Sqrt, scale=1.0 / D, bias=epsa),
                reads=[epssl], writes=[cs])
            p.step("vector", lambda e, rstd=rstd, std=std: e.reciprocal(
                rstd, std), writes=[cs])
            xno = (r * 4 + st) * 1024
            p.step("vector", lambda e, rstd=rstd, xno=xno, r=r: e.tensor_single_scalar(
                xnb[:, xno:xno + 1024], xc[:, r * 1024:(r + 1) * 1024], rstd, ALU.mult),
                reads=[xc_sl[r], cs], writes=[xnb_sl[r][st]])
            for dc in range(8 if "tr" in parts else 0):
                pt = dc % 2
                trs = tr_sl[pt]
                p.step("tensor", lambda e, xno=xno, dc=dc, pt=pt: e.transpose(
                    ps_tr[:, pt * 1024: pt * 1024 + 128],
                    xnb[:, xno + dc * 128: xno + (dc + 1) * 128], idn_sb[:, :]),
                    reads=[xnb_sl[r][st], idn_sl], writes=[trs])
                dst = (r * 8 + dc) * 512 + st * 128
                p.step("scalar", lambda e, dst=dst, pt=pt: e.copy(
                    xnt[:, dst:dst + 128], ps_tr[:, pt * 1024: pt * 1024 + 128]),
                    reads=[trs], writes=[xnt_sl[r][dc][st]])
        for dc in range(8 if "store" in parts else 0):
            src = (r * 8 + dc) * 512
            p.step("gpsimd", lambda e, src=src, dc=dc, g=g: e.dma_start(
                out=xnT[dc * 128:(dc + 1) * 128, g * 512:(g + 1) * 512],
                in_=xnt[:, src:src + 512]),
                reads=xnt_sl[r][dc], dma=True)
        if "proj" in parts:
            scratch = {nm: (scr[nm][0][:, r * 1024:(r + 1) * 1024], scr[nm][1][r])
                       for nm in scr}
            rhs_slots = [xnt_sl[r][dc] for dc in range(8)]
            _proj_emit(c, pr, xnt, rhs_slots, r * 8 * 512, dec, drv, b, s0,
                       scratch, ps_d, ps_u, psl)
    return c.done()


def build_scan(chunk=2048):
    c = Ctx()
    p = c.plan
    dec = c.inp("dec", [128, S])
    drv = c.inp("drv", [128, S])
    hs = c.out("hs", [128, S])
    nch = S // chunk
    sd = c.sbuf("sd", [128, 2 * chunk], F32)
    sv = c.sbuf("sv", [128, 2 * chunk], F32)
    sh = c.sbuf("sh", [128, 2 * chunk], F32)
    sd_sl = [Slot(), Slot()]
    sv_sl = [Slot(), Slot()]
    sh_sl = [Slot(), Slot()]
    for ci in range(nch):
        r = ci % 2
        o = r * chunk
        p.step("sync", lambda e, o=o, ci=ci: e.dma_start(
            out=sd[:, o:o + chunk], in_=dec[:, ci * chunk:(ci + 1) * chunk]),
            writes=[sd_sl[r]], dma=True)
        p.step("sync", lambda e, o=o, ci=ci: e.dma_start(
            out=sv[:, o:o + chunk], in_=drv[:, ci * chunk:(ci + 1) * chunk]),
            writes=[sv_sl[r]], dma=True)
        init = 0.0 if ci == 0 else sh[:, (1 - r) * chunk + chunk - 1:(1 - r) * chunk + chunk]
        reads = [sd_sl[r], sv_sl[r]] + ([sh_sl[1 - r]] if ci else [])
        p.step("vector", lambda e, o=o, init=init: e.tensor_tensor_scan(
            out=sh[:, o:o + chunk], data0=sd[:, o:o + chunk],
            data1=sv[:, o:o + chunk], initial=init,
            op0=ALU.mult, op1=ALU.add),
            reads=reads, writes=[sh_sl[r]])
        p.step("gpsimd", lambda e, o=o, ci=ci: e.dma_start(
            out=hs[:, ci * chunk:(ci + 1) * chunk], in_=sh[:, o:o + chunk]),
            reads=[sh_sl[r]], dma=True)
    return c.done()


def build_C():
    c = Ctx()
    p = c.plan
    hs0 = c.inp("hs0", [B, N, SL])
    f0t = c.inp("f0t", [N, B])
    wo = c.inp("wo", [N, D])
    ws = c.inp("ws", [N, D])
    bob = c.inp("bob", [128, 8])
    bsb = c.inp("bsb", [128, 8])
    wd = c.inp("wd", [D, N])
    wi = c.inp("wi", [D, N])
    bdd = c.inp("bd", [128, 2])
    bid = c.inp("bi", [128, 2])
    ald = c.inp("al", [128, 2])
    dec = c.out("dec", [B, N, SL])
    drv = c.out("drv", [B, N, SL])

    pr = _proj_setup(c, wd, wi, bdd, bid, ald, "1")

    wof = c.sbuf("wof", [128, 2 * D], F32)
    wob = c.sbuf("wob", [128, 2 * D], BF16)
    wsf = c.sbuf("wsf", [128, 2 * D], F32)
    wsb = c.sbuf("wsb", [128, 2 * D], BF16)
    f0f = c.sbuf("f0f", [128, 2 * B], F32)
    f0b = c.sbuf("f0b", [128, 2 * B], BF16)
    bo_sb = c.sbuf("bo_sb", [128, 8], F32)
    bs_sb = c.sbuf("bs_sb", [128, 8], F32)
    bsum = c.sbuf("bsum", [128, 8], F32)
    cls = c.sbuf("cls", [128, 32], F32)
    sl = {k: Slot() for k in ("wof", "wob", "wsf", "wsb", "f0f", "f0b",
                              "bo", "bs", "bsum", "cls")}
    for nm, dst, src in (("wof", wof, wo), ("wsf", wsf, ws)):
        p.step("sync", lambda e, dst=dst, src=src: e.dma_start(
            out=dst[:, :], in_=src.rearrange("(nc p) d -> p nc d", p=128)),
            writes=[sl[nm]], dma=True)
    p.step("gpsimd", lambda e: e.tensor_copy(out=wob[:, :], in_=wof[:, :]),
           reads=[sl["wof"]], writes=[sl["wob"]])
    p.step("gpsimd", lambda e: e.tensor_copy(out=wsb[:, :], in_=wsf[:, :]),
           reads=[sl["wsf"]], writes=[sl["wsb"]])
    p.step("sync", lambda e: e.dma_start(
        out=f0f[:, :], in_=f0t.rearrange("(nc p) b -> p nc b", p=128)),
        writes=[sl["f0f"]], dma=True)
    p.step("gpsimd", lambda e: e.tensor_copy(out=f0b[:, :], in_=f0f[:, :]),
           reads=[sl["f0f"]], writes=[sl["f0b"]])
    p.step("sync", lambda e: e.dma_start(out=bo_sb[:, :], in_=bob),
           writes=[sl["bo"]], dma=True)
    p.step("sync", lambda e: e.dma_start(out=bs_sb[:, :], in_=bsb),
           writes=[sl["bs"]], dma=True)
    p.step("vector", lambda e: e.tensor_add(bsum[:, :], bo_sb[:, :], bs_sb[:, :]),
           reads=[sl["bo"], sl["bs"]], writes=[sl["bsum"]])

    ps_c = c.psum("ps_c", [128, 512], F32)
    psc_sl = Slot()
    for dc in range(8):
        for nch in range(2):
            p.step("tensor", lambda e, dc=dc, nch=nch: e.matmul(
                ps_c[:, 0:4],
                wsb[:, nch * D + dc * 128: nch * D + (dc + 1) * 128],
                f0b[:, nch * B:(nch + 1) * B],
                start=(nch == 0), stop=(nch == 1)),
                reads=[sl["wsb"], sl["f0b"]], writes=[psc_sl])
        p.step("scalar", lambda e, dc=dc: e.activation(
            cls[:, dc * 4:(dc + 1) * 4], ps_c[:, 0:4],
            AF.Identity, bias=bsum[:, dc:dc + 1]),
            reads=[psc_sl, sl["bsum"]], writes=[sl["cls"]])

    hsf = c.sbuf("hsf", [128, 2 * 2 * 512], F32)
    hsb = c.sbuf("hsb", [128, 2 * 2 * 512], BF16)
    h1f = c.sbuf("h1f", [128, 2 * 512], F32)
    h1b = c.sbuf("h1b", [128, 2 * 8 * 512], BF16)
    ps_y = c.psum("ps_y", [128, 1024], F32)
    ps_d = c.psum("ps_d", [128, 1024], F32)
    ps_u = c.psum("ps_u", [128, 1024], F32)
    hsf_sl = [[Slot(), Slot()] for _ in range(2)]
    hsb_sl = [[Slot(), Slot()] for _ in range(2)]
    h1f_sl = [Slot(), Slot()]
    h1b_sl = [[Slot() for _ in range(8)] for _ in range(2)]
    psy_sl = [Slot(), Slot()]
    psl = [[Slot(), Slot()], [Slot(), Slot()]]
    scr = {}
    for nm in ("zd", "zu", "sm", "sa", "dr"):
        t = c.sbuf(f"{nm}1s", [128, 2 * 1024], F32)
        scr[nm] = (t, [[Slot(), Slot()] for _ in range(2)])

    for g in range(8):
        r = g % 2
        b = g // 2
        s0 = (g % 2) * 512
        for nch in range(2):
            o = (r * 2 + nch) * 512
            p.step("sync", lambda e, o=o, b=b, nch=nch, s0=s0: e.dma_start(
                out=hsf[:, o:o + 512],
                in_=hs0[b, nch * 128:(nch + 1) * 128, s0:s0 + 512]),
                writes=[hsf_sl[r][nch]], dma=True)
            p.step("gpsimd", lambda e, o=o: e.tensor_copy(
                out=hsb[:, o:o + 512], in_=hsf[:, o:o + 512]),
                reads=[hsf_sl[r][nch]], writes=[hsb_sl[r][nch]])
        for dc in range(8):
            pr_i = dc % 2
            for nch in range(2):
                p.step("tensor", lambda e, dc=dc, nch=nch, r=r, pr_i=pr_i: e.matmul(
                    ps_y[:, pr_i * 512:(pr_i + 1) * 512],
                    wob[:, nch * D + dc * 128: nch * D + (dc + 1) * 128],
                    hsb[:, (r * 2 + nch) * 512:(r * 2 + nch + 1) * 512],
                    start=(nch == 0), stop=(nch == 1)),
                    reads=[sl["wob"], hsb_sl[r][nch]], writes=[psy_sl[pr_i]])
            p.step("scalar", lambda e, dc=dc, b=b, r=r, pr_i=pr_i: e.activation(
                h1f[:, r * 512:(r + 1) * 512], ps_y[:, pr_i * 512:(pr_i + 1) * 512],
                AF.Identity, bias=cls[:, dc * 4 + b: dc * 4 + b + 1]),
                reads=[psy_sl[pr_i], sl["cls"]], writes=[h1f_sl[r]])
            p.step("vector", lambda e, dc=dc, r=r: e.tensor_copy(
                out=h1b[:, (r * 8 + dc) * 512:(r * 8 + dc + 1) * 512],
                in_=h1f[:, r * 512:(r + 1) * 512]),
                reads=[h1f_sl[r]], writes=[h1b_sl[r][dc]])
        scratch = {nm: (scr[nm][0][:, r * 1024:(r + 1) * 1024], scr[nm][1][r])
                   for nm in scr}
        rhs_slots = [[h1b_sl[r][dc]] for dc in range(8)]
        _proj_emit(c, pr, h1b, rhs_slots, r * 8 * 512, dec, drv, b, s0,
                   scratch, ps_d, ps_u, psl)
    return c.done()


def build_E1():
    c = Ctx()
    p = c.plan
    hs1 = c.inp("hs1", [B, N, SL])
    f1t = c.inp("f1t", [N, B])
    wo = c.inp("wo", [N, D])
    ws = c.inp("ws", [N, D])
    bor = c.inp("bor", [1, D])
    bsr = c.inp("bsr", [1, D])
    s0o = c.out("s0", [TOK, D])

    wof = c.sbuf("wof", [128, 2 * D], F32)
    wob = c.sbuf("wob", [128, 2 * D], BF16)
    wsf = c.sbuf("wsf", [128, 2 * D], F32)
    wsb = c.sbuf("wsb", [128, 2 * D], BF16)
    f1f = c.sbuf("f1f", [128, 2 * B], F32)
    f1b = c.sbuf("f1b", [128, 2 * B], BF16)
    brow = c.sbuf("brow", [1, D], F32)
    bsrow = c.sbuf("bsrow", [1, D], F32)
    ones4 = c.sbuf("ones4", [1, B], F32)
    ones1 = c.sbuf("ones1", [1, 128], F32)
    rall = c.sbuf("rall", [B, D], F32)
    rflat = c.sbuf("rflat", [1, B * D], F32)
    rhi = c.sbuf("rhi", [1, B * D], BF16)
    rhif = c.sbuf("rhif", [1, B * D], F32)
    rlo = c.sbuf("rlo", [1, B * D], BF16)
    onesb = c.sbuf("onesb", [1, 128], BF16)
    sl = {k: Slot() for k in ("wof", "wob", "wsf", "wsb", "f1f", "f1b", "bo",
                              "bs", "ones4", "ones1", "rall", "rflat", "rhi",
                              "rhif", "rlo", "onesb")}
    for nm, dst, src in (("wof", wof, wo), ("wsf", wsf, ws)):
        p.step("sync", lambda e, dst=dst, src=src: e.dma_start(
            out=dst[:, :], in_=src.rearrange("(nc p) d -> p nc d", p=128)),
            writes=[sl[nm]], dma=True)
    p.step("gpsimd", lambda e: e.tensor_copy(out=wob[:, :], in_=wof[:, :]),
           reads=[sl["wof"]], writes=[sl["wob"]])
    p.step("gpsimd", lambda e: e.tensor_copy(out=wsb[:, :], in_=wsf[:, :]),
           reads=[sl["wsf"]], writes=[sl["wsb"]])
    p.step("sync", lambda e: e.dma_start(
        out=f1f[:, :], in_=f1t.rearrange("(nc p) b -> p nc b", p=128)),
        writes=[sl["f1f"]], dma=True)
    p.step("gpsimd", lambda e: e.tensor_copy(out=f1b[:, :], in_=f1f[:, :]),
           reads=[sl["f1f"]], writes=[sl["f1b"]])
    p.step("sync", lambda e: e.dma_start(out=brow[:, :], in_=bor),
           writes=[sl["bo"]], dma=True)
    p.step("sync", lambda e: e.dma_start(out=bsrow[:, :], in_=bsr),
           writes=[sl["bs"]], dma=True)
    p.step("vector", lambda e: e.memset(ones4[:, :], 1.0), writes=[sl["ones4"]])
    p.step("vector", lambda e: e.memset(ones1[:, :], 1.0), writes=[sl["ones1"]])
    p.step("vector", lambda e: e.memset(onesb[:, :], 1.0), writes=[sl["onesb"]])

    ps_c = c.psum("ps_c", [128, 1024], F32)
    psc_sl = Slot()
    for dg in range(2):
        for nch in range(2):
            p.step("tensor", lambda e, dg=dg, nch=nch: e.matmul(
                ps_c[0:4, dg * 512:(dg + 1) * 512],
                f1b[:, nch * B:(nch + 1) * B],
                wsb[:, nch * D + dg * 512: nch * D + (dg + 1) * 512],
                start=(nch == 0), stop=False),
                reads=[sl["f1b"], sl["wsb"]], writes=[psc_sl])
        for bi_, (src, slot) in enumerate(((brow, sl["bo"]), (bsrow, sl["bs"]))):
            p.step("tensor", lambda e, dg=dg, src=src, bi_=bi_: e.matmul(
                ps_c[0:4, dg * 512:(dg + 1) * 512], ones4[:, :],
                src[:, dg * 512:(dg + 1) * 512], start=False, stop=(bi_ == 1)),
                reads=[sl["ones4"], slot], writes=[psc_sl])
    p.step("scalar", lambda e: e.copy(rall[:, :], ps_c[0:4, :]),
           reads=[psc_sl], writes=[sl["rall"]])
    for b in range(B):
        p.step("sync", lambda e, b=b: e.dma_start(
            out=rflat[:, b * D:(b + 1) * D], in_=rall[b:b + 1, :]),
            reads=[sl["rall"]], writes=[sl["rflat"]], dma=True)
    p.step("scalar", lambda e: e.copy(rhi[:, :], rflat[:, :]),
           reads=[sl["rflat"]], writes=[sl["rhi"]])
    p.step("scalar", lambda e: e.copy(rhif[:, :], rhi[:, :]),
           reads=[sl["rhi"]], writes=[sl["rhif"]])
    p.step("vector", lambda e: e.tensor_sub(rhif[:, :], rflat[:, :], rhif[:, :]),
           reads=[sl["rflat"]], writes=[sl["rhif"]])
    p.step("scalar", lambda e: e.copy(rlo[:, :], rhif[:, :]),
           reads=[sl["rhif"]], writes=[sl["rlo"]])

    hsf = c.sbuf("hsf", [128, 2 * 2 * 512], F32)
    hsb = c.sbuf("hsb", [128, 2 * 2 * 512], BF16)
    s0b = c.sbuf("s0b", [128, 2 * 4 * 1024], F32)
    ps_y = c.psum("ps_y", [128, 2 * 1024], F32)
    hsf_sl = [[Slot(), Slot()] for _ in range(2)]
    hsb_sl = [[Slot(), Slot()] for _ in range(2)]
    s0_sl = [[Slot() for _ in range(4)] for _ in range(2)]
    psy_sl = [[Slot(), Slot()], [Slot(), Slot()]]
    for g in range(8):
        r = g % 2
        b = g // 2
        s0 = (g % 2) * 512
        for nch in range(2):
            o = (r * 2 + nch) * 512
            p.step("sync", lambda e, o=o, b=b, nch=nch, s0=s0: e.dma_start(
                out=hsf[:, o:o + 512],
                in_=hs1[b, nch * 128:(nch + 1) * 128, s0:s0 + 512]),
                writes=[hsf_sl[r][nch]], dma=True)
            p.step("gpsimd", lambda e, o=o: e.tensor_copy(
                out=hsb[:, o:o + 512], in_=hsf[:, o:o + 512]),
                reads=[hsf_sl[r][nch]], writes=[hsb_sl[r][nch]])
        for st in range(4):
            rr = st % 2
            for dg in range(2):
                po = (rr * 2 + dg) * 512
                for nch in range(2):
                    p.step("tensor", lambda e, po=po, nch=nch, st=st, dg=dg, r=r: e.matmul(
                        ps_y[:, po:po + 512],
                        hsb[:, (r * 2 + nch) * 512 + st * 128:(r * 2 + nch) * 512 + (st + 1) * 128],
                        wob[:, nch * D + dg * 512: nch * D + (dg + 1) * 512],
                        start=(nch == 0), stop=False),
                        reads=[hsb_sl[r][nch], sl["wob"]], writes=[psy_sl[rr][dg]])
                for bi_, (src, slot) in enumerate(((rhi, sl["rhi"]), (rlo, sl["rlo"]))):
                    p.step("tensor", lambda e, po=po, src=src, b=b, dg=dg, bi_=bi_: e.matmul(
                        ps_y[:, po:po + 512], onesb[:, :],
                        src[:, b * D + dg * 512: b * D + (dg + 1) * 512],
                        start=False, stop=(bi_ == 1)),
                        reads=[sl["onesb"], slot], writes=[psy_sl[rr][dg]])
                p.step("scalar", lambda e, po=po, r=r, st=st, dg=dg: e.copy(
                    s0b[:, (r * 4 + st) * 1024 + dg * 512:(r * 4 + st) * 1024 + (dg + 1) * 512],
                    ps_y[:, po:po + 512]),
                    reads=[psy_sl[rr][dg]], writes=[s0_sl[r][st]])
            tok0 = g * 512 + st * 128
            p.step("gpsimd", lambda e, r=r, st=st, tok0=tok0: e.dma_start(
                out=s0o[tok0:tok0 + 128, :],
                in_=s0b[:, (r * 4 + st) * 1024:(r * 4 + st + 1) * 1024]),
                reads=[s0_sl[r][st]], dma=True)
    return c.done()


def build_E2():
    c = Ctx()
    p = c.plan
    xnT = c.inp("xnT", [D, TOK], BF16)
    gwt = c.inp("gwt", [D, E])
    w1t = c.inp("w1t", [E, D, I])
    w3t = c.inp("w3t", [E, D, I])
    w2t = c.inp("w2t", [E, I, D])
    go = c.out("g", [TOK, D])

    w1s = c.nc.dram_tensor("w1s", [E, 128, 16384], BF16).ap()
    w3s = c.nc.dram_tensor("w3s", [E, 128, 16384], BF16).ap()
    w2s = c.nc.dram_tensor("w2s", [E, 128, 16384], BF16).ap()

    gwf = c.sbuf("gwf", [128, 8 * E], F32)
    gwb = c.sbuf("gwb", [128, 8 * E], BF16)
    sl_gw = Slot()
    sl_gwb = Slot()
    p.step("sync", lambda e: e.dma_start(
        out=gwf[:, :], in_=gwt.rearrange("(dc p) ee -> p dc ee", p=128)),
        writes=[sl_gw], dma=True)
    p.step("gpsimd", lambda e: e.tensor_copy(out=gwb[:, :], in_=gwf[:, :]),
           reads=[sl_gw], writes=[sl_gwb])

    TG = 1024
    xnb = c.sbuf("xnb", [128, 8 * TG], BF16)
    outg = c.sbuf("outg", [128, 8 * 1024], F32)
    stg = c.sbuf("stg", [128, 4096], F32)
    stg2 = c.sbuf("stg2", [128, 4096], F32)
    w1h = c.sbuf("w1h", [128, 2 * 8192], BF16)
    w3h = c.sbuf("w3h", [128, 2 * 8192], BF16)
    w2b = c.sbuf("w2b", [128, 16 * 1024], BF16)
    actb = c.sbuf("actb", [128, 2 * 256], BF16)
    silu_t = c.sbuf("silu_t", [128, 2 * 256], F32)
    lg = c.sbuf("lg", [128, 2 * 8], F32)
    gsc = c.sbuf("gsc", [128, 6 * 8], F32)
    gs2 = c.sbuf("gs2", [128, 3 * 8], F32)
    gcol = c.sbuf("gcol", [128, 8], F32)
    dwb = c.sbuf("dwb", [128, 8 * 8], F32)
    gsb = c.sbuf("gsb", [128, 2 * 1024], F32)

    ps_l = c.psum("ps_l", [128, 512], F32)
    ps_h1 = c.psum("ps_h1", [128, 512], F32)
    ps_h3 = c.psum("ps_h3", [128, 512], F32)
    ps_y = c.psum("ps_y", [128, 4 * 512], F32)

    xnb_sl = [Slot() for _ in range(8)]
    outg_sl = [Slot() for _ in range(8)]
    stg_sl = Slot()
    stg2_sl = Slot()
    w1s_sl = [Slot() for _ in range(E)]
    w3s_sl = [Slot() for _ in range(E)]
    w2s_sl = [Slot() for _ in range(E)]
    w1h_sl = [[Slot(), Slot()] for _ in range(2)]
    w3h_sl = [[Slot(), Slot()] for _ in range(2)]
    w2b_sl = [Slot(), Slot()]
    actb_sl = [Slot(), Slot()]
    silu_sl = [Slot(), Slot()]
    lg_sl = [Slot(), Slot()]
    gsc_sl = [Slot() for _ in range(6)]
    gs2_sl = [Slot() for _ in range(3)]
    gcol_sl = [Slot() for _ in range(8)]
    dwb_sl = [Slot() for _ in range(8)]
    gsb_sl = [Slot(), Slot()]
    psl_l = Slot()
    psl_h1 = Slot()
    psl_h3 = Slot()
    psl_y = [Slot() for _ in range(4)]

    for tg in range(4):
        t0 = tg * TG
        for dc in range(8):
            p.step("sync", lambda e, dc=dc, t0=t0: e.dma_start(
                out=xnb[:, dc * TG:(dc + 1) * TG],
                in_=xnT[dc * 128:(dc + 1) * 128, t0:t0 + TG]),
                writes=[xnb_sl[dc]], dma=True)
        for tt in range(8):
            r = tt % 2
            lgr = lg[:, r * 8:(r + 1) * 8]
            lgs = lg_sl[r]
            for dc in range(8):
                p.step("tensor", lambda e, dc=dc, tt=tt: e.matmul(
                    ps_l[:, 0:E], xnb[:, dc * TG + tt * 128: dc * TG + (tt + 1) * 128],
                    gwb[:, dc * E:(dc + 1) * E], start=(dc == 0), stop=(dc == 7)),
                    reads=[xnb_sl[dc], sl_gwb], writes=[psl_l])
            p.step("scalar", lambda e, lgr=lgr: e.copy(lgr, ps_l[:, 0:E]),
                   reads=[psl_l], writes=[lgs])
            oh1, ab, f1, tml, m1, i1 = (gsc[:, k * 8:(k + 1) * 8] for k in range(6))
            s2v, oh2, e2v = (gs2[:, k * 8:(k + 1) * 8] for k in range(3))
            c1, c1n, c2, c3, c4, c4n, c5, c6 = (gcol[:, k:k + 1] for k in range(8))
            st_ = p.step
            st_("vector", lambda e, lgr=lgr, c1=c1: e.reduce_max(c1, lgr, axis=AX.X),
                reads=[lgs], writes=[gcol_sl[0]])
            st_("vector", lambda e, lgr=lgr, oh1=oh1, c1=c1: e.tensor_single_scalar(
                oh1, lgr, c1, ALU.is_equal), reads=[lgs, gcol_sl[0]], writes=[gsc_sl[0]])
            st_("scalar", lambda e, lgr=lgr, ab=ab: e.activation(ab, lgr, AF.Abs),
                reads=[lgs], writes=[gsc_sl[1]])
            st_("vector", lambda e, ab=ab, f1=f1, c1=c1: e.tensor_single_scalar(
                f1, ab, c1, ALU.max), reads=[gsc_sl[1], gcol_sl[0]], writes=[gsc_sl[2]])
            st_("vector", lambda e, lgr=lgr, tml=tml, c1=c1: e.tensor_scalar(
                tml, lgr, c1, -1.0, ALU.subtract, ALU.mult),
                reads=[lgs, gcol_sl[0]], writes=[gsc_sl[3]])
            st_("vector", lambda e, f1=f1, tml=tml, m1=m1: e.scalar_tensor_tensor(
                m1, f1, 0.02, tml, ALU.mult, ALU.is_lt),
                reads=[gsc_sl[2], gsc_sl[3]], writes=[gsc_sl[4]])
            st_("vector", lambda e, m1=m1, i1=i1: e.tensor_scalar(
                i1, m1, -1.0, 1.0, ALU.mult, ALU.add),
                reads=[gsc_sl[4]], writes=[gsc_sl[5]])
            st_("vector", lambda e, c1=c1, c1n=c1n: e.tensor_single_scalar(
                c1n, c1, -1.0, ALU.mult), reads=[gcol_sl[0]], writes=[gcol_sl[1]])
            st_("scalar", lambda e, lgr=lgr, e2v=e2v, c1n=c1n: e.activation(
                e2v, lgr, AF.Exp, bias=c1n), reads=[lgs, gcol_sl[1]], writes=[gs2_sl[2]])
            st_("vector", lambda e, e2v=e2v, i1=i1: e.tensor_mul(e2v, e2v, i1),
                reads=[gsc_sl[5]], writes=[gs2_sl[2]])
            st_("vector", lambda e, e2v=e2v, c2=c2: e.reduce_sum(c2, e2v, axis=AX.X),
                reads=[gs2_sl[2]], writes=[gcol_sl[2]])
            st_("vector", lambda e, c2=c2, c3=c3: e.reciprocal(c3, c2),
                reads=[gcol_sl[2]], writes=[gcol_sl[3]])
            st_("vector", lambda e, oh1=oh1, lgr=lgr, s2v=s2v: e.scalar_tensor_tensor(
                s2v, oh1, -BIG, lgr, ALU.mult, ALU.add),
                reads=[gsc_sl[0], lgs], writes=[gs2_sl[0]])
            st_("vector", lambda e, s2v=s2v, c4=c4: e.reduce_max(c4, s2v, axis=AX.X),
                reads=[gs2_sl[0]], writes=[gcol_sl[4]])
            st_("vector", lambda e, s2v=s2v, oh2=oh2, c4=c4: e.tensor_single_scalar(
                oh2, s2v, c4, ALU.is_equal), reads=[gs2_sl[0], gcol_sl[4]], writes=[gs2_sl[1]])
            st_("vector", lambda e, ab=ab, f1=f1, c4=c4: e.tensor_single_scalar(
                f1, ab, c4, ALU.max), reads=[gsc_sl[1], gcol_sl[4]], writes=[gsc_sl[2]])
            st_("vector", lambda e, lgr=lgr, tml=tml, c4=c4: e.tensor_scalar(
                tml, lgr, c4, -1.0, ALU.subtract, ALU.mult),
                reads=[lgs, gcol_sl[4]], writes=[gsc_sl[3]])
            st_("vector", lambda e, f1=f1, tml=tml, m1=m1: e.scalar_tensor_tensor(
                m1, f1, 0.02, tml, ALU.mult, ALU.is_lt),
                reads=[gsc_sl[2], gsc_sl[3]], writes=[gsc_sl[4]])
            st_("vector", lambda e, m1=m1, i1=i1: e.tensor_scalar(
                i1, m1, -1.0, 1.0, ALU.mult, ALU.add),
                reads=[gsc_sl[4]], writes=[gsc_sl[5]])
            st_("vector", lambda e, c4=c4, c4n=c4n: e.tensor_single_scalar(
                c4n, c4, -1.0, ALU.mult), reads=[gcol_sl[4]], writes=[gcol_sl[5]])
            st_("scalar", lambda e, s2v=s2v, e2v=e2v, c4n=c4n: e.activation(
                e2v, s2v, AF.Exp, bias=c4n), reads=[gs2_sl[0], gcol_sl[5]], writes=[gs2_sl[2]])
            st_("vector", lambda e, e2v=e2v, i1=i1: e.tensor_mul(e2v, e2v, i1),
                reads=[gsc_sl[5]], writes=[gs2_sl[2]])
            st_("vector", lambda e, e2v=e2v, c5=c5: e.reduce_sum(c5, e2v, axis=AX.X),
                reads=[gs2_sl[2]], writes=[gcol_sl[6]])
            st_("vector", lambda e, c5=c5, c6=c6: e.reciprocal(c6, c5),
                reads=[gcol_sl[6]], writes=[gcol_sl[7]])
            st_("vector", lambda e, oh1=oh1, c3=c3: e.tensor_single_scalar(
                oh1, oh1, c3, ALU.mult), reads=[gcol_sl[3]], writes=[gsc_sl[0]])
            st_("vector", lambda e, oh1=oh1, oh2=oh2, c6=c6, tt=tt: e.scalar_tensor_tensor(
                dwb[:, tt * 8:(tt + 1) * 8], oh2, c6, oh1, ALU.mult, ALU.add),
                reads=[gs2_sl[1], gcol_sl[7], gsc_sl[0]], writes=[dwb_sl[tt]])
        for tt in range(8):
            p.step("vector", lambda e, tt=tt: e.memset(
                outg[:, tt * 1024:(tt + 1) * 1024], 0.0), writes=[outg_sl[tt]])
        for ex in range(E):
            if tg == 0:
                li = 0
                for half in range(2):
                    for q in range(2):
                        sb, ssl = (stg, stg_sl) if li % 2 == 0 else (stg2, stg2_sl)
                        li += 1
                        p.step("sync", lambda e, ex=ex, half=half, q=q, sb=sb: e.dma_start(
                            out=sb[:, :],
                            in_=w1t[ex, :, half * 1024 + q * 512:
                                    half * 1024 + (q + 1) * 512].rearrange(
                                "(dc p) i -> p dc i", p=128)),
                            writes=[ssl], dma=True)
                        p.step("vector", lambda e, half=half, q=q, sb=sb: e.tensor_copy(
                            out=w1h[:, half * 8192 + q * 4096:
                                    half * 8192 + (q + 1) * 4096].rearrange(
                                "p (dc i) -> p dc i", dc=8),
                            in_=sb[:, :].rearrange("p (dc i) -> p dc i", dc=8)),
                            reads=[ssl], writes=[w1h_sl[half][q]])
                        sb, ssl = (stg, stg_sl) if li % 2 == 0 else (stg2, stg2_sl)
                        li += 1
                        p.step("sync", lambda e, ex=ex, half=half, q=q, sb=sb: e.dma_start(
                            out=sb[:, :],
                            in_=w3t[ex, :, half * 1024 + q * 512:
                                    half * 1024 + (q + 1) * 512].rearrange(
                                "(dc p) i -> p dc i", p=128)),
                            writes=[ssl], dma=True)
                        p.step("vector", lambda e, half=half, q=q, sb=sb: e.tensor_copy(
                            out=w3h[:, half * 8192 + q * 4096:
                                    half * 8192 + (q + 1) * 4096].rearrange(
                                "p (dc i) -> p dc i", dc=8),
                            in_=sb[:, :].rearrange("p (dc i) -> p dc i", dc=8)),
                            reads=[ssl], writes=[w3h_sl[half][q]])
                        sb, ssl = (stg, stg_sl) if li % 2 == 0 else (stg2, stg2_sl)
                        li += 1
                        p.step("sync", lambda e, ex=ex, half=half, q=q, sb=sb: e.dma_start(
                            out=sb[:, :],
                            in_=w2t[ex, half * 1024 + q * 512:
                                    half * 1024 + (q + 1) * 512, :].rearrange(
                                "(ic p) dd -> p ic dd", p=128)),
                            writes=[ssl], dma=True)
                        p.step("scalar", lambda e, half=half, q=q, sb=sb: e.copy(
                            out=w2b[:, half * 8192 + q * 4096:
                                    half * 8192 + (q + 1) * 4096],
                            in_=sb[:, :]),
                            reads=[ssl], writes=[w2b_sl[half]])
                p.step("gpsimd", lambda e, ex=ex: e.dma_start(
                    out=w1s[ex, :, :], in_=w1h[:, :]),
                    reads=[w1h_sl[0][0], w1h_sl[0][1], w1h_sl[1][0], w1h_sl[1][1]],
                    writes=[w1s_sl[ex]], dma=True)
                p.step("gpsimd", lambda e, ex=ex: e.dma_start(
                    out=w3s[ex, :, :], in_=w3h[:, :]),
                    reads=[w3h_sl[0][0], w3h_sl[0][1], w3h_sl[1][0], w3h_sl[1][1]],
                    writes=[w3s_sl[ex]], dma=True)
                p.step("gpsimd", lambda e, ex=ex: e.dma_start(
                    out=w2s[ex, :, :], in_=w2b[:, :]),
                    reads=[w2b_sl[0], w2b_sl[1]],
                    writes=[w2s_sl[ex]], dma=True)
            else:
                for half in range(2):
                    p.step("sync", lambda e, ex=ex, half=half: e.dma_start(
                        out=w1h[:, half * 8192:(half + 1) * 8192],
                        in_=w1s[ex, :, half * 8192:(half + 1) * 8192]),
                        reads=[w1s_sl[ex]],
                        writes=[w1h_sl[half][0], w1h_sl[half][1]],
                        dma=True)
                    p.step("gpsimd", lambda e, ex=ex, half=half: e.dma_start(
                        out=w2b[:, half * 8192:(half + 1) * 8192],
                        in_=w2s[ex, :, half * 8192:(half + 1) * 8192]),
                        reads=[w2s_sl[ex]],
                        writes=[w2b_sl[half]], dma=True)
                    p.step("sync", lambda e, ex=ex, half=half: e.dma_start(
                        out=w3h[:, half * 8192:(half + 1) * 8192],
                        in_=w3s[ex, :, half * 8192:(half + 1) * 8192]),
                        reads=[w3s_sl[ex]],
                        writes=[w3h_sl[half][0], w3h_sl[half][1]],
                        dma=True)
            for half in range(2):
                for ts in range(4):
                    to = ts * 256
                    for icl in range(8):
                        ic = half * 8 + icl
                        q, icq = divmod(icl, 4)
                        w1c = half * 8192 + q * 4096
                        ar = ic % 2
                        for dc in range(8):
                            col = w1c + dc * 512 + icq * 128
                            p.step("tensor", lambda e, col=col, dc=dc, to=to: e.matmul(
                                ps_h1[:, 0:256],
                                w1h[:, col:col + 128],
                                xnb[:, dc * TG + to: dc * TG + to + 256],
                                start=(dc == 0), stop=(dc == 7)),
                                reads=[w1h_sl[half][q], xnb_sl[dc]], writes=[psl_h1])
                            p.step("tensor", lambda e, col=col, dc=dc, to=to: e.matmul(
                                ps_h3[:, 0:256],
                                w3h[:, col:col + 128],
                                xnb[:, dc * TG + to: dc * TG + to + 256],
                                start=(dc == 0), stop=(dc == 7)),
                                reads=[w3h_sl[half][q], xnb_sl[dc]], writes=[psl_h3])
                        p.step("scalar", lambda e, ar=ar: e.activation(
                            silu_t[:, ar * 256:(ar + 1) * 256], ps_h1[:, 0:256], AF.Silu),
                            reads=[psl_h1], writes=[silu_sl[ar]])
                        p.step("vector", lambda e, ar=ar: e.tensor_mul(
                            actb[:, ar * 256:(ar + 1) * 256],
                            silu_t[:, ar * 256:(ar + 1) * 256], ps_h3[:, 0:256]),
                            reads=[silu_sl[ar], psl_h3], writes=[actb_sl[ar]])
                        for st4 in range(2):
                            for dg in range(2):
                                yk = st4 * 2 + dg
                                p.step("tensor", lambda e, ar=ar, st4=st4, dg=dg,
                                       yk=yk, ic=ic, half=half, icl=icl: e.matmul(
                                    ps_y[:, yk * 512:(yk + 1) * 512],
                                    actb[:, ar * 256 + st4 * 128: ar * 256 + (st4 + 1) * 128],
                                    w2b[:, ic * 1024 + dg * 512: ic * 1024 + (dg + 1) * 512],
                                    start=(icl == 0), stop=(icl == 7)),
                                    reads=[actb_sl[ar], w2b_sl[half]], writes=[psl_y[yk]])
                    for st4 in range(2):
                        tt = (to + st4 * 128) // 128
                        for dg in range(2):
                            yk = st4 * 2 + dg
                            p.step("vector", lambda e, tt=tt, dg=dg, ex=ex, yk=yk: e.scalar_tensor_tensor(
                                outg[:, tt * 1024 + dg * 512: tt * 1024 + (dg + 1) * 512],
                                ps_y[:, yk * 512:(yk + 1) * 512],
                                dwb[:, tt * 8 + ex: tt * 8 + ex + 1],
                                outg[:, tt * 1024 + dg * 512: tt * 1024 + (dg + 1) * 512],
                                ALU.mult, ALU.add),
                                reads=[psl_y[yk], dwb_sl[tt]], writes=[outg_sl[tt]])
        for tt in range(8):
            r = tt % 2
            p.step("scalar", lambda e, tt=tt, r=r: e.activation(
                gsb[:, r * 1024:(r + 1) * 1024], outg[:, tt * 1024:(tt + 1) * 1024],
                AF.Silu), reads=[outg_sl[tt]], writes=[gsb_sl[r]])
            p.step("gpsimd", lambda e, tt=tt, r=r, t0=t0: e.dma_start(
                out=go[t0 + tt * 128: t0 + (tt + 1) * 128, :],
                in_=gsb[:, r * 1024:(r + 1) * 1024]),
                reads=[gsb_sl[r]], dma=True)
    return c.done()


def build_E3():
    c = Ctx()
    p = c.plan
    x = c.inp("x", [TOK, D])
    s0i = c.inp("s0", [TOK, D])
    gi = c.inp("g", [TOK, D])
    idn = c.inp("idn", [128, 128], BF16)
    x1o = c.out("x1", [TOK, D])
    xn1o = c.out("xn1", [TOK, D])
    xn1T = c.out("xn1T", [D, TOK], BF16)

    idn_sb = c.sbuf("idn_sb", [128, 128], BF16)
    idn_sl = Slot()
    p.step("sync", lambda e: e.dma_start(out=idn_sb[:, :], in_=idn),
           writes=[idn_sl], dma=True)
    xs = c.sbuf("xs", [128, 2 * 1024], F32)
    ss = c.sbuf("ss", [128, 2 * 1024], F32)
    gs = c.sbuf("gs", [128, 2 * 1024], F32)
    x1t = c.sbuf("x1t", [128, 2 * 1024], F32)
    xc = c.sbuf("xc", [128, 2 * 1024], F32)
    sq = c.sbuf("sq", [128, 2 * 1024], F32)
    xnf = c.sbuf("xnf", [128, 2 * 1024], F32)
    xnb = c.sbuf("xnb", [128, 2 * 1024], BF16)
    xnt = c.sbuf("xnt", [128, 2 * 8 * 512], BF16)
    cols = c.sbuf("cols", [128, 2 * 4], F32)
    ps_tr = c.psum("ps_tr", [128, 2048], BF16)
    sls = {nm: [Slot(), Slot()] for nm in
           ("xs", "ss", "gs", "x1t", "xc", "sq", "xnf", "xnb", "cols")}
    xnt_sl = [[[Slot() for _ in range(4)] for _ in range(8)] for _ in range(2)]
    tr_sl = [Slot() for _ in range(4)]

    for g in range(8):
        rg = g % 2
        for st in range(4):
            tok0 = g * 512 + st * 128
            r = st % 2
            o = r * 1024
            for nm, buf, src in (("xs", xs, x), ("ss", ss, s0i), ("gs", gs, gi)):
                p.step("sync", lambda e, buf=buf, src=src, o=o, tok0=tok0: e.dma_start(
                    out=buf[:, o:o + 1024], in_=src[tok0:tok0 + 128, :]),
                    writes=[sls[nm][r]], dma=True)
            p.step("vector", lambda e, o=o: e.tensor_mul(
                gs[:, o:o + 1024], ss[:, o:o + 1024], gs[:, o:o + 1024]),
                reads=[sls["ss"][r]], writes=[sls["gs"][r]])
            p.step("vector", lambda e, o=o: e.tensor_add(
                x1t[:, o:o + 1024], xs[:, o:o + 1024], gs[:, o:o + 1024]),
                reads=[sls["xs"][r], sls["gs"][r]], writes=[sls["x1t"][r]])
            p.step("gpsimd", lambda e, o=o, tok0=tok0: e.dma_start(
                out=x1o[tok0:tok0 + 128, :], in_=x1t[:, o:o + 1024]),
                reads=[sls["x1t"][r]], dma=True)
            mu = cols[:, r * 4 + 0: r * 4 + 1]
            var = cols[:, r * 4 + 1: r * 4 + 2]
            std = cols[:, r * 4 + 2: r * 4 + 3]
            rstd = cols[:, r * 4 + 3: r * 4 + 4]
            cs = sls["cols"][r]
            p.step("vector", lambda e, mu=mu, o=o: e.reduce_sum(
                mu, x1t[:, o:o + 1024], axis=AX.X),
                reads=[sls["x1t"][r]], writes=[cs])
            p.step("scalar", lambda e, mu=mu: e.mul(mu, mu, 1.0 / D), writes=[cs])
            p.step("vector", lambda e, mu=mu, o=o: e.tensor_single_scalar(
                xc[:, o:o + 1024], x1t[:, o:o + 1024], mu, ALU.subtract),
                reads=[sls["x1t"][r], cs], writes=[sls["xc"][r]])
            p.step("scalar", lambda e, var=var, o=o: e.activation(
                sq[:, o:o + 1024], xc[:, o:o + 1024], AF.Square, accum_out=var),
                reads=[sls["xc"][r]], writes=[sls["sq"][r], cs])
            epsa, epssl = c.cval(EPS)
            p.step("scalar", lambda e, std=std, var=var, epsa=epsa: e.activation(
                std, var, AF.Sqrt, scale=1.0 / D, bias=epsa),
                reads=[epssl], writes=[cs])
            p.step("vector", lambda e, rstd=rstd, std=std: e.reciprocal(
                rstd, std), writes=[cs])
            p.step("vector", lambda e, rstd=rstd, o=o: e.tensor_single_scalar(
                xnf[:, o:o + 1024], xc[:, o:o + 1024], rstd, ALU.mult),
                reads=[sls["xc"][r], cs], writes=[sls["xnf"][r]])
            p.step("gpsimd", lambda e, o=o, tok0=tok0: e.dma_start(
                out=xn1o[tok0:tok0 + 128, :], in_=xnf[:, o:o + 1024]),
                reads=[sls["xnf"][r]], dma=True)
            p.step("vector", lambda e, o=o: e.tensor_copy(
                out=xnb[:, o:o + 1024], in_=xnf[:, o:o + 1024]),
                reads=[sls["xnf"][r]], writes=[sls["xnb"][r]])
            for dc in range(8):
                pt = dc % 2
                p.step("tensor", lambda e, o=o, dc=dc, pt=pt: e.transpose(
                    ps_tr[:, pt * 1024: pt * 1024 + 128],
                    xnb[:, o + dc * 128: o + (dc + 1) * 128], idn_sb[:, :]),
                    reads=[sls["xnb"][r], idn_sl], writes=[tr_sl[pt]])
                dst = (rg * 8 + dc) * 512 + st * 128
                p.step("scalar", lambda e, dst=dst, pt=pt: e.copy(
                    xnt[:, dst:dst + 128], ps_tr[:, pt * 1024: pt * 1024 + 128]),
                    reads=[tr_sl[pt]], writes=[xnt_sl[rg][dc][st]])
        for dc in range(8):
            src = (rg * 8 + dc) * 512
            p.step("gpsimd", lambda e, src=src, dc=dc, g=g: e.dma_start(
                out=xn1T[dc * 128:(dc + 1) * 128, g * 512:(g + 1) * 512],
                in_=xnt[:, src:src + 512]),
                reads=xnt_sl[rg][dc], dma=True)
    return c.done()


def build_E4():
    c = Ctx()
    p = c.plan
    xn1 = c.inp("xn1", [TOK, D])
    xn1T = c.inp("xn1T", [D, TOK], BF16)
    wqt = c.inp("wqt", [D, D])
    wkt = c.inp("wkt", [D, D])
    s1o = c.out("s1", [TOK, D])

    stg = c.sbuf("stg", [128, 4096], F32)
    wqb = c.sbuf("wqb", [128, 8 * 1024], BF16)
    wkb = c.sbuf("wkb", [128, 8 * 1024], BF16)
    xts = c.sbuf("xts", [128, 8 * 4096], BF16)
    stg_sl = Slot()
    wq_sl = Slot()
    wk_sl = Slot()
    xts_sl = Slot()
    for wsrc, wdst, wsl in ((wqt, wqb, wq_sl), (wkt, wkb, wk_sl)):
        for dg in range(2):
            p.step("sync", lambda e, wsrc=wsrc, dg=dg: e.dma_start(
                out=stg[:, :],
                in_=wsrc[:, dg * 512:(dg + 1) * 512].rearrange(
                    "(dc p) ee -> p dc ee", p=128)),
                writes=[stg_sl], dma=True)
            p.step("gpsimd", lambda e, wdst=wdst, dg=dg: e.tensor_copy(
                out=wdst[:, :].rearrange("p (dc ee) -> p dc ee",
                                         ee=1024)[:, :, dg * 512:(dg + 1) * 512],
                in_=stg[:, :].rearrange("p (dc ee) -> p dc ee", dc=8)),
                reads=[stg_sl], writes=[wsl])
    p.step("sync", lambda e: e.dma_start(
        out=xts[:, :], in_=xn1T.rearrange("(dc p) t -> p dc t", p=128)),
        writes=[xts_sl], dma=True)

    qsb = c.sbuf("qsb", [128, 4 * 1024], F32)
    ksb = c.sbuf("ksb", [128, 4 * 1024], F32)
    xtt = c.sbuf("xtt", [128, 4 * 1024], F32)
    qkw = c.sbuf("qkw", [128, 1024], F32)
    sc = c.sbuf("sc", [128, 16], F32)
    th = c.sbuf("th", [128, 16], F32)
    exb = c.sbuf("exb", [128, 16], F32)
    alb = c.sbuf("alb", [128, 16], F32)
    mcol = c.sbuf("mcol", [128, 4], F32)
    ob = c.sbuf("ob", [128, 2 * 1024], F32)
    ps_q = c.psum("ps_q", [128, 512], F32)
    ps_k = c.psum("ps_k", [128, 512], F32)
    q_sl = [Slot() for _ in range(4)]
    k_sl = [Slot() for _ in range(4)]
    xt_sl = [Slot() for _ in range(4)]
    qkw_sl = Slot()
    sc_sl = [Slot() for _ in range(16)]
    th_sl = Slot()
    ex_sl = [Slot() for _ in range(4)]
    al_sl = [Slot() for _ in range(4)]
    mc_sl = [Slot() for _ in range(4)]
    ob_sl = [Slot(), Slot()]
    psq_sl = Slot()
    psk_sl = Slot()

    for j in range(8):
        for s in range(4):
            t = s * 8 + j
            for dg in range(2):
                for dc in range(8):
                    p.step("tensor", lambda e, dc=dc, t=t, dg=dg: e.matmul(
                        ps_q[:, :], xts[:, dc * 4096 + t * 128: dc * 4096 + (t + 1) * 128],
                        wqb[:, dc * 1024 + dg * 512: dc * 1024 + (dg + 1) * 512],
                        start=(dc == 0), stop=(dc == 7)),
                        reads=[xts_sl, wq_sl], writes=[psq_sl])
                    p.step("tensor", lambda e, dc=dc, t=t, dg=dg: e.matmul(
                        ps_k[:, :], xts[:, dc * 4096 + t * 128: dc * 4096 + (t + 1) * 128],
                        wkb[:, dc * 1024 + dg * 512: dc * 1024 + (dg + 1) * 512],
                        start=(dc == 0), stop=(dc == 7)),
                        reads=[xts_sl, wk_sl], writes=[psk_sl])
                p.step("scalar", lambda e, s=s, dg=dg: e.copy(
                    qsb[:, s * 1024 + dg * 512: s * 1024 + (dg + 1) * 512], ps_q[:, :]),
                    reads=[psq_sl], writes=[q_sl[s]])
                p.step("scalar", lambda e, s=s, dg=dg: e.copy(
                    ksb[:, s * 1024 + dg * 512: s * 1024 + (dg + 1) * 512], ps_k[:, :]),
                    reads=[psk_sl], writes=[k_sl[s]])
            tok0 = t * 128
            p.step("sync", lambda e, s=s, tok0=tok0: e.dma_start(
                out=xtt[:, s * 1024:(s + 1) * 1024], in_=xn1[tok0:tok0 + 128, :]),
                writes=[xt_sl[s]], dma=True)
        for s in range(4):
            for t4 in range(4):
                k = s * 4 + t4
                p.step("vector", lambda e, s=s, t4=t4: e.tensor_mul(
                    qkw[:, :], qsb[:, s * 1024:(s + 1) * 1024],
                    ksb[:, t4 * 1024:(t4 + 1) * 1024]),
                    reads=[q_sl[s], k_sl[t4]], writes=[qkw_sl])
                p.step("vector", lambda e, k=k: e.reduce_sum(
                    sc[:, k:k + 1], qkw[:, :], axis=AX.X),
                    reads=[qkw_sl], writes=[sc_sl[k]])
        p.step("scalar", lambda e: e.activation(th[:, :], sc[:, :], AF.Tanh,
                                                scale=1.0 / 32.0),
               reads=sc_sl, writes=[th_sl])
        for s in range(4):
            p.step("vector", lambda e, s=s: e.reduce_max(
                mcol[:, s:s + 1], th[:, s * 4:(s + 1) * 4], axis=AX.X),
                reads=[th_sl], writes=[mc_sl[s]])
            p.step("vector", lambda e, s=s: e.tensor_single_scalar(
                mcol[:, s:s + 1], mcol[:, s:s + 1], -1.0, ALU.mult),
                writes=[mc_sl[s]])
            p.step("scalar", lambda e, s=s: e.activation(
                exb[:, s * 4:(s + 1) * 4], th[:, s * 4:(s + 1) * 4], AF.Exp,
                bias=mcol[:, s:s + 1]),
                reads=[th_sl, mc_sl[s]], writes=[ex_sl[s]])
            p.step("vector", lambda e, s=s: e.reduce_sum(
                mcol[:, s:s + 1], exb[:, s * 4:(s + 1) * 4], axis=AX.X),
                reads=[ex_sl[s]], writes=[mc_sl[s]])
            p.step("vector", lambda e, s=s: e.reciprocal(
                mcol[:, s:s + 1], mcol[:, s:s + 1]), writes=[mc_sl[s]])
            p.step("vector", lambda e, s=s: e.tensor_single_scalar(
                alb[:, s * 4:(s + 1) * 4], exb[:, s * 4:(s + 1) * 4],
                mcol[:, s:s + 1], ALU.mult),
                reads=[ex_sl[s], mc_sl[s]], writes=[al_sl[s]])
        for s in range(4):
            r = s % 2
            o = r * 1024
            p.step("vector", lambda e, s=s, o=o: e.tensor_single_scalar(
                ob[:, o:o + 1024], xtt[:, 0:1024], alb[:, s * 4:s * 4 + 1], ALU.mult),
                reads=[xt_sl[0], al_sl[s]], writes=[ob_sl[r]])
            for t4 in range(1, 4):
                p.step("vector", lambda e, s=s, t4=t4, o=o: e.scalar_tensor_tensor(
                    ob[:, o:o + 1024], xtt[:, t4 * 1024:(t4 + 1) * 1024],
                    alb[:, s * 4 + t4: s * 4 + t4 + 1], ob[:, o:o + 1024],
                    ALU.mult, ALU.add),
                    reads=[xt_sl[t4], al_sl[s]], writes=[ob_sl[r]])
            tok0 = (s * 8 + j) * 128
            p.step("gpsimd", lambda e, o=o, tok0=tok0: e.dma_start(
                out=s1o[tok0:tok0 + 128, :], in_=ob[:, o:o + 1024]),
                reads=[ob_sl[r]], dma=True)
    return c.done()


def build_E5a():
    c = Ctx()
    p = c.plan
    xn1 = c.inp("xn1", [TOK, D])
    w1t = c.inp("w1t", [D, H4])
    b1c = c.inp("b1c", [128, 32])
    g1r = c.inp("g1r", [1, D])
    b1r = c.inp("b1r", [1, D])
    idn = c.inp("idn", [128, 128], BF16)
    hG = c.out("hG", [H4, TOK])

    idn_sb = c.sbuf("idn_sb", [128, 128], BF16)
    idn_sl = Slot()
    p.step("sync", lambda e: e.dma_start(out=idn_sb[:, :], in_=idn),
           writes=[idn_sl], dma=True)
    ones1 = c.sbuf("ones1", [1, 128], F32)
    onesc = c.sbuf("onesc", [128, 1], F32)
    o1_sl = Slot()
    oc_sl = Slot()
    p.step("vector", lambda e: e.memset(ones1[:, :], 1.0), writes=[o1_sl])
    p.step("vector", lambda e: e.memset(onesc[:, :], 1.0), writes=[oc_sl])
    g1row = c.sbuf("g1row", [1, D], F32)
    b1row = c.sbuf("b1row", [1, D], F32)
    g1B = c.sbuf("g1B", [128, D], F32)
    b1B = c.sbuf("b1B", [128, D], F32)
    b1cs = c.sbuf("b1cs", [128, 32], F32)
    grow_sl = Slot()
    brow_sl = Slot()
    g1B_sl = Slot()
    b1B_sl = Slot()
    b1c_sl = Slot()
    p.step("sync", lambda e: e.dma_start(out=g1row[:, :], in_=g1r),
           writes=[grow_sl], dma=True)
    p.step("sync", lambda e: e.dma_start(out=b1row[:, :], in_=b1r),
           writes=[brow_sl], dma=True)
    p.step("sync", lambda e: e.dma_start(out=b1cs[:, :], in_=b1c),
           writes=[b1c_sl], dma=True)
    ps_b = c.psum("ps_b", [128, 1024], F32)
    psb_sl = Slot()
    for dg in range(2):
        p.step("tensor", lambda e, dg=dg: e.matmul(
            ps_b[:, dg * 512:(dg + 1) * 512], ones1[:, :],
            g1row[:, dg * 512:(dg + 1) * 512], start=True, stop=True),
            reads=[o1_sl, grow_sl], writes=[psb_sl])
    p.step("scalar", lambda e: e.copy(g1B[:, :], ps_b[:, :]),
           reads=[psb_sl], writes=[g1B_sl])
    for dg in range(2):
        p.step("tensor", lambda e, dg=dg: e.matmul(
            ps_b[:, dg * 512:(dg + 1) * 512], ones1[:, :],
            b1row[:, dg * 512:(dg + 1) * 512], start=True, stop=True),
            reads=[o1_sl, brow_sl], writes=[psb_sl])
    p.step("scalar", lambda e: e.copy(b1B[:, :], ps_b[:, :]),
           reads=[psb_sl], writes=[b1B_sl])

    stg = c.sbuf("stg", [128, 4096], F32)
    stg_sl = [Slot()]
    mc8 = c.sbuf("mc8", [128, 8], F32)
    mc8_sl = [Slot() for _ in range(8)]
    mtot = c.sbuf("mtot", [128, 1], F32)
    mval = c.sbuf("mval", [1, 1], F32)
    m1P = c.sbuf("m1P", [128, 1], F32)
    thr = c.sbuf("thr", [128, 1], F32)
    nthr = c.sbuf("nthr", [128, 1], F32)
    mtot_sl = Slot()
    mval_sl = Slot()
    m1P_sl = Slot()
    thr_sl = Slot()
    nthr_sl = Slot()
    ps_m = c.psum("ps_m", [128, 512], F32)
    psm_sl = Slot()
    for dc in range(8):
        p.step("sync", lambda e, dc=dc: e.dma_start(
            out=stg[:, :], in_=w1t[dc * 128:(dc + 1) * 128, :]),
            writes=[stg_sl[0]], dma=True)
        p.step("vector", lambda e, dc=dc: e.tensor_reduce(
            mc8[:, dc:dc + 1], stg[:, :], AX.X, ALU.add,
            apply_absolute_value=True),
            reads=[stg_sl[0]], writes=[mc8_sl[dc]])
    p.step("vector", lambda e: e.reduce_sum(mtot[:, :], mc8[:, :], axis=AX.X),
           reads=mc8_sl, writes=[mtot_sl])
    p.step("tensor", lambda e: e.matmul(ps_m[:1, :1], mtot[:, :], onesc[:, :],
                                        start=True, stop=True),
           reads=[mtot_sl, oc_sl], writes=[psm_sl])
    p.step("scalar", lambda e: e.activation(mval[:, :], ps_m[:1, :1], AF.Identity,
                                            scale=1.0 / (D * H4)),
           reads=[psm_sl], writes=[mval_sl])
    p.step("vector", lambda e: e.tensor_single_scalar(
        mval[:, :], mval[:, :], 1e-5, ALU.max), writes=[mval_sl])
    p.step("tensor", lambda e: e.matmul(ps_m[:, 0:1], ones1[:, :], mval[:, :],
                                        start=True, stop=True),
           reads=[o1_sl, mval_sl], writes=[psm_sl])
    p.step("scalar", lambda e: e.copy(m1P[:, :], ps_m[:, 0:1]),
           reads=[psm_sl], writes=[m1P_sl])
    p.step("vector", lambda e: e.tensor_single_scalar(
        thr[:, :], m1P[:, :], 0.5, ALU.mult), reads=[m1P_sl], writes=[thr_sl])
    p.step("vector", lambda e: e.tensor_single_scalar(
        nthr[:, :], m1P[:, :], -0.5, ALU.mult), reads=[m1P_sl], writes=[nthr_sl])

    xgT = c.sbuf("xgT", [128, 8 * 4096], BF16)
    xgT_sl = [[Slot() for _ in range(32)] for _ in range(8)]
    xs = c.sbuf("xs", [128, 2 * 1024], F32)
    xc = c.sbuf("xc", [128, 2 * 1024], F32)
    xg = c.sbuf("xg", [128, 2 * 1024], F32)
    sq = xg
    xgb = c.sbuf("xgb", [128, 2 * 1024], BF16)
    cols = c.sbuf("cols", [128, 2 * 4], F32)
    ps_tr = c.psum("ps_tr", [128, 2048], BF16)
    sls = {nm: [Slot(), Slot()] for nm in ("xs", "xc", "xg", "xgb", "cols")}
    sls["sq"] = sls["xg"]
    tr_sl = [Slot() for _ in range(4)]
    for ti in range(32):
        r = ti % 2
        o = r * 1024
        tok0 = ti * 128
        p.step("sync", lambda e, o=o, tok0=tok0: e.dma_start(
            out=xs[:, o:o + 1024], in_=xn1[tok0:tok0 + 128, :]),
            writes=[sls["xs"][r]], dma=True)
        mu = cols[:, r * 4 + 0: r * 4 + 1]
        var = cols[:, r * 4 + 1: r * 4 + 2]
        std = cols[:, r * 4 + 2: r * 4 + 3]
        rstd = cols[:, r * 4 + 3: r * 4 + 4]
        cs = sls["cols"][r]
        p.step("vector", lambda e, mu=mu, o=o: e.reduce_sum(
            mu, xs[:, o:o + 1024], axis=AX.X), reads=[sls["xs"][r]], writes=[cs])
        p.step("scalar", lambda e, mu=mu: e.mul(mu, mu, 1.0 / D), writes=[cs])
        p.step("vector", lambda e, mu=mu, o=o: e.tensor_single_scalar(
            xc[:, o:o + 1024], xs[:, o:o + 1024], mu, ALU.subtract),
            reads=[sls["xs"][r], cs], writes=[sls["xc"][r]])
        p.step("scalar", lambda e, var=var, o=o: e.activation(
            sq[:, o:o + 1024], xc[:, o:o + 1024], AF.Square, accum_out=var),
            reads=[sls["xc"][r]], writes=[sls["sq"][r], cs])
        epsa, epssl = c.cval(EPS)
        p.step("scalar", lambda e, std=std, var=var, epsa=epsa: e.activation(
            std, var, AF.Sqrt, scale=1.0 / D, bias=epsa),
            reads=[epssl], writes=[cs])
        p.step("vector", lambda e, rstd=rstd, std=std: e.reciprocal(rstd, std),
               writes=[cs])
        p.step("vector", lambda e, rstd=rstd, o=o: e.tensor_single_scalar(
            xg[:, o:o + 1024], xc[:, o:o + 1024], rstd, ALU.mult),
            reads=[sls["xc"][r], cs], writes=[sls["xg"][r]])
        p.step("vector", lambda e, o=o: e.tensor_mul(
            xg[:, o:o + 1024], xg[:, o:o + 1024], g1B[:, :]),
            reads=[g1B_sl], writes=[sls["xg"][r]])
        p.step("vector", lambda e, o=o: e.tensor_add(
            xgb[:, o:o + 1024], xg[:, o:o + 1024], b1B[:, :]),
            reads=[sls["xg"][r], b1B_sl], writes=[sls["xgb"][r]])
        for dc in range(8):
            pt = dc % 2
            p.step("tensor", lambda e, o=o, dc=dc, pt=pt: e.transpose(
                ps_tr[:, pt * 1024: pt * 1024 + 128],
                xgb[:, o + dc * 128: o + (dc + 1) * 128], idn_sb[:, :]),
                reads=[sls["xgb"][r], idn_sl], writes=[tr_sl[pt]])
            p.step("scalar", lambda e, dc=dc, ti=ti, pt=pt: e.copy(
                xgT[:, dc * 4096 + ti * 128: dc * 4096 + (ti + 1) * 128],
                ps_tr[:, pt * 1024: pt * 1024 + 128]),
                reads=[tr_sl[pt]], writes=[xgT_sl[dc][ti]])

    stw = c.sbuf("stw", [128, 2 * 1024], F32)
    qa = c.sbuf("qa", [128, 2 * 1024], F32)
    qb = c.sbuf("qb", [128, 2 * 1024], F32)
    w1q = c.sbuf("w1q", [128, 2 * 1024], BF16)
    hsp = c.sbuf("hsp", [128, 2 * 512], F32)
    stw_sl = [Slot(), Slot()]
    qa_sl = [Slot(), Slot()]
    qb_sl = [Slot(), Slot()]
    w1q_sl = [Slot(), Slot()]
    hsp_sl = [Slot(), Slot()]
    ps_h = c.psum("ps_h", [128, 2 * 512], F32)
    psh_sl = [Slot(), Slot()]
    for hc in range(32):
        r = hc % 2
        o = r * 1024
        p.step("sync", lambda e, hc=hc, o=o: e.dma_start(
            out=stw[:, o:o + 1024],
            in_=w1t[:, hc * 128:(hc + 1) * 128].rearrange("(dc p) h -> p dc h", p=128)),
            writes=[stw_sl[r]], dma=True)
        p.step("vector", lambda e, o=o: e.tensor_single_scalar(
            qa[:, o:o + 1024], stw[:, o:o + 1024], thr[:, :], ALU.is_gt),
            reads=[stw_sl[r], thr_sl], writes=[qa_sl[r]])
        p.step("vector", lambda e, o=o: e.tensor_single_scalar(
            qb[:, o:o + 1024], stw[:, o:o + 1024], nthr[:, :], ALU.is_lt),
            reads=[stw_sl[r], nthr_sl], writes=[qb_sl[r]])
        p.step("vector", lambda e, o=o: e.tensor_sub(
            w1q[:, o:o + 1024], qa[:, o:o + 1024], qb[:, o:o + 1024]),
            reads=[qa_sl[r], qb_sl[r]], writes=[w1q_sl[r]])
        for tt in range(8):
            rr = tt % 2
            for dc in range(8):
                p.step("tensor", lambda e, o=o, dc=dc, tt=tt, rr=rr: e.matmul(
                    ps_h[:, rr * 512:(rr + 1) * 512],
                    w1q[:, o + dc * 128: o + (dc + 1) * 128],
                    xgT[:, dc * 4096 + tt * 512: dc * 4096 + (tt + 1) * 512],
                    start=(dc == 0), stop=(dc == 7)),
                    reads=[w1q_sl[r]] + [xgT_sl[dc][tt * 4 + q] for q in range(4)],
                    writes=[psh_sl[rr]])
            p.step("scalar", lambda e, hc=hc, rr=rr: e.activation(
                hsp[:, rr * 512:(rr + 1) * 512], ps_h[:, rr * 512:(rr + 1) * 512],
                AF.Gelu, scale=m1P[:, :], bias=b1cs[:, hc:hc + 1]),
                reads=[psh_sl[rr], m1P_sl, b1c_sl], writes=[hsp_sl[rr]])
            p.step("gpsimd", lambda e, hc=hc, tt=tt, rr=rr: e.dma_start(
                out=hG[hc * 128:(hc + 1) * 128, tt * 512:(tt + 1) * 512],
                in_=hsp[:, rr * 512:(rr + 1) * 512]),
                reads=[hsp_sl[rr]], dma=True)
    return c.done()


def build_E5b():
    c = Ctx()
    p = c.plan
    hGi = c.inp("hG", [H4, TOK])
    w2t = c.inp("w2t", [H4, D])
    b2r = c.inp("b2r", [1, D])
    g2c = c.inp("g2c", [128, 32])
    b2c = c.inp("b2c", [128, 32])
    x1i = c.inp("x1", [TOK, D])
    s1i = c.inp("s1", [TOK, D])
    oo = c.out("out", [TOK, D])

    ones1 = c.sbuf("ones1", [1, 128], F32)
    onesc = c.sbuf("onesc", [128, 1], F32)
    onescb = c.sbuf("onescb", [128, 1], BF16)
    o1_sl = Slot()
    oc_sl = Slot()
    ocb_sl = Slot()
    p.step("vector", lambda e: e.memset(ones1[:, :], 1.0), writes=[o1_sl])
    p.step("vector", lambda e: e.memset(onesc[:, :], 1.0), writes=[oc_sl])
    p.step("vector", lambda e: e.memset(onescb[:, :], 1.0), writes=[ocb_sl])
    g2cs = c.sbuf("g2cs", [128, 32], F32)
    b2cs = c.sbuf("b2cs", [128, 32], F32)
    g2_sl = Slot()
    b2_sl = Slot()
    p.step("sync", lambda e: e.dma_start(out=g2cs[:, :], in_=g2c),
           writes=[g2_sl], dma=True)
    p.step("sync", lambda e: e.dma_start(out=b2cs[:, :], in_=b2c),
           writes=[b2_sl], dma=True)
    b2row = c.sbuf("b2row", [1, D], F32)
    b2B = c.sbuf("b2B", [128, D], F32)
    b2r_sl = Slot()
    b2B_sl = Slot()
    p.step("sync", lambda e: e.dma_start(out=b2row[:, :], in_=b2r),
           writes=[b2r_sl], dma=True)
    ps_b = c.psum("ps_b", [128, 1024], F32)
    psb_sl = Slot()
    for dg in range(2):
        p.step("tensor", lambda e, dg=dg: e.matmul(
            ps_b[:, dg * 512:(dg + 1) * 512], ones1[:, :],
            b2row[:, dg * 512:(dg + 1) * 512], start=True, stop=True),
            reads=[o1_sl, b2r_sl], writes=[psb_sl])
    p.step("scalar", lambda e: e.copy(b2B[:, :], ps_b[:, :]),
           reads=[psb_sl], writes=[b2B_sl])

    stg = c.sbuf("stg", [128, 2 * 1024], F32)
    stg_sl = [Slot(), Slot()]
    mc32 = c.sbuf("mc32", [128, 32], F32)
    mc_sl = [Slot() for _ in range(32)]
    mtot = c.sbuf("mtot", [128, 1], F32)
    mval = c.sbuf("mval", [1, 1], F32)
    m2P = c.sbuf("m2P", [128, 1], F32)
    thr = c.sbuf("thr", [128, 1], F32)
    nthr = c.sbuf("nthr", [128, 1], F32)
    mtot_sl = Slot()
    mval_sl = Slot()
    m2P_sl = Slot()
    thr_sl = Slot()
    nthr_sl = Slot()
    ps_m = c.psum("ps_m", [128, 512], F32)
    psm_sl = Slot()
    for hc in range(32):
        r = hc % 2
        p.step("sync", lambda e, hc=hc, r=r: e.dma_start(
            out=stg[:, r * 1024:(r + 1) * 1024],
            in_=w2t[hc * 128:(hc + 1) * 128, :]),
            writes=[stg_sl[r]], dma=True)
        p.step("vector", lambda e, hc=hc, r=r: e.tensor_reduce(
            mc32[:, hc:hc + 1], stg[:, r * 1024:(r + 1) * 1024], AX.X, ALU.add,
            apply_absolute_value=True),
            reads=[stg_sl[r]], writes=[mc_sl[hc]])
    p.step("vector", lambda e: e.reduce_sum(mtot[:, :], mc32[:, :], axis=AX.X),
           reads=mc_sl, writes=[mtot_sl])
    p.step("tensor", lambda e: e.matmul(ps_m[:1, :1], mtot[:, :], onesc[:, :],
                                        start=True, stop=True),
           reads=[mtot_sl, oc_sl], writes=[psm_sl])
    p.step("scalar", lambda e: e.activation(mval[:, :], ps_m[:1, :1], AF.Identity,
                                            scale=1.0 / (D * H4)),
           reads=[psm_sl], writes=[mval_sl])
    p.step("vector", lambda e: e.tensor_single_scalar(
        mval[:, :], mval[:, :], 1e-5, ALU.max), writes=[mval_sl])
    p.step("tensor", lambda e: e.matmul(ps_m[:, 0:1], ones1[:, :], mval[:, :],
                                        start=True, stop=True),
           reads=[o1_sl, mval_sl], writes=[psm_sl])
    p.step("scalar", lambda e: e.copy(m2P[:, :], ps_m[:, 0:1]),
           reads=[psm_sl], writes=[m2P_sl])
    p.step("vector", lambda e: e.tensor_single_scalar(
        thr[:, :], m2P[:, :], 0.5, ALU.mult), reads=[m2P_sl], writes=[thr_sl])
    p.step("vector", lambda e: e.tensor_single_scalar(
        nthr[:, :], m2P[:, :], -0.5, ALU.mult), reads=[m2P_sl], writes=[nthr_sl])
    w2q = c.sbuf("w2q", [128, 32 * 1024], BF16)
    w2q_sl = [Slot() for _ in range(32)]
    qa = c.sbuf("qa", [128, 2 * 1024], F32)
    qb = c.sbuf("qb", [128, 2 * 1024], F32)
    qa_sl = [Slot(), Slot()]
    qb_sl = [Slot(), Slot()]
    for hc in range(32):
        r = hc % 2
        o = r * 1024
        p.step("sync", lambda e, hc=hc, o=o: e.dma_start(
            out=stg[:, o:o + 1024], in_=w2t[hc * 128:(hc + 1) * 128, :]),
            writes=[stg_sl[r]], dma=True)
        p.step("vector", lambda e, o=o: e.tensor_single_scalar(
            qa[:, o:o + 1024], stg[:, o:o + 1024], thr[:, :], ALU.is_gt),
            reads=[stg_sl[r], thr_sl], writes=[qa_sl[r]])
        p.step("vector", lambda e, o=o: e.tensor_single_scalar(
            qb[:, o:o + 1024], stg[:, o:o + 1024], nthr[:, :], ALU.is_lt),
            reads=[stg_sl[r], nthr_sl], writes=[qb_sl[r]])
        p.step("vector", lambda e, o=o, hc=hc: e.tensor_sub(
            w2q[:, hc * 1024:(hc + 1) * 1024], qa[:, o:o + 1024], qb[:, o:o + 1024]),
            reads=[qa_sl[r], qb_sl[r]], writes=[w2q_sl[hc]])

    hgb = c.sbuf("hgb", [128, 2 * 512], F32)
    sqb = c.sbuf("sqb", [128, 2 * 512], BF16)
    xn2b = c.sbuf("xn2b", [128, 32 * 512], BF16)
    t1b = c.sbuf("t1b", [128, 2 * 512], F32)
    muB = c.sbuf("muB", [128, 512], F32)
    rsB = c.sbuf("rsB", [128, 512], F32)
    mrow = c.sbuf("mrow", [1, 512], F32)
    vrow = c.sbuf("vrow", [1, 512], F32)
    ot1 = c.sbuf("ot1", [128, 2 * 512], F32)
    gg = c.sbuf("gg", [128, 2 * 512], F32)
    x1t = c.sbuf("x1t", [128, 2 * 512], F32)
    s1t = c.sbuf("s1t", [128, 2 * 512], F32)
    fo = c.sbuf("fo", [128, 2 * 512], F32)
    hgb_sl = [Slot(), Slot()]
    sqb_sl = [Slot(), Slot()]
    xn2_sl = [Slot() for _ in range(32)]
    t1_sl = [Slot(), Slot()]
    muB_sl = Slot()
    rsB_sl = Slot()
    mrow_sl = Slot()
    vrow_sl = Slot()
    ot1_sl = [Slot(), Slot()]
    gg_sl = [Slot(), Slot()]
    x1_sl = [Slot(), Slot()]
    s1_sl = [Slot(), Slot()]
    fo_sl = [Slot(), Slot()]
    ps_s = c.psum("ps_s", [128, 512], F32)
    ps_v = c.psum("ps_v", [128, 512], F32)
    ps_bc = c.psum("ps_bc", [128, 512], F32)
    ps_o = c.psum("ps_o", [128, 2 * 512], F32)
    pss_sl = Slot()
    psv_sl = Slot()
    psbc_sl = Slot()
    pso_sl = [Slot(), Slot()]

    for tg in range(8):
        t0 = tg * 512
        for hc in range(32):
            r = hc % 2
            o = r * 512
            p.step("sync", lambda e, hc=hc, o=o, t0=t0: e.dma_start(
                out=hgb[:, o:o + 512], in_=hGi[hc * 128:(hc + 1) * 128, t0:t0 + 512]),
                writes=[hgb_sl[r]], dma=True)
            p.step("tensor", lambda e, hc=hc, o=o: e.matmul(
                ps_s[0:1, :], onesc[:, :], hgb[:, o:o + 512],
                start=(hc == 0), stop=(hc == 31)),
                reads=[oc_sl, hgb_sl[r]], writes=[pss_sl])
            p.step("scalar", lambda e, o=o: e.activation(
                sqb[:, o:o + 512], hgb[:, o:o + 512], AF.Square),
                reads=[hgb_sl[r]], writes=[sqb_sl[r]])
            p.step("tensor", lambda e, hc=hc, o=o: e.matmul(
                ps_v[0:1, :], onescb[:, :], sqb[:, o:o + 512],
                start=(hc == 0), stop=(hc == 31)),
                reads=[ocb_sl, sqb_sl[r]], writes=[psv_sl])
        p.step("scalar", lambda e: e.activation(mrow[:, :], ps_s[0:1, :], AF.Identity,
                                                scale=1.0 / H4),
               reads=[pss_sl], writes=[mrow_sl])
        p.step("scalar", lambda e: e.activation(vrow[:, :], ps_v[0:1, :], AF.Identity,
                                                scale=1.0 / H4),
               reads=[psv_sl], writes=[vrow_sl])
        p.step("vector", lambda e: e.tensor_tensor(
            t1b[:1, :512], mrow[:, :], mrow[:, :], ALU.mult),
            reads=[mrow_sl], writes=[t1_sl[0]])
        p.step("vector", lambda e: e.tensor_sub(
            vrow[:, :], vrow[:, :], t1b[:1, :512]),
            reads=[t1_sl[0]], writes=[vrow_sl])
        epsa, epssl = c.cval(EPS)
        p.step("scalar", lambda e, epsa=epsa: e.activation(
            vrow[:, :], vrow[:, :], AF.Sqrt, bias=epsa[0:1, :]),
            reads=[epssl], writes=[vrow_sl])
        p.step("vector", lambda e: e.reciprocal(vrow[:, :], vrow[:, :]),
               writes=[vrow_sl])
        p.step("tensor", lambda e: e.matmul(ps_bc[:, :], ones1[:, :], mrow[:, :],
                                            start=True, stop=True),
               reads=[o1_sl, mrow_sl], writes=[psbc_sl])
        p.step("scalar", lambda e: e.copy(muB[:, :], ps_bc[:, :]),
               reads=[psbc_sl], writes=[muB_sl])
        p.step("tensor", lambda e: e.matmul(ps_bc[:, :], ones1[:, :], vrow[:, :],
                                            start=True, stop=True),
               reads=[o1_sl, vrow_sl], writes=[psbc_sl])
        p.step("scalar", lambda e: e.copy(rsB[:, :], ps_bc[:, :]),
               reads=[psbc_sl], writes=[rsB_sl])
        for hc in range(32):
            r = hc % 2
            o = r * 512
            p.step("sync", lambda e, hc=hc, o=o, t0=t0: e.dma_start(
                out=hgb[:, o:o + 512], in_=hGi[hc * 128:(hc + 1) * 128, t0:t0 + 512]),
                writes=[hgb_sl[r]], dma=True)
            p.step("vector", lambda e, o=o: e.tensor_sub(
                t1b[:, o:o + 512], hgb[:, o:o + 512], muB[:, :]),
                reads=[hgb_sl[r], muB_sl], writes=[t1_sl[r]])
            p.step("vector", lambda e, o=o: e.tensor_mul(
                t1b[:, o:o + 512], t1b[:, o:o + 512], rsB[:, :]),
                reads=[rsB_sl], writes=[t1_sl[r]])
            p.step("scalar", lambda e, hc=hc, o=o: e.activation(
                xn2b[:, hc * 512:(hc + 1) * 512], t1b[:, o:o + 512],
                AF.Identity, scale=g2cs[:, hc:hc + 1], bias=b2cs[:, hc:hc + 1]),
                reads=[t1_sl[r], g2_sl, b2_sl], writes=[xn2_sl[hc]])
        for st in range(4):
            r = st % 2
            tok0 = t0 + st * 128
            for dg in range(2):
                for hc in range(32):
                    p.step("tensor", lambda e, hc=hc, st=st, dg=dg: e.matmul(
                        ps_o[:, dg * 512:(dg + 1) * 512],
                        xn2b[:, hc * 512 + st * 128: hc * 512 + (st + 1) * 128],
                        w2q[:, hc * 1024 + dg * 512: hc * 1024 + (dg + 1) * 512],
                        start=(hc == 0), stop=(hc == 31)),
                        reads=[xn2_sl[hc], w2q_sl[hc]], writes=[pso_sl[dg]])
                p.step("scalar", lambda e, dg=dg, r=r: e.activation(
                    ot1[:, r * 512:(r + 1) * 512], ps_o[:, dg * 512:(dg + 1) * 512],
                    AF.Copy, scale=m2P[:, :]),
                    reads=[pso_sl[dg], m2P_sl], writes=[ot1_sl[r]])
                p.step("vector", lambda e, dg=dg, r=r: e.tensor_add(
                    ot1[:, r * 512:(r + 1) * 512], ot1[:, r * 512:(r + 1) * 512],
                    b2B[:, dg * 512:(dg + 1) * 512]),
                    reads=[b2B_sl], writes=[ot1_sl[r]])
                p.step("scalar", lambda e, r=r: e.activation(
                    gg[:, r * 512:(r + 1) * 512], ot1[:, r * 512:(r + 1) * 512],
                    AF.Silu), reads=[ot1_sl[r]], writes=[gg_sl[r]])
                p.step("sync", lambda e, r=r, tok0=tok0, dg=dg: e.dma_start(
                    out=x1t[:, r * 512:(r + 1) * 512],
                    in_=x1i[tok0:tok0 + 128, dg * 512:(dg + 1) * 512]),
                    writes=[x1_sl[r]], dma=True)
                p.step("sync", lambda e, r=r, tok0=tok0, dg=dg: e.dma_start(
                    out=s1t[:, r * 512:(r + 1) * 512],
                    in_=s1i[tok0:tok0 + 128, dg * 512:(dg + 1) * 512]),
                    writes=[s1_sl[r]], dma=True)
                p.step("vector", lambda e, r=r: e.tensor_mul(
                    gg[:, r * 512:(r + 1) * 512], gg[:, r * 512:(r + 1) * 512],
                    s1t[:, r * 512:(r + 1) * 512]),
                    reads=[s1_sl[r]], writes=[gg_sl[r]])
                p.step("vector", lambda e, r=r: e.tensor_add(
                    fo[:, r * 512:(r + 1) * 512], x1t[:, r * 512:(r + 1) * 512],
                    gg[:, r * 512:(r + 1) * 512]),
                    reads=[x1_sl[r], gg_sl[r]], writes=[fo_sl[r]])
                p.step("gpsimd", lambda e, r=r, tok0=tok0, dg=dg: e.dma_start(
                    out=oo[tok0:tok0 + 128, dg * 512:(dg + 1) * 512],
                    in_=fo[:, r * 512:(r + 1) * 512]),
                    reads=[fo_sl[r]], dma=True)
    return c.done()


def _run(nc, in_maps):
    return run_bass_kernel_spmd(nc, in_maps, core_ids=list(range(NCORES)))


def kernel(**inp):
    f32 = np.float32
    x_full = np.ascontiguousarray(inp["L0_attn_shot_embeddings"], dtype=f32)
    Wd = np.ascontiguousarray(inp["L0_attn_Wd"], dtype=f32)
    bd = np.ascontiguousarray(inp["L0_attn_bd"], dtype=f32)
    Wi = np.ascontiguousarray(inp["L0_attn_Wi"], dtype=f32)
    bi = np.ascontiguousarray(inp["L0_attn_bi"], dtype=f32)
    A_log = np.ascontiguousarray(inp["L0_attn_A_log"], dtype=f32)
    Wo = np.ascontiguousarray(inp["L0_attn_Wo"], dtype=f32)
    bo = np.ascontiguousarray(inp["L0_attn_bo"], dtype=f32)
    Ws = np.ascontiguousarray(inp["L0_attn_Ws"], dtype=f32)
    bs = np.ascontiguousarray(inp["L0_attn_bs"], dtype=f32)
    gate_w = np.ascontiguousarray(inp["L0_ffn_gate_w"], dtype=f32)
    w1 = inp["L0_ffn_w1"]
    w2 = inp["L0_ffn_w2"]
    w3 = inp["L0_ffn_w3"]
    Wq = np.ascontiguousarray(inp["L1_attn_Wq"], dtype=f32)
    Wk = np.ascontiguousarray(inp["L1_attn_Wk"], dtype=f32)
    ln1_g = np.ascontiguousarray(inp["L1_ffn_ln1_g"], dtype=f32)
    ln1_b = np.ascontiguousarray(inp["L1_ffn_ln1_b"], dtype=f32)
    w1L = np.ascontiguousarray(inp["L1_ffn_w1"], dtype=f32)
    b1L = np.ascontiguousarray(inp["L1_ffn_b1"], dtype=f32)
    ln2_g = np.ascontiguousarray(inp["L1_ffn_ln2_g"], dtype=f32)
    ln2_b = np.ascontiguousarray(inp["L1_ffn_ln2_b"], dtype=f32)
    w2L = np.ascontiguousarray(inp["L1_ffn_w2"], dtype=f32)
    b2L = np.ascontiguousarray(inp["L1_ffn_b2"], dtype=f32)

    idn = np.eye(128, dtype=ml_dtypes.bfloat16)
    C = np.ascontiguousarray

    def col2(v):
        return C(v.reshape(2, 128).T)

    xk = [C(x_full[:, k * SL:(k + 1) * SL, :].reshape(TOK, D)) for k in range(NCORES)]

    ncA = build_A()
    maps = [dict(x=xk[k], wd=Wd[0], wi=Wi[0], bd=col2(bd[0]), bi=col2(bi[0]),
                 al=col2(A_log[0]), idn=idn) for k in range(NCORES)]
    resA = _run(ncA, maps).results
    xnT_k = [r["xnT"] for r in resA]
    dec_full = np.concatenate([r["dec"] for r in resA], axis=2).reshape(B * N, S)
    drv_full = np.concatenate([r["drv"] for r in resA], axis=2).reshape(B * N, S)

    ncS = build_scan()
    maps = [dict(dec=C(dec_full[k * 128:(k + 1) * 128]),
                 drv=C(drv_full[k * 128:(k + 1) * 128])) for k in range(NCORES)]
    resS = _run(ncS, maps).results
    hs0 = np.concatenate([r["hs"] for r in resS], axis=0)
    f0 = C(hs0[:, -1].reshape(B, N).T)
    hs0_k = [C(hs0.reshape(B, N, S)[:, :, k * SL:(k + 1) * SL]) for k in range(NCORES)]

    ncC = build_C()
    maps = [dict(hs0=hs0_k[k], f0t=f0, wo=Wo[0], ws=Ws[0],
                 bob=C(bo[0].reshape(8, 128).T), bsb=C(bs[0].reshape(8, 128).T),
                 wd=Wd[1], wi=Wi[1], bd=col2(bd[1]), bi=col2(bi[1]),
                 al=col2(A_log[1])) for k in range(NCORES)]
    resC = _run(ncC, maps).results
    dec_full = np.concatenate([r["dec"] for r in resC], axis=2).reshape(B * N, S)
    drv_full = np.concatenate([r["drv"] for r in resC], axis=2).reshape(B * N, S)

    maps = [dict(dec=C(dec_full[k * 128:(k + 1) * 128]),
                 drv=C(drv_full[k * 128:(k + 1) * 128])) for k in range(NCORES)]
    resS = _run(ncS, maps).results
    hs1 = np.concatenate([r["hs"] for r in resS], axis=0)
    f1 = C(hs1[:, -1].reshape(B, N).T)
    hs1_k = [C(hs1.reshape(B, N, S)[:, :, k * SL:(k + 1) * SL]) for k in range(NCORES)]

    ncE1 = build_E1()
    maps = [dict(hs1=hs1_k[k], f1t=f1, wo=Wo[1], ws=Ws[1],
                 bor=C(bo[1].reshape(1, D)), bsr=C(bs[1].reshape(1, D)))
            for k in range(NCORES)]
    resE1 = _run(ncE1, maps).results
    s0_k = [r["s0"] for r in resE1]

    ncE2 = build_E2()
    gwt = C(gate_w.T)
    w1t = C(np.transpose(w1, (0, 2, 1)))
    w3t = C(np.transpose(w3, (0, 2, 1)))
    w2t = C(np.transpose(w2, (0, 2, 1)))
    maps = [dict(xnT=xnT_k[k], gwt=gwt, w1t=w1t, w3t=w3t, w2t=w2t)
            for k in range(NCORES)]
    resE2 = _run(ncE2, maps).results
    g_k = [r["g"] for r in resE2]

    ncE3 = build_E3()
    maps = [dict(x=xk[k], s0=s0_k[k], g=g_k[k], idn=idn) for k in range(NCORES)]
    resE3 = _run(ncE3, maps).results
    x1_k = [r["x1"] for r in resE3]
    xn1_k = [r["xn1"] for r in resE3]
    xn1T_k = [r["xn1T"] for r in resE3]

    ncE4 = build_E4()
    wqt = C(Wq.T)
    wkt = C(Wk.T)
    maps = [dict(xn1=xn1_k[k], xn1T=xn1T_k[k], wqt=wqt, wkt=wkt)
            for k in range(NCORES)]
    resE4 = _run(ncE4, maps).results
    s1_k = [r["s1"] for r in resE4]

    ncE5a = build_E5a()
    w1Lt = C(w1L.T)
    maps = [dict(xn1=xn1_k[k], w1t=w1Lt, b1c=C(b1L.reshape(32, 128).T),
                 g1r=C(ln1_g.reshape(1, D)), b1r=C(ln1_b.reshape(1, D)), idn=idn)
            for k in range(NCORES)]
    resE5a = _run(ncE5a, maps).results
    hG_k = [r["hG"] for r in resE5a]

    ncE5b = build_E5b()
    w2Lt = C(w2L.T)
    maps = [dict(hG=hG_k[k], w2t=w2Lt, b2r=C(b2L.reshape(1, D)),
                 g2c=C(ln2_g.reshape(32, 128).T), b2c=C(ln2_b.reshape(32, 128).T),
                 x1=x1_k[k], s1=s1_k[k]) for k in range(NCORES)]
    resE5b = _run(ncE5b, maps).results
    out_k = [r["out"] for r in resE5b]

    out = np.concatenate([o.reshape(B, SL, D) for o in out_k], axis=1)
    return np.ascontiguousarray(out, dtype=np.float32)


# version 10 (final)
# speedup vs baseline: 2.1027x; 1.1272x over previous
import contextlib

import numpy as np
import ml_dtypes

import concourse.bass as bass
import concourse.mybir as mybir
from concourse.bass_utils import run_bass_kernel_spmd

F32 = mybir.dt.float32
BF16 = mybir.dt.bfloat16
AF = mybir.ActivationFunctionType
ALU = mybir.AluOpType
AX = mybir.AxisListType

NCORES = 8
B = 4
S = 8192
SL = S // NCORES
D = 1024
N = 256
TOK = B * SL
E = 8
I = 2048
H4 = 4096
EPS = 1e-5
BIG = 1e30

ENGINES = ("sync", "tensor", "scalar", "vector", "gpsimd")


class Step:
    __slots__ = ("eng", "chan", "emit", "deps", "inc", "after", "needed", "waits")

    def __init__(self, eng, emit, deps, inc):
        self.eng = eng
        self.chan = eng + ("_d" if inc == 16 else "_c")
        self.emit = emit
        self.deps = deps
        self.inc = inc
        self.after = 0
        self.needed = False
        self.waits = []


class Slot:

    __slots__ = ("writer", "readers")

    def __init__(self):
        self.writer = None
        self.readers = []


class Plan:

    def __init__(self):
        self.per_engine = {e: [] for e in ENGINES}

    def add(self, eng, emit, deps=(), dma=False):
        deps = [d for d in deps if d is not None]
        s = Step(eng, emit, deps, 16 if dma else 1)
        self.per_engine[eng].append(s)
        return s

    def step(self, eng, emit, reads=(), writes=(), extra=(), dma=False):
        deps = list(extra)
        for sl in reads:
            if sl.writer is not None:
                deps.append(sl.writer)
        for sl in writes:
            if sl.writer is not None:
                deps.append(sl.writer)
            deps.extend(sl.readers)
        st = self.add(eng, emit, deps, dma)
        for sl in reads:
            sl.readers.append(st)
        for sl in writes:
            sl.writer = st
            sl.readers = []
        return st

    def finalize(self):
        for e in ENGINES:
            for s in self.per_engine[e]:
                if s.inc == 16:
                    s.needed = True
                for d in s.deps:
                    if not (d.eng == "tensor" and s.eng == "tensor"):
                        d.needed = True
        chans = {}
        for e in ENGINES:
            for s in self.per_engine[e]:
                if s.needed:
                    chans[s.chan] = chans.get(s.chan, 0) + s.inc
                s.after = chans.get(s.chan, 0)
        for e in ENGINES:
            prev_dma_after = 0
            for s in self.per_engine[e]:
                w = {}
                for d in s.deps:
                    if d.eng == "tensor" and s.eng == "tensor":
                        continue
                    assert d.needed and d.after > 0
                    w[d.chan] = max(w.get(d.chan, 0), d.after)
                if s.inc == 16:
                    if prev_dma_after:
                        w[s.chan] = max(w.get(s.chan, 0), prev_dma_after)
                    prev_dma_after = s.after
                s.waits = sorted(w.items())
        self.chans = chans

    def build(self, nc):
        self.finalize()
        with contextlib.ExitStack() as st:
            sems = {ch: st.enter_context(nc.semaphore(f"ch_{ch}"))
                    for ch in self.chans}
            block = st.enter_context(nc.Block())

            def section(eng_name):
                steps = self.per_engine[eng_name]

                def body(eng):
                    waited = {}
                    for s in steps:
                        for (src, val) in s.waits:
                            if val > waited.get(src, 0):
                                eng.wait_ge(sems[src], val)
                                waited[src] = val
                        inst = s.emit(eng)
                        if s.needed:
                            inst.then_inc(sems[s.chan], s.inc)
                    if eng_name == "gpsimd":
                        dch = "gpsimd_d"
                        tot = self.chans.get(dch, 0)
                        if tot:
                            eng.wait_ge(sems[dch], tot)
                return body

            for e in ENGINES:
                getattr(block, e)(section(e))


def _mk_nc():
    return bass.Bass("TRN2", target_bir_lowering=False)


class Ctx:

    def __init__(self):
        self.nc = _mk_nc()
        self.plan = Plan()
        self.stack = contextlib.ExitStack()
        self._cbuf = None
        self._cmap = {}

    def cval(self, value):
        if value not in self._cmap:
            if self._cbuf is None:
                self._cbuf = self.sbuf("cconst", [128, 8], F32)
            k = len(self._cmap)
            ap = self._cbuf[:, k:k + 1]
            slot = Slot()
            self.plan.step("vector", lambda e, ap=ap, value=value: e.memset(
                ap, value), writes=[slot])
            self._cmap[value] = (ap, slot)
        return self._cmap[value]

    def sbuf(self, name, shape, dtype):
        return self.stack.enter_context(self.nc.sbuf_tensor(name, shape, dtype))

    def psum(self, name, shape, dtype=F32):
        return self.stack.enter_context(self.nc.psum_tensor(name, shape, dtype))

    def inp(self, name, shape, dtype=F32):
        return self.nc.dram_tensor(name, shape, dtype, kind="ExternalInput").ap()

    def out(self, name, shape, dtype=F32):
        return self.nc.dram_tensor(name, shape, dtype, kind="ExternalOutput").ap()

    def done(self):
        self.plan.build(self.nc)
        self.stack.close()
        return self.nc

    def softplus(self, out_ap, z_ap, m_ap, a_ap, wslots, rslots):
        p = self.plan
        oslot, mslot, aslot = wslots
        s1 = p.step("vector", lambda e: e.tensor_single_scalar(
            m_ap, z_ap, 0.0, ALU.max), reads=rslots, writes=[mslot])
        s2 = p.step("scalar", lambda e: e.activation(
            a_ap, z_ap, AF.Abs), reads=rslots, writes=[aslot])
        s3 = p.step("scalar", lambda e: e.activation(
            a_ap, a_ap, AF.Exp, scale=-1.0), reads=[], writes=[aslot])
        onea, onesl = self.cval(1.0)
        s4 = p.step("scalar", lambda e: e.activation(
            a_ap, a_ap, AF.Ln, bias=onea), reads=[onesl], writes=[aslot])
        s5 = p.step("vector", lambda e: e.tensor_add(
            out_ap, m_ap, a_ap), reads=[mslot, aslot], writes=[oslot])
        return s5


def _proj_setup(c, wd_dram, wi_dram, bd_dram, bi_dram, al_dram, tag):
    p = c.plan
    wdf = c.sbuf(f"wdf{tag}", [128, 8 * N], F32)
    wdb = c.sbuf(f"wdb{tag}", [128, 8 * N], BF16)
    wif = c.sbuf(f"wif{tag}", [128, 8 * N], F32)
    wib = c.sbuf(f"wib{tag}", [128, 8 * N], BF16)
    bd = c.sbuf(f"bd{tag}", [128, 2], F32)
    bi = c.sbuf(f"bi{tag}", [128, 2], F32)
    al = c.sbuf(f"al{tag}", [128, 2], F32)
    spa = c.sbuf(f"spa{tag}", [128, 2], F32)
    spm = c.sbuf(f"spm{tag}", [128, 2], F32)
    sl = {k: Slot() for k in ("wdf", "wdb", "wif", "wib", "bd", "bi", "al",
                              "spa", "spm", "spaa")}
    ld1 = p.step("sync", lambda e: e.dma_start(
        out=wdf[:, :], in_=wd_dram.rearrange("(dc p) n -> p dc n", p=128)),
        writes=[sl["wdf"]], dma=True)
    p.step("gpsimd", lambda e: e.tensor_copy(out=wdb[:, :], in_=wdf[:, :]),
           reads=[sl["wdf"]], writes=[sl["wdb"]])
    ld2 = p.step("sync", lambda e: e.dma_start(
        out=wif[:, :], in_=wi_dram.rearrange("(dc p) n -> p dc n", p=128)),
        writes=[sl["wif"]], dma=True)
    p.step("gpsimd", lambda e: e.tensor_copy(out=wib[:, :], in_=wif[:, :]),
           reads=[sl["wif"]], writes=[sl["wib"]])
    p.step("sync", lambda e: e.dma_start(out=bd[:, :], in_=bd_dram),
           writes=[sl["bd"]], dma=True)
    p.step("sync", lambda e: e.dma_start(out=bi[:, :], in_=bi_dram),
           writes=[sl["bi"]], dma=True)
    p.step("sync", lambda e: e.dma_start(out=al[:, :], in_=al_dram),
           writes=[sl["al"]], dma=True)
    c.softplus(spa[:, :], al[:, :], spm[:, :], spa[:, :],
               (sl["spaa"], sl["spm"], sl["spa"]), [sl["al"]])
    return dict(wdb=wdb, wib=wib, bd=bd, bi=bi, spa=spa, sl=sl)


def _proj_emit(c, pr, rhs_sb, rhs_slots, rhs_col0, dec_dram, drv_dram,
               b, s0, scratch, ps_d, ps_u, psl):
    p = c.plan
    wdb, wib, bd, bi, spa = pr["wdb"], pr["wib"], pr["bd"], pr["bi"], pr["spa"]
    prsl = pr["sl"]
    for ncase, (wsb, wslot, psb, pslot) in enumerate(
            [(wdb, prsl["wdb"], ps_d, psl[0]), (wib, prsl["wib"], ps_u, psl[1])]):
        for nch in range(2):
            for dc in range(8):
                lhsT = wsb[:, dc * N + nch * 128: dc * N + (nch + 1) * 128]
                rhs = rhs_sb[:, rhs_col0 + dc * 512: rhs_col0 + (dc + 1) * 512]
                p.step("tensor", lambda e, lhsT=lhsT, rhs=rhs, psb=psb, nch=nch,
                       dc=dc: e.matmul(psb[:, nch * 512:(nch + 1) * 512],
                                       lhsT, rhs, start=(dc == 0), stop=(dc == 7)),
                       reads=[wslot] + rhs_slots[dc], writes=[pslot[nch]])
    for nch in range(2):
        zd, zd_s = scratch["zd"]
        zu, zu_s = scratch["zu"]
        sm, sm_s = scratch["sm"]
        sa, sa_s = scratch["sa"]
        dr, dr_s = scratch["dr"]
        o = nch * 512
        p.step("scalar", lambda e, nch=nch, o=o: e.activation(
            zd[:, o:o + 512], ps_d[:, o:o + 512], AF.Identity, bias=bd[:, nch:nch + 1]),
            reads=[psl[0][nch], prsl["bd"]], writes=[zd_s[nch]])
        p.step("scalar", lambda e, nch=nch, o=o: e.activation(
            zu[:, o:o + 512], ps_u[:, o:o + 512], AF.Identity, bias=bi[:, nch:nch + 1]),
            reads=[psl[1][nch], prsl["bi"]], writes=[zu_s[nch]])
        p.step("vector", lambda e, o=o: e.tensor_single_scalar(
            sm[:, o:o + 512], zd[:, o:o + 512], 0.0, ALU.max),
            reads=[zd_s[nch]], writes=[sm_s[nch]])
        p.step("scalar", lambda e, o=o: e.activation(
            sa[:, o:o + 512], zd[:, o:o + 512], AF.Abs),
            reads=[zd_s[nch]], writes=[sa_s[nch]])
        p.step("scalar", lambda e, o=o: e.activation(
            sa[:, o:o + 512], sa[:, o:o + 512], AF.Exp, scale=-1.0),
            writes=[sa_s[nch]])
        onea, onesl = c.cval(1.0)
        p.step("scalar", lambda e, o=o, onea=onea: e.activation(
            sa[:, o:o + 512], sa[:, o:o + 512], AF.Ln, bias=onea),
            reads=[onesl], writes=[sa_s[nch]])
        p.step("vector", lambda e, o=o: e.tensor_add(
            zd[:, o:o + 512], sm[:, o:o + 512], sa[:, o:o + 512]),
            reads=[sm_s[nch], sa_s[nch]], writes=[zd_s[nch]])
        p.step("vector", lambda e, nch=nch, o=o: e.tensor_single_scalar(
            sm[:, o:o + 512], zd[:, o:o + 512], spa[:, nch:nch + 1], ALU.mult),
            reads=[zd_s[nch], prsl["spaa"]], writes=[sm_s[nch]])
        p.step("scalar", lambda e, o=o: e.activation(
            sa[:, o:o + 512], sm[:, o:o + 512], AF.Exp, scale=-1.0),
            reads=[sm_s[nch]], writes=[sa_s[nch]])
        p.step("vector", lambda e, o=o: e.tensor_mul(
            dr[:, o:o + 512], zd[:, o:o + 512], zu[:, o:o + 512]),
            reads=[zd_s[nch], zu_s[nch]], writes=[dr_s[nch]])
        p.step("gpsimd", lambda e, nch=nch, o=o, b=b, s0=s0: e.dma_start(
            out=dec_dram[b, nch * 128:(nch + 1) * 128, s0:s0 + 512],
            in_=sa[:, o:o + 512]), reads=[sa_s[nch]], dma=True)
        p.step("gpsimd", lambda e, nch=nch, o=o, b=b, s0=s0: e.dma_start(
            out=drv_dram[b, nch * 128:(nch + 1) * 128, s0:s0 + 512],
            in_=dr[:, o:o + 512]), reads=[dr_s[nch]], dma=True)


def build_A(ngroups=8, parts=("tr", "store", "proj")):
    c = Ctx()
    p = c.plan
    x = c.inp("x", [TOK, D])
    wd = c.inp("wd", [D, N])
    wi = c.inp("wi", [D, N])
    bdd = c.inp("bd", [128, 2])
    bid = c.inp("bi", [128, 2])
    ald = c.inp("al", [128, 2])
    idn = c.inp("idn", [128, 128], BF16)
    xnT = c.out("xnT", [D, TOK], BF16)
    dec = c.out("dec", [B, N, SL])
    drv = c.out("drv", [B, N, SL])

    idn_sb = c.sbuf("idn_sb", [128, 128], BF16)
    idn_sl = Slot()
    p.step("sync", lambda e: e.dma_start(out=idn_sb[:, :], in_=idn),
           writes=[idn_sl], dma=True)
    pr = _proj_setup(c, wd, wi, bdd, bid, ald, "0") if "proj" in parts else None

    xs = c.sbuf("xs", [128, 8 * 1024], F32)
    xc = c.sbuf("xc", [128, 2 * 1024], F32)
    sq = c.sbuf("sq", [128, 2 * 1024], F32)
    cols = c.sbuf("cols", [128, 2 * 16], F32)
    xnb = c.sbuf("xnb", [128, 2 * 4 * 1024], BF16)
    xnt = c.sbuf("xnt", [128, 2 * 8 * 512], BF16)
    ps_tr = c.psum("ps_tr", [128, 2048], BF16)
    ps_d = c.psum("ps_d", [128, 1024], F32)
    ps_u = c.psum("ps_u", [128, 1024], F32)

    xs_sl = [[Slot() for _ in range(4)] for _ in range(2)]
    xc_sl = [Slot() for _ in range(2)]
    sq_sl = [Slot() for _ in range(2)]
    col_sl = [[Slot() for _ in range(4)] for _ in range(2)]
    xnb_sl = [[Slot() for _ in range(4)] for _ in range(2)]
    xnt_sl = [[[Slot() for _ in range(4)] for _ in range(8)] for _ in range(2)]
    tr_sl = [Slot() for _ in range(4)]
    psl = [[Slot(), Slot()], [Slot(), Slot()]]

    scr = {}
    for nm in ("zd", "zu", "sm", "sa", "dr"):
        t = c.sbuf(f"{nm}0s", [128, 2 * 1024], F32)
        scr[nm] = (t, [[Slot(), Slot()] for _ in range(2)])

    for g in range(ngroups):
        r = g % 2
        b = g // 2
        s0 = (g % 2) * 512
        for st in range(4):
            tok0 = g * 512 + st * 128
            xo = (r * 4 + st) * 1024
            p.step("sync", lambda e, xo=xo, tok0=tok0: e.dma_start(
                out=xs[:, xo:xo + 1024], in_=x[tok0:tok0 + 128, :]),
                writes=[xs_sl[r][st]], dma=True)
            mu = cols[:, r * 16 + st * 4 + 0: r * 16 + st * 4 + 1]
            var = cols[:, r * 16 + st * 4 + 1: r * 16 + st * 4 + 2]
            std = cols[:, r * 16 + st * 4 + 2: r * 16 + st * 4 + 3]
            rstd = cols[:, r * 16 + st * 4 + 3: r * 16 + st * 4 + 4]
            cs = col_sl[r][st]
            p.step("vector", lambda e, mu=mu, xo=xo: e.reduce_sum(
                mu, xs[:, xo:xo + 1024], axis=AX.X),
                reads=[xs_sl[r][st]], writes=[cs])
            p.step("scalar", lambda e, mu=mu: e.mul(mu, mu, 1.0 / D), writes=[cs])
            p.step("vector", lambda e, mu=mu, xo=xo, r=r: e.tensor_single_scalar(
                xc[:, r * 1024:(r + 1) * 1024], xs[:, xo:xo + 1024], mu, ALU.subtract),
                reads=[xs_sl[r][st], cs], writes=[xc_sl[r]])
            p.step("scalar", lambda e, var=var, r=r: e.activation(
                sq[:, r * 1024:(r + 1) * 1024], xc[:, r * 1024:(r + 1) * 1024],
                AF.Square, accum_out=var),
                reads=[xc_sl[r]], writes=[sq_sl[r], cs])
            epsa, epssl = c.cval(EPS)
            p.step("scalar", lambda e, std=std, var=var, epsa=epsa: e.activation(
                std, var, AF.Sqrt, scale=1.0 / D, bias=epsa),
                reads=[epssl], writes=[cs])
            p.step("vector", lambda e, rstd=rstd, std=std: e.reciprocal(
                rstd, std), writes=[cs])
            xno = (r * 4 + st) * 1024
            p.step("vector", lambda e, rstd=rstd, xno=xno, r=r: e.tensor_single_scalar(
                xnb[:, xno:xno + 1024], xc[:, r * 1024:(r + 1) * 1024], rstd, ALU.mult),
                reads=[xc_sl[r], cs], writes=[xnb_sl[r][st]])
            for dc in range(8 if "tr" in parts else 0):
                pt = dc % 2
                trs = tr_sl[pt]
                p.step("tensor", lambda e, xno=xno, dc=dc, pt=pt: e.transpose(
                    ps_tr[:, pt * 1024: pt * 1024 + 128],
                    xnb[:, xno + dc * 128: xno + (dc + 1) * 128], idn_sb[:, :]),
                    reads=[xnb_sl[r][st], idn_sl], writes=[trs])
                dst = (r * 8 + dc) * 512 + st * 128
                p.step("scalar", lambda e, dst=dst, pt=pt: e.copy(
                    xnt[:, dst:dst + 128], ps_tr[:, pt * 1024: pt * 1024 + 128]),
                    reads=[trs], writes=[xnt_sl[r][dc][st]])
        for dc in range(8 if "store" in parts else 0):
            src = (r * 8 + dc) * 512
            p.step("gpsimd", lambda e, src=src, dc=dc, g=g: e.dma_start(
                out=xnT[dc * 128:(dc + 1) * 128, g * 512:(g + 1) * 512],
                in_=xnt[:, src:src + 512]),
                reads=xnt_sl[r][dc], dma=True)
        if "proj" in parts:
            scratch = {nm: (scr[nm][0][:, r * 1024:(r + 1) * 1024], scr[nm][1][r])
                       for nm in scr}
            rhs_slots = [xnt_sl[r][dc] for dc in range(8)]
            _proj_emit(c, pr, xnt, rhs_slots, r * 8 * 512, dec, drv, b, s0,
                       scratch, ps_d, ps_u, psl)
    return c.done()


def build_scan(chunk=2048):
    c = Ctx()
    p = c.plan
    dec = c.inp("dec", [128, S])
    drv = c.inp("drv", [128, S])
    hs = c.out("hs", [128, S])
    nch = S // chunk
    sd = c.sbuf("sd", [128, 2 * chunk], F32)
    sv = c.sbuf("sv", [128, 2 * chunk], F32)
    sh = c.sbuf("sh", [128, 2 * chunk], F32)
    sd_sl = [Slot(), Slot()]
    sv_sl = [Slot(), Slot()]
    sh_sl = [Slot(), Slot()]
    for ci in range(nch):
        r = ci % 2
        o = r * chunk
        p.step("sync", lambda e, o=o, ci=ci: e.dma_start(
            out=sd[:, o:o + chunk], in_=dec[:, ci * chunk:(ci + 1) * chunk]),
            writes=[sd_sl[r]], dma=True)
        p.step("sync", lambda e, o=o, ci=ci: e.dma_start(
            out=sv[:, o:o + chunk], in_=drv[:, ci * chunk:(ci + 1) * chunk]),
            writes=[sv_sl[r]], dma=True)
        init = 0.0 if ci == 0 else sh[:, (1 - r) * chunk + chunk - 1:(1 - r) * chunk + chunk]
        reads = [sd_sl[r], sv_sl[r]] + ([sh_sl[1 - r]] if ci else [])
        p.step("vector", lambda e, o=o, init=init: e.tensor_tensor_scan(
            out=sh[:, o:o + chunk], data0=sd[:, o:o + chunk],
            data1=sv[:, o:o + chunk], initial=init,
            op0=ALU.mult, op1=ALU.add),
            reads=reads, writes=[sh_sl[r]])
        p.step("gpsimd", lambda e, o=o, ci=ci: e.dma_start(
            out=hs[:, ci * chunk:(ci + 1) * chunk], in_=sh[:, o:o + chunk]),
            reads=[sh_sl[r]], dma=True)
    return c.done()


def build_C():
    c = Ctx()
    p = c.plan
    hs0 = c.inp("hs0", [B, N, SL])
    f0t = c.inp("f0t", [N, B])
    wo = c.inp("wo", [N, D])
    ws = c.inp("ws", [N, D])
    bob = c.inp("bob", [128, 8])
    bsb = c.inp("bsb", [128, 8])
    wd = c.inp("wd", [D, N])
    wi = c.inp("wi", [D, N])
    bdd = c.inp("bd", [128, 2])
    bid = c.inp("bi", [128, 2])
    ald = c.inp("al", [128, 2])
    dec = c.out("dec", [B, N, SL])
    drv = c.out("drv", [B, N, SL])

    pr = _proj_setup(c, wd, wi, bdd, bid, ald, "1")

    wof = c.sbuf("wof", [128, 2 * D], F32)
    wob = c.sbuf("wob", [128, 2 * D], BF16)
    wsf = c.sbuf("wsf", [128, 2 * D], F32)
    wsb = c.sbuf("wsb", [128, 2 * D], BF16)
    f0f = c.sbuf("f0f", [128, 2 * B], F32)
    f0b = c.sbuf("f0b", [128, 2 * B], BF16)
    bo_sb = c.sbuf("bo_sb", [128, 8], F32)
    bs_sb = c.sbuf("bs_sb", [128, 8], F32)
    bsum = c.sbuf("bsum", [128, 8], F32)
    cls = c.sbuf("cls", [128, 32], F32)
    sl = {k: Slot() for k in ("wof", "wob", "wsf", "wsb", "f0f", "f0b",
                              "bo", "bs", "bsum", "cls")}
    for nm, dst, src in (("wof", wof, wo), ("wsf", wsf, ws)):
        p.step("sync", lambda e, dst=dst, src=src: e.dma_start(
            out=dst[:, :], in_=src.rearrange("(nc p) d -> p nc d", p=128)),
            writes=[sl[nm]], dma=True)
    p.step("gpsimd", lambda e: e.tensor_copy(out=wob[:, :], in_=wof[:, :]),
           reads=[sl["wof"]], writes=[sl["wob"]])
    p.step("gpsimd", lambda e: e.tensor_copy(out=wsb[:, :], in_=wsf[:, :]),
           reads=[sl["wsf"]], writes=[sl["wsb"]])
    p.step("sync", lambda e: e.dma_start(
        out=f0f[:, :], in_=f0t.rearrange("(nc p) b -> p nc b", p=128)),
        writes=[sl["f0f"]], dma=True)
    p.step("gpsimd", lambda e: e.tensor_copy(out=f0b[:, :], in_=f0f[:, :]),
           reads=[sl["f0f"]], writes=[sl["f0b"]])
    p.step("sync", lambda e: e.dma_start(out=bo_sb[:, :], in_=bob),
           writes=[sl["bo"]], dma=True)
    p.step("sync", lambda e: e.dma_start(out=bs_sb[:, :], in_=bsb),
           writes=[sl["bs"]], dma=True)
    p.step("vector", lambda e: e.tensor_add(bsum[:, :], bo_sb[:, :], bs_sb[:, :]),
           reads=[sl["bo"], sl["bs"]], writes=[sl["bsum"]])

    ps_c = c.psum("ps_c", [128, 512], F32)
    psc_sl = Slot()
    for dc in range(8):
        for nch in range(2):
            p.step("tensor", lambda e, dc=dc, nch=nch: e.matmul(
                ps_c[:, 0:4],
                wsb[:, nch * D + dc * 128: nch * D + (dc + 1) * 128],
                f0b[:, nch * B:(nch + 1) * B],
                start=(nch == 0), stop=(nch == 1)),
                reads=[sl["wsb"], sl["f0b"]], writes=[psc_sl])
        p.step("scalar", lambda e, dc=dc: e.activation(
            cls[:, dc * 4:(dc + 1) * 4], ps_c[:, 0:4],
            AF.Identity, bias=bsum[:, dc:dc + 1]),
            reads=[psc_sl, sl["bsum"]], writes=[sl["cls"]])

    hsf = c.sbuf("hsf", [128, 2 * 2 * 512], F32)
    hsb = c.sbuf("hsb", [128, 2 * 2 * 512], BF16)
    h1f = c.sbuf("h1f", [128, 2 * 512], F32)
    h1b = c.sbuf("h1b", [128, 2 * 8 * 512], BF16)
    ps_y = c.psum("ps_y", [128, 1024], F32)
    ps_d = c.psum("ps_d", [128, 1024], F32)
    ps_u = c.psum("ps_u", [128, 1024], F32)
    hsf_sl = [[Slot(), Slot()] for _ in range(2)]
    hsb_sl = [[Slot(), Slot()] for _ in range(2)]
    h1f_sl = [Slot(), Slot()]
    h1b_sl = [[Slot() for _ in range(8)] for _ in range(2)]
    psy_sl = [Slot(), Slot()]
    psl = [[Slot(), Slot()], [Slot(), Slot()]]
    scr = {}
    for nm in ("zd", "zu", "sm", "sa", "dr"):
        t = c.sbuf(f"{nm}1s", [128, 2 * 1024], F32)
        scr[nm] = (t, [[Slot(), Slot()] for _ in range(2)])

    for g in range(8):
        r = g % 2
        b = g // 2
        s0 = (g % 2) * 512
        for nch in range(2):
            o = (r * 2 + nch) * 512
            p.step("sync", lambda e, o=o, b=b, nch=nch, s0=s0: e.dma_start(
                out=hsf[:, o:o + 512],
                in_=hs0[b, nch * 128:(nch + 1) * 128, s0:s0 + 512]),
                writes=[hsf_sl[r][nch]], dma=True)
            p.step("gpsimd", lambda e, o=o: e.tensor_copy(
                out=hsb[:, o:o + 512], in_=hsf[:, o:o + 512]),
                reads=[hsf_sl[r][nch]], writes=[hsb_sl[r][nch]])
        for dc in range(8):
            pr_i = dc % 2
            for nch in range(2):
                p.step("tensor", lambda e, dc=dc, nch=nch, r=r, pr_i=pr_i: e.matmul(
                    ps_y[:, pr_i * 512:(pr_i + 1) * 512],
                    wob[:, nch * D + dc * 128: nch * D + (dc + 1) * 128],
                    hsb[:, (r * 2 + nch) * 512:(r * 2 + nch + 1) * 512],
                    start=(nch == 0), stop=(nch == 1)),
                    reads=[sl["wob"], hsb_sl[r][nch]], writes=[psy_sl[pr_i]])
            p.step("scalar", lambda e, dc=dc, b=b, r=r, pr_i=pr_i: e.activation(
                h1f[:, r * 512:(r + 1) * 512], ps_y[:, pr_i * 512:(pr_i + 1) * 512],
                AF.Identity, bias=cls[:, dc * 4 + b: dc * 4 + b + 1]),
                reads=[psy_sl[pr_i], sl["cls"]], writes=[h1f_sl[r]])
            p.step("vector", lambda e, dc=dc, r=r: e.tensor_copy(
                out=h1b[:, (r * 8 + dc) * 512:(r * 8 + dc + 1) * 512],
                in_=h1f[:, r * 512:(r + 1) * 512]),
                reads=[h1f_sl[r]], writes=[h1b_sl[r][dc]])
        scratch = {nm: (scr[nm][0][:, r * 1024:(r + 1) * 1024], scr[nm][1][r])
                   for nm in scr}
        rhs_slots = [[h1b_sl[r][dc]] for dc in range(8)]
        _proj_emit(c, pr, h1b, rhs_slots, r * 8 * 512, dec, drv, b, s0,
                   scratch, ps_d, ps_u, psl)
    return c.done()


def build_E1():
    c = Ctx()
    p = c.plan
    hs1 = c.inp("hs1", [B, N, SL])
    f1t = c.inp("f1t", [N, B])
    wo = c.inp("wo", [N, D])
    ws = c.inp("ws", [N, D])
    bor = c.inp("bor", [1, D])
    bsr = c.inp("bsr", [1, D])
    s0o = c.out("s0", [TOK, D])

    wof = c.sbuf("wof", [128, 2 * D], F32)
    wob = c.sbuf("wob", [128, 2 * D], BF16)
    wsf = c.sbuf("wsf", [128, 2 * D], F32)
    wsb = c.sbuf("wsb", [128, 2 * D], BF16)
    f1f = c.sbuf("f1f", [128, 2 * B], F32)
    f1b = c.sbuf("f1b", [128, 2 * B], BF16)
    brow = c.sbuf("brow", [1, D], F32)
    bsrow = c.sbuf("bsrow", [1, D], F32)
    ones4 = c.sbuf("ones4", [1, B], F32)
    ones1 = c.sbuf("ones1", [1, 128], F32)
    rall = c.sbuf("rall", [B, D], F32)
    rflat = c.sbuf("rflat", [1, B * D], F32)
    rhi = c.sbuf("rhi", [1, B * D], BF16)
    rhif = c.sbuf("rhif", [1, B * D], F32)
    rlo = c.sbuf("rlo", [1, B * D], BF16)
    onesb = c.sbuf("onesb", [1, 128], BF16)
    sl = {k: Slot() for k in ("wof", "wob", "wsf", "wsb", "f1f", "f1b", "bo",
                              "bs", "ones4", "ones1", "rall", "rflat", "rhi",
                              "rhif", "rlo", "onesb")}
    for nm, dst, src in (("wof", wof, wo), ("wsf", wsf, ws)):
        p.step("sync", lambda e, dst=dst, src=src: e.dma_start(
            out=dst[:, :], in_=src.rearrange("(nc p) d -> p nc d", p=128)),
            writes=[sl[nm]], dma=True)
    p.step("gpsimd", lambda e: e.tensor_copy(out=wob[:, :], in_=wof[:, :]),
           reads=[sl["wof"]], writes=[sl["wob"]])
    p.step("gpsimd", lambda e: e.tensor_copy(out=wsb[:, :], in_=wsf[:, :]),
           reads=[sl["wsf"]], writes=[sl["wsb"]])
    p.step("sync", lambda e: e.dma_start(
        out=f1f[:, :], in_=f1t.rearrange("(nc p) b -> p nc b", p=128)),
        writes=[sl["f1f"]], dma=True)
    p.step("gpsimd", lambda e: e.tensor_copy(out=f1b[:, :], in_=f1f[:, :]),
           reads=[sl["f1f"]], writes=[sl["f1b"]])
    p.step("sync", lambda e: e.dma_start(out=brow[:, :], in_=bor),
           writes=[sl["bo"]], dma=True)
    p.step("sync", lambda e: e.dma_start(out=bsrow[:, :], in_=bsr),
           writes=[sl["bs"]], dma=True)
    p.step("vector", lambda e: e.memset(ones4[:, :], 1.0), writes=[sl["ones4"]])
    p.step("vector", lambda e: e.memset(ones1[:, :], 1.0), writes=[sl["ones1"]])
    p.step("vector", lambda e: e.memset(onesb[:, :], 1.0), writes=[sl["onesb"]])

    ps_c = c.psum("ps_c", [128, 1024], F32)
    psc_sl = Slot()
    for dg in range(2):
        for nch in range(2):
            p.step("tensor", lambda e, dg=dg, nch=nch: e.matmul(
                ps_c[0:4, dg * 512:(dg + 1) * 512],
                f1b[:, nch * B:(nch + 1) * B],
                wsb[:, nch * D + dg * 512: nch * D + (dg + 1) * 512],
                start=(nch == 0), stop=False),
                reads=[sl["f1b"], sl["wsb"]], writes=[psc_sl])
        for bi_, (src, slot) in enumerate(((brow, sl["bo"]), (bsrow, sl["bs"]))):
            p.step("tensor", lambda e, dg=dg, src=src, bi_=bi_: e.matmul(
                ps_c[0:4, dg * 512:(dg + 1) * 512], ones4[:, :],
                src[:, dg * 512:(dg + 1) * 512], start=False, stop=(bi_ == 1)),
                reads=[sl["ones4"], slot], writes=[psc_sl])
    p.step("scalar", lambda e: e.copy(rall[:, :], ps_c[0:4, :]),
           reads=[psc_sl], writes=[sl["rall"]])
    for b in range(B):
        p.step("sync", lambda e, b=b: e.dma_start(
            out=rflat[:, b * D:(b + 1) * D], in_=rall[b:b + 1, :]),
            reads=[sl["rall"]], writes=[sl["rflat"]], dma=True)
    p.step("scalar", lambda e: e.copy(rhi[:, :], rflat[:, :]),
           reads=[sl["rflat"]], writes=[sl["rhi"]])
    p.step("scalar", lambda e: e.copy(rhif[:, :], rhi[:, :]),
           reads=[sl["rhi"]], writes=[sl["rhif"]])
    p.step("vector", lambda e: e.tensor_sub(rhif[:, :], rflat[:, :], rhif[:, :]),
           reads=[sl["rflat"]], writes=[sl["rhif"]])
    p.step("scalar", lambda e: e.copy(rlo[:, :], rhif[:, :]),
           reads=[sl["rhif"]], writes=[sl["rlo"]])

    hsf = c.sbuf("hsf", [128, 2 * 2 * 512], F32)
    hsb = c.sbuf("hsb", [128, 2 * 2 * 512], BF16)
    s0b = c.sbuf("s0b", [128, 2 * 4 * 1024], F32)
    ps_y = c.psum("ps_y", [128, 2 * 1024], F32)
    hsf_sl = [[Slot(), Slot()] for _ in range(2)]
    hsb_sl = [[Slot(), Slot()] for _ in range(2)]
    s0_sl = [[Slot() for _ in range(4)] for _ in range(2)]
    psy_sl = [[Slot(), Slot()], [Slot(), Slot()]]
    for g in range(8):
        r = g % 2
        b = g // 2
        s0 = (g % 2) * 512
        for nch in range(2):
            o = (r * 2 + nch) * 512
            p.step("sync", lambda e, o=o, b=b, nch=nch, s0=s0: e.dma_start(
                out=hsf[:, o:o + 512],
                in_=hs1[b, nch * 128:(nch + 1) * 128, s0:s0 + 512]),
                writes=[hsf_sl[r][nch]], dma=True)
            p.step("gpsimd", lambda e, o=o: e.tensor_copy(
                out=hsb[:, o:o + 512], in_=hsf[:, o:o + 512]),
                reads=[hsf_sl[r][nch]], writes=[hsb_sl[r][nch]])
        for st in range(4):
            rr = st % 2
            for dg in range(2):
                po = (rr * 2 + dg) * 512
                for nch in range(2):
                    p.step("tensor", lambda e, po=po, nch=nch, st=st, dg=dg, r=r: e.matmul(
                        ps_y[:, po:po + 512],
                        hsb[:, (r * 2 + nch) * 512 + st * 128:(r * 2 + nch) * 512 + (st + 1) * 128],
                        wob[:, nch * D + dg * 512: nch * D + (dg + 1) * 512],
                        start=(nch == 0), stop=False),
                        reads=[hsb_sl[r][nch], sl["wob"]], writes=[psy_sl[rr][dg]])
                for bi_, (src, slot) in enumerate(((rhi, sl["rhi"]), (rlo, sl["rlo"]))):
                    p.step("tensor", lambda e, po=po, src=src, b=b, dg=dg, bi_=bi_: e.matmul(
                        ps_y[:, po:po + 512], onesb[:, :],
                        src[:, b * D + dg * 512: b * D + (dg + 1) * 512],
                        start=False, stop=(bi_ == 1)),
                        reads=[sl["onesb"], slot], writes=[psy_sl[rr][dg]])
                p.step("scalar", lambda e, po=po, r=r, st=st, dg=dg: e.copy(
                    s0b[:, (r * 4 + st) * 1024 + dg * 512:(r * 4 + st) * 1024 + (dg + 1) * 512],
                    ps_y[:, po:po + 512]),
                    reads=[psy_sl[rr][dg]], writes=[s0_sl[r][st]])
            tok0 = g * 512 + st * 128
            p.step("gpsimd", lambda e, r=r, st=st, tok0=tok0: e.dma_start(
                out=s0o[tok0:tok0 + 128, :],
                in_=s0b[:, (r * 4 + st) * 1024:(r * 4 + st + 1) * 1024]),
                reads=[s0_sl[r][st]], dma=True)
    return c.done()


def build_E2():
    c = Ctx()
    p = c.plan
    xnT = c.inp("xnT", [D, TOK], BF16)
    gwt = c.inp("gwt", [D, E])
    w1t = c.inp("w1t", [E, D, I])
    w3t = c.inp("w3t", [E, D, I])
    w2t = c.inp("w2t", [E, I, D])
    go = c.out("g", [TOK, D])

    w1s = c.nc.dram_tensor("w1s", [E, 128, 16384], BF16).ap()
    w3s = c.nc.dram_tensor("w3s", [E, 128, 16384], BF16).ap()
    w2s = c.nc.dram_tensor("w2s", [E, 128, 16384], BF16).ap()

    gwf = c.sbuf("gwf", [128, 8 * E], F32)
    gwb = c.sbuf("gwb", [128, 8 * E], BF16)
    sl_gw = Slot()
    sl_gwb = Slot()
    p.step("sync", lambda e: e.dma_start(
        out=gwf[:, :], in_=gwt.rearrange("(dc p) ee -> p dc ee", p=128)),
        writes=[sl_gw], dma=True)
    p.step("gpsimd", lambda e: e.tensor_copy(out=gwb[:, :], in_=gwf[:, :]),
           reads=[sl_gw], writes=[sl_gwb])

    TG = 1024
    xnb = c.sbuf("xnb", [128, 8 * TG], BF16)
    outg = c.sbuf("outg", [128, 8 * 1024], F32)
    stg = c.sbuf("stg", [128, 4096], F32)
    stg2 = c.sbuf("stg2", [128, 4096], F32)
    w1h = c.sbuf("w1h", [128, 2 * 8192], BF16)
    w3h = c.sbuf("w3h", [128, 2 * 8192], BF16)
    w2b = c.sbuf("w2b", [128, 16 * 1024], BF16)
    actb = c.sbuf("actb", [128, 2 * 256], BF16)
    silu_t = c.sbuf("silu_t", [128, 2 * 256], F32)
    lg = c.sbuf("lg", [128, 2 * 8], F32)
    gsc = c.sbuf("gsc", [128, 6 * 8], F32)
    gs2 = c.sbuf("gs2", [128, 3 * 8], F32)
    gcol = c.sbuf("gcol", [128, 8], F32)
    dwb = c.sbuf("dwb", [128, 8 * 8], F32)
    gsb = c.sbuf("gsb", [128, 2 * 1024], F32)

    ps_l = c.psum("ps_l", [128, 512], F32)
    ps_h1 = c.psum("ps_h1", [128, 512], F32)
    ps_h3 = c.psum("ps_h3", [128, 512], F32)
    ps_y = c.psum("ps_y", [128, 4 * 512], F32)

    xnb_sl = [Slot() for _ in range(8)]
    outg_sl = [Slot() for _ in range(8)]
    stg_sl = Slot()
    stg2_sl = Slot()
    w1s_sl = [Slot() for _ in range(E)]
    w3s_sl = [Slot() for _ in range(E)]
    w2s_sl = [Slot() for _ in range(E)]
    w1h_sl = [[Slot(), Slot()] for _ in range(2)]
    w3h_sl = [[Slot(), Slot()] for _ in range(2)]
    w2b_sl = [Slot(), Slot()]
    actb_sl = [Slot(), Slot()]
    silu_sl = [Slot(), Slot()]
    lg_sl = [Slot(), Slot()]
    gsc_sl = [Slot() for _ in range(6)]
    gs2_sl = [Slot() for _ in range(3)]
    gcol_sl = [Slot() for _ in range(8)]
    dwb_sl = [Slot() for _ in range(8)]
    gsb_sl = [Slot(), Slot()]
    psl_l = Slot()
    psl_h1 = [Slot()] * 2
    psl_h3 = [Slot()] * 2
    psl_y = [Slot() for _ in range(4)]

    for tg in range(4):
        t0 = tg * TG
        for dc in range(8):
            p.step("sync", lambda e, dc=dc, t0=t0: e.dma_start(
                out=xnb[:, dc * TG:(dc + 1) * TG],
                in_=xnT[dc * 128:(dc + 1) * 128, t0:t0 + TG]),
                writes=[xnb_sl[dc]], dma=True)
        for tt in range(8):
            r = tt % 2
            lgr = lg[:, r * 8:(r + 1) * 8]
            lgs = lg_sl[r]
            for dc in range(8):
                p.step("tensor", lambda e, dc=dc, tt=tt: e.matmul(
                    ps_l[:, 0:E], xnb[:, dc * TG + tt * 128: dc * TG + (tt + 1) * 128],
                    gwb[:, dc * E:(dc + 1) * E], start=(dc == 0), stop=(dc == 7)),
                    reads=[xnb_sl[dc], sl_gwb], writes=[psl_l])
            p.step("scalar", lambda e, lgr=lgr: e.copy(lgr, ps_l[:, 0:E]),
                   reads=[psl_l], writes=[lgs])
            oh1, ab, f1, tml, m1, i1 = (gsc[:, k * 8:(k + 1) * 8] for k in range(6))
            s2v, oh2, e2v = (gs2[:, k * 8:(k + 1) * 8] for k in range(3))
            c1, c1n, c2, c3, c4, c4n, c5, c6 = (gcol[:, k:k + 1] for k in range(8))
            st_ = p.step
            st_("vector", lambda e, lgr=lgr, c1=c1: e.reduce_max(c1, lgr, axis=AX.X),
                reads=[lgs], writes=[gcol_sl[0]])
            st_("vector", lambda e, lgr=lgr, oh1=oh1, c1=c1: e.tensor_single_scalar(
                oh1, lgr, c1, ALU.is_equal), reads=[lgs, gcol_sl[0]], writes=[gsc_sl[0]])
            st_("scalar", lambda e, lgr=lgr, ab=ab: e.activation(ab, lgr, AF.Abs),
                reads=[lgs], writes=[gsc_sl[1]])
            st_("vector", lambda e, ab=ab, f1=f1, c1=c1: e.tensor_single_scalar(
                f1, ab, c1, ALU.max), reads=[gsc_sl[1], gcol_sl[0]], writes=[gsc_sl[2]])
            st_("vector", lambda e, lgr=lgr, tml=tml, c1=c1: e.tensor_scalar(
                tml, lgr, c1, -1.0, ALU.subtract, ALU.mult),
                reads=[lgs, gcol_sl[0]], writes=[gsc_sl[3]])
            st_("vector", lambda e, f1=f1, tml=tml, m1=m1: e.scalar_tensor_tensor(
                m1, f1, 0.02, tml, ALU.mult, ALU.is_lt),
                reads=[gsc_sl[2], gsc_sl[3]], writes=[gsc_sl[4]])
            st_("vector", lambda e, m1=m1, i1=i1: e.tensor_scalar(
                i1, m1, -1.0, 1.0, ALU.mult, ALU.add),
                reads=[gsc_sl[4]], writes=[gsc_sl[5]])
            st_("vector", lambda e, c1=c1, c1n=c1n: e.tensor_single_scalar(
                c1n, c1, -1.0, ALU.mult), reads=[gcol_sl[0]], writes=[gcol_sl[1]])
            st_("scalar", lambda e, lgr=lgr, e2v=e2v, c1n=c1n: e.activation(
                e2v, lgr, AF.Exp, bias=c1n), reads=[lgs, gcol_sl[1]], writes=[gs2_sl[2]])
            st_("vector", lambda e, e2v=e2v, i1=i1: e.tensor_mul(e2v, e2v, i1),
                reads=[gsc_sl[5]], writes=[gs2_sl[2]])
            st_("vector", lambda e, e2v=e2v, c2=c2: e.reduce_sum(c2, e2v, axis=AX.X),
                reads=[gs2_sl[2]], writes=[gcol_sl[2]])
            st_("vector", lambda e, c2=c2, c3=c3: e.reciprocal(c3, c2),
                reads=[gcol_sl[2]], writes=[gcol_sl[3]])
            st_("vector", lambda e, oh1=oh1, lgr=lgr, s2v=s2v: e.scalar_tensor_tensor(
                s2v, oh1, -BIG, lgr, ALU.mult, ALU.add),
                reads=[gsc_sl[0], lgs], writes=[gs2_sl[0]])
            st_("vector", lambda e, s2v=s2v, c4=c4: e.reduce_max(c4, s2v, axis=AX.X),
                reads=[gs2_sl[0]], writes=[gcol_sl[4]])
            st_("vector", lambda e, s2v=s2v, oh2=oh2, c4=c4: e.tensor_single_scalar(
                oh2, s2v, c4, ALU.is_equal), reads=[gs2_sl[0], gcol_sl[4]], writes=[gs2_sl[1]])
            st_("vector", lambda e, ab=ab, f1=f1, c4=c4: e.tensor_single_scalar(
                f1, ab, c4, ALU.max), reads=[gsc_sl[1], gcol_sl[4]], writes=[gsc_sl[2]])
            st_("vector", lambda e, lgr=lgr, tml=tml, c4=c4: e.tensor_scalar(
                tml, lgr, c4, -1.0, ALU.subtract, ALU.mult),
                reads=[lgs, gcol_sl[4]], writes=[gsc_sl[3]])
            st_("vector", lambda e, f1=f1, tml=tml, m1=m1: e.scalar_tensor_tensor(
                m1, f1, 0.02, tml, ALU.mult, ALU.is_lt),
                reads=[gsc_sl[2], gsc_sl[3]], writes=[gsc_sl[4]])
            st_("vector", lambda e, m1=m1, i1=i1: e.tensor_scalar(
                i1, m1, -1.0, 1.0, ALU.mult, ALU.add),
                reads=[gsc_sl[4]], writes=[gsc_sl[5]])
            st_("vector", lambda e, c4=c4, c4n=c4n: e.tensor_single_scalar(
                c4n, c4, -1.0, ALU.mult), reads=[gcol_sl[4]], writes=[gcol_sl[5]])
            st_("scalar", lambda e, s2v=s2v, e2v=e2v, c4n=c4n: e.activation(
                e2v, s2v, AF.Exp, bias=c4n), reads=[gs2_sl[0], gcol_sl[5]], writes=[gs2_sl[2]])
            st_("vector", lambda e, e2v=e2v, i1=i1: e.tensor_mul(e2v, e2v, i1),
                reads=[gsc_sl[5]], writes=[gs2_sl[2]])
            st_("vector", lambda e, e2v=e2v, c5=c5: e.reduce_sum(c5, e2v, axis=AX.X),
                reads=[gs2_sl[2]], writes=[gcol_sl[6]])
            st_("vector", lambda e, c5=c5, c6=c6: e.reciprocal(c6, c5),
                reads=[gcol_sl[6]], writes=[gcol_sl[7]])
            st_("vector", lambda e, oh1=oh1, c3=c3: e.tensor_single_scalar(
                oh1, oh1, c3, ALU.mult), reads=[gcol_sl[3]], writes=[gsc_sl[0]])
            st_("vector", lambda e, oh1=oh1, oh2=oh2, c6=c6, tt=tt: e.scalar_tensor_tensor(
                dwb[:, tt * 8:(tt + 1) * 8], oh2, c6, oh1, ALU.mult, ALU.add),
                reads=[gs2_sl[1], gcol_sl[7], gsc_sl[0]], writes=[dwb_sl[tt]])
        for tt in range(8):
            p.step("vector", lambda e, tt=tt: e.memset(
                outg[:, tt * 1024:(tt + 1) * 1024], 0.0), writes=[outg_sl[tt]])
        for ex in range(E):
            if tg == 0:
                li = 0
                for half in range(2):
                    for q in range(2):
                        sb, ssl = (stg, stg_sl) if li % 2 == 0 else (stg2, stg2_sl)
                        li += 1
                        p.step("sync", lambda e, ex=ex, half=half, q=q, sb=sb: e.dma_start(
                            out=sb[:, :],
                            in_=w1t[ex, :, half * 1024 + q * 512:
                                    half * 1024 + (q + 1) * 512].rearrange(
                                "(dc p) i -> p dc i", p=128)),
                            writes=[ssl], dma=True)
                        p.step("vector", lambda e, half=half, q=q, sb=sb: e.tensor_copy(
                            out=w1h[:, half * 8192 + q * 4096:
                                    half * 8192 + (q + 1) * 4096].rearrange(
                                "p (dc i) -> p dc i", dc=8),
                            in_=sb[:, :].rearrange("p (dc i) -> p dc i", dc=8)),
                            reads=[ssl], writes=[w1h_sl[half][q]])
                        sb, ssl = (stg, stg_sl) if li % 2 == 0 else (stg2, stg2_sl)
                        li += 1
                        p.step("sync", lambda e, ex=ex, half=half, q=q, sb=sb: e.dma_start(
                            out=sb[:, :],
                            in_=w3t[ex, :, half * 1024 + q * 512:
                                    half * 1024 + (q + 1) * 512].rearrange(
                                "(dc p) i -> p dc i", p=128)),
                            writes=[ssl], dma=True)
                        p.step("vector", lambda e, half=half, q=q, sb=sb: e.tensor_copy(
                            out=w3h[:, half * 8192 + q * 4096:
                                    half * 8192 + (q + 1) * 4096].rearrange(
                                "p (dc i) -> p dc i", dc=8),
                            in_=sb[:, :].rearrange("p (dc i) -> p dc i", dc=8)),
                            reads=[ssl], writes=[w3h_sl[half][q]])
                        sb, ssl = (stg, stg_sl) if li % 2 == 0 else (stg2, stg2_sl)
                        li += 1
                        p.step("sync", lambda e, ex=ex, half=half, q=q, sb=sb: e.dma_start(
                            out=sb[:, :],
                            in_=w2t[ex, half * 1024 + q * 512:
                                    half * 1024 + (q + 1) * 512, :].rearrange(
                                "(ic p) dd -> p ic dd", p=128)),
                            writes=[ssl], dma=True)
                        p.step("scalar", lambda e, half=half, q=q, sb=sb: e.copy(
                            out=w2b[:, half * 8192 + q * 4096:
                                    half * 8192 + (q + 1) * 4096],
                            in_=sb[:, :]),
                            reads=[ssl], writes=[w2b_sl[half]])
                p.step("gpsimd", lambda e, ex=ex: e.dma_start(
                    out=w1s[ex, :, :], in_=w1h[:, :]),
                    reads=[w1h_sl[0][0], w1h_sl[0][1], w1h_sl[1][0], w1h_sl[1][1]],
                    writes=[w1s_sl[ex]], dma=True)
                p.step("gpsimd", lambda e, ex=ex: e.dma_start(
                    out=w3s[ex, :, :], in_=w3h[:, :]),
                    reads=[w3h_sl[0][0], w3h_sl[0][1], w3h_sl[1][0], w3h_sl[1][1]],
                    writes=[w3s_sl[ex]], dma=True)
                p.step("gpsimd", lambda e, ex=ex: e.dma_start(
                    out=w2s[ex, :, :], in_=w2b[:, :]),
                    reads=[w2b_sl[0], w2b_sl[1]],
                    writes=[w2s_sl[ex]], dma=True)
            else:
                for half in range(2):
                    p.step("sync", lambda e, ex=ex, half=half: e.dma_start(
                        out=w1h[:, half * 8192:(half + 1) * 8192],
                        in_=w1s[ex, :, half * 8192:(half + 1) * 8192]),
                        reads=[w1s_sl[ex]],
                        writes=[w1h_sl[half][0], w1h_sl[half][1]],
                        dma=True)
                    p.step("gpsimd", lambda e, ex=ex, half=half: e.dma_start(
                        out=w2b[:, half * 8192:(half + 1) * 8192],
                        in_=w2s[ex, :, half * 8192:(half + 1) * 8192]),
                        reads=[w2s_sl[ex]],
                        writes=[w2b_sl[half]], dma=True)
                    p.step("sync", lambda e, ex=ex, half=half: e.dma_start(
                        out=w3h[:, half * 8192:(half + 1) * 8192],
                        in_=w3s[ex, :, half * 8192:(half + 1) * 8192]),
                        reads=[w3s_sl[ex]],
                        writes=[w3h_sl[half][0], w3h_sl[half][1]],
                        dma=True)
            for half in range(2):
                for ts in range(4):
                    to = ts * 256
                    pend_y = None
                    for icl in range(8):
                        ic = half * 8 + icl
                        q, icq = divmod(icl, 4)
                        w1c = half * 8192 + q * 4096
                        ar = ic % 2
                        po = 0
                        for dc in range(8):
                            col = w1c + dc * 512 + icq * 128
                            p.step("tensor", lambda e, col=col, dc=dc, to=to, po=po: e.matmul(
                                ps_h1[:, po:po + 256],
                                w1h[:, col:col + 128],
                                xnb[:, dc * TG + to: dc * TG + to + 256],
                                start=(dc == 0), stop=(dc == 7)),
                                reads=[w1h_sl[half][q], xnb_sl[dc]], writes=[psl_h1[ar]])
                            p.step("tensor", lambda e, col=col, dc=dc, to=to, po=po: e.matmul(
                                ps_h3[:, po:po + 256],
                                w3h[:, col:col + 128],
                                xnb[:, dc * TG + to: dc * TG + to + 256],
                                start=(dc == 0), stop=(dc == 7)),
                                reads=[w3h_sl[half][q], xnb_sl[dc]], writes=[psl_h3[ar]])
                        if pend_y is not None:
                            pend_y()
                        p.step("scalar", lambda e, ar=ar, po=po: e.activation(
                            silu_t[:, ar * 256:(ar + 1) * 256], ps_h1[:, po:po + 256],
                            AF.Silu), reads=[psl_h1[ar]], writes=[silu_sl[ar]])
                        p.step("vector", lambda e, ar=ar, po=po: e.tensor_mul(
                            actb[:, ar * 256:(ar + 1) * 256],
                            silu_t[:, ar * 256:(ar + 1) * 256], ps_h3[:, po:po + 256]),
                            reads=[silu_sl[ar], psl_h3[ar]], writes=[actb_sl[ar]])

                        def pend_y(ar=ar, ic=ic, half=half, icl=icl):
                            for st4 in range(2):
                                for dg in range(2):
                                    yk = st4 * 2 + dg
                                    p.step("tensor", lambda e, ar=ar, st4=st4, dg=dg,
                                           yk=yk, ic=ic, half=half, icl=icl: e.matmul(
                                        ps_y[:, yk * 512:(yk + 1) * 512],
                                        actb[:, ar * 256 + st4 * 128: ar * 256 + (st4 + 1) * 128],
                                        w2b[:, ic * 1024 + dg * 512: ic * 1024 + (dg + 1) * 512],
                                        start=(icl == 0), stop=(icl == 7)),
                                        reads=[actb_sl[ar], w2b_sl[half]], writes=[psl_y[yk]])
                    pend_y()
                    for st4 in range(2):
                        tt = (to + st4 * 128) // 128
                        for dg in range(2):
                            yk = st4 * 2 + dg
                            p.step("vector", lambda e, tt=tt, dg=dg, ex=ex, yk=yk: e.scalar_tensor_tensor(
                                outg[:, tt * 1024 + dg * 512: tt * 1024 + (dg + 1) * 512],
                                ps_y[:, yk * 512:(yk + 1) * 512],
                                dwb[:, tt * 8 + ex: tt * 8 + ex + 1],
                                outg[:, tt * 1024 + dg * 512: tt * 1024 + (dg + 1) * 512],
                                ALU.mult, ALU.add),
                                reads=[psl_y[yk], dwb_sl[tt]], writes=[outg_sl[tt]])
        for tt in range(8):
            r = tt % 2
            p.step("scalar", lambda e, tt=tt, r=r: e.activation(
                gsb[:, r * 1024:(r + 1) * 1024], outg[:, tt * 1024:(tt + 1) * 1024],
                AF.Silu), reads=[outg_sl[tt]], writes=[gsb_sl[r]])
            p.step("gpsimd", lambda e, tt=tt, r=r, t0=t0: e.dma_start(
                out=go[t0 + tt * 128: t0 + (tt + 1) * 128, :],
                in_=gsb[:, r * 1024:(r + 1) * 1024]),
                reads=[gsb_sl[r]], dma=True)
    return c.done()


def build_E3():
    c = Ctx()
    p = c.plan
    x = c.inp("x", [TOK, D])
    s0i = c.inp("s0", [TOK, D])
    gi = c.inp("g", [TOK, D])
    idn = c.inp("idn", [128, 128], BF16)
    x1o = c.out("x1", [TOK, D])
    xn1o = c.out("xn1", [TOK, D])
    xn1T = c.out("xn1T", [D, TOK], BF16)

    idn_sb = c.sbuf("idn_sb", [128, 128], BF16)
    idn_sl = Slot()
    p.step("sync", lambda e: e.dma_start(out=idn_sb[:, :], in_=idn),
           writes=[idn_sl], dma=True)
    xs = c.sbuf("xs", [128, 2 * 1024], F32)
    ss = c.sbuf("ss", [128, 2 * 1024], F32)
    gs = c.sbuf("gs", [128, 2 * 1024], F32)
    x1t = c.sbuf("x1t", [128, 2 * 1024], F32)
    xc = c.sbuf("xc", [128, 2 * 1024], F32)
    sq = c.sbuf("sq", [128, 2 * 1024], F32)
    xnf = c.sbuf("xnf", [128, 2 * 1024], F32)
    xnb = c.sbuf("xnb", [128, 2 * 1024], BF16)
    xnt = c.sbuf("xnt", [128, 2 * 8 * 512], BF16)
    cols = c.sbuf("cols", [128, 2 * 4], F32)
    ps_tr = c.psum("ps_tr", [128, 2048], BF16)
    sls = {nm: [Slot(), Slot()] for nm in
           ("xs", "ss", "gs", "x1t", "xc", "sq", "xnf", "xnb", "cols")}
    xnt_sl = [[[Slot() for _ in range(4)] for _ in range(8)] for _ in range(2)]
    tr_sl = [Slot() for _ in range(4)]

    for g in range(8):
        rg = g % 2
        for st in range(4):
            tok0 = g * 512 + st * 128
            r = st % 2
            o = r * 1024
            for nm, buf, src in (("xs", xs, x), ("ss", ss, s0i), ("gs", gs, gi)):
                p.step("sync", lambda e, buf=buf, src=src, o=o, tok0=tok0: e.dma_start(
                    out=buf[:, o:o + 1024], in_=src[tok0:tok0 + 128, :]),
                    writes=[sls[nm][r]], dma=True)
            p.step("vector", lambda e, o=o: e.tensor_mul(
                gs[:, o:o + 1024], ss[:, o:o + 1024], gs[:, o:o + 1024]),
                reads=[sls["ss"][r]], writes=[sls["gs"][r]])
            p.step("vector", lambda e, o=o: e.tensor_add(
                x1t[:, o:o + 1024], xs[:, o:o + 1024], gs[:, o:o + 1024]),
                reads=[sls["xs"][r], sls["gs"][r]], writes=[sls["x1t"][r]])
            p.step("gpsimd", lambda e, o=o, tok0=tok0: e.dma_start(
                out=x1o[tok0:tok0 + 128, :], in_=x1t[:, o:o + 1024]),
                reads=[sls["x1t"][r]], dma=True)
            mu = cols[:, r * 4 + 0: r * 4 + 1]
            var = cols[:, r * 4 + 1: r * 4 + 2]
            std = cols[:, r * 4 + 2: r * 4 + 3]
            rstd = cols[:, r * 4 + 3: r * 4 + 4]
            cs = sls["cols"][r]
            p.step("vector", lambda e, mu=mu, o=o: e.reduce_sum(
                mu, x1t[:, o:o + 1024], axis=AX.X),
                reads=[sls["x1t"][r]], writes=[cs])
            p.step("scalar", lambda e, mu=mu: e.mul(mu, mu, 1.0 / D), writes=[cs])
            p.step("vector", lambda e, mu=mu, o=o: e.tensor_single_scalar(
                xc[:, o:o + 1024], x1t[:, o:o + 1024], mu, ALU.subtract),
                reads=[sls["x1t"][r], cs], writes=[sls["xc"][r]])
            p.step("scalar", lambda e, var=var, o=o: e.activation(
                sq[:, o:o + 1024], xc[:, o:o + 1024], AF.Square, accum_out=var),
                reads=[sls["xc"][r]], writes=[sls["sq"][r], cs])
            epsa, epssl = c.cval(EPS)
            p.step("scalar", lambda e, std=std, var=var, epsa=epsa: e.activation(
                std, var, AF.Sqrt, scale=1.0 / D, bias=epsa),
                reads=[epssl], writes=[cs])
            p.step("vector", lambda e, rstd=rstd, std=std: e.reciprocal(
                rstd, std), writes=[cs])
            p.step("vector", lambda e, rstd=rstd, o=o: e.tensor_single_scalar(
                xnf[:, o:o + 1024], xc[:, o:o + 1024], rstd, ALU.mult),
                reads=[sls["xc"][r], cs], writes=[sls["xnf"][r]])
            p.step("gpsimd", lambda e, o=o, tok0=tok0: e.dma_start(
                out=xn1o[tok0:tok0 + 128, :], in_=xnf[:, o:o + 1024]),
                reads=[sls["xnf"][r]], dma=True)
            p.step("vector", lambda e, o=o: e.tensor_copy(
                out=xnb[:, o:o + 1024], in_=xnf[:, o:o + 1024]),
                reads=[sls["xnf"][r]], writes=[sls["xnb"][r]])
            for dc in range(8):
                pt = dc % 2
                p.step("tensor", lambda e, o=o, dc=dc, pt=pt: e.transpose(
                    ps_tr[:, pt * 1024: pt * 1024 + 128],
                    xnb[:, o + dc * 128: o + (dc + 1) * 128], idn_sb[:, :]),
                    reads=[sls["xnb"][r], idn_sl], writes=[tr_sl[pt]])
                dst = (rg * 8 + dc) * 512 + st * 128
                p.step("scalar", lambda e, dst=dst, pt=pt: e.copy(
                    xnt[:, dst:dst + 128], ps_tr[:, pt * 1024: pt * 1024 + 128]),
                    reads=[tr_sl[pt]], writes=[xnt_sl[rg][dc][st]])
        for dc in range(8):
            src = (rg * 8 + dc) * 512
            p.step("gpsimd", lambda e, src=src, dc=dc, g=g: e.dma_start(
                out=xn1T[dc * 128:(dc + 1) * 128, g * 512:(g + 1) * 512],
                in_=xnt[:, src:src + 512]),
                reads=xnt_sl[rg][dc], dma=True)
    return c.done()


def build_E4():
    c = Ctx()
    p = c.plan
    xn1 = c.inp("xn1", [TOK, D])
    xn1T = c.inp("xn1T", [D, TOK], BF16)
    wqt = c.inp("wqt", [D, D])
    wkt = c.inp("wkt", [D, D])
    s1o = c.out("s1", [TOK, D])

    stg = c.sbuf("stg", [128, 4096], F32)
    wqb = c.sbuf("wqb", [128, 8 * 1024], BF16)
    wkb = c.sbuf("wkb", [128, 8 * 1024], BF16)
    xts = c.sbuf("xts", [128, 8 * 4096], BF16)
    stg_sl = Slot()
    wq_sl = Slot()
    wk_sl = Slot()
    xts_sl = Slot()
    for wsrc, wdst, wsl in ((wqt, wqb, wq_sl), (wkt, wkb, wk_sl)):
        for dg in range(2):
            p.step("sync", lambda e, wsrc=wsrc, dg=dg: e.dma_start(
                out=stg[:, :],
                in_=wsrc[:, dg * 512:(dg + 1) * 512].rearrange(
                    "(dc p) ee -> p dc ee", p=128)),
                writes=[stg_sl], dma=True)
            p.step("gpsimd", lambda e, wdst=wdst, dg=dg: e.tensor_copy(
                out=wdst[:, :].rearrange("p (dc ee) -> p dc ee",
                                         ee=1024)[:, :, dg * 512:(dg + 1) * 512],
                in_=stg[:, :].rearrange("p (dc ee) -> p dc ee", dc=8)),
                reads=[stg_sl], writes=[wsl])
    p.step("sync", lambda e: e.dma_start(
        out=xts[:, :], in_=xn1T.rearrange("(dc p) t -> p dc t", p=128)),
        writes=[xts_sl], dma=True)

    qsb = c.sbuf("qsb", [128, 4 * 1024], F32)
    ksb = c.sbuf("ksb", [128, 4 * 1024], F32)
    xtt = c.sbuf("xtt", [128, 4 * 1024], F32)
    qkw = c.sbuf("qkw", [128, 1024], F32)
    sc = c.sbuf("sc", [128, 16], F32)
    th = c.sbuf("th", [128, 16], F32)
    exb = c.sbuf("exb", [128, 16], F32)
    alb = c.sbuf("alb", [128, 16], F32)
    mcol = c.sbuf("mcol", [128, 4], F32)
    ob = c.sbuf("ob", [128, 2 * 1024], F32)
    ps_q = c.psum("ps_q", [128, 512], F32)
    ps_k = c.psum("ps_k", [128, 512], F32)
    q_sl = [Slot() for _ in range(4)]
    k_sl = [Slot() for _ in range(4)]
    xt_sl = [Slot() for _ in range(4)]
    qkw_sl = Slot()
    sc_sl = [Slot() for _ in range(16)]
    th_sl = Slot()
    ex_sl = [Slot() for _ in range(4)]
    al_sl = [Slot() for _ in range(4)]
    mc_sl = [Slot() for _ in range(4)]
    ob_sl = [Slot(), Slot()]
    psq_sl = Slot()
    psk_sl = Slot()

    for j in range(8):
        for s in range(4):
            t = s * 8 + j
            for dg in range(2):
                for dc in range(8):
                    p.step("tensor", lambda e, dc=dc, t=t, dg=dg: e.matmul(
                        ps_q[:, :], xts[:, dc * 4096 + t * 128: dc * 4096 + (t + 1) * 128],
                        wqb[:, dc * 1024 + dg * 512: dc * 1024 + (dg + 1) * 512],
                        start=(dc == 0), stop=(dc == 7)),
                        reads=[xts_sl, wq_sl], writes=[psq_sl])
                    p.step("tensor", lambda e, dc=dc, t=t, dg=dg: e.matmul(
                        ps_k[:, :], xts[:, dc * 4096 + t * 128: dc * 4096 + (t + 1) * 128],
                        wkb[:, dc * 1024 + dg * 512: dc * 1024 + (dg + 1) * 512],
                        start=(dc == 0), stop=(dc == 7)),
                        reads=[xts_sl, wk_sl], writes=[psk_sl])
                p.step("scalar", lambda e, s=s, dg=dg: e.copy(
                    qsb[:, s * 1024 + dg * 512: s * 1024 + (dg + 1) * 512], ps_q[:, :]),
                    reads=[psq_sl], writes=[q_sl[s]])
                p.step("scalar", lambda e, s=s, dg=dg: e.copy(
                    ksb[:, s * 1024 + dg * 512: s * 1024 + (dg + 1) * 512], ps_k[:, :]),
                    reads=[psk_sl], writes=[k_sl[s]])
            tok0 = t * 128
            p.step("sync", lambda e, s=s, tok0=tok0: e.dma_start(
                out=xtt[:, s * 1024:(s + 1) * 1024], in_=xn1[tok0:tok0 + 128, :]),
                writes=[xt_sl[s]], dma=True)
        for s in range(4):
            for t4 in range(4):
                k = s * 4 + t4
                p.step("vector", lambda e, s=s, t4=t4: e.tensor_mul(
                    qkw[:, :], qsb[:, s * 1024:(s + 1) * 1024],
                    ksb[:, t4 * 1024:(t4 + 1) * 1024]),
                    reads=[q_sl[s], k_sl[t4]], writes=[qkw_sl])
                p.step("vector", lambda e, k=k: e.reduce_sum(
                    sc[:, k:k + 1], qkw[:, :], axis=AX.X),
                    reads=[qkw_sl], writes=[sc_sl[k]])
        p.step("scalar", lambda e: e.activation(th[:, :], sc[:, :], AF.Tanh,
                                                scale=1.0 / 32.0),
               reads=sc_sl, writes=[th_sl])
        for s in range(4):
            p.step("vector", lambda e, s=s: e.reduce_max(
                mcol[:, s:s + 1], th[:, s * 4:(s + 1) * 4], axis=AX.X),
                reads=[th_sl], writes=[mc_sl[s]])
            p.step("vector", lambda e, s=s: e.tensor_single_scalar(
                mcol[:, s:s + 1], mcol[:, s:s + 1], -1.0, ALU.mult),
                writes=[mc_sl[s]])
            p.step("scalar", lambda e, s=s: e.activation(
                exb[:, s * 4:(s + 1) * 4], th[:, s * 4:(s + 1) * 4], AF.Exp,
                bias=mcol[:, s:s + 1]),
                reads=[th_sl, mc_sl[s]], writes=[ex_sl[s]])
            p.step("vector", lambda e, s=s: e.reduce_sum(
                mcol[:, s:s + 1], exb[:, s * 4:(s + 1) * 4], axis=AX.X),
                reads=[ex_sl[s]], writes=[mc_sl[s]])
            p.step("vector", lambda e, s=s: e.reciprocal(
                mcol[:, s:s + 1], mcol[:, s:s + 1]), writes=[mc_sl[s]])
            p.step("vector", lambda e, s=s: e.tensor_single_scalar(
                alb[:, s * 4:(s + 1) * 4], exb[:, s * 4:(s + 1) * 4],
                mcol[:, s:s + 1], ALU.mult),
                reads=[ex_sl[s], mc_sl[s]], writes=[al_sl[s]])
        for s in range(4):
            r = s % 2
            o = r * 1024
            p.step("vector", lambda e, s=s, o=o: e.tensor_single_scalar(
                ob[:, o:o + 1024], xtt[:, 0:1024], alb[:, s * 4:s * 4 + 1], ALU.mult),
                reads=[xt_sl[0], al_sl[s]], writes=[ob_sl[r]])
            for t4 in range(1, 4):
                p.step("vector", lambda e, s=s, t4=t4, o=o: e.scalar_tensor_tensor(
                    ob[:, o:o + 1024], xtt[:, t4 * 1024:(t4 + 1) * 1024],
                    alb[:, s * 4 + t4: s * 4 + t4 + 1], ob[:, o:o + 1024],
                    ALU.mult, ALU.add),
                    reads=[xt_sl[t4], al_sl[s]], writes=[ob_sl[r]])
            tok0 = (s * 8 + j) * 128
            p.step("gpsimd", lambda e, o=o, tok0=tok0: e.dma_start(
                out=s1o[tok0:tok0 + 128, :], in_=ob[:, o:o + 1024]),
                reads=[ob_sl[r]], dma=True)
    return c.done()


def build_E5a():
    c = Ctx()
    p = c.plan
    xn1 = c.inp("xn1", [TOK, D])
    w1t = c.inp("w1t", [D, H4])
    b1c = c.inp("b1c", [128, 32])
    g1r = c.inp("g1r", [1, D])
    b1r = c.inp("b1r", [1, D])
    idn = c.inp("idn", [128, 128], BF16)
    hG = c.out("hG", [H4, TOK])

    idn_sb = c.sbuf("idn_sb", [128, 128], BF16)
    idn_sl = Slot()
    p.step("sync", lambda e: e.dma_start(out=idn_sb[:, :], in_=idn),
           writes=[idn_sl], dma=True)
    ones1 = c.sbuf("ones1", [1, 128], F32)
    onesc = c.sbuf("onesc", [128, 1], F32)
    o1_sl = Slot()
    oc_sl = Slot()
    p.step("vector", lambda e: e.memset(ones1[:, :], 1.0), writes=[o1_sl])
    p.step("vector", lambda e: e.memset(onesc[:, :], 1.0), writes=[oc_sl])
    g1row = c.sbuf("g1row", [1, D], F32)
    b1row = c.sbuf("b1row", [1, D], F32)
    g1B = c.sbuf("g1B", [128, D], F32)
    b1B = c.sbuf("b1B", [128, D], F32)
    b1cs = c.sbuf("b1cs", [128, 32], F32)
    grow_sl = Slot()
    brow_sl = Slot()
    g1B_sl = Slot()
    b1B_sl = Slot()
    b1c_sl = Slot()
    p.step("sync", lambda e: e.dma_start(out=g1row[:, :], in_=g1r),
           writes=[grow_sl], dma=True)
    p.step("sync", lambda e: e.dma_start(out=b1row[:, :], in_=b1r),
           writes=[brow_sl], dma=True)
    p.step("sync", lambda e: e.dma_start(out=b1cs[:, :], in_=b1c),
           writes=[b1c_sl], dma=True)
    ps_b = c.psum("ps_b", [128, 1024], F32)
    psb_sl = Slot()
    for dg in range(2):
        p.step("tensor", lambda e, dg=dg: e.matmul(
            ps_b[:, dg * 512:(dg + 1) * 512], ones1[:, :],
            g1row[:, dg * 512:(dg + 1) * 512], start=True, stop=True),
            reads=[o1_sl, grow_sl], writes=[psb_sl])
    p.step("scalar", lambda e: e.copy(g1B[:, :], ps_b[:, :]),
           reads=[psb_sl], writes=[g1B_sl])
    for dg in range(2):
        p.step("tensor", lambda e, dg=dg: e.matmul(
            ps_b[:, dg * 512:(dg + 1) * 512], ones1[:, :],
            b1row[:, dg * 512:(dg + 1) * 512], start=True, stop=True),
            reads=[o1_sl, brow_sl], writes=[psb_sl])
    p.step("scalar", lambda e: e.copy(b1B[:, :], ps_b[:, :]),
           reads=[psb_sl], writes=[b1B_sl])

    stg = c.sbuf("stg", [128, 4096], F32)
    stg_sl = [Slot()]
    mc8 = c.sbuf("mc8", [128, 8], F32)
    mc8_sl = [Slot() for _ in range(8)]
    mtot = c.sbuf("mtot", [128, 1], F32)
    mval = c.sbuf("mval", [1, 1], F32)
    m1P = c.sbuf("m1P", [128, 1], F32)
    thr = c.sbuf("thr", [128, 1], F32)
    nthr = c.sbuf("nthr", [128, 1], F32)
    mtot_sl = Slot()
    mval_sl = Slot()
    m1P_sl = Slot()
    thr_sl = Slot()
    nthr_sl = Slot()
    ps_m = c.psum("ps_m", [128, 512], F32)
    psm_sl = Slot()
    for dc in range(8):
        p.step("sync", lambda e, dc=dc: e.dma_start(
            out=stg[:, :], in_=w1t[dc * 128:(dc + 1) * 128, :]),
            writes=[stg_sl[0]], dma=True)
        p.step("vector", lambda e, dc=dc: e.tensor_reduce(
            mc8[:, dc:dc + 1], stg[:, :], AX.X, ALU.add,
            apply_absolute_value=True),
            reads=[stg_sl[0]], writes=[mc8_sl[dc]])
    p.step("vector", lambda e: e.reduce_sum(mtot[:, :], mc8[:, :], axis=AX.X),
           reads=mc8_sl, writes=[mtot_sl])
    p.step("tensor", lambda e: e.matmul(ps_m[:1, :1], mtot[:, :], onesc[:, :],
                                        start=True, stop=True),
           reads=[mtot_sl, oc_sl], writes=[psm_sl])
    p.step("scalar", lambda e: e.activation(mval[:, :], ps_m[:1, :1], AF.Identity,
                                            scale=1.0 / (D * H4)),
           reads=[psm_sl], writes=[mval_sl])
    p.step("vector", lambda e: e.tensor_single_scalar(
        mval[:, :], mval[:, :], 1e-5, ALU.max), writes=[mval_sl])
    p.step("tensor", lambda e: e.matmul(ps_m[:, 0:1], ones1[:, :], mval[:, :],
                                        start=True, stop=True),
           reads=[o1_sl, mval_sl], writes=[psm_sl])
    p.step("scalar", lambda e: e.copy(m1P[:, :], ps_m[:, 0:1]),
           reads=[psm_sl], writes=[m1P_sl])
    p.step("vector", lambda e: e.tensor_single_scalar(
        thr[:, :], m1P[:, :], 0.5, ALU.mult), reads=[m1P_sl], writes=[thr_sl])
    p.step("vector", lambda e: e.tensor_single_scalar(
        nthr[:, :], m1P[:, :], -0.5, ALU.mult), reads=[m1P_sl], writes=[nthr_sl])

    xgT = c.sbuf("xgT", [128, 8 * 4096], BF16)
    xgT_sl = [[Slot() for _ in range(32)] for _ in range(8)]
    xs = c.sbuf("xs", [128, 2 * 1024], F32)
    xc = c.sbuf("xc", [128, 2 * 1024], F32)
    xg = c.sbuf("xg", [128, 2 * 1024], F32)
    sq = xg
    xgb = c.sbuf("xgb", [128, 2 * 1024], BF16)
    cols = c.sbuf("cols", [128, 2 * 4], F32)
    ps_tr = c.psum("ps_tr", [128, 2048], BF16)
    sls = {nm: [Slot(), Slot()] for nm in ("xs", "xc", "xg", "xgb", "cols")}
    sls["sq"] = sls["xg"]
    tr_sl = [Slot() for _ in range(4)]
    for ti in range(32):
        r = ti % 2
        o = r * 1024
        tok0 = ti * 128
        p.step("sync", lambda e, o=o, tok0=tok0: e.dma_start(
            out=xs[:, o:o + 1024], in_=xn1[tok0:tok0 + 128, :]),
            writes=[sls["xs"][r]], dma=True)
        mu = cols[:, r * 4 + 0: r * 4 + 1]
        var = cols[:, r * 4 + 1: r * 4 + 2]
        std = cols[:, r * 4 + 2: r * 4 + 3]
        rstd = cols[:, r * 4 + 3: r * 4 + 4]
        cs = sls["cols"][r]
        p.step("vector", lambda e, mu=mu, o=o: e.reduce_sum(
            mu, xs[:, o:o + 1024], axis=AX.X), reads=[sls["xs"][r]], writes=[cs])
        p.step("scalar", lambda e, mu=mu: e.mul(mu, mu, 1.0 / D), writes=[cs])
        p.step("vector", lambda e, mu=mu, o=o: e.tensor_single_scalar(
            xc[:, o:o + 1024], xs[:, o:o + 1024], mu, ALU.subtract),
            reads=[sls["xs"][r], cs], writes=[sls["xc"][r]])
        p.step("scalar", lambda e, var=var, o=o: e.activation(
            sq[:, o:o + 1024], xc[:, o:o + 1024], AF.Square, accum_out=var),
            reads=[sls["xc"][r]], writes=[sls["sq"][r], cs])
        epsa, epssl = c.cval(EPS)
        p.step("scalar", lambda e, std=std, var=var, epsa=epsa: e.activation(
            std, var, AF.Sqrt, scale=1.0 / D, bias=epsa),
            reads=[epssl], writes=[cs])
        p.step("vector", lambda e, rstd=rstd, std=std: e.reciprocal(rstd, std),
               writes=[cs])
        p.step("vector", lambda e, rstd=rstd, o=o: e.tensor_single_scalar(
            xg[:, o:o + 1024], xc[:, o:o + 1024], rstd, ALU.mult),
            reads=[sls["xc"][r], cs], writes=[sls["xg"][r]])
        p.step("vector", lambda e, o=o: e.tensor_mul(
            xg[:, o:o + 1024], xg[:, o:o + 1024], g1B[:, :]),
            reads=[g1B_sl], writes=[sls["xg"][r]])
        p.step("vector", lambda e, o=o: e.tensor_add(
            xgb[:, o:o + 1024], xg[:, o:o + 1024], b1B[:, :]),
            reads=[sls["xg"][r], b1B_sl], writes=[sls["xgb"][r]])
        for dc in range(8):
            pt = dc % 2
            p.step("tensor", lambda e, o=o, dc=dc, pt=pt: e.transpose(
                ps_tr[:, pt * 1024: pt * 1024 + 128],
                xgb[:, o + dc * 128: o + (dc + 1) * 128], idn_sb[:, :]),
                reads=[sls["xgb"][r], idn_sl], writes=[tr_sl[pt]])
            p.step("scalar", lambda e, dc=dc, ti=ti, pt=pt: e.copy(
                xgT[:, dc * 4096 + ti * 128: dc * 4096 + (ti + 1) * 128],
                ps_tr[:, pt * 1024: pt * 1024 + 128]),
                reads=[tr_sl[pt]], writes=[xgT_sl[dc][ti]])

    stw = c.sbuf("stw", [128, 2 * 1024], F32)
    qa = c.sbuf("qa", [128, 2 * 1024], F32)
    qb = c.sbuf("qb", [128, 2 * 1024], F32)
    w1q = c.sbuf("w1q", [128, 2 * 1024], BF16)
    hsp = c.sbuf("hsp", [128, 2 * 512], F32)
    stw_sl = [Slot(), Slot()]
    qa_sl = [Slot(), Slot()]
    qb_sl = [Slot(), Slot()]
    w1q_sl = [Slot(), Slot()]
    hsp_sl = [Slot(), Slot()]
    ps_h = c.psum("ps_h", [128, 2 * 512], F32)
    psh_sl = [Slot(), Slot()]
    for hc in range(32):
        r = hc % 2
        o = r * 1024
        p.step("sync", lambda e, hc=hc, o=o: e.dma_start(
            out=stw[:, o:o + 1024],
            in_=w1t[:, hc * 128:(hc + 1) * 128].rearrange("(dc p) h -> p dc h", p=128)),
            writes=[stw_sl[r]], dma=True)
        p.step("vector", lambda e, o=o: e.tensor_single_scalar(
            qa[:, o:o + 1024], stw[:, o:o + 1024], thr[:, :], ALU.is_gt),
            reads=[stw_sl[r], thr_sl], writes=[qa_sl[r]])
        p.step("vector", lambda e, o=o: e.tensor_single_scalar(
            qb[:, o:o + 1024], stw[:, o:o + 1024], nthr[:, :], ALU.is_lt),
            reads=[stw_sl[r], nthr_sl], writes=[qb_sl[r]])
        p.step("vector", lambda e, o=o: e.tensor_sub(
            w1q[:, o:o + 1024], qa[:, o:o + 1024], qb[:, o:o + 1024]),
            reads=[qa_sl[r], qb_sl[r]], writes=[w1q_sl[r]])
        for tt in range(8):
            rr = tt % 2
            for dc in range(8):
                p.step("tensor", lambda e, o=o, dc=dc, tt=tt, rr=rr: e.matmul(
                    ps_h[:, rr * 512:(rr + 1) * 512],
                    w1q[:, o + dc * 128: o + (dc + 1) * 128],
                    xgT[:, dc * 4096 + tt * 512: dc * 4096 + (tt + 1) * 512],
                    start=(dc == 0), stop=(dc == 7)),
                    reads=[w1q_sl[r]] + [xgT_sl[dc][tt * 4 + q] for q in range(4)],
                    writes=[psh_sl[rr]])
            p.step("scalar", lambda e, hc=hc, rr=rr: e.activation(
                hsp[:, rr * 512:(rr + 1) * 512], ps_h[:, rr * 512:(rr + 1) * 512],
                AF.Gelu, scale=m1P[:, :], bias=b1cs[:, hc:hc + 1]),
                reads=[psh_sl[rr], m1P_sl, b1c_sl], writes=[hsp_sl[rr]])
            p.step("gpsimd", lambda e, hc=hc, tt=tt, rr=rr: e.dma_start(
                out=hG[hc * 128:(hc + 1) * 128, tt * 512:(tt + 1) * 512],
                in_=hsp[:, rr * 512:(rr + 1) * 512]),
                reads=[hsp_sl[rr]], dma=True)
    return c.done()


def build_E5b():
    c = Ctx()
    p = c.plan
    hGi = c.inp("hG", [H4, TOK])
    w2t = c.inp("w2t", [H4, D])
    b2r = c.inp("b2r", [1, D])
    g2c = c.inp("g2c", [128, 32])
    b2c = c.inp("b2c", [128, 32])
    x1i = c.inp("x1", [TOK, D])
    s1i = c.inp("s1", [TOK, D])
    oo = c.out("out", [TOK, D])

    ones1 = c.sbuf("ones1", [1, 128], F32)
    onesc = c.sbuf("onesc", [128, 1], F32)
    onescb = c.sbuf("onescb", [128, 1], BF16)
    o1_sl = Slot()
    oc_sl = Slot()
    ocb_sl = Slot()
    p.step("vector", lambda e: e.memset(ones1[:, :], 1.0), writes=[o1_sl])
    p.step("vector", lambda e: e.memset(onesc[:, :], 1.0), writes=[oc_sl])
    p.step("vector", lambda e: e.memset(onescb[:, :], 1.0), writes=[ocb_sl])
    g2cs = c.sbuf("g2cs", [128, 32], F32)
    b2cs = c.sbuf("b2cs", [128, 32], F32)
    g2_sl = Slot()
    b2_sl = Slot()
    p.step("sync", lambda e: e.dma_start(out=g2cs[:, :], in_=g2c),
           writes=[g2_sl], dma=True)
    p.step("sync", lambda e: e.dma_start(out=b2cs[:, :], in_=b2c),
           writes=[b2_sl], dma=True)
    b2row = c.sbuf("b2row", [1, D], F32)
    b2B = c.sbuf("b2B", [128, D], F32)
    b2r_sl = Slot()
    b2B_sl = Slot()
    p.step("sync", lambda e: e.dma_start(out=b2row[:, :], in_=b2r),
           writes=[b2r_sl], dma=True)
    ps_b = c.psum("ps_b", [128, 1024], F32)
    psb_sl = Slot()
    for dg in range(2):
        p.step("tensor", lambda e, dg=dg: e.matmul(
            ps_b[:, dg * 512:(dg + 1) * 512], ones1[:, :],
            b2row[:, dg * 512:(dg + 1) * 512], start=True, stop=True),
            reads=[o1_sl, b2r_sl], writes=[psb_sl])
    p.step("scalar", lambda e: e.copy(b2B[:, :], ps_b[:, :]),
           reads=[psb_sl], writes=[b2B_sl])

    stg = c.sbuf("stg", [128, 2 * 1024], F32)
    stg_sl = [Slot(), Slot()]
    mc32 = c.sbuf("mc32", [128, 32], F32)
    mc_sl = [Slot() for _ in range(32)]
    mtot = c.sbuf("mtot", [128, 1], F32)
    mval = c.sbuf("mval", [1, 1], F32)
    m2P = c.sbuf("m2P", [128, 1], F32)
    thr = c.sbuf("thr", [128, 1], F32)
    nthr = c.sbuf("nthr", [128, 1], F32)
    mtot_sl = Slot()
    mval_sl = Slot()
    m2P_sl = Slot()
    thr_sl = Slot()
    nthr_sl = Slot()
    ps_m = c.psum("ps_m", [128, 512], F32)
    psm_sl = Slot()
    for hc in range(32):
        r = hc % 2
        p.step("sync", lambda e, hc=hc, r=r: e.dma_start(
            out=stg[:, r * 1024:(r + 1) * 1024],
            in_=w2t[hc * 128:(hc + 1) * 128, :]),
            writes=[stg_sl[r]], dma=True)
        p.step("vector", lambda e, hc=hc, r=r: e.tensor_reduce(
            mc32[:, hc:hc + 1], stg[:, r * 1024:(r + 1) * 1024], AX.X, ALU.add,
            apply_absolute_value=True),
            reads=[stg_sl[r]], writes=[mc_sl[hc]])
    p.step("vector", lambda e: e.reduce_sum(mtot[:, :], mc32[:, :], axis=AX.X),
           reads=mc_sl, writes=[mtot_sl])
    p.step("tensor", lambda e: e.matmul(ps_m[:1, :1], mtot[:, :], onesc[:, :],
                                        start=True, stop=True),
           reads=[mtot_sl, oc_sl], writes=[psm_sl])
    p.step("scalar", lambda e: e.activation(mval[:, :], ps_m[:1, :1], AF.Identity,
                                            scale=1.0 / (D * H4)),
           reads=[psm_sl], writes=[mval_sl])
    p.step("vector", lambda e: e.tensor_single_scalar(
        mval[:, :], mval[:, :], 1e-5, ALU.max), writes=[mval_sl])
    p.step("tensor", lambda e: e.matmul(ps_m[:, 0:1], ones1[:, :], mval[:, :],
                                        start=True, stop=True),
           reads=[o1_sl, mval_sl], writes=[psm_sl])
    p.step("scalar", lambda e: e.copy(m2P[:, :], ps_m[:, 0:1]),
           reads=[psm_sl], writes=[m2P_sl])
    p.step("vector", lambda e: e.tensor_single_scalar(
        thr[:, :], m2P[:, :], 0.5, ALU.mult), reads=[m2P_sl], writes=[thr_sl])
    p.step("vector", lambda e: e.tensor_single_scalar(
        nthr[:, :], m2P[:, :], -0.5, ALU.mult), reads=[m2P_sl], writes=[nthr_sl])
    w2q = c.sbuf("w2q", [128, 32 * 1024], BF16)
    w2q_sl = [Slot() for _ in range(32)]
    qa = c.sbuf("qa", [128, 2 * 1024], F32)
    qb = c.sbuf("qb", [128, 2 * 1024], F32)
    qa_sl = [Slot(), Slot()]
    qb_sl = [Slot(), Slot()]
    for hc in range(32):
        r = hc % 2
        o = r * 1024
        p.step("sync", lambda e, hc=hc, o=o: e.dma_start(
            out=stg[:, o:o + 1024], in_=w2t[hc * 128:(hc + 1) * 128, :]),
            writes=[stg_sl[r]], dma=True)
        p.step("vector", lambda e, o=o: e.tensor_single_scalar(
            qa[:, o:o + 1024], stg[:, o:o + 1024], thr[:, :], ALU.is_gt),
            reads=[stg_sl[r], thr_sl], writes=[qa_sl[r]])
        p.step("vector", lambda e, o=o: e.tensor_single_scalar(
            qb[:, o:o + 1024], stg[:, o:o + 1024], nthr[:, :], ALU.is_lt),
            reads=[stg_sl[r], nthr_sl], writes=[qb_sl[r]])
        p.step("vector", lambda e, o=o, hc=hc: e.tensor_sub(
            w2q[:, hc * 1024:(hc + 1) * 1024], qa[:, o:o + 1024], qb[:, o:o + 1024]),
            reads=[qa_sl[r], qb_sl[r]], writes=[w2q_sl[hc]])

    hgb = c.sbuf("hgb", [128, 2 * 512], F32)
    sqb = c.sbuf("sqb", [128, 2 * 512], BF16)
    xn2b = c.sbuf("xn2b", [128, 32 * 512], BF16)
    t1b = c.sbuf("t1b", [128, 2 * 512], F32)
    muB = c.sbuf("muB", [128, 512], F32)
    rsB = c.sbuf("rsB", [128, 512], F32)
    mrow = c.sbuf("mrow", [1, 512], F32)
    vrow = c.sbuf("vrow", [1, 512], F32)
    ot1 = c.sbuf("ot1", [128, 2 * 512], F32)
    gg = c.sbuf("gg", [128, 2 * 512], F32)
    x1t = c.sbuf("x1t", [128, 2 * 512], F32)
    s1t = c.sbuf("s1t", [128, 2 * 512], F32)
    fo = c.sbuf("fo", [128, 2 * 512], F32)
    hgb_sl = [Slot(), Slot()]
    sqb_sl = [Slot(), Slot()]
    xn2_sl = [Slot() for _ in range(32)]
    t1_sl = [Slot(), Slot()]
    muB_sl = Slot()
    rsB_sl = Slot()
    mrow_sl = Slot()
    vrow_sl = Slot()
    ot1_sl = [Slot(), Slot()]
    gg_sl = [Slot(), Slot()]
    x1_sl = [Slot(), Slot()]
    s1_sl = [Slot(), Slot()]
    fo_sl = [Slot(), Slot()]
    ps_s = c.psum("ps_s", [128, 512], F32)
    ps_v = c.psum("ps_v", [128, 512], F32)
    ps_bc = c.psum("ps_bc", [128, 512], F32)
    ps_o = c.psum("ps_o", [128, 2 * 512], F32)
    pss_sl = Slot()
    psv_sl = Slot()
    psbc_sl = Slot()
    pso_sl = [Slot(), Slot()]

    for tg in range(8):
        t0 = tg * 512
        for hc in range(32):
            r = hc % 2
            o = r * 512
            p.step("sync", lambda e, hc=hc, o=o, t0=t0: e.dma_start(
                out=hgb[:, o:o + 512], in_=hGi[hc * 128:(hc + 1) * 128, t0:t0 + 512]),
                writes=[hgb_sl[r]], dma=True)
            p.step("tensor", lambda e, hc=hc, o=o: e.matmul(
                ps_s[0:1, :], onesc[:, :], hgb[:, o:o + 512],
                start=(hc == 0), stop=(hc == 31)),
                reads=[oc_sl, hgb_sl[r]], writes=[pss_sl])
            p.step("scalar", lambda e, o=o: e.activation(
                sqb[:, o:o + 512], hgb[:, o:o + 512], AF.Square),
                reads=[hgb_sl[r]], writes=[sqb_sl[r]])
            p.step("tensor", lambda e, hc=hc, o=o: e.matmul(
                ps_v[0:1, :], onescb[:, :], sqb[:, o:o + 512],
                start=(hc == 0), stop=(hc == 31)),
                reads=[ocb_sl, sqb_sl[r]], writes=[psv_sl])
        p.step("scalar", lambda e: e.activation(mrow[:, :], ps_s[0:1, :], AF.Identity,
                                                scale=1.0 / H4),
               reads=[pss_sl], writes=[mrow_sl])
        p.step("scalar", lambda e: e.activation(vrow[:, :], ps_v[0:1, :], AF.Identity,
                                                scale=1.0 / H4),
               reads=[psv_sl], writes=[vrow_sl])
        p.step("vector", lambda e: e.tensor_tensor(
            t1b[:1, :512], mrow[:, :], mrow[:, :], ALU.mult),
            reads=[mrow_sl], writes=[t1_sl[0]])
        p.step("vector", lambda e: e.tensor_sub(
            vrow[:, :], vrow[:, :], t1b[:1, :512]),
            reads=[t1_sl[0]], writes=[vrow_sl])
        epsa, epssl = c.cval(EPS)
        p.step("scalar", lambda e, epsa=epsa: e.activation(
            vrow[:, :], vrow[:, :], AF.Sqrt, bias=epsa[0:1, :]),
            reads=[epssl], writes=[vrow_sl])
        p.step("vector", lambda e: e.reciprocal(vrow[:, :], vrow[:, :]),
               writes=[vrow_sl])
        p.step("tensor", lambda e: e.matmul(ps_bc[:, :], ones1[:, :], mrow[:, :],
                                            start=True, stop=True),
               reads=[o1_sl, mrow_sl], writes=[psbc_sl])
        p.step("scalar", lambda e: e.copy(muB[:, :], ps_bc[:, :]),
               reads=[psbc_sl], writes=[muB_sl])
        p.step("tensor", lambda e: e.matmul(ps_bc[:, :], ones1[:, :], vrow[:, :],
                                            start=True, stop=True),
               reads=[o1_sl, vrow_sl], writes=[psbc_sl])
        p.step("scalar", lambda e: e.copy(rsB[:, :], ps_bc[:, :]),
               reads=[psbc_sl], writes=[rsB_sl])
        for hc in range(32):
            r = hc % 2
            o = r * 512
            p.step("sync", lambda e, hc=hc, o=o, t0=t0: e.dma_start(
                out=hgb[:, o:o + 512], in_=hGi[hc * 128:(hc + 1) * 128, t0:t0 + 512]),
                writes=[hgb_sl[r]], dma=True)
            p.step("vector", lambda e, o=o: e.tensor_sub(
                t1b[:, o:o + 512], hgb[:, o:o + 512], muB[:, :]),
                reads=[hgb_sl[r], muB_sl], writes=[t1_sl[r]])
            p.step("vector", lambda e, o=o: e.tensor_mul(
                t1b[:, o:o + 512], t1b[:, o:o + 512], rsB[:, :]),
                reads=[rsB_sl], writes=[t1_sl[r]])
            p.step("scalar", lambda e, hc=hc, o=o: e.activation(
                xn2b[:, hc * 512:(hc + 1) * 512], t1b[:, o:o + 512],
                AF.Identity, scale=g2cs[:, hc:hc + 1], bias=b2cs[:, hc:hc + 1]),
                reads=[t1_sl[r], g2_sl, b2_sl], writes=[xn2_sl[hc]])
        for st in range(4):
            r = st % 2
            tok0 = t0 + st * 128
            for dg in range(2):
                for hc in range(32):
                    p.step("tensor", lambda e, hc=hc, st=st, dg=dg: e.matmul(
                        ps_o[:, dg * 512:(dg + 1) * 512],
                        xn2b[:, hc * 512 + st * 128: hc * 512 + (st + 1) * 128],
                        w2q[:, hc * 1024 + dg * 512: hc * 1024 + (dg + 1) * 512],
                        start=(hc == 0), stop=(hc == 31)),
                        reads=[xn2_sl[hc], w2q_sl[hc]], writes=[pso_sl[dg]])
                p.step("scalar", lambda e, dg=dg, r=r: e.activation(
                    ot1[:, r * 512:(r + 1) * 512], ps_o[:, dg * 512:(dg + 1) * 512],
                    AF.Copy, scale=m2P[:, :]),
                    reads=[pso_sl[dg], m2P_sl], writes=[ot1_sl[r]])
                p.step("vector", lambda e, dg=dg, r=r: e.tensor_add(
                    ot1[:, r * 512:(r + 1) * 512], ot1[:, r * 512:(r + 1) * 512],
                    b2B[:, dg * 512:(dg + 1) * 512]),
                    reads=[b2B_sl], writes=[ot1_sl[r]])
                p.step("scalar", lambda e, r=r: e.activation(
                    gg[:, r * 512:(r + 1) * 512], ot1[:, r * 512:(r + 1) * 512],
                    AF.Silu), reads=[ot1_sl[r]], writes=[gg_sl[r]])
                p.step("sync", lambda e, r=r, tok0=tok0, dg=dg: e.dma_start(
                    out=x1t[:, r * 512:(r + 1) * 512],
                    in_=x1i[tok0:tok0 + 128, dg * 512:(dg + 1) * 512]),
                    writes=[x1_sl[r]], dma=True)
                p.step("sync", lambda e, r=r, tok0=tok0, dg=dg: e.dma_start(
                    out=s1t[:, r * 512:(r + 1) * 512],
                    in_=s1i[tok0:tok0 + 128, dg * 512:(dg + 1) * 512]),
                    writes=[s1_sl[r]], dma=True)
                p.step("vector", lambda e, r=r: e.tensor_mul(
                    gg[:, r * 512:(r + 1) * 512], gg[:, r * 512:(r + 1) * 512],
                    s1t[:, r * 512:(r + 1) * 512]),
                    reads=[s1_sl[r]], writes=[gg_sl[r]])
                p.step("vector", lambda e, r=r: e.tensor_add(
                    fo[:, r * 512:(r + 1) * 512], x1t[:, r * 512:(r + 1) * 512],
                    gg[:, r * 512:(r + 1) * 512]),
                    reads=[x1_sl[r], gg_sl[r]], writes=[fo_sl[r]])
                p.step("gpsimd", lambda e, r=r, tok0=tok0, dg=dg: e.dma_start(
                    out=oo[tok0:tok0 + 128, dg * 512:(dg + 1) * 512],
                    in_=fo[:, r * 512:(r + 1) * 512]),
                    reads=[fo_sl[r]], dma=True)
    return c.done()


def _run(nc, in_maps):
    return run_bass_kernel_spmd(nc, in_maps, core_ids=list(range(NCORES)))


def kernel(**inp):
    f32 = np.float32
    x_full = np.ascontiguousarray(inp["L0_attn_shot_embeddings"], dtype=f32)
    Wd = np.ascontiguousarray(inp["L0_attn_Wd"], dtype=f32)
    bd = np.ascontiguousarray(inp["L0_attn_bd"], dtype=f32)
    Wi = np.ascontiguousarray(inp["L0_attn_Wi"], dtype=f32)
    bi = np.ascontiguousarray(inp["L0_attn_bi"], dtype=f32)
    A_log = np.ascontiguousarray(inp["L0_attn_A_log"], dtype=f32)
    Wo = np.ascontiguousarray(inp["L0_attn_Wo"], dtype=f32)
    bo = np.ascontiguousarray(inp["L0_attn_bo"], dtype=f32)
    Ws = np.ascontiguousarray(inp["L0_attn_Ws"], dtype=f32)
    bs = np.ascontiguousarray(inp["L0_attn_bs"], dtype=f32)
    gate_w = np.ascontiguousarray(inp["L0_ffn_gate_w"], dtype=f32)
    w1 = inp["L0_ffn_w1"]
    w2 = inp["L0_ffn_w2"]
    w3 = inp["L0_ffn_w3"]
    Wq = np.ascontiguousarray(inp["L1_attn_Wq"], dtype=f32)
    Wk = np.ascontiguousarray(inp["L1_attn_Wk"], dtype=f32)
    ln1_g = np.ascontiguousarray(inp["L1_ffn_ln1_g"], dtype=f32)
    ln1_b = np.ascontiguousarray(inp["L1_ffn_ln1_b"], dtype=f32)
    w1L = np.ascontiguousarray(inp["L1_ffn_w1"], dtype=f32)
    b1L = np.ascontiguousarray(inp["L1_ffn_b1"], dtype=f32)
    ln2_g = np.ascontiguousarray(inp["L1_ffn_ln2_g"], dtype=f32)
    ln2_b = np.ascontiguousarray(inp["L1_ffn_ln2_b"], dtype=f32)
    w2L = np.ascontiguousarray(inp["L1_ffn_w2"], dtype=f32)
    b2L = np.ascontiguousarray(inp["L1_ffn_b2"], dtype=f32)

    idn = np.eye(128, dtype=ml_dtypes.bfloat16)
    C = np.ascontiguousarray

    def col2(v):
        return C(v.reshape(2, 128).T)

    xk = [C(x_full[:, k * SL:(k + 1) * SL, :].reshape(TOK, D)) for k in range(NCORES)]

    ncA = build_A()
    maps = [dict(x=xk[k], wd=Wd[0], wi=Wi[0], bd=col2(bd[0]), bi=col2(bi[0]),
                 al=col2(A_log[0]), idn=idn) for k in range(NCORES)]
    resA = _run(ncA, maps).results
    xnT_k = [r["xnT"] for r in resA]
    dec_full = np.concatenate([r["dec"] for r in resA], axis=2).reshape(B * N, S)
    drv_full = np.concatenate([r["drv"] for r in resA], axis=2).reshape(B * N, S)

    ncS = build_scan()
    maps = [dict(dec=C(dec_full[k * 128:(k + 1) * 128]),
                 drv=C(drv_full[k * 128:(k + 1) * 128])) for k in range(NCORES)]
    resS = _run(ncS, maps).results
    hs0 = np.concatenate([r["hs"] for r in resS], axis=0)
    f0 = C(hs0[:, -1].reshape(B, N).T)
    hs0_k = [C(hs0.reshape(B, N, S)[:, :, k * SL:(k + 1) * SL]) for k in range(NCORES)]

    ncC = build_C()
    maps = [dict(hs0=hs0_k[k], f0t=f0, wo=Wo[0], ws=Ws[0],
                 bob=C(bo[0].reshape(8, 128).T), bsb=C(bs[0].reshape(8, 128).T),
                 wd=Wd[1], wi=Wi[1], bd=col2(bd[1]), bi=col2(bi[1]),
                 al=col2(A_log[1])) for k in range(NCORES)]
    resC = _run(ncC, maps).results
    dec_full = np.concatenate([r["dec"] for r in resC], axis=2).reshape(B * N, S)
    drv_full = np.concatenate([r["drv"] for r in resC], axis=2).reshape(B * N, S)

    maps = [dict(dec=C(dec_full[k * 128:(k + 1) * 128]),
                 drv=C(drv_full[k * 128:(k + 1) * 128])) for k in range(NCORES)]
    resS = _run(ncS, maps).results
    hs1 = np.concatenate([r["hs"] for r in resS], axis=0)
    f1 = C(hs1[:, -1].reshape(B, N).T)
    hs1_k = [C(hs1.reshape(B, N, S)[:, :, k * SL:(k + 1) * SL]) for k in range(NCORES)]

    ncE1 = build_E1()
    maps = [dict(hs1=hs1_k[k], f1t=f1, wo=Wo[1], ws=Ws[1],
                 bor=C(bo[1].reshape(1, D)), bsr=C(bs[1].reshape(1, D)))
            for k in range(NCORES)]
    resE1 = _run(ncE1, maps).results
    s0_k = [r["s0"] for r in resE1]

    ncE2 = build_E2()
    gwt = C(gate_w.T)
    w1t = C(np.transpose(w1, (0, 2, 1)))
    w3t = C(np.transpose(w3, (0, 2, 1)))
    w2t = C(np.transpose(w2, (0, 2, 1)))
    maps = [dict(xnT=xnT_k[k], gwt=gwt, w1t=w1t, w3t=w3t, w2t=w2t)
            for k in range(NCORES)]
    resE2 = _run(ncE2, maps).results
    g_k = [r["g"] for r in resE2]

    ncE3 = build_E3()
    maps = [dict(x=xk[k], s0=s0_k[k], g=g_k[k], idn=idn) for k in range(NCORES)]
    resE3 = _run(ncE3, maps).results
    x1_k = [r["x1"] for r in resE3]
    xn1_k = [r["xn1"] for r in resE3]
    xn1T_k = [r["xn1T"] for r in resE3]

    ncE4 = build_E4()
    wqt = C(Wq.T)
    wkt = C(Wk.T)
    maps = [dict(xn1=xn1_k[k], xn1T=xn1T_k[k], wqt=wqt, wkt=wkt)
            for k in range(NCORES)]
    resE4 = _run(ncE4, maps).results
    s1_k = [r["s1"] for r in resE4]

    ncE5a = build_E5a()
    w1Lt = C(w1L.T)
    maps = [dict(xn1=xn1_k[k], w1t=w1Lt, b1c=C(b1L.reshape(32, 128).T),
                 g1r=C(ln1_g.reshape(1, D)), b1r=C(ln1_b.reshape(1, D)), idn=idn)
            for k in range(NCORES)]
    resE5a = _run(ncE5a, maps).results
    hG_k = [r["hG"] for r in resE5a]

    ncE5b = build_E5b()
    w2Lt = C(w2L.T)
    maps = [dict(hG=hG_k[k], w2t=w2Lt, b2r=C(b2L.reshape(1, D)),
                 g2c=C(ln2_g.reshape(32, 128).T), b2c=C(ln2_b.reshape(32, 128).T),
                 x1=x1_k[k], s1=s1_k[k]) for k in range(NCORES)]
    resE5b = _run(ncE5b, maps).results
    out_k = [r["out"] for r in resE5b]

    out = np.concatenate([o.reshape(B, SL, D) for o in out_k], axis=1)
    return np.ascontiguousarray(out, dtype=np.float32)
